# version 02
# baseline (speedup 1.0000x reference)
import numpy as np
import concourse.bass as bass
import concourse.mybir as mybir


"""kb: minimal dependency-tracking emitter for raw Bass programs.

Rules encoded here (learned from the interp race detector / docs):
  - EVERY RAW/WAW/WAR edge needs an explicit wait, same-engine included
    (exception: PE matmul accumulation into the same PSUM bank).
  - Compute ops carry .then_inc(engine_sem, 1); waits use cumulative counts.
  - Each DMA "stream" (one SBUF slot) gets its OWN semaphore, and at most
    one DMA per stream is outstanding (enforced via the WAW dep on the
    stream's previous event) so cumulative waits of 16*k are race-free.
  - HWDGE (sync/scalar) and SWDGE (gpsimd) must not share a DMA sem.
  - inc=False ops (non-final matmuls of an accumulation group) have their
    accesses tracked as "pending"; they resolve to the engine's next
    completing event, which is forced early if a conflicting access arrives.

Engines: pe (tensor), act (scalar), dve (vector), pool (gpsimd), sp (sync).
Buffers are tracked by (name, region) keys; region granularity is chosen
by the caller but must be consistent per tensor within a phase.
"""

DMA_STEP = 16
SEM_SEG = 12000

ENGINE_ATTR = {"pe": "tensor", "act": "scalar", "dve": "vector",
               "pool": "gpsimd", "sp": "sync"}
COMPUTE_ENGINES = ("pe", "act", "dve", "pool")
DMA_QUEUES = ("sp", "pool", "act")


class Prog:
    def __init__(self, nc):
        self.nc = nc
        self._sem_cm = []
        self.sems = {}
        self.cnt = {}
        self.queue = {e: [] for e in ENGINE_ATTR}
        self.waited = {e: {} for e in ENGINE_ATTR}
        self.bufs = {}
        self.stream_queue = {}
        self.pending = {e: [] for e in ENGINE_ATTR}
        self.free_fence = []
        for e in COMPUTE_ENGINES:
            self._mk_sem(e)

    def note_free(self, names):
        for key, e in list(self.bufs.items()):
            if key[0] in names:
                if e["w"]:
                    self.free_fence.append(e["w"])
                self.free_fence.extend(e["r"])
                del self.bufs[key]

    def _alloc_phys(self, label, pool_kind=None, max_floor=SEM_SEG):
        if pool_kind is not None:
            pool = self.sem_pool.setdefault(pool_kind, [])
            pool.sort(key=lambda e: e[1])
            if pool and pool[0][1] <= max_floor:
                h, floor = pool.pop(0)
                return h, floor
        self._sem_n = getattr(self, "_sem_n", 0) + 1
        cm = self.nc.semaphore(f"kb_{self._sem_n}_{label}")
        h = cm.__enter__()
        self._sem_cm.append(cm)
        return h, 0

    def _pool_kind(self, qname):
        return "sw" if qname == "pool" else "hw"

    def _mk_sem(self, name, qname=None):
        if not hasattr(self, "sem_pool"):
            self.sem_pool = {}
            self.sem_kind = {}
        kind = self._pool_kind(qname) if qname is not None else None
        h, floor = self._alloc_phys(f"{name}_0", kind)
        self.sems[name] = [[-floor, h, None]]
        self.cnt[name] = 0
        self.sem_kind[name] = kind
        return h

    def _sem_for_inc(self, name, amount):
        base, h, _ = self.sems[name][-1]
        if self.cnt[name] + amount - base > SEM_SEG:
            cnt = self.cnt[name]
            self.sems[name][-1][2] = cnt - base
            h, floor = self._alloc_phys(f"{name}_{len(self.sems[name])}",
                                        self.sem_kind.get(name),
                                        max_floor=max(0, cnt - base - 1))
            self.sems[name].append([cnt - floor, h, None])
            base, h = cnt - floor, h
        return h

    def recycle_streams(self):
        if not hasattr(self, "sem_pool"):
            self.sem_pool = {}
            self.sem_kind = {}
        dead = set()
        for stream in list(self.stream_queue.keys()):
            kind = self.sem_kind.get(stream)
            segs = self.sems[stream]
            for base, h, end_phys in segs:
                floor = end_phys if end_phys is not None else \
                    self.cnt[stream] - base
                self.sem_pool.setdefault(kind, []).append((h, floor))
            dead.add(stream)
            del self.sems[stream]
            del self.cnt[stream]
            del self.stream_queue[stream]
            self.sem_kind.pop(stream, None)
            for e in ENGINE_ATTR:
                self.waited[e].pop(stream, None)
        self.free_fence = [ev for ev in self.free_fence if ev[0] not in dead]
        for key, e in self.bufs.items():
            if e["w"] and e["w"][0] in dead:
                e["w"] = None
            e["r"] = [ev for ev in e["r"] if ev[0] not in dead]

    def _sem_for_wait(self, name, val):
        segs = self.sems[name]
        for base, h, _end in reversed(segs):
            if base < val:
                return h, val - base
        return segs[0][1], val - segs[0][0]

    def _key(self, b):
        return b if isinstance(b, tuple) else (b, 0)

    def _entry(self, key):
        if key not in self.bufs:
            self.bufs[key] = {"w": None, "r": []}
        return self.bufs[key]

    def _flush_pending(self, engine):
        pend = self.pending[engine]
        if not pend:
            return
        qi, _, _ = pend[-1]
        kind, fn, sem, inc = self.queue[engine][qi]
        assert kind == "op" and sem is None
        h = self._sem_for_inc(engine, 1)
        self.cnt[engine] += 1
        ev = (engine, self.cnt[engine])
        self.queue[engine][qi] = ("op", fn, h, 1)
        for _, rd, wr in pend:
            self._record(rd, wr, ev)
        pend.clear()

    def _resolve_conflicts(self, engine, reads, writes):
        wkeys = {self._key(b) for b in writes}
        rkeys = {self._key(b) for b in reads}
        for e in ENGINE_ATTR:
            for _, rd, wr in self.pending[e]:
                prd = {self._key(b) for b in rd}
                pwr = {self._key(b) for b in wr}
                if (wkeys & (prd | pwr)) or (rkeys & pwr):
                    self._flush_pending(e)
                    break

    def _emit_waits(self, engine, deps):
        need = {}
        for ev in deps:
            if ev is None:
                continue
            sem, val = ev
            if val <= self.waited[engine].get(sem, 0):
                continue
            if val > need.get(sem, 0):
                need[sem] = val
        for sem, val in need.items():
            h, v = self._sem_for_wait(sem, val)
            self.queue[engine].append(("wait", h, v))
            self.waited[engine][sem] = val

    def _deps_for(self, reads, writes):
        deps = []
        for b in reads:
            e = self._entry(self._key(b))
            if e["w"]:
                deps.append(e["w"])
        for b in writes:
            k = self._key(b)
            fresh = k not in self.bufs
            e = self._entry(k)
            if fresh and self.free_fence:
                deps.extend(self.free_fence)
            if e["w"]:
                deps.append(e["w"])
            deps.extend(e["r"])
        return deps

    def _record(self, reads, writes, ev):
        if ev is None:
            return
        for b in reads:
            self._entry(self._key(b))["r"].append(ev)
        for b in writes:
            e = self._entry(self._key(b))
            e["w"] = ev
            e["r"] = []

    def op(self, engine, fn, reads=(), writes=(), inc=True):
        assert engine in COMPUTE_ENGINES
        self._resolve_conflicts(engine, reads, writes)
        self._emit_waits(engine, self._deps_for(reads, writes))
        if inc:
            h = self._sem_for_inc(engine, 1)
            self.cnt[engine] += 1
            ev = (engine, self.cnt[engine])
            self.queue[engine].append(("op", fn, h, 1))
            for _, rd, wr in self.pending[engine]:
                self._record(rd, wr, ev)
            self.pending[engine].clear()
            self._record(reads, writes, ev)
        else:
            ev = None
            self.queue[engine].append(("op", fn, None, 0))
            self.pending[engine].append((len(self.queue[engine]) - 1,
                                         list(reads), list(writes)))
        return ev

    def dma_accum(self, qname, out, in_, reads=(), writes=(), stream=None):
        return self.dma(qname, out, in_, reads=reads, writes=writes, stream=stream,
                        accum=mybir.AluOpType.add)

    def dma(self, qname, out, in_, reads=(), writes=(), stream=None, accum=None):
        assert qname in DMA_QUEUES
        if stream is None:
            if writes:
                stream = "dma_" + str(self._key(writes[0])[0])
            else:
                stream = "dma_st_" + str(self._key(reads[0])[0])
        if stream not in self.sems:
            self._mk_sem(stream, qname)
            self.stream_queue[stream] = qname
        assert self.stream_queue[stream] == qname, (
            f"stream {stream} used from {qname} but bound to {self.stream_queue[stream]}")
        self._resolve_conflicts(qname, reads, writes)
        deps = self._deps_for(reads, writes)
        prev = self.cnt[stream]
        if prev:
            deps.append((stream, prev))
        self._emit_waits(qname, deps)
        h = self._sem_for_inc(stream, DMA_STEP)
        self.cnt[stream] += DMA_STEP
        ev = (stream, self.cnt[stream])
        self.queue[qname].append(("dma", out, in_, h, DMA_STEP, accum))
        self._record(reads, writes, ev)
        return ev

    def custom_dma(self, qname, emit, reads=(), writes=(), stream=None):
        assert qname in DMA_QUEUES and stream is not None
        if stream not in self.sems:
            self._mk_sem(stream, qname)
            self.stream_queue[stream] = qname
        assert self.stream_queue[stream] == qname
        self._resolve_conflicts(qname, reads, writes)
        deps = self._deps_for(reads, writes)
        prev = self.cnt[stream]
        if prev:
            deps.append((stream, prev))
        self._emit_waits(qname, deps)
        h = self._sem_for_inc(stream, 1)
        self.cnt[stream] += 1
        ev = (stream, self.cnt[stream])
        self.queue[qname].append(("op", emit, h, None))
        self._record(reads, writes, ev)
        return ev

    def dma_barrier(self):
        for stream, cnt in list(self.cnt.items()):
            if stream not in self.stream_queue:
                continue
            if cnt == 0:
                continue
            for q in DMA_QUEUES:
                if self.waited[q].get(stream, 0) >= cnt:
                    continue
                h, v = self._sem_for_wait(stream, cnt)
                self.queue[q].append(("wait", h, v))
                self.waited[q][stream] = cnt

    def barrier_read(self, engine, bufs):
        deps = []
        for b in bufs:
            e = self._entry(self._key(b))
            if e["w"]:
                deps.append(e["w"])
        self._emit_waits(engine, deps)

    def wait_event(self, engine, ev):
        self._emit_waits(engine, [ev])

    def output_fence(self):
        for stream, q in self.stream_queue.items():
            if self.cnt[stream] > self.waited[q].get(stream, 0):
                h, v = self._sem_for_wait(stream, self.cnt[stream])
                self.queue[q].append(("wait", h, v))
                self.waited[q][stream] = self.cnt[stream]

    def finalize(self, block):
        for engine, items in self.queue.items():
            if not items:
                continue

            def make_section(items=items):
                def section(eng):
                    for it in items:
                        if it[0] == "wait":
                            eng.wait_ge(it[1], it[2])
                        elif it[0] == "op":
                            inst = it[1](eng)
                            if it[2] is not None:
                                inst.then_inc(it[2], it[3])
                        elif it[5] is None:
                            eng.dma_start(out=it[1], in_=it[2]).then_inc(it[3], it[4])
                        else:
                            eng.dma_start(out=it[1], in_=it[2],
                                          accum_op=it[5]).then_inc(it[3], it[4])
                return section
            getattr(block, ENGINE_ATTR[engine])(make_section())

"""klib: block emitters for the mk_seq_preln_d4 megakernel (per-core programs).

Conventions:
  - feature-major ("fm") activations: [128 features, T=4096 tokens] tiles,
    full sequence in the free dim; d_model=1024 -> 8 tiles.
  - residual x lives in HBM as [1024, 4096] fp32 (host pre-transposed).
  - weights in HBM as [in, out] fp32 (host pre-transposed); streamed to SBUF
    via gpsimd cast-DMA to bf16.
  - per-feature vectors host-packed as [128, n_tiles] ("pcol"): column j =
    feature-tile j's 128 values.
  - matmul: out = lhsT.T @ rhs, operands at partition base 0 always.
  - SBUF frees must be LIFO; Ctx.aget/afree implements arena reuse instead
    (kb free_fence handles aliasing edges).
"""

F32 = mybir.dt.float32
BF16 = mybir.dt.bfloat16
AF = mybir.ActivationFunctionType
ALU = mybir.AluOpType
AX = mybir.AxisListType

T = 4096
NTC = T // 512


class Ctx:
    def __init__(self, nc, p, stack):
        self.nc = nc
        self.p = p
        self.stack = stack
        self.n = 0
        self.ps = []
        self.ones_col = None
        self.ones_row = None
        self._afree, self._abusy = {}, {}
        self._consts = {}

    def uniq(self, pref):
        self.n += 1
        return f"{pref}{self.n}"

    def sb(self, name, shape, dtype):
        import os as _os
        if _os.environ.get("ARENA_LOG"):
            dsz = 4 if dtype == F32 else 2
            print(f"[sb] {name} {shape} {shape[1] * dsz}B rem={self.nc.sbuf_bytes_remaining}")
        return self.stack.enter_context(self.nc.sbuf_tensor(name, shape, dtype))

    ARENA_B = 65536
    GRAN = 256

    def _arena_new(self, min_bytes):
        self.n += 1
        rem = self.nc.sbuf_bytes_remaining
        size = min(self.ARENA_B, (rem - 512) // self.GRAN * self.GRAN)
        size = max(size, min_bytes)
        h = self.sb(f"arena{self.n}", [128, size // 4], F32)
        a = {"h": h, "free": [(0, size)], "size": size}
        self._arenas.append(a)
        busy = sum(v[2] for v in self._abusy.values())
        import sys
        print(f"[arena] new {size}B (total arenas {sum(x['size'] for x in self._arenas)}B, busy {busy}B)", file=sys.stderr)
        return a

    def aget(self, name, shape, dtype):
        if not hasattr(self, "_arenas"):
            self._arenas = []
        assert name not in self._abusy, name
        dsz = 4 if dtype == F32 else 2
        need = -(-shape[1] * dsz // self.GRAN) * self.GRAN
        assert need <= self.ARENA_B, (name, shape)
        spot = None
        for a in self._arenas:
            for fi, (off, size) in enumerate(a["free"]):
                if size >= need:
                    spot = (a, fi, off)
                    break
            if spot:
                break
        if spot is None:
            a = self._arena_new(need)
            spot = (a, 0, 0)
        a, fi, off = spot
        o2, s2 = a["free"][fi]
        if s2 == need:
            a["free"].pop(fi)
        else:
            a["free"][fi] = (o2 + need, s2 - need)
        ap = a["h"][:].bitcast(dtype)[0:shape[0], off // dsz:off // dsz + shape[1]]
        self._abusy[name] = (a, off, need, ap)
        import os as _os
        if _os.environ.get("ARENA_LOG"):
            ai = self._arenas.index(a)
            print(f"[aget] {name} arena{ai} bytes {off}..{off + need} shape {shape} "
                  f"dsz {dsz}")
        return ap

    def afree(self, *names):
        self.p.note_free(list(names))
        for n in names:
            if n in self._abusy:
                a, off, need, _ = self._abusy.pop(n)
                a["free"].append((off, need))
                a["free"].sort()
                merged = []
                for o, s in a["free"]:
                    if merged and merged[-1][0] + merged[-1][1] == o:
                        merged[-1] = (merged[-1][0], merged[-1][1] + s)
                    else:
                        merged.append((o, s))
                a["free"] = [tuple(x) for x in merged]

    def psum(self, name, shape, dtype=F32):
        return self.stack.enter_context(self.nc.psum_tensor(name, shape, dtype))

    def afree_all(self):
        if self._abusy:
            self.afree(*list(self._abusy.keys()))

    def init_common(self):
        p = self.p
        for k in range(7):
            self.ps.append(self.psum(f"psb{k}", [128, 512], F32))
        self.ones_col = self.sb("ones_col", [128, 1], BF16)
        self.ones_row = self.sb("ones_row", [1, 128], BF16)
        self.ones_rowf = self.sb("ones_rowf", [1, 128], F32)
        self.ones512 = self.sb("ones512", [1, 512], BF16)
        self.pstT = self.psum("pstT", [128, 512], BF16)
        p.op("dve", lambda v: v.memset(self.ones_col[:], 1.0), writes=["ones_col"])
        p.op("dve", lambda v: v.memset(self.ones_row[:], 1.0), writes=["ones_row"])
        p.op("dve", lambda v: v.memset(self.ones_rowf[:], 1.0), writes=["ones_rowf"])
        p.op("dve", lambda v: v.memset(self.ones512[:], 1.0), writes=["ones512"])

    def cf(self, value):
        if value not in self._consts:
            name = f"cst{len(self._consts)}"
            t = self.sb(name, [128, 1], F32)
            self.p.op("dve", lambda v: v.memset(t[:], float(value)), writes=[name])
            self._consts[value] = (t, name)
        return self._consts[value]


def load_pcol(ctx, name, hbm_ap, ntiles):
    t = ctx.aget(name, [128, ntiles], F32)
    ctx.p.dma("sp", t[:], hbm_ap, writes=[name])
    return t


def load_xbf(ctx, name, x_hbm, ntiles=8, reads=(), dtype=BF16, queue="pool"):
    tiles = []
    for i in range(ntiles):
        tn = f"{name}{i}"
        t = ctx.aget(tn, [128, T], dtype)
        rd = [(reads, i)] if reads else []
        rd = [(reads[0], i)] if reads else []
        ctx.p.dma(queue, t[:], x_hbm[i * 128:(i + 1) * 128, :], reads=rd, writes=[tn])
        tiles.append(t)
    return tiles


def bcast_row_bf(ctx, name, row_ap, row_key, out=None, psbank=6, n=T):
    p = ctx.p
    t = out if out is not None else ctx.aget(name, [128, n], BF16)
    ps = ctx.ps[psbank]
    pskey = f"psb{psbank}"
    for tc in range(n // 512):
        sl = slice(tc * 512, (tc + 1) * 512)
        p.op("pe", lambda t_, sl=sl: t_.matmul(out=ps[:, 0:512], lhsT=ctx.ones_row[:],
             rhs=row_ap[0:1, sl], start=True, stop=True),
             reads=["ones_row", row_key], writes=[pskey])
        p.op("act", lambda s, sl=sl: s.copy(out=t[:, sl], in_=ps[:, 0:512]),
             reads=[pskey], writes=[name])
    return t


def ln_rows(ctx, xbf_tiles, xkeys, kind="ln", eps=1e-5, d=1024,
            banks=(5, 6), name=None, dscale=1.0, plain_eps=None):
    p = ctx.p
    nm = name or ctx.uniq("lnr")
    ntiles = len(xbf_tiles)
    sq = [ctx.aget(f"{nm}_sq{j}", [128, 512], BF16) for j in range(2)]
    a_row = ctx.aget(f"{nm}_a", [1, T], BF16)
    mu_row = ctx.aget(f"{nm}_mu", [1, T], BF16) if kind == "ln" else None
    var_f = ctx.aget(f"{nm}_vf", [1, 512], F32)
    a2_row = ctx.aget(f"{nm}_a2", [1, T], BF16) if plain_eps is not None else None
    v2_f = ctx.aget(f"{nm}_v2", [1, 512], F32) if plain_eps is not None else None
    mu_f = ctx.aget(f"{nm}_mf", [1, 512], F32) if kind == "ln" else None
    m2_f = ctx.aget(f"{nm}_m2", [1, 512], F32) if kind == "ln" else None
    psS = ctx.ps[banks[0]]
    psM = ctx.ps[banks[1]] if kind == "ln" else None
    kS, kM = f"psb{banks[0]}", f"psb{banks[1]}"
    for tc in range(NTC):
        sl = slice(tc * 512, (tc + 1) * 512)
        for i in range(ntiles):
            sqb = sq[i % 2]
            sqk = f"{nm}_sq{i % 2}"
            p.op("act", lambda s, i=i, sl=sl, sqb=sqb: s.activation(out=sqb[:],
                 in_=xbf_tiles[i][:, sl], func=AF.Square), reads=[xkeys[i]], writes=[sqk])
            p.op("pe", lambda t_, i=i, sqb=sqb: t_.matmul(out=psS[0:1, 0:512],
                 lhsT=ctx.ones_col[:], rhs=sqb[:], start=(i == 0), stop=(i == ntiles - 1)),
                 reads=[sqk, "ones_col"], writes=[kS], inc=(i == ntiles - 1))
            if kind == "ln":
                p.op("pe", lambda t_, i=i, sl=sl: t_.matmul(out=psM[0:1, 0:512],
                     lhsT=ctx.ones_col[:], rhs=xbf_tiles[i][:, sl],
                     start=(i == 0), stop=(i == ntiles - 1)),
                     reads=[xkeys[i], "ones_col"], writes=[kM], inc=(i == ntiles - 1))
        if kind == "ln":
            p.op("dve", lambda v: v.tensor_scalar(out=mu_f[0:1, :], in0=psM[0:1, 0:512],
                 scalar1=1.0 / d, scalar2=None, op0=ALU.mult), reads=[kM], writes=[f"{nm}_mf"])
            p.op("dve", lambda v: v.tensor_scalar(out=var_f[0:1, :], in0=psS[0:1, 0:512],
                 scalar1=dscale / d, scalar2=None, op0=ALU.mult), reads=[kS], writes=[f"{nm}_vf"])
            p.op("dve", lambda v: v.scalar_tensor_tensor(out=m2_f[0:1, :],
                 in0=mu_f[0:1, :], scalar=-dscale, in1=mu_f[0:1, :], op0=ALU.mult,
                 op1=ALU.mult), reads=[f"{nm}_mf"], writes=[f"{nm}_m2"])
            p.op("dve", lambda v: v.tensor_tensor(out=var_f[0:1, :], in0=var_f[0:1, :],
                 in1=m2_f[0:1, :], op=ALU.add), reads=[f"{nm}_vf", f"{nm}_m2"],
                 writes=[f"{nm}_vf"])
            cneg, ckneg = ctx.cf(-1.0)
            p.op("act", lambda s, sl=sl: s.activation(out=mu_row[0:1, sl], in_=mu_f[0:1, :],
                 func=AF.Identity, scale=cneg[0:1, 0:1]), reads=[f"{nm}_mf", ckneg],
                 writes=[nm + "_mu"])
        else:
            p.op("dve", lambda v: v.tensor_scalar(out=var_f[0:1, :], in0=psS[0:1, 0:512],
                 scalar1=1.0 / d, scalar2=None, op0=ALU.mult), reads=[kS], writes=[f"{nm}_vf"])
        ce, cke = ctx.cf(float(eps))
        ch, ckh = ctx.cf(-0.5)
        if plain_eps is not None:
            c2, ck2 = ctx.cf(float(plain_eps))
            p.op("act", lambda s: s.activation(out=v2_f[0:1, :], in_=var_f[0:1, :],
                 func=AF.Ln, bias=c2[0:1, 0:1]), reads=[f"{nm}_vf", ck2],
                 writes=[f"{nm}_v2"])
            p.op("act", lambda s, sl=sl: s.activation(out=a2_row[0:1, sl], in_=v2_f[0:1, :],
                 func=AF.Exp, scale=ch[0:1, 0:1]), reads=[f"{nm}_v2", ckh],
                 writes=[nm + "_a2"])
        p.op("act", lambda s: s.activation(out=var_f[0:1, :], in_=var_f[0:1, :],
             func=AF.Ln, bias=ce[0:1, 0:1]), reads=[f"{nm}_vf", cke], writes=[f"{nm}_vf"])
        p.op("act", lambda s, sl=sl: s.activation(out=a_row[0:1, sl], in_=var_f[0:1, :],
             func=AF.Exp, scale=ch[0:1, 0:1]), reads=[f"{nm}_vf", ckh], writes=[nm + "_a"])
    ctx.afree(f"{nm}_sq0", f"{nm}_sq1", f"{nm}_vf",
              *([f"{nm}_mf", f"{nm}_m2"] if kind == "ln" else []),
              *([f"{nm}_v2"] if plain_eps is not None else []))
    if plain_eps is not None:
        return a_row, nm + "_a", mu_row, nm + "_mu", a2_row, nm + "_a2"
    return a_row, nm + "_a", mu_row, (nm + "_mu" if kind == "ln" else None)


def ln_materialize(ctx, xbf_tiles, xkeys, a_row, a_key, negmu_row, negmu_key,
                   gamma=None, gamma_key=None, beta=None, beta_key=None,
                   name=None, inplace=False, free_rows=True):
    p = ctx.p
    nm = name or ctx.uniq("xln")
    a_b = bcast_row_bf(ctx, nm + "_ab", a_row, a_key, psbank=6)
    mu_b = None
    if negmu_row is not None:
        mu_b = bcast_row_bf(ctx, nm + "_mb", negmu_row, negmu_key, psbank=5)
    tiles, keys = [], []
    for i in range(len(xbf_tiles)):
        if inplace:
            t, tn = xbf_tiles[i], xkeys[i]
        else:
            tn = f"{nm}{i}"
            t = ctx.aget(tn, [128, T], BF16)
        if mu_b is not None:
            p.op("dve", lambda v, i=i, t=t, tn=tn: v.tensor_tensor(out=t[:],
                 in0=xbf_tiles[i][:], in1=mu_b[:], op=ALU.add),
                 reads=[xkeys[i], nm + "_mb"], writes=[tn])
            p.op("dve", lambda v, t=t, tn=tn: v.tensor_tensor(out=t[:], in0=t[:],
                 in1=a_b[:], op=ALU.mult), reads=[tn, nm + "_ab"], writes=[tn])
        else:
            p.op("dve", lambda v, i=i, t=t, tn=tn: v.tensor_tensor(out=t[:],
                 in0=xbf_tiles[i][:], in1=a_b[:], op=ALU.mult),
                 reads=[xkeys[i], nm + "_ab"], writes=[tn])
        if gamma is not None:
            bk = [beta_key] if beta is not None else []
            p.op("act", lambda s, t=t, i=i, tn=tn: s.activation(out=t[:], in_=t[:],
                 func=AF.Identity, scale=gamma[:, i:i + 1],
                 bias=(beta[:, i:i + 1] if beta is not None else 0.0)),
                 reads=[tn, gamma_key] + bk, writes=[tn])
        tiles.append(t)
        keys.append(tn)
    ctx.afree(nm + "_ab")
    if mu_b is not None:
        ctx.afree(nm + "_mb")
    if free_rows:
        ctx.afree(a_key, *([negmu_key] if negmu_key else []))
    return tiles, keys


def emit_L0F(ctx, d):
    p = ctx.p
    xbf = load_xbf(ctx, "l0f_x", d["x"])
    xkeys = [f"l0f_x{i}" for i in range(8)]
    scale = load_pcol(ctx, "l0f_sc", d["scale_pc"], 8)
    bias = load_pcol(ctx, "l0f_bi", d["bias_pc"], 8)
    a_row, ak, mu_row, mk = ln_rows(ctx, xbf, xkeys, kind="ln", eps=1e-5, name="l0f_ln")
    xln, lnkeys = ln_materialize(ctx, xbf, xkeys, a_row, ak, mu_row, mk,
                                 name="l0f_xln", inplace=True)

    wsb = []
    for i in range(8):
        tn = f"l0f_ws{i}"
        t = ctx.aget(tn, [128, 1024], BF16)
        p.dma("pool", t[:], d["wT"][i * 128:(i + 1) * 128, :], writes=[tn])
        p.op("dve", lambda v, t=t, tn=tn: v.tensor_scalar(out=t[:], in0=t[:], scalar1=0.0,
             scalar2=2.0, op0=ALU.is_ge, op1=ALU.mult), reads=[tn], writes=[tn])
        p.op("dve", lambda v, t=t, tn=tn: v.tensor_scalar(out=t[:], in0=t[:], scalar1=-1.0,
             scalar2=None, op0=ALU.add), reads=[tn], writes=[tn])
        wsb.append(t)

    xf32 = [ctx.aget(f"l0f_xf{j}", [128, T], F32) for j in range(2)]
    outt = [ctx.aget(f"l0f_o{i}", [128, T], F32) for i in range(2)]

    for od in range(8):
        ot, ok = outt[od % 2], f"l0f_o{od % 2}"
        xk = f"l0f_xf{od % 2}"
        p.dma("sp", xf32[od % 2][:], d["x"][od * 128:(od + 1) * 128, :], writes=[xk])
        for tc in range(NTC):
            sl = slice(tc * 512, (tc + 1) * 512)
            bank = (od * NTC + tc) % 4
            ps, pk = ctx.ps[bank], f"psb{bank}"
            for i in range(8):
                p.op("pe", lambda t_, i=i, sl=sl, od=od, ps=ps:
                     t_.matmul(out=ps[:, 0:512], lhsT=wsb[i][:, od * 128:(od + 1) * 128],
                               rhs=xln[i][:, sl], start=(i == 0), stop=(i == 7)),
                     reads=[f"l0f_ws{i}", lnkeys[i]], writes=[pk], inc=(i == 7))
            p.op("act", lambda s, sl=sl, od=od, ot=ot, ps=ps: s.activation(out=ot[:, sl],
                 in_=ps[:, 0:512], func=AF.Identity, bias=bias[:, od:od + 1],
                 scale=scale[:, od:od + 1]), reads=[pk, "l0f_sc", "l0f_bi"], writes=[ok])
            p.op("dve", lambda v, sl=sl, ot=ot: v.tensor_scalar(out=ot[:, sl],
                 in0=ot[:, sl], scalar1=100.0, scalar2=-100.0, op0=ALU.min, op1=ALU.max),
                 reads=[ok], writes=[ok])
            p.op("dve", lambda v, sl=sl, ot=ot, j=od % 2: v.tensor_tensor(out=ot[:, sl],
                 in0=ot[:, sl], in1=xf32[j][:, sl], op=ALU.add), reads=[ok, xk], writes=[ok])
        p.dma("sp", d["xout"][od * 128:(od + 1) * 128, :], ot[:], reads=[ok],
              stream="dma_st_l0f")
    ctx.afree(*xkeys, *[f"l0f_ws{i}" for i in range(8)], "l0f_xf0", "l0f_xf1",
              "l0f_o0", "l0f_o1")


def host_pcol(v):
    return np.ascontiguousarray(np.asarray(v).reshape(-1, 128).T.astype(np.float32))


def build_block(emit_fn, dram_specs, host_prep, core_ids=range(8)):
    from contextlib import ExitStack
    nc = bass.Bass("TRN2", target_bir_lowering=False)
    d = {}
    for name, (shape, dt, kind) in dram_specs.items():
        d[name] = nc.dram_tensor(name, list(shape), dt, kind=kind).ap()
    with ExitStack() as stack:
        p = Prog(nc)
        ctx = Ctx(nc, p, stack)
        ctx.init_common()
        emit_fn(ctx, d)
        p.output_fence()
        block = stack.enter_context(nc.Block())
        p.finalize(block)
    in_maps = [host_prep(c) for c in core_ids]
    return nc, in_maps

"""L0_attn block emitter: GLA block with conv path, per-head decay scans,
output gating and an internal gated FFN (inter=2816).

x_out = x_mid + ffn(rms2(x_mid));  x_mid = x + gmix*local + (1-gmix)*attn
where the block input is h0 = LN(x) and h = rmsnorm(h0, norm1_w, 1e-6)
(fused into one scaled LN here).
"""

NH = 8
INTER = 2816
NI = INTER // 128


def mac(p, ps, pk, pairs, n=512):
    last = len(pairs) - 1
    for j, (la, lk, ra, rk) in enumerate(pairs):
        p.op("pe", lambda t_, la=la, ra=ra, j=j: t_.matmul(
            out=ps[:, 0:n], lhsT=la, rhs=ra, start=(j == 0), stop=(j == last)),
            reads=[lk, rk], writes=[pk], inc=(j == last))


def wload(ctx, name, w_hbm, IN, col0, ncols):
    n_in = IN // 128
    if name in ctx._abusy:
        t = ctx._abusy[name][3]
    else:
        t = ctx.aget(name, [128, n_in * ncols], BF16)
    src = w_hbm[:, col0:col0 + ncols].rearrange("(i p) c -> p i c", p=128)
    ctx.p.dma("pool", t[:].rearrange("p (i c) -> p i c", c=ncols), src, writes=[name])
    return t


def wslice(wt, i, ncols):
    return wt[:, i * ncols:(i + 1) * ncols]


DBG_SLOTS = {}


def dbg(ctx, d, name, ap, key, queue="pool"):
    import os
    if "dbg" not in d or not os.environ.get("L0A_DBG"):
        return
    slot = DBG_SLOTS.setdefault(name, len(DBG_SLOTS))
    ctx.p.dma(queue, d["dbg"][slot, :, :], ap, reads=[key], stream="dma_dbg_" + queue)


def emit_L0A(ctx, d):
    p = ctx.p
    nc = ctx.nc
    vsp = nc.dram_tensor("A_vsp", [1024, T], BF16, kind="Internal").ap()
    lsp = nc.dram_tensor("A_lsp", [1024, T], BF16, kind="Internal").ap()
    xmid = nc.dram_tensor("A_xmid", [1024, T], F32, kind="Internal").ap()
    xsum = nc.dram_tensor("A_xsum", [1024, T], F32, kind="Internal").ap()
    dsp = nc.dram_tensor("A_dsp", [1024, T], BF16, kind="Internal").ap()
    msp = nc.dram_tensor("A_msp", [INTER, T], BF16, kind="Internal").ap()

    g1 = load_pcol(ctx, "A_g1", d["norm1_pc"], 8)
    g2 = load_pcol(ctx, "A_g2", d["norm2_pc"], 8)
    cw = load_pcol(ctx, "A_cw", d["convw_pc"], 32)
    cb = load_pcol(ctx, "A_cb", d["convb_pc"], 8)
    ogb = load_pcol(ctx, "A_ogb", d["outgateb_pc"], 8)
    dp = ctx.aget("A_dp", [1, 8], F32)
    p.dma("sp", dp[:], d["decay"], writes=["A_dp"])
    dec8 = ctx.aget("A_dec8", [1, 8], F32)
    p.op("act", lambda s: s.activation(out=dec8[:], in_=dp[:], func=AF.Sigmoid),
         reads=["A_dp"], writes=["A_dec8"])

    xbf = load_xbf(ctx, "A_x", d["x"])
    hk = [f"A_x{i}" for i in range(8)]
    a_row, ak, mu_row, mk, a2_row, a2k = ln_rows(ctx, xbf, hk, kind="ln", eps=1e-11,
                                                 dscale=1.000001, name="A_ln",
                                                 plain_eps=1e-5)
    h, _ = ln_materialize(ctx, xbf, hk, a_row, ak, mu_row, mk, gamma=g1,
                          gamma_key="A_g1", name="A_hm", inplace=True,
                          free_rows=False)
    ctx.afree(ak)

    for i in range(8):
        dbg(ctx, d, f"h{i}", h[i][:], hk[i])
    cacc = [ctx.aget(f"A_cv{j}", [128, T], BF16) for j in (0, 1)]
    sc = [ctx.aget(f"A_sc{i}", [128, T], BF16) for i in range(8)]
    for i in range(8):
        cv, ck = cacc[i % 2], f"A_cv{i % 2}"
        p.op("dve", lambda v, i=i, cv=cv: v.tensor_scalar(out=cv[:], in0=h[i][:],
             scalar1=cw[:, 24 + i:25 + i], scalar2=None, op0=ALU.mult),
             reads=[hk[i], "A_cw"], writes=[ck])
        for k in (1, 2, 3):
            tap = 3 - k
            p.op("dve", lambda v, i=i, k=k, tap=tap, cv=cv: v.scalar_tensor_tensor(
                 out=cv[:, k:T], in0=h[i][:, 0:T - k], scalar=cw[:, tap * 8 + i:tap * 8 + i + 1],
                 in1=cv[:, k:T], op0=ALU.mult, op1=ALU.add),
                 reads=[hk[i], ck, "A_cw"], writes=[ck])
        p.op("act", lambda s, i=i, cv=cv: s.activation(out=sc[i][:], in_=cv[:],
             func=AF.Silu, bias=cb[:, i:i + 1]), reads=[ck, "A_cb"], writes=[f"A_sc{i}"])
    ctx.afree("A_cv0", "A_cv1")
    lo = [ctx.aget(f"A_lo{j}", [128, T], BF16) for j in (0, 1)]
    for od in range(8):
        wt = wload(ctx, f"A_pw{od % 2}", d["pw_wT"], 1024, od * 128, 128)
        lt, lk = lo[od % 2], f"A_lo{od % 2}"
        for tc in range(NTC):
            bank = tc % 2
            ps, pk = ctx.ps[bank], f"psb{bank}"
            mac(p, ps, pk, [(wslice(wt, i, 128), f"A_pw{od % 2}",
                             sc[i][:, tc * 512:(tc + 1) * 512], f"A_sc{i}") for i in range(8)])
            p.op("act", lambda s, lt=lt, tc=tc, ps=ps: s.copy(
                 out=lt[:, tc * 512:(tc + 1) * 512], in_=ps[:, 0:512]),
                 reads=[pk], writes=[lk])
        dbg(ctx, d, f"local{od}", lt[:], lk)
        p.dma("pool", lsp[od * 128:(od + 1) * 128, :], lt[:], reads=[lk], writes=[("d_lsp", od)], stream="dma_st_lsp")
    ctx.afree("A_lo0", "A_lo1", "A_pw0", "A_pw1", *[f"A_sc{i}" for i in range(8)])

    ot = [ctx.aget(f"A_o{hd}", [128, T], BF16) for hd in range(NH)]
    tA = ctx.aget("A_tA", [128, T], BF16)
    qb = ctx.aget("A_qb", [128, T], BF16)
    kb2 = ctx.aget("A_kb", [128, T], BF16)
    vb = ctx.aget("A_vb", [128, T], BF16)
    skb = ctx.aget("A_skb", [128, T], BF16)
    dfs = ctx.aget("A_dfs", [1, 512], F32)
    dech = ctx.aget("A_dech", [128, 1], F32)

    for hd in range(NH):
        p.op("pe", lambda t_, hd=hd: t_.matmul(out=ctx.ps[4][:, 0:1], lhsT=ctx.ones_rowf[:],
             rhs=dec8[0:1, hd:hd + 1], start=True, stop=True),
             reads=["ones_rowf", "A_dec8"], writes=["psb4"])
        p.op("dve", lambda v: v.tensor_copy(out=dech[:], in_=ctx.ps[4][:, 0:1]),
             reads=["psb4"], writes=["A_dech"])

        for part, colbase, dst, dk in (("q", hd * 128, qb, "A_qb"),
                                       ("k", 1024 + hd * 128, kb2, "A_kb"),
                                       ("v", 2048 + hd * 128, vb, "A_vb")):
            wt = wload(ctx, f"A_wqkv{['q', 'k', 'v'].index(part) % 2}", d["qkv_wT"],
                       1024, colbase, 128)
            wk = f"A_wqkv{['q', 'k', 'v'].index(part) % 2}"
            for tc in range(NTC):
                bank = tc % 2
                ps, pk = ctx.ps[bank], f"psb{bank}"
                mac(p, ps, pk, [(wslice(wt, i, 128), wk,
                                 h[i][:, tc * 512:(tc + 1) * 512], hk[i]) for i in range(8)])
                p.op("act", lambda s, tc=tc, ps=ps, dst=dst: s.copy(
                     out=dst[:, tc * 512:(tc + 1) * 512], in_=ps[:, 0:512]),
                     reads=[pk], writes=[dk])
            if part == "v":
                p.dma("pool", vsp[hd * 128:(hd + 1) * 128, :], vb[:], reads=["A_vb"],
                      writes=[("d_vsp", hd)], stream="dma_st_vsp")
            else:
                p.op("dve", lambda v, dst=dst: v.tensor_scalar(out=tA[:], in0=dst[:],
                     scalar1=0.0, scalar2=None, op0=ALU.min), reads=[dk], writes=["A_tA"])
                p.op("act", lambda s: s.activation(out=skb[:], in_=tA[:], func=AF.Exp),
                     reads=["A_tA"], writes=["A_skb"])
                p.op("dve", lambda v, dst=dst: v.tensor_scalar(out=dst[:], in0=dst[:],
                     scalar1=0.0, scalar2=None, op0=ALU.max), reads=[dk], writes=[dk])
                p.op("dve", lambda v, dst=dst: v.tensor_tensor(out=dst[:], in0=dst[:],
                     in1=skb[:], op=ALU.add), reads=[dk, "A_skb"], writes=[dk])
        if hd == 0:
            dbg(ctx, d, "q0", qb[:], "A_qb")
            dbg(ctx, d, "k0", kb2[:], "A_kb")
            dbg(ctx, d, "v0", vb[:], "A_vb")
        p.op("dve", lambda v: v.tensor_tensor_scan(out=skb[:],
             data0=dech[:, 0:1].broadcast_to((128, T)), data1=kb2[:], initial=0.0,
             op0=ALU.mult, op1=ALU.add), reads=["A_dech", "A_kb"], writes=["A_skb"])
        p.op("dve", lambda v: v.tensor_tensor(out=tA[:], in0=qb[:], in1=skb[:],
             op=ALU.mult), reads=["A_qb", "A_skb"], writes=["A_tA"])
        p.op("dve", lambda v: v.tensor_tensor(out=vb[:], in0=kb2[:], in1=vb[:],
             op=ALU.mult), reads=["A_kb", "A_vb"], writes=["A_vb"])
        p.op("dve", lambda v: v.tensor_tensor_scan(out=skb[:],
             data0=dech[:, 0:1].broadcast_to((128, T)), data1=vb[:], initial=0.0,
             op0=ALU.mult, op1=ALU.add), reads=["A_dech", "A_vb"], writes=["A_skb"])
        p.op("dve", lambda v: v.tensor_tensor(out=qb[:], in0=qb[:], in1=skb[:],
             op=ALU.mult), reads=["A_qb", "A_skb"], writes=["A_qb"])
        for tc in range(NTC):
            sl = slice(tc * 512, (tc + 1) * 512)
            ps, pk = ctx.ps[4], "psb4"
            p.op("pe", lambda t_, sl=sl, ps=ps: t_.matmul(out=ps[0:1, 0:512],
                 lhsT=ctx.ones_col[:], rhs=tA[:, sl], start=True, stop=True),
                 reads=["A_tA", "ones_col"], writes=[pk])
            p.op("dve", lambda v, ps=ps: v.tensor_scalar(out=dfs[0:1, :],
                 in0=ps[0:1, 0:512], scalar1=1e-6, scalar2=None, op0=ALU.max),
                 reads=[pk], writes=["A_dfs"])
            p.op("dve", lambda v: v.reciprocal(out=dfs[0:1, :], in_=dfs[0:1, :]),
                 reads=["A_dfs"], writes=["A_dfs"])
            p.op("pe", lambda t_, ps=ps: t_.matmul(out=ps[:, 0:512],
                 lhsT=ctx.ones_rowf[:], rhs=dfs[0:1, :], start=True, stop=True),
                 reads=["ones_rowf", "A_dfs"], writes=[pk])
            p.op("dve", lambda v, hd=hd, sl=sl, ps=ps: v.tensor_tensor(
                 out=ot[hd][:, sl], in0=qb[:, sl], in1=ps[:, 0:512], op=ALU.mult),
                 reads=["A_qb", pk], writes=[f"A_o{hd}"])

    for i in range(8):
        dbg(ctx, d, f"o{i}", ot[i][:], f"A_o{i}")
    ctx.afree("A_tA", "A_qb", "A_kb", "A_vb", "A_skb",
              "A_dfs", "A_dech", "A_wqkv0", "A_wqkv1",
              *[f"A_x{i}" for i in range(8)])

    gt = [ctx.aget(f"A_gt{od}", [128, T], BF16) for od in range(8)]
    vre = [ctx.aget(f"A_vr{j}", [128, T], BF16) for j in (0, 1)]
    for od in range(8):
        wt = wload(ctx, f"A_wog{od % 2}", d["outgate_wT"], 1024, od * 128, 128)
        wk = f"A_wog{od % 2}"
        g, gk = gt[od], f"A_gt{od}"
        for tc in range(NTC):
            bank = tc % 2
            ps, pk = ctx.ps[bank], f"psb{bank}"
            mac(p, ps, pk, [(wslice(wt, i, 128), wk,
                             ot[i][:, tc * 512:(tc + 1) * 512], f"A_o{i}") for i in range(8)])
            p.op("act", lambda s, g=g, tc=tc, ps=ps, od=od: s.activation(
                 out=g[:, tc * 512:(tc + 1) * 512], in_=ps[:, 0:512], func=AF.Sigmoid,
                 bias=ogb[:, od:od + 1]), reads=[pk, "A_ogb"], writes=[gk])
    for od in range(8):
        g, gk = gt[od], f"A_gt{od}"
        vr, vk = vre[od % 2], f"A_vr{od % 2}"
        p.dma("pool", vr[:], vsp[od * 128:(od + 1) * 128, :], reads=[("d_vsp", od)], writes=[vk])
        p.op("dve", lambda v, od=od, vr=vr: v.tensor_tensor(out=ot[od][:], in0=ot[od][:],
             in1=vr[:], op=ALU.subtract), reads=[f"A_o{od}", vk], writes=[f"A_o{od}"])
        p.op("dve", lambda v, od=od, g=g: v.tensor_tensor(out=ot[od][:], in0=ot[od][:],
             in1=g[:], op=ALU.mult), reads=[f"A_o{od}", gk], writes=[f"A_o{od}"])
        p.op("dve", lambda v, od=od, vr=vr: v.tensor_tensor(out=ot[od][:], in0=ot[od][:],
             in1=vr[:], op=ALU.add), reads=[f"A_o{od}", vk], writes=[f"A_o{od}"])
    ctx.afree(*[f"A_gt{od}" for od in range(8)], "A_vr0", "A_vr1", "A_wog0", "A_wog1")

    at = [ctx.aget(f"A_at{od}", [128, T], BF16) for od in range(8)]
    for od in range(8):
        wt = wload(ctx, f"A_wop{od % 2}", d["outproj_wT"], 1024, od * 128, 128)
        wk = f"A_wop{od % 2}"
        for tc in range(NTC):
            bank = tc % 2
            ps, pk = ctx.ps[bank], f"psb{bank}"
            mac(p, ps, pk, [(wslice(wt, i, 128), wk,
                             ot[i][:, tc * 512:(tc + 1) * 512], f"A_o{i}") for i in range(8)])
            p.op("act", lambda s, od=od, tc=tc, ps=ps: s.copy(
                 out=at[od][:, tc * 512:(tc + 1) * 512], in_=ps[:, 0:512]),
                 reads=[pk], writes=[f"A_at{od}"])
    ctx.afree("A_wop0", "A_wop1", *[f"A_o{i}" for i in range(8)])

    lo_t = []
    for i in range(8):
        tn = f"A_lr{i}"
        tt = ctx.aget(tn, [128, T], BF16)
        p.dma("pool", tt[:], lsp[i * 128:(i + 1) * 128, :], reads=[("d_lsp", i)], writes=[tn])
        lo_t.append(tt)
    lok = [f"A_lr{i}" for i in range(8)]
    gm = [ctx.aget(f"A_gm{j}", [128, T], BF16) for j in (0, 1)]
    tD = [ctx.aget(f"A_tD{j}", [128, T], BF16) for j in (0, 1)]
    for od in range(8):
        wt = wload(ctx, f"A_wgx{od % 2}", d["gate_wT"], 2048, od * 128, 128)
        wk = f"A_wgx{od % 2}"
        g, gk = gm[od % 2], f"A_gm{od % 2}"
        ins = [(lo_t[i], lok[i]) for i in range(8)] + [(at[i], f"A_at{i}") for i in range(8)]
        for tc in range(NTC):
            bank = tc % 2
            ps, pk = ctx.ps[bank], f"psb{bank}"
            mac(p, ps, pk, [(wslice(wt, i, 128), wk,
                             ins[i][0][:, tc * 512:(tc + 1) * 512], ins[i][1])
                            for i in range(16)])
            p.op("act", lambda s, g=g, tc=tc, ps=ps: s.activation(
                 out=g[:, tc * 512:(tc + 1) * 512], in_=ps[:, 0:512], func=AF.Sigmoid),
                 reads=[pk], writes=[gk])
        td, tdk = tD[od % 2], f"A_tD{od % 2}"
        p.op("dve", lambda v, od=od, td=td, tdk=tdk: v.tensor_tensor(out=td[:],
             in0=lo_t[od][:], in1=at[od][:], op=ALU.subtract),
             reads=[lok[od], f"A_at{od}"], writes=[tdk])
        p.op("dve", lambda v, td=td, tdk=tdk, g=g, gk=gk: v.tensor_tensor(out=td[:],
             in0=td[:], in1=g[:], op=ALU.mult), reads=[tdk, gk], writes=[tdk])
        p.op("dve", lambda v, od=od, td=td, tdk=tdk: v.tensor_tensor(out=td[:],
             in0=td[:], in1=at[od][:], op=ALU.add), reads=[tdk, f"A_at{od}"], writes=[tdk])
        p.dma("pool", dsp[od * 128:(od + 1) * 128, :], td[:], reads=[tdk],
              writes=[("d_dsp", od)], stream="dma_st_dsp")
    ctx.afree(*lok, *[f"A_at{i}" for i in range(8)], "A_gm0", "A_gm1", "A_wgx0",
              "A_wgx1", "A_tD0", "A_tD1")

    mu_b = bcast_row_bf(ctx, "A_mub", mu_row, "A_ln_mu", psbank=5)
    a2_b = bcast_row_bf(ctx, "A_a2b", a2_row, "A_ln_a2", psbank=6)
    xf = [ctx.aget(f"A_xf{j}", [128, T], F32) for j in (0, 1)]
    xh = [ctx.aget(f"A_xh{j}", [128, T], F32) for j in (0, 1)]
    dd = [ctx.aget(f"A_dd{j}", [128, T], BF16) for j in (0, 1)]
    for od in range(8):
        xfj, xfk = xf[od % 2], f"A_xf{od % 2}"
        xhj, xhk = xh[od % 2], f"A_xh{od % 2}"
        ddj, ddk = dd[od % 2], f"A_dd{od % 2}"
        p.dma("sp", xfj[:], d["x"][od * 128:(od + 1) * 128, :], writes=[xfk])
        p.dma("pool", ddj[:], dsp[od * 128:(od + 1) * 128, :], reads=[("d_dsp", od)],
              writes=[ddk])
        p.op("dve", lambda v, xhj=xhj, xhk=xhk, xfj=xfj, xfk=xfk: v.tensor_tensor(
             out=xhj[:], in0=xfj[:], in1=mu_b[:], op=ALU.add),
             reads=[xfk, "A_mub"], writes=[xhk])
        p.op("dve", lambda v, xhj=xhj, xhk=xhk: v.tensor_tensor(out=xhj[:], in0=xhj[:],
             in1=a2_b[:], op=ALU.mult), reads=[xhk, "A_a2b"], writes=[xhk])
        p.op("dve", lambda v, xhj=xhj, xhk=xhk, ddj=ddj, ddk=ddk: v.tensor_tensor(
             out=xhj[:], in0=xhj[:], in1=ddj[:], op=ALU.add), reads=[xhk, ddk],
             writes=[xhk])
        p.dma("sp", xmid[od * 128:(od + 1) * 128, :], xhj[:], reads=[xhk],
              writes=[("d_xmid", od)], stream="dma_st_xmid")
        p.op("dve", lambda v, xfj=xfj, xfk=xfk, xhj=xhj, xhk=xhk: v.tensor_tensor(
             out=xfj[:], in0=xfj[:], in1=xhj[:], op=ALU.add), reads=[xfk, xhk],
             writes=[xfk])
        p.dma("sp", xsum[od * 128:(od + 1) * 128, :], xfj[:], reads=[xfk],
              writes=[("d_xsum", od)], stream="dma_st_xsum")
    ctx.afree("A_mub", "A_a2b", "A_ln_mu", "A_ln_a2", "A_xf0", "A_xf1",
              "A_xh0", "A_xh1", "A_dd0", "A_dd1")

    xmb = []
    for i in range(8):
        tn = f"A_xmb{i}"
        tt = ctx.aget(tn, [128, T], BF16)
        p.dma("pool", tt[:], xmid[i * 128:(i + 1) * 128, :], reads=[("d_xmid", i)],
              writes=[tn])
        xmb.append(tt)

    xmbk = [f"A_xmb{i}" for i in range(8)]
    a2, a2k, _, _ = ln_rows(ctx, xmb, xmbk, kind="rms", eps=1e-6, name="A_r2")
    h2, _ = ln_materialize(ctx, xmb, xmbk, a2, a2k, None, None, gamma=g2,
                           gamma_key="A_g2", name="A_h2", inplace=True)
    dbg(ctx, d, "h2_0", h2[0][:], xmbk[0])
    dbg(ctx, d, "h2_7", h2[7][:], xmbk[7])
    ut = [ctx.aget(f"A_u{j}", [128, T], BF16) for j in (0, 1)]
    mt = [ctx.aget(f"A_m{j}", [128, T], BF16) for j in (0, 1)]
    for od in range(NI):
        w1s = wload(ctx, f"A_w1s{od % 2}", d["w1T"], 1024, od * 128, 128)
        w2s = wload(ctx, f"A_w2s{od % 2}", d["w2T"], 1024, od * 128, 128)
        k1, k2 = f"A_w1s{od % 2}", f"A_w2s{od % 2}"
        u, uk = ut[od % 2], f"A_u{od % 2}"
        m, mk2 = mt[od % 2], f"A_m{od % 2}"
        for tc in range(NTC):
            sl = slice(tc * 512, (tc + 1) * 512)
            b1, b2 = (tc % 2) * 2, (tc % 2) * 2 + 1
            ps1, pk1 = ctx.ps[b1], f"psb{b1}"
            ps2, pk2 = ctx.ps[b2], f"psb{b2}"
            mac(p, ps1, pk1, [(wslice(w1s, i, 128), k1, h2[i][:, sl], xmbk[i])
                              for i in range(8)])
            mac(p, ps2, pk2, [(wslice(w2s, i, 128), k2, h2[i][:, sl], xmbk[i])
                              for i in range(8)])
            p.op("act", lambda s, u=u, sl=sl, ps1=ps1: s.activation(out=u[:, sl],
                 in_=ps1[:, 0:512], func=AF.Silu), reads=[pk1], writes=[uk])
            p.op("act", lambda s, m=m, sl=sl, ps2=ps2: s.copy(out=m[:, sl],
                 in_=ps2[:, 0:512]), reads=[pk2], writes=[mk2])
            p.op("dve", lambda v, u=u, m=m, sl=sl: v.tensor_tensor(out=m[:, sl],
                 in0=u[:, sl], in1=m[:, sl], op=ALU.mult), reads=[uk, mk2], writes=[mk2])
        if od in (0, 21):
            dbg(ctx, d, f"m{od}", m[:], mk2)
        p.dma("pool", msp[od * 128:(od + 1) * 128, :], m[:], reads=[mk2],
              writes=[("d_msp", od)], stream="dma_st_msp")
    ctx.afree("A_u0", "A_u1", "A_m0", "A_m1", "A_w1s0", "A_w1s1", "A_w2s0",
              "A_w2s1", *[f"A_xmb{i}" for i in range(8)])

    w3s = []
    for od in range(8):
        w3s.append(wload(ctx, f"A_w3s{od}", d["w3T"], INTER, od * 128, 128))
    mre = [ctx.aget(f"A_mr{j}", [128, NI * 512], BF16) for j in (0, 1)]
    xr = [ctx.aget(f"A_xr{j}", [128, 512], F32) for j in (0, 1)]
    out = [ctx.aget(f"A_ot{j}", [128, 512], F32) for j in (0, 1)]
    for t8 in range(NTC):
        mr, mrk = mre[t8 % 2], f"A_mr{t8 % 2}"
        p.dma("pool", mr[:].rearrange("p (i c) -> p i c", c=512),
              msp[:, t8 * 512:(t8 + 1) * 512].rearrange("(i p) c -> p i c", p=128),
              reads=[("d_msp", i2) for i2 in range(NI)], writes=[mrk])
        for od in range(8):
            xrj, xrk = xr[od % 2], f"A_xr{od % 2}"
            oj, ok = out[od % 2], f"A_ot{od % 2}"
            p.dma("sp", xrj[:], xsum[od * 128:(od + 1) * 128, t8 * 512:(t8 + 1) * 512],
                  reads=[("d_xsum", od)], writes=[xrk])
            ps, pk = ctx.ps[2 + od % 4], f"psb{2 + od % 4}"
            mac(p, ps, pk, [(w3s[od][:, i * 128:(i + 1) * 128], f"A_w3s{od}",
                             mr[:, i * 512:(i + 1) * 512], mrk) for i in range(NI)])
            p.op("dve", lambda v, oj=oj, ps=ps, xrj=xrj: v.tensor_tensor(
                 out=oj[:], in0=ps[:, 0:512], in1=xrj[:], op=ALU.add),
                 reads=[pk, xrk], writes=[ok])
            p.dma("sp", d["xout"][od * 128:(od + 1) * 128, t8 * 512:(t8 + 1) * 512],
                  oj[:], reads=[ok], stream="dma_st_xout")
    ctx.afree("A_mr0", "A_mr1", "A_xr0", "A_xr1", "A_ot0", "A_ot1",
              *[f"A_w3s{od}" for od in range(8)])

"""L1_attn block emitter: SSM block.

Math (x_in = LN_outer(x); the SSM collapses to per-channel scalar scans):
  x_ln = layernorm(x_in, g, b)  [fused with outer LN via dscale trick]
  xp|gate = x_ln @ W_in.T + b_in           (E=2048 each)
  xc = silu(causal_conv4(xp) + conv_b)
  u = (xc @ W_c.T + b_c) * sigmoid(xc @ W_d.T + b_d)
  a_e = exp(-softplus(log_delta)); dot_e = sum_n A[e,n]*Bm[e,n]
  y = dot * scan(u; a)
  x_gated = y * sigmoid(gate)
  out = x_gated @ W_out.T + b_out
  x_next = x + x_in + out
"""



E = 2048
NE = E // 128


def emit_L1A(ctx, d):
    p = ctx.p
    nc = ctx.nc
    gsp = nc.dram_tensor("B_gsp", [E, T], BF16, kind="Internal").ap()
    xpsp = nc.dram_tensor("B_xpsp", [E, T], BF16, kind="Internal").ap()
    ysp = nc.dram_tensor("B_ysp", [E, T], BF16, kind="Internal").ap()
    xsum = nc.dram_tensor("B_xsum", [1024, T], F32, kind="Internal").ap()

    lng = load_pcol(ctx, "B_lng", d["lng_pc"], 8)
    lnb = load_pcol(ctx, "B_lnb", d["lnb_pc"], 8)
    bin_ = load_pcol(ctx, "B_bin", d["bin_pc"], 32)
    cw = load_pcol(ctx, "B_cw", d["convw_pc"], 64)
    cb = load_pcol(ctx, "B_cb", d["convb_pc"], 16)
    bc = load_pcol(ctx, "B_bc", d["bc_pc"], 16)
    bd = load_pcol(ctx, "B_bd", d["bd_pc"], 16)
    bout = load_pcol(ctx, "B_bout", d["bout_pc"], 8)
    ld_ = load_pcol(ctx, "B_ld", d["logdelta_pc"], 16)
    apc = load_pcol(ctx, "B_apc", d["a_pc"], 256)
    bpc = load_pcol(ctx, "B_bpc", d["bm_pc"], 256)

    adiag = ctx.aget("B_adiag", [128, 16], F32)
    t1 = ctx.aget("B_t1", [128, 16], F32)
    p.op("dve", lambda v: v.tensor_scalar(out=t1[:], in0=ld_[:], scalar1=-1.0,
         scalar2=None, op0=ALU.mult), reads=["B_ld"], writes=["B_t1"])
    p.op("dve", lambda v: v.tensor_tensor(out=t1[:], in0=t1[:], in1=ld_[:],
         op=ALU.max), reads=["B_t1", "B_ld"], writes=["B_t1"])
    cn1, ckn1 = ctx.cf(-1.0)
    c1, ck1 = ctx.cf(1.0)
    p.op("act", lambda s: s.activation(out=t1[:], in_=t1[:], func=AF.Exp,
         scale=cn1[:, 0:1]), reads=["B_t1", ckn1], writes=["B_t1"])
    p.op("act", lambda s: s.activation(out=t1[:], in_=t1[:], func=AF.Ln,
         bias=c1[:, 0:1]), reads=["B_t1", ck1], writes=["B_t1"])
    p.op("dve", lambda v: v.tensor_scalar(out=adiag[:], in0=ld_[:], scalar1=0.0,
         scalar2=None, op0=ALU.max), reads=["B_ld"], writes=["B_adiag"])
    p.op("dve", lambda v: v.tensor_tensor(out=adiag[:], in0=adiag[:], in1=t1[:],
         op=ALU.add), reads=["B_adiag", "B_t1"], writes=["B_adiag"])
    p.op("act", lambda s: s.activation(out=adiag[:], in_=adiag[:], func=AF.Exp,
         scale=cn1[:, 0:1]), reads=["B_adiag", ckn1], writes=["B_adiag"])
    dot = ctx.aget("B_dot", [128, 16], F32)
    ab = ctx.aget("B_ab", [128, 256], F32)
    p.op("dve", lambda v: v.tensor_tensor(out=ab[:], in0=apc[:], in1=bpc[:],
         op=ALU.mult), reads=["B_apc", "B_bpc"], writes=["B_ab"])
    p.op("dve", lambda v: v.tensor_reduce(out=dot[:, 0:16],
         in_=ab[:].rearrange("p (e n) -> p e n", n=16), axis=AX.X, op=ALU.add),
         reads=["B_ab"], writes=["B_dot"])

    xbf = load_xbf(ctx, "B_x", d["x"])
    xk = [f"B_x{i}" for i in range(8)]
    a_row, ak, mu_row, mk, a2_row, a2k = ln_rows(ctx, xbf, xk, kind="ln",
        eps=1e-10, dscale=1.00001, name="B_ln", plain_eps=1e-5)
    mu_b = bcast_row_bf(ctx, "B_mub", mu_row, mk, psbank=5)
    a2_b = bcast_row_bf(ctx, "B_a2b", a2_row, a2k, psbank=6)
    xf = [ctx.aget(f"B_xf{j}", [128, T], F32) for j in (0, 1)]
    xh = [ctx.aget(f"B_xh{j}", [128, T], BF16) for j in (0, 1)]
    for od in range(8):
        xfj, xfk = xf[od % 2], f"B_xf{od % 2}"
        xhj, xhk = xh[od % 2], f"B_xh{od % 2}"
        p.dma("sp", xfj[:], d["x"][od * 128:(od + 1) * 128, :], writes=[xfk])
        p.op("dve", lambda v, od=od, xhj=xhj: v.tensor_tensor(out=xhj[:],
             in0=xbf[od][:], in1=mu_b[:], op=ALU.add), reads=[xk[od], "B_mub"],
             writes=[xhk])
        p.op("dve", lambda v, xhj=xhj: v.tensor_tensor(out=xhj[:], in0=xhj[:],
             in1=a2_b[:], op=ALU.mult), reads=[xhk, "B_a2b"], writes=[xhk])
        p.op("dve", lambda v, xfj=xfj, xhj=xhj: v.tensor_tensor(out=xfj[:],
             in0=xfj[:], in1=xhj[:], op=ALU.add), reads=[xfk, xhk], writes=[xfk])
        p.dma("sp", xsum[od * 128:(od + 1) * 128, :], xfj[:], reads=[xfk],
              writes=[("d_xsum", od)], stream="dma_st_xsum")
    ctx.afree("B_mub", "B_a2b", a2k, "B_xf0", "B_xf1", "B_xh0", "B_xh1")
    xln, _ = ln_materialize(ctx, xbf, xk, a_row, ak, mu_row, mk, gamma=lng,
                            gamma_key="B_lng", beta=lnb, beta_key="B_lnb",
                            name="B_xm", inplace=True, free_rows=True)

    xp = [ctx.aget(f"B_xp{j}", [128, T], BF16) for j in (0, 1)]
    for od in range(32):
        wt = wload(ctx, f"B_win{od % 2}", d["winT"], 1024, od * 128, 128)
        wk = f"B_win{od % 2}"
        dst, dk = xp[od % 2], f"B_xp{od % 2}"
        for tc in range(NTC):
            bank = tc % 2
            ps, pk = ctx.ps[bank], f"psb{bank}"
            mac(p, ps, pk, [(wslice(wt, i, 128), wk,
                             xln[i][:, tc * 512:(tc + 1) * 512], xk[i]) for i in range(8)])
            p.op("act", lambda s, tc=tc, ps=ps, dst=dst, od=od: s.activation(
                 out=dst[:, tc * 512:(tc + 1) * 512], in_=ps[:, 0:512],
                 func=AF.Identity, bias=bin_[:, od:od + 1]),
                 reads=[pk, "B_bin"], writes=[dk])
        if od < 16:
            p.dma("pool", xpsp[od * 128:(od + 1) * 128, :], dst[:], reads=[dk],
                  writes=[("d_xpsp", od)], stream="dma_st_xpsp")
        else:
            p.dma("pool", gsp[(od - 16) * 128:(od - 15) * 128, :], dst[:], reads=[dk],
                  writes=[("d_gsp", od - 16)], stream="dma_st_gsp")
    ctx.afree("B_xp0", "B_xp1", "B_win0", "B_win1", *[xk[i] for i in range(8)])

    xc = [ctx.aget(f"B_xc{e}", [128, T], BF16) for e in range(NE)]
    xpr = [ctx.aget(f"B_xr2{j}", [128, T], BF16) for j in (0, 1)]
    cv = [ctx.aget(f"B_cv{j}", [128, T], BF16) for j in (0, 1)]
    for e in range(NE):
        dst, dk = xpr[e % 2], f"B_xr2{e % 2}"
        p.dma("pool", dst[:], xpsp[e * 128:(e + 1) * 128, :], reads=[("d_xpsp", e)],
              writes=[dk])
        cvj, cvk = cv[e % 2], f"B_cv{e % 2}"
        p.op("dve", lambda v, dst=dst, cvj=cvj, e=e, dk=dk, cvk=cvk: v.tensor_scalar(
             out=cvj[:], in0=dst[:], scalar1=cw[:, 48 + e:49 + e], scalar2=None,
             op0=ALU.mult), reads=[dk, "B_cw"], writes=[cvk])
        for k in (1, 2, 3):
            tap = 3 - k
            p.op("dve", lambda v, dst=dst, cvj=cvj, e=e, k=k, tap=tap, dk=dk, cvk=cvk:
                 v.scalar_tensor_tensor(out=cvj[:, k:T], in0=dst[:, 0:T - k],
                 scalar=cw[:, tap * 16 + e:tap * 16 + e + 1], in1=cvj[:, k:T],
                 op0=ALU.mult, op1=ALU.add), reads=[dk, cvk, "B_cw"], writes=[cvk])
        p.op("act", lambda s, cvj=cvj, e=e, cvk=cvk: s.activation(out=xc[e][:],
             in_=cvj[:], func=AF.Silu, bias=cb[:, e:e + 1]), reads=[cvk, "B_cb"],
             writes=[f"B_xc{e}"])
    ctx.afree("B_xr20", "B_xr21", "B_cv0", "B_cv1")

    uc = [ctx.aget(f"B_uc{j}", [128, T], BF16) for j in (0, 1)]
    ud = [ctx.aget(f"B_ud{j}", [128, T], BF16) for j in (0, 1)]
    gr = [ctx.aget(f"B_gr{j}", [128, T], BF16) for j in (0, 1)]
    xck = [f"B_xc{e}" for e in range(NE)]
    for od in range(NE):
        wc = wload(ctx, f"B_wc{od % 2}", d["wcT"], E, od * 128, 128)
        wd = wload(ctx, f"B_wd{od % 2}", d["wdT"], E, od * 128, 128)
        kc, kd = f"B_wc{od % 2}", f"B_wd{od % 2}"
        ucj, uck = uc[od % 2], f"B_uc{od % 2}"
        udj, udk = ud[od % 2], f"B_ud{od % 2}"
        for tc in range(NTC):
            sl = slice(tc * 512, (tc + 1) * 512)
            b1, b2 = (tc % 2) * 2, (tc % 2) * 2 + 1
            ps1, pk1 = ctx.ps[b1], f"psb{b1}"
            ps2, pk2 = ctx.ps[b2], f"psb{b2}"
            mac(p, ps1, pk1, [(wslice(wc, i, 128), kc, xc[i][:, sl], xck[i])
                              for i in range(NE)])
            mac(p, ps2, pk2, [(wslice(wd, i, 128), kd, xc[i][:, sl], xck[i])
                              for i in range(NE)])
            p.op("act", lambda s, sl=sl, ps1=ps1, ucj=ucj, od=od: s.activation(
                 out=ucj[:, sl], in_=ps1[:, 0:512], func=AF.Identity,
                 bias=bc[:, od:od + 1]), reads=[pk1, "B_bc"], writes=[uck])
            p.op("act", lambda s, sl=sl, ps2=ps2, udj=udj, od=od: s.activation(
                 out=udj[:, sl], in_=ps2[:, 0:512], func=AF.Sigmoid,
                 bias=bd[:, od:od + 1]), reads=[pk2, "B_bd"], writes=[udk])
        p.op("dve", lambda v, ucj=ucj, udj=udj: v.tensor_tensor(out=ucj[:],
             in0=ucj[:], in1=udj[:], op=ALU.mult), reads=[uck, udk], writes=[uck])
        p.op("dve", lambda v, ucj=ucj, od=od: v.tensor_tensor_scan(out=ucj[:],
             data0=adiag[:, od:od + 1].broadcast_to((128, T)), data1=ucj[:],
             initial=0.0, op0=ALU.mult, op1=ALU.add),
             reads=["B_adiag", uck], writes=[uck])
        p.op("dve", lambda v, ucj=ucj, od=od: v.tensor_scalar(out=ucj[:], in0=ucj[:],
             scalar1=dot[:, od:od + 1], scalar2=None, op0=ALU.mult),
             reads=[uck, "B_dot"], writes=[uck])
        grj, grk = gr[od % 2], f"B_gr{od % 2}"
        p.dma("pool", grj[:], gsp[od * 128:(od + 1) * 128, :], reads=[("d_gsp", od)],
              writes=[grk])
        p.op("act", lambda s, grj=grj: s.activation(out=grj[:], in_=grj[:],
             func=AF.Sigmoid), reads=[grk], writes=[grk])
        p.op("dve", lambda v, ucj=ucj, grj=grj: v.tensor_tensor(out=ucj[:],
             in0=ucj[:], in1=grj[:], op=ALU.mult), reads=[uck, grk], writes=[uck])
        p.dma("pool", ysp[od * 128:(od + 1) * 128, :], ucj[:], reads=[uck],
              writes=[("d_ysp", od)], stream="dma_st_ysp")
    ctx.afree("B_uc0", "B_uc1", "B_ud0", "B_ud1", "B_gr0", "B_gr1",
              "B_wc0", "B_wc1", "B_wd0", "B_wd1", *xck)

    wo = []
    for od in range(8):
        wo.append(wload(ctx, f"B_wo{od}", d["woT"], E, od * 128, 128))
    yre = [ctx.aget(f"B_yr{j}", [128, NE * 512], BF16) for j in (0, 1)]
    xr = [ctx.aget(f"B_xr{j}", [128, 512], F32) for j in (0, 1)]
    out = [ctx.aget(f"B_ot{j}", [128, 512], F32) for j in (0, 1)]
    for t8 in range(NTC):
        yr, yrk = yre[t8 % 2], f"B_yr{t8 % 2}"
        p.dma("pool", yr[:].rearrange("p (i c) -> p i c", c=512),
              ysp[:, t8 * 512:(t8 + 1) * 512].rearrange("(i p) c -> p i c", p=128),
              reads=[("d_ysp", i2) for i2 in range(NE)], writes=[yrk])
        for od in range(8):
            xrj, xrk = xr[od % 2], f"B_xr{od % 2}"
            oj, ok = out[od % 2], f"B_ot{od % 2}"
            p.dma("sp", xrj[:], xsum[od * 128:(od + 1) * 128, t8 * 512:(t8 + 1) * 512],
                  reads=[("d_xsum", od)], writes=[xrk])
            ps, pk = ctx.ps[2 + od % 4], f"psb{2 + od % 4}"
            mac(p, ps, pk, [(wo[od][:, i * 128:(i + 1) * 128], f"B_wo{od}",
                             yr[:, i * 512:(i + 1) * 512], yrk) for i in range(NE)])
            p.op("act", lambda s, oj=oj, ps=ps, od=od: s.activation(out=oj[:],
                 in_=ps[:, 0:512], func=AF.Identity, bias=bout[:, od:od + 1]),
                 reads=[pk, "B_bout"], writes=[ok])
            p.op("dve", lambda v, oj=oj, xrj=xrj: v.tensor_tensor(out=oj[:],
                 in0=oj[:], in1=xrj[:], op=ALU.add), reads=[ok, xrk], writes=[ok])
            p.dma("sp", d["xout"][od * 128:(od + 1) * 128, t8 * 512:(t8 + 1) * 512],
                  oj[:], reads=[ok], stream="dma_st_xout")
    ctx.afree("B_yr0", "B_yr1", "B_xr0", "B_xr1", "B_ot0", "B_ot1",
              *[f"B_wo{od}" for od in range(8)])

"""L1_ffn block emitter: dense 8-expert top-2 MoE + shared FFN (F=4096).

x_in = LN_outer(x); logits = x_in @ Wg; probs = softmax; top2 -> cw (T,8)
out = sum_e cw_e*(silu(x_in@W1[e]+b1[e])@W2[e]+b2[e]) + 0.1*(shared)
x_next = x + out       (no inner residual)

Dense v1: all 8 experts computed for all tokens; shared = expert 8 with
cw==0.1. Expert h spilled to HBM; expert outputs DMA-accumulated into an
HBM f32 accumulator (probed exact on HW).
"""



FF = 4096
NF = FF // 128


def emit_L1F(ctx, d):
    p = ctx.p
    nc = ctx.nc
    hsp = nc.dram_tensor("C_hsp", [FF, T], BF16, kind="Internal").ap()
    xlsp = nc.dram_tensor("C_xlsp", [1024, T], BF16, kind="Internal").ap()
    accsp = nc.dram_tensor("C_acc", [1024, T], F32, kind="Internal").ap()
    cwtsp = nc.dram_tensor("C_cwt", [8, T], BF16, kind="Internal").ap()

    b1pc = load_pcol(ctx, "C_b1", d["b1_pc"], 9 * 32)
    b2sb = ctx.aget("C_b2", [1, 9 * 1024], BF16)
    p.dma("pool", b2sb[:], d["b2_rows"], writes=["C_b2"])
    wg = ctx.aget("C_wg", [128, 64], BF16)
    p.dma("pool", wg[:].rearrange("p (i c) -> p i c", c=8),
          d["wg"].rearrange("(i p) c -> p i c", p=128), writes=["C_wg"])
    ident = ctx.aget("C_id", [128, 128], BF16)
    p.dma("pool", ident[:], d["ident"], writes=["C_id"])

    xbf = load_xbf(ctx, "C_x", d["x"])
    xk = [f"C_x{i}" for i in range(8)]
    a_row, ak, mu_row, mk = ln_rows(ctx, xbf, xk, kind="ln", eps=1e-5, name="C_ln")
    xln, _ = ln_materialize(ctx, xbf, xk, a_row, ak, mu_row, mk, name="C_xm",
                            inplace=True)

    import os
    STAGE = os.environ.get("L1F_STAGE", "")
    cwT = ctx.aget("C_cwT", [8, T], BF16)
    lg = ctx.aget("C_lg", [128, 8], F32)
    r1 = ctx.aget("C_r1", [128, 1], F32)
    r2 = ctx.aget("C_r2", [128, 1], F32)
    msk = ctx.aget("C_msk", [128, 8], F32)
    cwt = ctx.aget("C_cw", [128, 8], BF16)
    pst = ctx.pstT[0:128, 0:128]
    if STAGE == "2":
        p.op("dve", lambda v: v.memset(cwT[:], 0.2), writes=["C_cwT"])
    for tt in range(T // 128) if STAGE != "2" else []:
        sl = slice(tt * 128, (tt + 1) * 128)
        ps, pk = ctx.ps[6], "psb6"
        for i in range(8):
            p.op("pe", lambda t_, i=i, sl=sl, ps=ps: t_.matmul(out=ps[:, 0:8],
                 lhsT=xln[i][:, sl], rhs=wg[:, i * 8:(i + 1) * 8],
                 start=(i == 0), stop=(i == 7)), reads=[xk[i], "C_wg"],
                 writes=[pk], inc=(i == 7))
        p.op("dve", lambda v, ps=ps: v.tensor_reduce(out=r1[:, 0:1], in_=ps[:, 0:8],
             axis=AX.X, op=ALU.max), reads=[pk], writes=["C_r1"])
        p.op("dve", lambda v: v.tensor_scalar(out=r1[:], in0=r1[:], scalar1=-1.0,
             scalar2=None, op0=ALU.mult), reads=["C_r1"], writes=["C_r1"])
        p.op("act", lambda s, ps=ps: s.activation(out=lg[:], in_=ps[:, 0:8],
             func=AF.Exp, bias=r1[:, 0:1]), reads=[pk, "C_r1"], writes=["C_lg"])
        p.op("dve", lambda v: v.tensor_reduce(out=r2[:, 0:1], in_=lg[:], axis=AX.X,
             op=ALU.add), reads=["C_lg"], writes=["C_r2"])
        p.op("dve", lambda v: v.reciprocal(out=r2[:], in_=r2[:]), reads=["C_r2"],
             writes=["C_r2"])
        p.op("dve", lambda v: v.tensor_scalar(out=lg[:], in0=lg[:],
             scalar1=r2[:, 0:1], scalar2=None, op0=ALU.mult), reads=["C_lg", "C_r2"],
             writes=["C_lg"])
        p.op("dve", lambda v: v.tensor_reduce(out=r1[:, 0:1], in_=lg[:], axis=AX.X,
             op=ALU.max), reads=["C_lg"], writes=["C_r1"])
        p.op("dve", lambda v: v.tensor_scalar(out=msk[:], in0=lg[:],
             scalar1=r1[:, 0:1], scalar2=None, op0=ALU.is_ge), reads=["C_lg", "C_r1"],
             writes=["C_msk"])
        p.op("dve", lambda v: v.scalar_tensor_tensor(out=msk[:], in0=msk[:],
             scalar=-1e9, in1=lg[:], op0=ALU.mult, op1=ALU.add),
             reads=["C_msk", "C_lg"], writes=["C_msk"])
        p.op("dve", lambda v: v.tensor_reduce(out=r2[:, 0:1], in_=msk[:], axis=AX.X,
             op=ALU.max), reads=["C_msk"], writes=["C_r2"])
        p.op("dve", lambda v: v.tensor_scalar(out=msk[:], in0=lg[:],
             scalar1=r2[:, 0:1], scalar2=None, op0=ALU.is_ge), reads=["C_lg", "C_r2"],
             writes=["C_msk"])
        p.op("dve", lambda v: v.tensor_tensor(out=r2[:], in0=r2[:], in1=r1[:],
             op=ALU.add), reads=["C_r2", "C_r1"], writes=["C_r2"])
        p.op("dve", lambda v: v.tensor_scalar(out=r2[:], in0=r2[:], scalar1=1e-9,
             scalar2=None, op0=ALU.add), reads=["C_r2"], writes=["C_r2"])
        p.op("dve", lambda v: v.reciprocal(out=r2[:], in_=r2[:]), reads=["C_r2"],
             writes=["C_r2"])
        p.op("dve", lambda v: v.tensor_tensor(out=cwt[:], in0=lg[:], in1=msk[:],
             op=ALU.mult), reads=["C_lg", "C_msk"], writes=["C_cw"])
        p.op("dve", lambda v: v.tensor_scalar(out=cwt[:], in0=cwt[:],
             scalar1=r2[:, 0:1], scalar2=None, op0=ALU.mult), reads=["C_cw", "C_r2"],
             writes=["C_cw"])
        p.op("pe", lambda t_: t_.transpose(out=pst[0:8, 0:128], in_=cwt[:, 0:8],
             identity=ident[:]), reads=["C_cw", "C_id"], writes=["pstT"])
        p.op("act", lambda s, sl=sl: s.copy(out=cwT[0:8, sl], in_=pst[0:8, 0:128]),
             reads=["pstT"], writes=["C_cwT"])
    p.dma("pool", cwtsp, cwT[:], reads=["C_cwT"], writes=[("d_cwt", 0)],
          stream="dma_st_cwt")
    ctx.afree("C_cwT", "C_lg", "C_r1", "C_r2", "C_msk", "C_cw")

    for i in range(8):
        p.dma("pool", xlsp[i * 128:(i + 1) * 128, :], xln[i][:], reads=[xk[i]],
              writes=[("d_xl", i)], stream="dma_st_xl")
    ctx.afree(*xk)

    hrot = [ctx.aget(f"C_h{j}", [128, T], BF16) for j in (0, 1)]
    NE_LIM = int(os.environ.get("L1F_NE", "9"))
    for e in list(range(9))[:NE_LIM] if STAGE != "1" else []:
        w1r = ctx.aget("C_w1r", [128, 8 * FF], BF16)
        w1src = d["ws1T"] if e == 8 else d["w1T"][e]
        p.dma("pool", w1r[:].rearrange("p (i c) -> p i c", c=FF),
              w1src.rearrange("(i p) c -> p i c", p=128), writes=["C_w1r"])
        xl2 = []
        for i in range(8):
            tn = f"C_xl{i}"
            tt2 = ctx.aget(tn, [128, T], BF16)
            p.dma("pool", tt2[:], xlsp[i * 128:(i + 1) * 128, :], reads=[("d_xl", i)],
                  writes=[tn])
            xl2.append(tt2)
        xl2k = [f"C_xl{i}" for i in range(8)]
        for hod in range(NF):
            h, hk2 = hrot[hod % 2], f"C_h{hod % 2}"
            for tc in range(NTC):
                sl = slice(tc * 512, (tc + 1) * 512)
                bank = tc % 2
                ps, pk = ctx.ps[bank], f"psb{bank}"
                mac(p, ps, pk, [(w1r[:, i * FF + hod * 128:i * FF + (hod + 1) * 128],
                                 "C_w1r", xl2[i][:, sl], xl2k[i]) for i in range(8)])
                p.op("act", lambda s, h=h, sl=sl, ps=ps, e=e, hod=hod, hk2=hk2:
                     s.activation(out=h[:, sl], in_=ps[:, 0:512], func=AF.Silu,
                     bias=b1pc[:, e * 32 + hod:e * 32 + hod + 1]),
                     reads=[pk, "C_b1"], writes=[hk2])
            p.dma("pool", hsp[hod * 128:(hod + 1) * 128, :], h[:], reads=[hk2],
                  writes=[("d_hsp", hod)], stream="dma_st_hsp")
        ctx.afree("C_w1r", *xl2k)

        ctx.afree("C_h0", "C_h1")
        w2src = d["ws2T"] if e == 8 else d["w2T"][e]
        cwr = ctx.aget("C_cwr", [1, T], BF16)
        if e == 8:
            p.op("dve", lambda v, cwr=cwr: v.memset(cwr[:], 0.1), writes=["C_cwr"])
        else:
            p.dma("pool", cwr[:], cwtsp[e:e + 1, :], reads=[("d_cwt", 0)],
                  writes=["C_cwr"])
        for og in range(2):
            w2r = ctx.aget("C_w2r", [128, NF * 512], BF16)
            p.dma("pool", w2r[:].rearrange("p (i c) -> p i c", c=512),
                  w2src[:, og * 512:(og + 1) * 512].rearrange("(i p) c -> p i c", p=128),
                  writes=["C_w2r"])
            hre = [ctx.aget(f"C_hr{j}", [128, NF * 512], BF16) for j in (0, 1)]
            cwb = [ctx.aget(f"C_cb{j}", [128, 512], BF16) for j in (0, 1)]
            yt = [ctx.aget(f"C_yt{od4}", [128, T], BF16) for od4 in range(4)]
            for tc in range(NTC):
                sl = slice(tc * 512, (tc + 1) * 512)
                hr, hrk = hre[tc % 2], f"C_hr{tc % 2}"
                p.dma("pool", hr[:].rearrange("p (i c) -> p i c", c=512),
                      hsp[:, sl].rearrange("(i p) c -> p i c", p=128),
                      reads=[("d_hsp", i2) for i2 in range(NF)], writes=[hrk])
                cb_, cbk = cwb[tc % 2], f"C_cb{tc % 2}"
                ps7, pk7 = ctx.ps[6], "psb6"
                p.op("pe", lambda t_, sl=sl, cwr=cwr, ps7=ps7: t_.matmul(
                     out=ps7[:, 0:512], lhsT=ctx.ones_row[:], rhs=cwr[0:1, sl],
                     start=True, stop=True), reads=["ones_row", "C_cwr"], writes=[pk7])
                p.op("act", lambda s, cb_=cb_, ps7=ps7, cbk=cbk: s.copy(out=cb_[:],
                     in_=ps7[:, 0:512]), reads=[pk7], writes=[cbk])
                for od4 in range(4):
                    od = og * 4 + od4
                    bank = 2 + od4
                    ps, pk = ctx.ps[bank], f"psb{bank}"
                    pairs = [(b2sb[0:1, e * 1024 + od * 128:e * 1024 + (od + 1) * 128],
                              "C_b2", ctx.ones512[:], "ones512")]
                    pairs += [(w2r[:, i * 512 + od4 * 128:i * 512 + (od4 + 1) * 128],
                               "C_w2r", hr[:, i * 512:(i + 1) * 512], hrk)
                              for i in range(NF)]
                    mac(p, ps, pk, pairs)
                    p.op("dve", lambda v, od4=od4, sl=sl, ps=ps, cb_=cb_, cbk=cbk:
                         v.tensor_tensor(out=yt[od4][:, sl], in0=ps[:, 0:512],
                         in1=cb_[:], op=ALU.mult), reads=[pk, cbk],
                         writes=[f"C_yt{od4}"])
            for od4 in range(4):
                od = og * 4 + od4
                if e == 0:
                    p.dma("pool", accsp[od * 128:(od + 1) * 128, :], yt[od4][:],
                          reads=[f"C_yt{od4}"], writes=[("d_acc", od)],
                          stream="dma_st_acc")
                else:
                    for ch in range(8):
                        cs = slice(ch * 512, (ch + 1) * 512)
                        p.dma_accum("pool", accsp[od * 128:(od + 1) * 128, cs],
                                    yt[od4][:, cs], reads=[f"C_yt{od4}"],
                                    writes=[("d_acc", od)], stream="dma_st_acc")
            ctx.afree("C_w2r", "C_hr0", "C_hr1", "C_cb0", "C_cb1",
                      *[f"C_yt{od4}" for od4 in range(4)])
        ctx.afree("C_cwr")
        hrot = [ctx.aget(f"C_h{j}", [128, T], BF16) for j in (0, 1)]
    ctx.afree("C_h0", "C_h1")
    ctx.afree("C_h0", "C_h1")

    xf = [ctx.aget(f"C_xf{j}", [128, T], F32) for j in (0, 1)]
    ar = [ctx.aget(f"C_ar{j}", [128, T], F32) for j in (0, 1)]
    if STAGE == "1":
        for od in range(8):
            xfj, xfk = xf[od % 2], f"C_xf{od % 2}"
            p.dma("sp", xfj[:], d["x"][od * 128:(od + 1) * 128, :], writes=[xfk])
            p.dma("sp", d["xout"][od * 128:(od + 1) * 128, :], xfj[:], reads=[xfk],
                  stream="dma_st_xout")
        return
    for od in range(8):
        xfj, xfk = xf[od % 2], f"C_xf{od % 2}"
        arj, ark = ar[od % 2], f"C_ar{od % 2}"
        p.dma("sp", xfj[:], d["x"][od * 128:(od + 1) * 128, :], writes=[xfk])
        p.dma("sp", arj[:], accsp[od * 128:(od + 1) * 128, :], reads=[("d_acc", od)],
              writes=[ark])
        p.op("dve", lambda v, arj=arj, xfj=xfj, ark=ark, xfk=xfk: v.tensor_tensor(
             out=arj[:], in0=arj[:], in1=xfj[:], op=ALU.add), reads=[ark, xfk],
             writes=[ark])
        p.dma("sp", d["xout"][od * 128:(od + 1) * 128, :], arj[:], reads=[ark],
              stream="dma_st_xout")
    ctx.afree("C_xf0", "C_xf1", "C_ar0", "C_ar1")

"""L2_attn block emitter.

x_in = LN_outer(x)
h = relu(rmsnorm(x_in) @ w1.T)^2            [256]
qkv = h @ w2.T                              [6144] -> q,k,v each [16h x 128d]
q,k: per-head rmsnorm (eps=f32 eps), then RoPE where only dims
[0:32)u[64:96) of each head rotate. HOST PERMUTES the q/k/v feature order
to [rotA(16hx32) rotB idA idB] so rope groups are partition-contiguous.
y = relu(q) * cumsum(sigmoid(k) * v, t)     [2048]
uv = y @ cw1.T @ cw2.T -> u|g               [1024|1024]
x_next = x + x_in + u * silu(g)
"""



EPS_F32 = 1.1920929e-07


def emit_L2A(ctx, d):
    p = ctx.p
    nc = ctx.nc
    qkvsp = nc.dram_tensor("D_qkv", [6144, T], BF16, kind="Internal").ap()
    xsum = nc.dram_tensor("D_xsum", [1024, T], F32, kind="Internal").ap()
    ssp = nc.dram_tensor("D_ssp", [2048, T], BF16, kind="Internal").ap()
    ysp = nc.dram_tensor("D_ysp", [2048, T], BF16, kind="Internal").ap()
    qr_sp = nc.dram_tensor("D_qr", [16, T], BF16, kind="Internal").ap()
    kr_sp = nc.dram_tensor("D_kr", [16, T], BF16, kind="Internal").ap()

    seg4 = ctx.aget("D_seg4", [4, 128], BF16)
    p.dma("pool", seg4[:], d["seg4T"], writes=["D_seg4"])
    seg4c = ctx.aget("D_seg4c", [128, 4], BF16)
    p.dma("pool", seg4c[:], d["seg4"], writes=["D_seg4c"])
    cosb = ctx.aget("D_cos", [128, T], BF16)
    sinb = ctx.aget("D_sin", [128, T], BF16)
    p.dma("pool", cosb[:], d["cos_t"], writes=["D_cos"])
    p.dma("pool", sinb[:], d["sin_t"], writes=["D_sin"])

    xbf = load_xbf(ctx, "D_x", d["x"])
    xk = [f"D_x{i}" for i in range(8)]
    a_row, ak, mu_row, mk, a2_row, a2k = ln_rows(ctx, xbf, xk, kind="ln",
        eps=EPS_F32 * 1e-5, dscale=1.0 + EPS_F32, name="D_ln", plain_eps=1e-5)
    mu_b = bcast_row_bf(ctx, "D_mub", mu_row, mk, psbank=5)
    a2_b = bcast_row_bf(ctx, "D_a2b", a2_row, a2k, psbank=6)
    xf = [ctx.aget(f"D_xf{j}", [128, T], F32) for j in (0, 1)]
    xh = [ctx.aget(f"D_xh{j}", [128, T], BF16) for j in (0, 1)]
    for od in range(8):
        xfj, xfk = xf[od % 2], f"D_xf{od % 2}"
        xhj, xhk = xh[od % 2], f"D_xh{od % 2}"
        p.dma("sp", xfj[:], d["x"][od * 128:(od + 1) * 128, :], writes=[xfk])
        p.op("dve", lambda v, od=od, xhj=xhj, xhk=xhk: v.tensor_tensor(out=xhj[:],
             in0=xbf[od][:], in1=mu_b[:], op=ALU.add), reads=[xk[od], "D_mub"],
             writes=[xhk])
        p.op("dve", lambda v, xhj=xhj, xhk=xhk: v.tensor_tensor(out=xhj[:], in0=xhj[:],
             in1=a2_b[:], op=ALU.mult), reads=[xhk, "D_a2b"], writes=[xhk])
        p.op("dve", lambda v, xfj=xfj, xfk=xfk, xhj=xhj, xhk=xhk: v.tensor_tensor(
             out=xfj[:], in0=xfj[:], in1=xhj[:], op=ALU.add), reads=[xfk, xhk],
             writes=[xfk])
        p.dma("sp", xsum[od * 128:(od + 1) * 128, :], xfj[:], reads=[xfk],
              writes=[("d_xsum", od)], stream="dma_st_xsum")
    ctx.afree("D_mub", "D_a2b", a2k, "D_xf0", "D_xf1", "D_xh0", "D_xh1")
    xn, _ = ln_materialize(ctx, xbf, xk, a_row, ak, mu_row, mk, name="D_xn",
                           inplace=True, free_rows=True)

    ht = [ctx.aget(f"D_h{j}", [128, T], BF16) for j in (0, 1)]
    for od in range(2):
        wt = wload(ctx, f"D_w1{od}", d["w1T"], 1024, od * 128, 128)
        h, hk = ht[od], f"D_h{od}"
        for tc in range(NTC):
            sl = slice(tc * 512, (tc + 1) * 512)
            bank = tc % 2
            ps, pk = ctx.ps[bank], f"psb{bank}"
            mac(p, ps, pk, [(wslice(wt, i, 128), f"D_w1{od}", xn[i][:, sl], xk[i])
                            for i in range(8)])
            p.op("dve", lambda v, h=h, sl=sl, ps=ps, hk=hk: v.tensor_scalar(
                 out=h[:, sl], in0=ps[:, 0:512], scalar1=0.0, scalar2=None,
                 op0=ALU.max), reads=[pk], writes=[hk])
        p.op("act", lambda s, h=h, hk=hk: s.activation(out=h[:], in_=h[:],
             func=AF.Square), reads=[hk], writes=[hk])
    ctx.afree(*xk, "D_w10", "D_w11")

    qv = [ctx.aget(f"D_q{j}", [128, T], BF16) for j in (0, 1)]
    for od in range(48):
        wt = wload(ctx, f"D_w2{od % 2}", d["w2T"], 256, od * 128, 128)
        dst, dk = qv[od % 2], f"D_q{od % 2}"
        for tc in range(NTC):
            sl = slice(tc * 512, (tc + 1) * 512)
            bank = tc % 2
            ps, pk = ctx.ps[bank], f"psb{bank}"
            mac(p, ps, pk, [(wslice(wt, i, 128), f"D_w2{od % 2}", ht[i][:, sl],
                             f"D_h{i}") for i in range(2)])
            p.op("act", lambda s, dst=dst, sl=sl, ps=ps, dk=dk: s.copy(
                 out=dst[:, sl], in_=ps[:, 0:512]), reads=[pk], writes=[dk])
        p.dma("pool", qkvsp[od * 128:(od + 1) * 128, :], dst[:], reads=[dk],
              writes=[("d_qkv", od)], stream="dma_st_qkv")
    ctx.afree("D_h0", "D_h1", "D_q0", "D_q1", "D_w20", "D_w21")

    sq = [ctx.aget(f"D_sq{j}", [128, 512], BF16) for j in (0, 1)]
    rr = ctx.aget("D_rr", [4, 512], F32)
    ce, cke = ctx.cf(float(EPS_F32))
    ch, ckh = ctx.cf(-0.5)
    c128, ck128 = ctx.cf(1.0 / 128.0)
    for qk, rsp in (("q", qr_sp), ("k", kr_sp)):
        base = 0 if qk == "q" else 16
        rows = ctx.aget("D_rows", [4, T], BF16)
        tl = [ctx.aget(f"D_tl{j}", [128, T], BF16) for j in range(4)]
        for pt in range(4):
            for g in range(4):
                od = base + g * 4 + pt
                p.dma("pool", tl[g][:], qkvsp[od * 128:(od + 1) * 128, :],
                      reads=[("d_qkv", od)], writes=[f"D_tl{g}"])
            for tc in range(NTC):
                sl = slice(tc * 512, (tc + 1) * 512)
                ps, pk = ctx.ps[4], "psb4"
                for g in range(4):
                    sqb, sqk = sq[g % 2], f"D_sq{g % 2}"
                    p.op("act", lambda s, g=g, sl=sl, sqb=sqb:
                         s.activation(out=sqb[:], in_=tl[g][:, sl], func=AF.Square),
                         reads=[f"D_tl{g}"], writes=[sqk])
                    p.op("pe", lambda t_, sqb=sqb, g=g, ps=ps, sqk=sqk: t_.matmul(
                         out=ps[0:4, 0:512], lhsT=seg4c[:], rhs=sqb[:],
                         start=(g == 0), stop=(g == 3)),
                         reads=[sqk, "D_seg4c"], writes=[pk], inc=(g == 3))
                p.op("dve", lambda v, ps=ps: v.tensor_scalar(out=rr[:], in0=ps[0:4, 0:512],
                     scalar1=1.0 / 128, scalar2=None, op0=ALU.mult), reads=[pk],
                     writes=["D_rr"])
                p.op("act", lambda s: s.activation(out=rr[:], in_=rr[:], func=AF.Ln,
                     bias=ce[0:4, 0:1]), reads=["D_rr", cke], writes=["D_rr"])
                p.op("act", lambda s, sl=sl: s.activation(out=rows[0:4, sl], in_=rr[:],
                     func=AF.Exp, scale=ch[0:4, 0:1]), reads=["D_rr", ckh],
                     writes=["D_rows"])
            p.dma("pool", rsp[pt * 4:(pt + 1) * 4, :], rows[0:4, :], reads=["D_rows"],
                  writes=[("d_r" + qk, pt)], stream="dma_st_r" + qk)
        ctx.afree("D_rows", "D_tl0", "D_tl1", "D_tl2", "D_tl3")
    ctx.afree("D_sq0", "D_sq1", "D_rr")

    one_sc = ctx.aget("D_one", [128, 1], F32)
    p.op("dve", lambda v: v.memset(one_sc[:], 1.0), writes=["D_one"])

    def process(qk, base, rsp, rkey, consume):
        rows = ctx.aget("D_rw2", [4, T], BF16)
        ta = ctx.aget("D_ta", [128, T], BF16)
        tb = ctx.aget("D_tb", [128, T], BF16)
        tn1 = ctx.aget("D_tn1", [128, T], BF16)
        tn2 = ctx.aget("D_tn2", [128, T], BF16)
        rb = ctx.aget("D_rb", [128, T], BF16)
        for pt in range(4):
            p.dma("pool", rows[0:4, :], rsp[pt * 4:(pt + 1) * 4, :],
                  reads=[(rkey, pt)], writes=["D_rw2"])
            for tc in range(NTC):
                sl = slice(tc * 512, (tc + 1) * 512)
                ps, pk = ctx.ps[4], "psb4"
                p.op("pe", lambda t_, sl=sl, ps=ps: t_.matmul(out=ps[:, 0:512],
                     lhsT=seg4[:], rhs=rows[0:4, sl], start=True, stop=True),
                     reads=["D_seg4", "D_rw2"], writes=[pk])
                p.op("act", lambda s, sl=sl, ps=ps: s.copy(out=rb[:, sl],
                     in_=ps[:, 0:512]), reads=[pk], writes=["D_rb"])
            odA, odB = base + pt, base + 4 + pt
            p.dma("pool", ta[:], qkvsp[odA * 128:(odA + 1) * 128, :],
                  reads=[("d_qkv", odA)], writes=["D_ta"])
            p.dma("pool", tb[:], qkvsp[odB * 128:(odB + 1) * 128, :],
                  reads=[("d_qkv", odB)], writes=["D_tb"])
            p.op("dve", lambda v: v.tensor_tensor(out=ta[:], in0=ta[:], in1=rb[:],
                 op=ALU.mult), reads=["D_ta", "D_rb"], writes=["D_ta"])
            p.op("dve", lambda v: v.tensor_tensor(out=tb[:], in0=tb[:], in1=rb[:],
                 op=ALU.mult), reads=["D_tb", "D_rb"], writes=["D_tb"])
            p.op("dve", lambda v: v.tensor_tensor(out=tn1[:], in0=ta[:], in1=cosb[:],
                 op=ALU.mult), reads=["D_ta", "D_cos"], writes=["D_tn1"])
            p.op("dve", lambda v: v.tensor_tensor(out=tn2[:], in0=tb[:], in1=sinb[:],
                 op=ALU.mult), reads=["D_tb", "D_sin"], writes=["D_tn2"])
            p.op("dve", lambda v: v.tensor_tensor(out=tn1[:], in0=tn1[:], in1=tn2[:],
                 op=ALU.add), reads=["D_tn1", "D_tn2"], writes=["D_tn1"])
            p.op("dve", lambda v: v.tensor_tensor(out=tn2[:], in0=tb[:], in1=cosb[:],
                 op=ALU.mult), reads=["D_tb", "D_cos"], writes=["D_tn2"])
            p.op("dve", lambda v: v.tensor_tensor(out=ta[:], in0=ta[:], in1=sinb[:],
                 op=ALU.mult), reads=["D_ta", "D_sin"], writes=["D_ta"])
            p.op("dve", lambda v: v.tensor_tensor(out=tn2[:], in0=tn2[:], in1=ta[:],
                 op=ALU.subtract), reads=["D_tn2", "D_ta"], writes=["D_tn2"])
            consume(pt, tn1, "D_tn1")
            consume(4 + pt, tn2, "D_tn2")
            for gi, odI in ((8, base + 8 + pt), (12, base + 12 + pt)):
                p.dma("pool", ta[:], qkvsp[odI * 128:(odI + 1) * 128, :],
                      reads=[("d_qkv", odI)], writes=["D_ta"])
                p.op("dve", lambda v: v.tensor_tensor(out=ta[:], in0=ta[:], in1=rb[:],
                     op=ALU.mult), reads=["D_ta", "D_rb"], writes=["D_ta"])
                consume(gi + pt, ta, "D_ta")
        ctx.afree("D_rw2", "D_ta", "D_tb", "D_tn1", "D_tn2", "D_rb")

    vl = ctx.aget("D_vl", [128, T], BF16)
    sc = ctx.aget("D_sc", [128, T], BF16)

    def consume_k(odp, tile, tkey):
        vod = 32 + odp
        p.dma("pool", vl[:], qkvsp[vod * 128:(vod + 1) * 128, :],
              reads=[("d_qkv", vod)], writes=["D_vl"])
        p.op("act", lambda s, tile=tile: s.activation(out=sc[:], in_=tile[:],
             func=AF.Sigmoid), reads=[tkey], writes=["D_sc"])
        p.op("dve", lambda v: v.tensor_tensor(out=sc[:], in0=sc[:], in1=vl[:],
             op=ALU.mult), reads=["D_sc", "D_vl"], writes=["D_sc"])
        p.op("dve", lambda v: v.tensor_tensor_scan(out=sc[:],
             data0=one_sc[:, 0:1].broadcast_to((128, T)), data1=sc[:], initial=0.0,
             op0=ALU.mult, op1=ALU.add), reads=["D_one", "D_sc"], writes=["D_sc"])
        p.dma("pool", ssp[odp * 128:(odp + 1) * 128, :], sc[:], reads=["D_sc"],
              writes=[("d_ssp", odp)], stream="dma_st_ssp")

    process("k", 16, kr_sp, "d_rk", consume_k)
    ctx.afree("D_vl", "D_sc")

    yt = [ctx.aget(f"D_y{j}", [128, T], BF16) for j in (0, 1)]
    sl_ = ctx.aget("D_sl", [128, T], BF16)

    def consume_q(odp, tile, tkey):
        yj, yk = yt[odp % 2], f"D_y{odp % 2}"
        p.dma("pool", sl_[:], ssp[odp * 128:(odp + 1) * 128, :],
              reads=[("d_ssp", odp)], writes=["D_sl"])
        p.op("dve", lambda v, yj=yj, tile=tile, yk=yk: v.tensor_scalar(out=yj[:],
             in0=tile[:], scalar1=0.0, scalar2=None, op0=ALU.max),
             reads=[tkey], writes=[yk])
        p.op("dve", lambda v, yj=yj, yk=yk: v.tensor_tensor(out=yj[:], in0=yj[:],
             in1=sl_[:], op=ALU.mult), reads=[yk, "D_sl"], writes=[yk])
        p.dma("pool", ysp[odp * 128:(odp + 1) * 128, :], yj[:], reads=[yk],
              writes=[("d_ysp", odp)], stream="dma_st_ysp")

    process("q", 0, qr_sp, "d_rq", consume_q)
    ctx.afree("D_sl", "D_y0", "D_y1")

    mid = [ctx.aget(f"D_m{j}", [128, T], BF16) for j in (0, 1)]
    yre = [ctx.aget(f"D_yr{j}", [128, 16 * 512], BF16) for j in (0, 1)]
    w10 = wload(ctx, "D_cw1a", d["cw1T"], 2048, 0, 128)
    w11 = wload(ctx, "D_cw1b", d["cw1T"], 2048, 128, 128)
    for tc in range(NTC):
        sl = slice(tc * 512, (tc + 1) * 512)
        yr, yrk = yre[tc % 2], f"D_yr{tc % 2}"
        p.dma("pool", yr[:].rearrange("p (i c) -> p i c", c=512),
              ysp[:, sl].rearrange("(i p) c -> p i c", p=128),
              reads=[("d_ysp", i2) for i2 in range(16)], writes=[yrk])
        for od, wt, wtk in ((0, w10, "D_cw1a"), (1, w11, "D_cw1b")):
            bank = 2 + od
            ps, pk = ctx.ps[bank], f"psb{bank}"
            mac(p, ps, pk, [(wslice(wt, i, 128), wtk, yr[:, i * 512:(i + 1) * 512],
                             yrk) for i in range(16)])
            p.op("act", lambda s, od=od, sl=sl, ps=ps: s.copy(out=mid[od][:, sl],
                 in_=ps[:, 0:512]), reads=[pk], writes=[f"D_m{od}"])
    ctx.afree("D_yr0", "D_yr1", "D_cw1a", "D_cw1b")

    gsil = [ctx.aget(f"D_g{od}", [128, T], BF16) for od in range(8)]
    for od in range(8):
        wt = wload(ctx, f"D_cw2{od % 2}", d["cw2T"], 256, 1024 + od * 128, 128)
        g, gk = gsil[od], f"D_g{od}"
        for tc in range(NTC):
            sl = slice(tc * 512, (tc + 1) * 512)
            bank = tc % 2
            ps, pk = ctx.ps[bank], f"psb{bank}"
            mac(p, ps, pk, [(wslice(wt, i, 128), f"D_cw2{od % 2}", mid[i][:, sl],
                             f"D_m{i}") for i in range(2)])
            p.op("act", lambda s, g=g, sl=sl, ps=ps, gk=gk: s.activation(out=g[:, sl],
                 in_=ps[:, 0:512], func=AF.Silu), reads=[pk], writes=[gk])
    xr = [ctx.aget(f"D_xr{j}", [128, T], F32) for j in (0, 1)]
    ot = [ctx.aget(f"D_ot{j}", [128, T], F32) for j in (0, 1)]
    for od in range(8):
        wt = wload(ctx, f"D_cw2{od % 2}", d["cw2T"], 256, od * 128, 128)
        xrj, xrk = xr[od % 2], f"D_xr{od % 2}"
        oj, ok = ot[od % 2], f"D_ot{od % 2}"
        p.dma("sp", xrj[:], xsum[od * 128:(od + 1) * 128, :],
              reads=[("d_xsum", od)], writes=[xrk])
        for tc in range(NTC):
            sl = slice(tc * 512, (tc + 1) * 512)
            bank = tc % 2
            ps, pk = ctx.ps[bank], f"psb{bank}"
            mac(p, ps, pk, [(wslice(wt, i, 128), f"D_cw2{od % 2}", mid[i][:, sl],
                             f"D_m{i}") for i in range(2)])
            p.op("dve", lambda v, oj=oj, sl=sl, ps=ps, od=od, ok=ok: v.tensor_tensor(
                 out=oj[:, sl], in0=ps[:, 0:512], in1=gsil[od][:, sl], op=ALU.mult),
                 reads=[pk, f"D_g{od}"], writes=[ok])
            p.op("dve", lambda v, oj=oj, sl=sl, xrj=xrj, ok=ok, xrk=xrk:
                 v.tensor_tensor(out=oj[:, sl], in0=oj[:, sl], in1=xrj[:, sl],
                 op=ALU.add), reads=[ok, xrk], writes=[ok])
        p.dma("sp", d["xout"][od * 128:(od + 1) * 128, :], oj[:], reads=[ok],
              stream="dma_st_xout")
    ctx.afree("D_m0", "D_m1", "D_cw20", "D_cw21", "D_xr0", "D_xr1", "D_ot0",
              "D_ot1", *[f"D_g{od}" for od in range(8)])

"""L2_ffn block emitter: dense 16-expert top-2 MoE (DFF=512) + fused shared
(FUSED_DFF=1024, weight 1).

x_in = LN_outer(x); logits = x_in @ gate_w.T; scores = sigmoid(logits);
top2 by (logits + expert_bias); cw = scores@top2 / (sum + 1e-10)
per e: h = silu(x@w1[e].T+b1[e]) * (x@w3[e].T+b3[e]); y += cw_e*(h@w2[e].T+b2[e])
shared: same with sw1/sw3/sw2, cw=1
x_next = x + y
"""




def emit_L2F(ctx, d):
    p = ctx.p
    nc = ctx.nc
    accsp = nc.dram_tensor("E_acc", [1024, T], F32, kind="Internal").ap()
    cwtsp = nc.dram_tensor("E_cwt", [16, T], BF16, kind="Internal").ap()

    b1pc = load_pcol(ctx, "E_b1", d["b1_pc"], 16 * 4 + 8)
    b3pc = load_pcol(ctx, "E_b3", d["b3_pc"], 16 * 4 + 8)
    b2sb = ctx.aget("E_b2", [1, 18 * 1024], BF16)
    p.dma("pool", b2sb[:], d["b2_rows"], writes=["E_b2"])
    wg = ctx.aget("E_wg", [128, 8 * 16], BF16)
    p.dma("pool", wg[:].rearrange("p (i c) -> p i c", c=16),
          d["gwT"].rearrange("(i p) c -> p i c", p=128), writes=["E_wg"])
    ident = ctx.aget("E_id", [128, 128], BF16)
    p.dma("pool", ident[:], d["ident"], writes=["E_id"])
    ebias = ctx.aget("E_eb", [128, 16], F32)
    p.dma("sp", ebias[:], d["ebias_b"], writes=["E_eb"])

    xbf = load_xbf(ctx, "E_x", d["x"])
    xk = [f"E_x{i}" for i in range(8)]
    a_row, ak, mu_row, mk = ln_rows(ctx, xbf, xk, kind="ln", eps=1e-5, name="E_ln")
    xln, _ = ln_materialize(ctx, xbf, xk, a_row, ak, mu_row, mk, name="E_xm",
                            inplace=True)

    cwT = ctx.aget("E_cwT", [16, T], BF16)
    lg = ctx.aget("E_lg", [128, 16], F32)
    sc = ctx.aget("E_sc", [128, 16], F32)
    m1 = ctx.aget("E_m1", [128, 1], F32)
    m2 = ctx.aget("E_m2", [128, 1], F32)
    s12 = ctx.aget("E_s12", [128, 1], F32)
    msk = ctx.aget("E_msk", [128, 16], F32)
    sel = ctx.aget("E_sel", [128, 16], F32)
    cwt = ctx.aget("E_cw", [128, 16], BF16)
    pst = ctx.pstT[0:128, 0:128]
    for tt in range(T // 128):
        sl = slice(tt * 128, (tt + 1) * 128)
        ps, pk = ctx.ps[6], "psb6"
        for i in range(8):
            p.op("pe", lambda t_, i=i, sl=sl, ps=ps: t_.matmul(out=ps[:, 0:16],
                 lhsT=xln[i][:, sl], rhs=wg[:, i * 16:(i + 1) * 16],
                 start=(i == 0), stop=(i == 7)), reads=[xk[i], "E_wg"],
                 writes=[pk], inc=(i == 7))
        p.op("act", lambda s, ps=ps: s.activation(out=sc[:], in_=ps[:, 0:16],
             func=AF.Sigmoid), reads=[pk], writes=["E_sc"])
        p.op("dve", lambda v, ps=ps: v.tensor_tensor(out=lg[:], in0=ps[:, 0:16],
             in1=ebias[:, 0:16], op=ALU.add), reads=[pk, "E_eb"], writes=["E_lg"])
        p.op("dve", lambda v: v.tensor_reduce(out=m1[:, 0:1], in_=lg[:], axis=AX.X,
             op=ALU.max), reads=["E_lg"], writes=["E_m1"])
        p.op("dve", lambda v: v.tensor_scalar(out=msk[:], in0=lg[:],
             scalar1=m1[:, 0:1], scalar2=None, op0=ALU.is_ge), reads=["E_lg", "E_m1"],
             writes=["E_msk"])
        p.op("dve", lambda v: v.scalar_tensor_tensor(out=sel[:], in0=msk[:],
             scalar=-1e9, in1=lg[:], op0=ALU.mult, op1=ALU.add),
             reads=["E_msk", "E_lg"], writes=["E_sel"])
        p.op("dve", lambda v: v.tensor_reduce(out=m2[:, 0:1], in_=sel[:], axis=AX.X,
             op=ALU.max), reads=["E_sel"], writes=["E_m2"])
        p.op("dve", lambda v: v.tensor_scalar(out=sel[:], in0=sel[:],
             scalar1=m2[:, 0:1], scalar2=None, op0=ALU.is_ge), reads=["E_sel", "E_m2"],
             writes=["E_sel"])
        p.op("dve", lambda v: v.tensor_tensor(out=msk[:], in0=msk[:], in1=sel[:],
             op=ALU.add), reads=["E_msk", "E_sel"], writes=["E_msk"])
        p.op("dve", lambda v: v.tensor_tensor(out=msk[:], in0=msk[:], in1=sc[:],
             op=ALU.mult), reads=["E_msk", "E_sc"], writes=["E_msk"])
        p.op("dve", lambda v: v.tensor_reduce(out=s12[:, 0:1], in_=msk[:], axis=AX.X,
             op=ALU.add), reads=["E_msk"], writes=["E_s12"])
        p.op("dve", lambda v: v.tensor_scalar(out=s12[:], in0=s12[:], scalar1=1e-10,
             scalar2=None, op0=ALU.add), reads=["E_s12"], writes=["E_s12"])
        p.op("dve", lambda v: v.reciprocal(out=s12[:], in_=s12[:]), reads=["E_s12"],
             writes=["E_s12"])
        p.op("dve", lambda v: v.tensor_scalar(out=cwt[:], in0=msk[:],
             scalar1=s12[:, 0:1], scalar2=None, op0=ALU.mult),
             reads=["E_msk", "E_s12"], writes=["E_cw"])
        p.op("pe", lambda t_: t_.transpose(out=pst[0:16, 0:128], in_=cwt[:, 0:16],
             identity=ident[:]), reads=["E_cw", "E_id"], writes=["pstT"])
        p.op("act", lambda s, sl=sl: s.copy(out=cwT[0:16, sl], in_=pst[0:16, 0:128]),
             reads=["pstT"], writes=["E_cwT"])
    p.dma("pool", cwtsp, cwT[:], reads=["E_cwT"], writes=[("d_cwt", 0)],
          stream="dma_st_cwt")
    ctx.afree("E_cwT", "E_lg", "E_sc", "E_m1", "E_m2", "E_s12", "E_msk", "E_sel",
              "E_cw")

    for e in range(18):
        nh = 4
        if e == 16:
            w1src, w3src = d["sw1T"][:, 0:512], d["sw3T"][:, 0:512]
            w2src = d["sw2T"][0:512, :]
        elif e == 17:
            w1src, w3src = d["sw1T"][:, 512:1024], d["sw3T"][:, 512:1024]
            w2src = d["sw2T"][512:1024, :]
        else:
            w1src, w3src, w2src = d["w1T"][e], d["w3T"][e], d["w2T"][e]
        w1r = ctx.aget("E_w1r", [128, 8 * 512], BF16)
        w3r = ctx.aget("E_w3r", [128, 8 * 512], BF16)
        w2r = ctx.aget("E_w2r", [128, 4 * 1024], BF16)
        p.dma("pool", w1r[:].rearrange("p (i c) -> p i c", c=nh * 128),
              w1src.rearrange("(i p) c -> p i c", p=128), writes=["E_w1r"])
        p.dma("pool", w3r[:].rearrange("p (i c) -> p i c", c=nh * 128),
              w3src.rearrange("(i p) c -> p i c", p=128), writes=["E_w3r"])
        p.dma("pool", w2r[:].rearrange("p (i c) -> p i c", c=1024),
              w2src.rearrange("(i p) c -> p i c", p=128), writes=["E_w2r"])
        cwr = ctx.aget("E_cwr", [1, T], BF16)
        if e >= 16:
            p.op("dve", lambda v, cwr=cwr: v.memset(cwr[:], 1.0), writes=["E_cwr"])
        else:
            p.dma("pool", cwr[:], cwtsp[e:e + 1, :], reads=[("d_cwt", 0)],
                  writes=["E_cwr"])
        ht = [ctx.aget(f"E_h{j}", [128, T], BF16) for j in range(nh)]
        t2r = [ctx.aget(f"E_t2{j}", [128, 512], BF16) for j in (0, 1)]
        boff = e * 4
        for hod in range(nh):
            h, hk2 = ht[hod], f"E_h{hod}"
            for tc in range(NTC):
                sl = slice(tc * 512, (tc + 1) * 512)
                b1_, b2_ = (tc % 2) * 2, (tc % 2) * 2 + 1
                ps1, pk1 = ctx.ps[b1_], f"psb{b1_}"
                ps2, pk2 = ctx.ps[b2_], f"psb{b2_}"
                mac(p, ps1, pk1, [(w1r[:, i * nh * 128 + hod * 128:
                                       i * nh * 128 + (hod + 1) * 128], "E_w1r",
                                   xln[i][:, sl], xk[i]) for i in range(8)])
                mac(p, ps2, pk2, [(w3r[:, i * nh * 128 + hod * 128:
                                       i * nh * 128 + (hod + 1) * 128], "E_w3r",
                                   xln[i][:, sl], xk[i]) for i in range(8)])
                t2, t2k = t2r[tc % 2], f"E_t2{tc % 2}"
                p.op("act", lambda s, h=h, sl=sl, ps1=ps1, hod=hod, hk2=hk2:
                     s.activation(out=h[:, sl], in_=ps1[:, 0:512], func=AF.Silu,
                     bias=b1pc[:, boff + hod:boff + hod + 1]),
                     reads=[pk1, "E_b1"], writes=[hk2])
                p.op("act", lambda s, t2=t2, ps2=ps2, hod=hod, t2k=t2k:
                     s.activation(out=t2[:], in_=ps2[:, 0:512], func=AF.Identity,
                     bias=b3pc[:, boff + hod:boff + hod + 1]),
                     reads=[pk2, "E_b3"], writes=[t2k])
                p.op("dve", lambda v, h=h, sl=sl, t2=t2, hk2=hk2, t2k=t2k:
                     v.tensor_tensor(out=h[:, sl], in0=h[:, sl], in1=t2[:],
                     op=ALU.mult), reads=[hk2, t2k], writes=[hk2])
        ctx.afree("E_w1r", "E_w3r")
        for og in range(2):
            cwb = [ctx.aget(f"E_cb{j}", [128, 512], BF16) for j in (0, 1)]
            yt = [ctx.aget(f"E_yt{od4}", [128, T], BF16) for od4 in range(4)]
            for tc in range(NTC):
                sl = slice(tc * 512, (tc + 1) * 512)
                cb_, cbk = cwb[tc % 2], f"E_cb{tc % 2}"
                ps7, pk7 = ctx.ps[6], "psb6"
                p.op("pe", lambda t_, sl=sl, cwr=cwr, ps7=ps7: t_.matmul(
                     out=ps7[:, 0:512], lhsT=ctx.ones_row[:], rhs=cwr[0:1, sl],
                     start=True, stop=True), reads=["ones_row", "E_cwr"], writes=[pk7])
                p.op("act", lambda s, cb_=cb_, ps7=ps7, cbk=cbk: s.copy(out=cb_[:],
                     in_=ps7[:, 0:512]), reads=[pk7], writes=[cbk])
                for od4 in range(4):
                    od = og * 4 + od4
                    bank = 2 + od4
                    ps, pk = ctx.ps[bank], f"psb{bank}"
                    pairs = [(b2sb[0:1, e * 1024 + od * 128:e * 1024 + (od + 1) * 128],
                              "E_b2", ctx.ones512[:], "ones512")]
                    pairs += [(w2r[:, i * 1024 + od * 128:i * 1024 + (od + 1) * 128],
                               "E_w2r", ht[i][:, sl], f"E_h{i}")
                              for i in range(nh)]
                    mac(p, ps, pk, pairs)
                    p.op("dve", lambda v, od4=od4, sl=sl, ps=ps, cb_=cb_, cbk=cbk:
                         v.tensor_tensor(out=yt[od4][:, sl], in0=ps[:, 0:512],
                         in1=cb_[:], op=ALU.mult), reads=[pk, cbk],
                         writes=[f"E_yt{od4}"])
            for od4 in range(4):
                od = og * 4 + od4
                if e == 0:
                    p.dma("pool", accsp[od * 128:(od + 1) * 128, :], yt[od4][:],
                          reads=[f"E_yt{od4}"], writes=[("d_acc", od)],
                          stream="dma_st_acc")
                else:
                    for ch in range(8):
                        cs = slice(ch * 512, (ch + 1) * 512)
                        p.dma_accum("pool", accsp[od * 128:(od + 1) * 128, cs],
                                    yt[od4][:, cs], reads=[f"E_yt{od4}"],
                                    writes=[("d_acc", od)], stream="dma_st_acc")
            ctx.afree("E_cb0", "E_cb1", *[f"E_yt{od4}" for od4 in range(4)])
        ctx.afree("E_w1r", "E_w3r", "E_w2r", "E_cwr",
                  *[f"E_h{j}" for j in range(nh)], "E_t20", "E_t21")

    ctx.afree(*xk)
    xf = [ctx.aget(f"E_xf{j}", [128, T], F32) for j in (0, 1)]
    ar = [ctx.aget(f"E_ar{j}", [128, T], F32) for j in (0, 1)]
    for od in range(8):
        xfj, xfk = xf[od % 2], f"E_xf{od % 2}"
        arj, ark = ar[od % 2], f"E_ar{od % 2}"
        p.dma("sp", xfj[:], d["x"][od * 128:(od + 1) * 128, :], writes=[xfk])
        p.dma("sp", arj[:], accsp[od * 128:(od + 1) * 128, :], reads=[("d_acc", od)],
              writes=[ark])
        p.op("dve", lambda v, arj=arj, xfj=xfj, ark=ark, xfk=xfk: v.tensor_tensor(
             out=arj[:], in0=arj[:], in1=xfj[:], op=ALU.add), reads=[ark, xfk],
             writes=[ark])
        p.dma("sp", d["xout"][od * 128:(od + 1) * 128, :], arj[:], reads=[ark],
              stream="dma_st_xout")
    ctx.afree("E_xf0", "E_xf1", "E_ar0", "E_ar1")

"""L3_attn block emitter: causal softmax attention, d=1024, T=4096.

x_in = LN_outer(x); q=(x_in@Wq+bq)/32; k=x_in@Wk+bk; v=x_in@Wv+bv
P = softmax(causal(q @ k.T)); ctx = P @ v; out = ctx@Wo+bo; x_next = x + out
Weights are [in,out] native. Per q-tile (128): scores vs all k<=q in 512
windows (fp32 in SBUF), single max/exp pass (accum rowsum), P^T via PE
transpose, PV accumulation, ctx->fm via transpose, Wo in fm + residual.
"""




def emit_L3A(ctx, d):
    p = ctx.p
    nc = ctx.nc
    qsp = nc.dram_tensor("F_qsp", [1024, T], BF16, kind="Internal").ap()
    ksp = nc.dram_tensor("F_ksp", [1024, T], BF16, kind="Internal").ap()

    bqp = load_pcol(ctx, "F_bq", d["bq_pc"], 8)
    bkp = load_pcol(ctx, "F_bk", d["bk_pc"], 8)
    bop = load_pcol(ctx, "F_bo", d["bo_pc"], 8)
    bvr = ctx.aget("F_bv", [1, 1024], BF16)
    p.dma("pool", bvr[:], d["bv_row"], writes=["F_bv"])
    ident = ctx.aget("F_id", [128, 128], BF16)
    p.dma("pool", ident[:], d["ident"], writes=["F_id"])
    cmask = ctx.aget("F_cm", [128, 4 * 512], F32)
    p.dma("sp", cmask[:], d["cmask"], writes=["F_cm"])
    c32, ck32 = ctx.cf(1.0 / 32.0)

    p.op("dve", lambda v: v.tensor_scalar(out=bqp[:], in0=bqp[:], scalar1=1.0 / 32,
         scalar2=None, op0=ALU.mult), reads=["F_bq"], writes=["F_bq"])

    xbf = load_xbf(ctx, "F_x", d["x"])
    xk = [f"F_x{i}" for i in range(8)]
    a_row, ak, mu_row, mk = ln_rows(ctx, xbf, xk, kind="ln", eps=1e-5, name="F_ln")
    xln, _ = ln_materialize(ctx, xbf, xk, a_row, ak, mu_row, mk, name="F_xm",
                            inplace=True)

    qv = [ctx.aget(f"F_q{j}", [128, T], BF16) for j in (0, 1)]
    for part, sp, key in (("q", qsp, "d_qsp"), ("k", ksp, "d_ksp")):
        wt0 = d["wqT"] if part == "q" else d["wkT"]
        for od in range(8):
            wt = wload(ctx, f"F_w{od % 2}", wt0, 1024, od * 128, 128)
            wk2 = f"F_w{od % 2}"
            dst, dk = qv[od % 2], f"F_q{od % 2}"
            for tc in range(NTC):
                sl = slice(tc * 512, (tc + 1) * 512)
                bank = tc % 2
                ps, pk = ctx.ps[bank], f"psb{bank}"
                mac(p, ps, pk, [(wslice(wt, i, 128), wk2, xln[i][:, sl], xk[i])
                                for i in range(8)])
                if part == "q":
                    p.op("act", lambda s, dst=dst, sl=sl, ps=ps, od=od, dk=dk:
                         s.activation(out=dst[:, sl], in_=ps[:, 0:512],
                         func=AF.Identity, bias=bqp[:, od:od + 1],
                         scale=c32[:, 0:1]), reads=[pk, "F_bq", ck32], writes=[dk])
                else:
                    p.op("act", lambda s, dst=dst, sl=sl, ps=ps, od=od, dk=dk:
                         s.activation(out=dst[:, sl], in_=ps[:, 0:512],
                         func=AF.Identity, bias=bkp[:, od:od + 1]),
                         reads=[pk, "F_bk"], writes=[dk])
            p.dma("pool", sp[od * 128:(od + 1) * 128, :], dst[:], reads=[dk],
                  writes=[(key, od)], stream="dma_st_" + part)
    ctx.afree("F_q0", "F_q1", "F_w0", "F_w1")

    vres = ctx.aget("F_vres", [128, 32 * 1024], BF16)
    wv = []
    for half in range(2):
        wv.append(wload(ctx, f"F_wv{half}", d["wvT"], 1024, half * 512, 512))
    for vt in range(32):
        tsl = slice(vt * 128, (vt + 1) * 128)
        for half in range(2):
            bank = 4 + half
            ps, pk = ctx.ps[bank], f"psb{bank}"
            pairs = [(ctx.ones_row[:], "ones_row",
                      bvr[0:1, half * 512:(half + 1) * 512], "F_bv")]
            pairs += [(xln[i][:, tsl], xk[i],
                       wslice(wv[half], i, 512), f"F_wv{half}") for i in range(8)]
            mac(p, ps, pk, pairs)
            p.op("act", lambda s, vt=vt, half=half, ps=ps: s.copy(
                 out=vres[:, vt * 1024 + half * 512:vt * 1024 + (half + 1) * 512],
                 in_=ps[:, 0:512]), reads=[pk], writes=["F_vres"])
    ctx.afree("F_wv0", "F_wv1")

    ctx.afree(*xk)
    kt_ = []
    for i in range(8):
        tn = f"F_k{i}"
        tt2 = ctx.aget(tn, [128, T], BF16)
        p.dma("pool", tt2[:], ksp[i * 128:(i + 1) * 128, :], reads=[("d_ksp", i)],
              writes=[tn])
        kt_.append(tt2)

    wo = wload(ctx, "F_wo", d["woT"], 1024, 0, 1024)
    qsl = ctx.aget("F_qsl", [128, 8 * 128], BF16)
    srow = ctx.aget("F_sr", [128, T], F32)
    prow = ctx.aget("F_pr", [128, T], BF16)
    mrow = ctx.aget("F_mr", [128, 1], F32)
    lrow = ctx.aget("F_lr", [128, 1], F32)
    ptall = ctx.aget("F_pta", [128, 32 * 128], BF16)
    cfm = ctx.aget("F_cfm", [128, 8 * 128], BF16)
    xr = ctx.aget("F_xr", [128, 128], F32)
    pst = ctx.pstT[0:128, 0:512]
    for qt in range(32):
        p.dma("pool", qsl[:].rearrange("p (i c) -> p i c", c=128),
              qsp[:, qt * 128:(qt + 1) * 128].rearrange("(i p) c -> p i c", p=128),
              reads=[("d_qsp", i2) for i2 in range(8)], writes=["F_qsl"])
        nw = qt // 4 + 1
        dw = qt // 4
        for w in range(nw):
            bank = w % 2
            ps, pk = ctx.ps[bank], f"psb{bank}"
            mac(p, ps, pk, [(qsl[:, i * 128:(i + 1) * 128], "F_qsl",
                             kt_[i][:, w * 512:(w + 1) * 512], f"F_k{i}")
                            for i in range(8)])
            if w == dw:
                m4 = qt % 4
                p.op("dve", lambda v, w=w, ps=ps, m4=m4: v.tensor_tensor(
                     out=srow[:, w * 512:(w + 1) * 512], in0=ps[:, 0:512],
                     in1=cmask[:, m4 * 512:(m4 + 1) * 512], op=ALU.add),
                     reads=[pk, "F_cm"], writes=["F_sr"])
            else:
                p.op("act", lambda s, w=w, ps=ps: s.copy(
                     out=srow[:, w * 512:(w + 1) * 512], in_=ps[:, 0:512]),
                     reads=[pk], writes=["F_sr"])
        ncol = nw * 512
        p.op("dve", lambda v, ncol=ncol: v.tensor_reduce(out=mrow[:, 0:1],
             in_=srow[:, 0:ncol], axis=AX.X, op=ALU.max), reads=["F_sr"],
             writes=["F_mr"])
        p.op("dve", lambda v: v.tensor_scalar(out=mrow[:], in0=mrow[:], scalar1=-1.0,
             scalar2=None, op0=ALU.mult), reads=["F_mr"], writes=["F_mr"])
        p.op("act", lambda s, ncol=ncol: s.activation(out=prow[:, 0:ncol],
             in_=srow[:, 0:ncol], func=AF.Exp, bias=mrow[:, 0:1],
             accum_out=lrow[:, 0:1]), reads=["F_sr", "F_mr"],
             writes=["F_pr", "F_lr"])
        p.op("dve", lambda v: v.reciprocal(out=lrow[:], in_=lrow[:]),
             reads=["F_lr"], writes=["F_lr"])
        for kt in range(qt + 1):
            p.op("pe", lambda t_, kt=kt: t_.transpose(out=pst[0:128, 0:128],
                 in_=prow[:, kt * 128:(kt + 1) * 128], identity=ident[:]),
                 reads=["F_pr", "F_id"], writes=["pstT"])
            p.op("act", lambda s, kt=kt: s.copy(out=ptall[:, kt * 128:(kt + 1) * 128],
                 in_=pst[0:128, 0:128]), reads=["pstT"], writes=["F_pta"])
        for half in range(2):
            bank = 4 + half
            ps, pk = ctx.ps[bank], f"psb{bank}"
            for kt in range(qt + 1):
                p.op("pe", lambda t_, kt=kt, half=half, ps=ps, qt=qt: t_.matmul(
                     out=ps[:, 0:512], lhsT=ptall[:, kt * 128:(kt + 1) * 128],
                     rhs=vres[:, kt * 1024 + half * 512:kt * 1024 + (half + 1) * 512],
                     start=(kt == 0), stop=(kt == qt)),
                     reads=["F_pta", "F_vres"], writes=[pk], inc=(kt == qt))
        for half in range(2):
            bank = 4 + half
            ps, pk = ctx.ps[bank], f"psb{bank}"
            p.op("act", lambda s, half=half, ps=ps: s.activation(
                 out=prow[:, half * 512:(half + 1) * 512], in_=ps[:, 0:512],
                 func=AF.Identity, scale=lrow[:, 0:1]),
                 reads=[pk, "F_lr"], writes=["F_pr"])
        for i in range(8):
            p.op("pe", lambda t_, i=i: t_.transpose(out=pst[0:128, 0:128],
                 in_=prow[:, i * 128:(i + 1) * 128], identity=ident[:]),
                 reads=["F_pr", "F_id"], writes=["pstT"])
            p.op("act", lambda s, i=i: s.copy(out=cfm[:, i * 128:(i + 1) * 128],
                 in_=pst[0:128, 0:128]), reads=["pstT"], writes=["F_cfm"])
        for od in range(8):
            bank = 2 + od % 2
            ps, pk = ctx.ps[bank], f"psb{bank}"
            mac(p, ps, pk, [(wo[:, i * 1024 + od * 128:i * 1024 + (od + 1) * 128],
                             "F_wo", cfm[:, i * 128:(i + 1) * 128], "F_cfm")
                            for i in range(8)], n=128)
            p.dma("sp", xr[:], d["x"][od * 128:(od + 1) * 128,
                                      qt * 128:(qt + 1) * 128], writes=["F_xr"])
            p.op("act", lambda s, od=od, ps=ps: s.activation(out=srow[:, 0:128],
                 in_=ps[:, 0:128], func=AF.Identity, bias=bop[:, od:od + 1]),
                 reads=[pk, "F_bo"], writes=["F_sr"])
            p.op("dve", lambda v: v.tensor_tensor(out=srow[:, 0:128],
                 in0=srow[:, 0:128], in1=xr[:], op=ALU.add),
                 reads=["F_sr", "F_xr"], writes=["F_sr"])
            p.dma("sp", d["xout"][od * 128:(od + 1) * 128, qt * 128:(qt + 1) * 128],
                  srow[:, 0:128], reads=["F_sr"], stream="dma_st_xout")
    ctx.afree("F_qsl", "F_sr", "F_pr", "F_mr", "F_lr", "F_pta",
              "F_cfm", "F_xr", "F_wo", "F_vres", *[f"F_k{i}" for i in range(8)])

"""L3_ffn block emitter: spiky MoE with global percentile + NaN-usage branch.

Unfused operation: a separate tiny "pre" program computes each core's imp
row; the host concatenates to impall [8, 4096] which this program receives.

x_in = LN(x); imp = 32*sqrt(v/(v+1e-5)) per token (exact from LN stats)
thr = percentile(impall, 90) (two bisections + linear interp)
temp = 1 + 0.1*std(impall); xs = where(imp>thr, x_in, 0)/((L1+1e-8)*temp)
gate: h1=silu(xs@gw1+gb1); hn=LN(h1)*lns+lno; h2=silu(hn@gw2+gb2);
logits=(h2@gw3+gb3+0.3*aff+noise*ns)/1.5 ; aff = mean_t(xs)@aff_w+aff_b
flag = any(usage+1e-8<=0): cw = flag ? [1/8,1/8,0..] : top2-masked softmax
experts: id k -> (e0,e1,e2 types); out = sum_k cw_k*expert_k(xs) + 0.1*xs
x_next = x + out
"""



KRANK0 = 29491
KRANK1 = 29492


def emit_L3F_pre(ctx, d):
    p = ctx.p
    xbf = load_xbf(ctx, "P_x", d["x"])
    xk = [f"P_x{i}" for i in range(8)]
    sq = [ctx.aget(f"P_sq{j}", [128, 512], BF16) for j in range(2)]
    vrow = ctx.aget("P_v", [1, T], F32)
    r1 = ctx.aget("P_r1", [1, 512], F32)
    r2 = ctx.aget("P_r2", [1, 512], F32)
    for tc in range(NTC):
        sl = slice(tc * 512, (tc + 1) * 512)
        psS, kS = ctx.ps[5], "psb5"
        psM, kM = ctx.ps[6], "psb6"
        for i in range(8):
            sqb, sqk = sq[i % 2], f"P_sq{i % 2}"
            p.op("act", lambda s, i=i, sl=sl, sqb=sqb: s.activation(out=sqb[:],
                 in_=xbf[i][:, sl], func=AF.Square), reads=[xk[i]], writes=[sqk])
            p.op("pe", lambda t_, i=i, sqb=sqb, psS=psS: t_.matmul(out=psS[0:1, 0:512],
                 lhsT=ctx.ones_col[:], rhs=sqb[:], start=(i == 0), stop=(i == 7)),
                 reads=[sqk, "ones_col"], writes=[kS], inc=(i == 7))
            p.op("pe", lambda t_, i=i, sl=sl, psM=psM: t_.matmul(out=psM[0:1, 0:512],
                 lhsT=ctx.ones_col[:], rhs=xbf[i][:, sl], start=(i == 0), stop=(i == 7)),
                 reads=[xk[i], "ones_col"], writes=[kM], inc=(i == 7))
        p.op("dve", lambda v, psM=psM: v.tensor_scalar(out=r1[0:1, :],
             in0=psM[0:1, 0:512], scalar1=1.0 / 1024, scalar2=None, op0=ALU.mult),
             reads=[kM], writes=["P_r1"])
        p.op("dve", lambda v: v.scalar_tensor_tensor(out=r1[0:1, :], in0=r1[0:1, :],
             scalar=-1.0, in1=r1[0:1, :], op0=ALU.mult, op1=ALU.mult),
             reads=["P_r1"], writes=["P_r1"])
        p.op("dve", lambda v, psS=psS: v.tensor_scalar(out=r2[0:1, :],
             in0=psS[0:1, 0:512], scalar1=1.0 / 1024, scalar2=None, op0=ALU.mult),
             reads=[kS], writes=["P_r2"])
        p.op("dve", lambda v, sl=sl: v.tensor_tensor(out=vrow[0:1, sl], in0=r2[0:1, :],
             in1=r1[0:1, :], op=ALU.add), reads=["P_r2", "P_r1"], writes=["P_v"])
    t1 = ctx.aget("P_t1", [1, T], F32)
    ce, cke = ctx.cf(1e-5)
    ch, ckh = ctx.cf(0.5)
    p.op("act", lambda s: s.activation(out=t1[0:1, :], in_=vrow[0:1, :], func=AF.Ln,
         bias=ce[0:1, 0:1]), reads=["P_v", cke], writes=["P_t1"])
    p.op("act", lambda s: s.activation(out=vrow[0:1, :], in_=vrow[0:1, :], func=AF.Ln),
         reads=["P_v"], writes=["P_v"])
    p.op("dve", lambda v: v.tensor_tensor(out=vrow[0:1, :], in0=vrow[0:1, :],
         in1=t1[0:1, :], op=ALU.subtract), reads=["P_v", "P_t1"], writes=["P_v"])
    p.op("act", lambda s: s.activation(out=vrow[0:1, :], in_=vrow[0:1, :], func=AF.Exp,
         scale=ch[0:1, 0:1]), reads=["P_v", ckh], writes=["P_v"])
    p.op("dve", lambda v: v.tensor_scalar(out=vrow[0:1, :], in0=vrow[0:1, :],
         scalar1=32.0, scalar2=None, op0=ALU.mult), reads=["P_v"], writes=["P_v"])
    p.dma("sp", d["imp_out"], vrow[0:1, :], reads=["P_v"], stream="dma_st_imp")
    ctx.afree(*xk, "P_sq0", "P_sq1", "P_v", "P_r1", "P_r2", "P_t1")


def bisect_rank(ctx, impall, krank, lo0, hi0, name):
    p = ctx.p
    lo = ctx.aget(name + "lo", [1, 1], F32)
    hi = ctx.aget(name + "hi", [1, 1], F32)
    mid = ctx.aget(name + "md", [1, 1], F32)
    mid8 = ctx.aget(name + "m8", [8, 1], F32)
    cmp8 = ctx.aget(name + "c8", [8, T], F32)
    cnt8 = ctx.aget(name + "ct", [8, 1], F32)
    cnt = ctx.aget(name + "cn", [1, 1], F32)
    pred = ctx.aget(name + "pd", [1, 1], F32)
    ta = ctx.aget(name + "ta", [1, 1], F32)
    tb = ctx.aget(name + "tb", [1, 1], F32)
    klo, khi, kmd = name + "lo", name + "hi", name + "md"
    kta, ktb = name + "ta", name + "tb"
    p.op("dve", lambda v: v.memset(lo[:], lo0), writes=[klo])
    p.op("dve", lambda v: v.memset(hi[:], hi0), writes=[khi])
    for it in range(36):
        p.op("dve", lambda v: v.tensor_tensor(out=mid[:], in0=lo[:], in1=hi[:],
             op=ALU.add), reads=[klo, khi], writes=[kmd])
        p.op("dve", lambda v: v.tensor_scalar(out=mid[:], in0=mid[:], scalar1=0.5,
             scalar2=None, op0=ALU.mult), reads=[kmd], writes=[kmd])
        ps, pk = ctx.ps[6], "psb6"
        p.op("pe", lambda t_: t_.matmul(out=ps[0:8, 0:1],
             lhsT=ctx.ones_rowf[0:1, 0:8], rhs=mid[0:1, 0:1], start=True, stop=True),
             reads=["ones_rowf", kmd], writes=[pk])
        p.op("dve", lambda v, ps=ps: v.tensor_copy(out=mid8[:], in_=ps[0:8, 0:1]),
             reads=[pk], writes=[name + "m8"])
        p.op("dve", lambda v: v.tensor_scalar(out=cmp8[:], in0=impall[:],
             scalar1=mid8[0:8, 0:1], scalar2=None, op0=ALU.is_le),
             reads=["G_imp", name + "m8"], writes=[name + "c8"])
        p.op("dve", lambda v: v.tensor_reduce(out=cnt8[:, 0:1], in_=cmp8[:],
             axis=AX.X, op=ALU.add), reads=[name + "c8"], writes=[name + "ct"])
        p.op("pe", lambda t_, ps=ps: t_.matmul(out=ps[0:1, 0:1],
             lhsT=ctx.ones_colf[0:8, 0:1], rhs=cnt8[0:8, 0:1], start=True, stop=True),
             reads=["ones_colf", name + "ct"], writes=[pk])
        p.op("dve", lambda v, ps=ps: v.tensor_copy(out=cnt[:], in_=ps[0:1, 0:1]),
             reads=[pk], writes=[name + "cn"])
        p.op("dve", lambda v, krank=krank: v.tensor_scalar(out=pred[:], in0=cnt[:],
             scalar1=float(krank), scalar2=None, op0=ALU.is_ge),
             reads=[name + "cn"], writes=[name + "pd"])
        p.op("dve", lambda v: v.tensor_tensor(out=ta[:], in0=mid[:],
             in1=hi[:], op=ALU.subtract), reads=[kmd, khi], writes=[kta])
        p.op("dve", lambda v: v.tensor_tensor(out=ta[:], in0=ta[:],
             in1=pred[:], op=ALU.mult), reads=[kta, name + "pd"], writes=[kta])
        p.op("dve", lambda v: v.tensor_tensor(out=hi[:], in0=hi[:], in1=ta[:],
             op=ALU.add), reads=[khi, kta], writes=[khi])
        p.op("dve", lambda v: v.tensor_tensor(out=ta[:], in0=mid[:],
             in1=lo[:], op=ALU.subtract), reads=[kmd, klo], writes=[kta])
        p.op("dve", lambda v: v.tensor_scalar(out=tb[:], in0=pred[:],
             scalar1=-1.0, scalar2=1.0, op0=ALU.mult, op1=ALU.add),
             reads=[name + "pd"], writes=[ktb])
        p.op("dve", lambda v: v.tensor_tensor(out=ta[:], in0=ta[:],
             in1=tb[:], op=ALU.mult), reads=[kta, ktb], writes=[kta])
        p.op("dve", lambda v: v.tensor_tensor(out=lo[:], in0=lo[:],
             in1=ta[:], op=ALU.add), reads=[klo, kta], writes=[klo])
    ctx.afree(name + "md", name + "m8", name + "c8", name + "ct", name + "cn",
              name + "pd", name + "ta", name + "tb", name + "lo")
    return hi, khi


def emit_L3F(ctx, d):
    p = ctx.p
    nc = ctx.nc
    accsp = nc.dram_tensor("G_acc", [1024, T], F32, kind="Internal").ap()
    cwtsp = nc.dram_tensor("G_cwt", [8, T], BF16, kind="Internal").ap()
    xssp = nc.dram_tensor("G_xssp", [1024, T], BF16, kind="Internal").ap()
    h1sp = nc.dram_tensor("G_h1sp", [4096, T], BF16, kind="Internal").ap()
    h2sp = nc.dram_tensor("G_h2sp", [2048, T], BF16, kind="Internal").ap()

    ctx.ones_colf = ctx.aget("ones_colf", [128, 1], F32)
    p.op("dve", lambda v: v.memset(ctx.ones_colf[:], 1.0), writes=["ones_colf"])

    impall = ctx.aget("G_imp", [8, T], F32)
    p.dma("sp", impall[:], d["impall"], writes=["G_imp"])
    usage = ctx.aget("G_us", [1, 8], F32)
    p.dma("sp", usage[:], d["usage"], writes=["G_us"])
    gb3b = ctx.aget("G_gb3", [128, 8], F32)
    p.dma("sp", gb3b[:], d["gb3_b"], writes=["G_gb3"])
    nanb = ctx.aget("G_nan", [128, 8], F32)
    p.dma("sp", nanb[:], d["nan_b"], writes=["G_nan"])
    affb = ctx.aget("G_afb", [1, 8], F32)
    p.dma("sp", affb[:], d["aff_b"], writes=["G_afb"])
    noise = ctx.aget("G_no", [128, 256], F32)
    p.dma("sp", noise[:], d["noise_pc"], writes=["G_no"])
    ident = ctx.aget("G_id", [128, 128], BF16)
    p.dma("pool", ident[:], d["ident"], writes=["G_id"])
    gb1 = load_pcol(ctx, "G_gb1", d["gb1_pc"], 8)
    lns = load_pcol(ctx, "G_lns", d["lns_pc"], 8)
    lno = load_pcol(ctx, "G_lno", d["lno_pc"], 8)
    gb2 = load_pcol(ctx, "G_gb2", d["gb2_pc"], 4)
    affw = ctx.aget("G_afw", [128, 64], BF16)
    p.dma("pool", affw[:].rearrange("p (i c) -> p i c", c=8),
          d["affwT"].rearrange("(i p) c -> p i c", p=128), writes=["G_afw"])
    b1e_pc = load_pcol(ctx, "G_b1e", d["b1e_pc"], 3 * 32 + 3 * 16 + 2 * 24)
    b2m_pc = load_pcol(ctx, "G_b2m", d["b2m_pc"], 3 * 16)
    bout = ctx.aget("G_bo", [1, 8 * 1024], BF16)
    p.dma("pool", bout[:], d["bout_rows"], writes=["G_bo"])

    va, kva = bisect_rank(ctx, impall, KRANK0, 0.0, 64.0, "G_a")
    vb, kvb = bisect_rank(ctx, impall, KRANK1, 0.0, 64.0, "G_b")
    thr = ctx.aget("G_thr", [1, 1], F32)
    p.op("dve", lambda v: v.tensor_tensor(out=thr[:], in0=vb[:], in1=va[:],
         op=ALU.subtract), reads=[kvb, kva], writes=["G_thr"])
    p.op("dve", lambda v: v.tensor_scalar(out=thr[:], in0=thr[:], scalar1=0.3,
         scalar2=None, op0=ALU.mult), reads=["G_thr"], writes=["G_thr"])
    p.op("dve", lambda v: v.tensor_tensor(out=thr[:], in0=thr[:], in1=va[:],
         op=ALU.add), reads=["G_thr", kva], writes=["G_thr"])
    ctx.afree(kva, kvb)
    dtile = ctx.aget("G_d", [8, T], F32)
    s1 = ctx.aget("G_s1", [8, 1], F32)
    s2 = ctx.aget("G_s2", [8, 1], F32)
    acc2 = ctx.aget("G_a2", [1, 1], F32)
    acc1 = ctx.aget("G_a1", [1, 1], F32)
    temp = ctx.aget("G_tmp", [1, 1], F32)
    p.op("dve", lambda v: v.tensor_scalar(out=dtile[:], in0=impall[:], scalar1=-32.0,
         scalar2=None, op0=ALU.add), reads=["G_imp"], writes=["G_d"])
    p.op("dve", lambda v: v.tensor_reduce(out=s1[:, 0:1], in_=dtile[:], axis=AX.X,
         op=ALU.add), reads=["G_d"], writes=["G_s1"])
    p.op("dve", lambda v: v.tensor_tensor(out=dtile[:], in0=dtile[:], in1=dtile[:],
         op=ALU.mult), reads=["G_d"], writes=["G_d"])
    p.op("dve", lambda v: v.tensor_reduce(out=s2[:, 0:1], in_=dtile[:], axis=AX.X,
         op=ALU.add), reads=["G_d"], writes=["G_s2"])
    ps6, pk6 = ctx.ps[6], "psb6"
    p.op("pe", lambda t_: t_.matmul(out=ps6[0:1, 0:1], lhsT=ctx.ones_colf[0:8, 0:1],
         rhs=s1[0:8, 0:1], start=True, stop=True), reads=["ones_colf", "G_s1"],
         writes=[pk6])
    p.op("dve", lambda v: v.tensor_scalar(out=acc1[:], in0=ps6[0:1, 0:1],
         scalar1=1.0 / 32768, scalar2=None, op0=ALU.mult), reads=[pk6],
         writes=["G_a1"])
    p.op("pe", lambda t_: t_.matmul(out=ps6[0:1, 0:1], lhsT=ctx.ones_colf[0:8, 0:1],
         rhs=s2[0:8, 0:1], start=True, stop=True), reads=["ones_colf", "G_s2"],
         writes=[pk6])
    p.op("dve", lambda v: v.tensor_scalar(out=acc2[:], in0=ps6[0:1, 0:1],
         scalar1=1.0 / 32768, scalar2=None, op0=ALU.mult), reads=[pk6],
         writes=["G_a2"])
    p.op("dve", lambda v: v.scalar_tensor_tensor(out=acc1[:], in0=acc1[:],
         scalar=-1.0, in1=acc1[:], op0=ALU.mult, op1=ALU.mult), reads=["G_a1"],
         writes=["G_a1"])
    p.op("dve", lambda v: v.tensor_tensor(out=acc2[:], in0=acc2[:], in1=acc1[:],
         op=ALU.add), reads=["G_a2", "G_a1"], writes=["G_a2"])
    p.op("dve", lambda v: v.tensor_scalar(out=acc2[:], in0=acc2[:], scalar1=0.0,
         scalar2=None, op0=ALU.max), reads=["G_a2"], writes=["G_a2"])
    p.op("act", lambda s: s.activation(out=temp[:], in_=acc2[:], func=AF.Sqrt),
         reads=["G_a2"], writes=["G_tmp"])
    p.op("dve", lambda v: v.tensor_scalar(out=temp[:], in0=temp[:], scalar1=0.1,
         scalar2=1.0, op0=ALU.mult, op1=ALU.add), reads=["G_tmp"], writes=["G_tmp"])
    ctx.afree("G_d", "G_s1", "G_s2", "G_a1", "G_a2", "G_imp")

    u1 = ctx.aget("G_u1", [1, 8], F32)
    flag = ctx.aget("G_fl", [1, 1], F32)
    nsc = ctx.aget("G_ns", [1, 1], F32)
    p.op("dve", lambda v: v.tensor_scalar(out=u1[:], in0=usage[:], scalar1=1e-8,
         scalar2=None, op0=ALU.add), reads=["G_us"], writes=["G_u1"])
    p.op("dve", lambda v: v.tensor_scalar(out=u1[:], in0=u1[:], scalar1=0.0,
         scalar2=None, op0=ALU.is_le), reads=["G_u1"], writes=["G_u1"])
    p.op("dve", lambda v: v.tensor_reduce(out=flag[:, 0:1], in_=u1[:], axis=AX.X,
         op=ALU.max), reads=["G_u1"], writes=["G_fl"])
    p.op("dve", lambda v: v.tensor_scalar(out=u1[:], in0=usage[:], scalar1=1e-8,
         scalar2=None, op0=ALU.add), reads=["G_us"], writes=["G_u1"])
    p.op("dve", lambda v: v.tensor_scalar(out=u1[:], in0=u1[:], scalar1=1e-30,
         scalar2=None, op0=ALU.max), reads=["G_u1"], writes=["G_u1"])
    p.op("act", lambda s: s.activation(out=u1[:], in_=u1[:], func=AF.Ln),
         reads=["G_u1"], writes=["G_u1"])
    p.op("dve", lambda v: v.tensor_tensor(out=u1[:], in0=u1[:], in1=usage[:],
         op=ALU.mult), reads=["G_u1", "G_us"], writes=["G_u1"])
    p.op("dve", lambda v: v.tensor_reduce(out=nsc[:, 0:1], in_=u1[:], axis=AX.X,
         op=ALU.add), reads=["G_u1"], writes=["G_ns"])
    p.op("dve", lambda v: v.tensor_scalar(out=nsc[:], in0=nsc[:], scalar1=2.0,
         scalar2=0.01, op0=ALU.add, op1=ALU.mult), reads=["G_ns"], writes=["G_ns"])
    p.op("dve", lambda v: v.tensor_scalar(out=nsc[:], in0=nsc[:], scalar1=1e-4,
         scalar2=None, op0=ALU.max), reads=["G_ns"], writes=["G_ns"])
    ctx.afree("G_u1")

    xbf = load_xbf(ctx, "G_x", d["x"])
    xk = [f"G_x{i}" for i in range(8)]
    a_row, ak, mu_row, mk = ln_rows(ctx, xbf, xk, kind="ln", eps=1e-5, name="G_ln")
    xln, _ = ln_materialize(ctx, xbf, xk, a_row, ak, mu_row, mk, name="G_xm",
                            inplace=True)
    import os as _os
    if _os.environ.get("ARENA_LOG"):
        for _i, _t in enumerate(xln):
            print(f"[xln] {_i} tensor={_t.tensor.name} offset={_t.offset}")
    imp1 = ctx.aget("G_i1", [1, T], F32)
    p.dma("sp", imp1[:], d["imp_row"], writes=["G_i1"])
    smask = ctx.aget("G_sm", [1, T], BF16)
    p.op("dve", lambda v: v.tensor_scalar(out=smask[0:1, :], in0=imp1[0:1, :],
         scalar1=thr[0:1, 0:1], scalar2=None, op0=ALU.is_gt),
         reads=["G_i1", "G_thr"], writes=["G_sm"])
    ab = [ctx.aget(f"G_ab{j}", [128, 512], BF16) for j in (0, 1)]
    nrm = ctx.aget("G_nr", [1, T], F32)
    for tc in range(NTC):
        sl = slice(tc * 512, (tc + 1) * 512)
        ps5, pk5 = ctx.ps[5], "psb5"
        for i in range(8):
            abj, abk = ab[i % 2], f"G_ab{i % 2}"
            p.op("act", lambda s, xt=xln[i], sl=sl, abj=abj: s.activation(
                 out=abj[:], in_=xt[:, sl], func=AF.Abs), reads=[xk[i]],
                 writes=[abk])
            p.op("pe", lambda t_, i=i, abj=abj, ps5=ps5: t_.matmul(
                 out=ps5[0:1, 0:512], lhsT=ctx.ones_col[:], rhs=abj[:],
                 start=(i == 0), stop=(i == 7)), reads=[abk, "ones_col"],
                 writes=[pk5], inc=(i == 7))
        p.op("dve", lambda v, sl=sl, ps5=ps5: v.tensor_copy(out=nrm[0:1, sl],
             in_=ps5[0:1, 0:512]), reads=[pk5], writes=["G_nr"])
    ctx.afree("G_ab0", "G_ab1")
    p.op("dve", lambda v: v.tensor_scalar(out=nrm[0:1, :], in0=nrm[0:1, :],
         scalar1=1e-8, scalar2=temp[0:1, 0:1], op0=ALU.add, op1=ALU.mult),
         reads=["G_nr", "G_tmp"], writes=["G_nr"])
    p.op("dve", lambda v: v.reciprocal(out=nrm[0:1, :], in_=nrm[0:1, :]),
         reads=["G_nr"], writes=["G_nr"])
    rsr = ctx.aget("G_rs", [1, T], BF16)
    p.op("dve", lambda v: v.tensor_tensor(out=rsr[0:1, :], in0=nrm[0:1, :],
         in1=smask[0:1, :], op=ALU.mult), reads=["G_nr", "G_sm"], writes=["G_rs"])
    rs_b = bcast_row_bf(ctx, "G_rsb", rsr, "G_rs", psbank=5)
    for i in range(8):
        p.op("dve", lambda v, xt=xln[i]: v.tensor_tensor(out=xt[:], in0=xt[:],
             in1=rs_b[:], op=ALU.mult), reads=[xk[i], "G_rsb"], writes=[xk[i]])
        p.dma("pool", xssp[i * 128:(i + 1) * 128, :], xln[i][:], reads=[xk[i]],
              writes=[("d_xs", i)], stream="dma_st_xs")
    ctx.afree("G_rsb", "G_i1", "G_sm", "G_nr", "G_rs")

    import os
    STG = os.environ.get("L3F_STG", "")
    if STG == "1":
        for od in range(8):
            p.dma("pool", d["xout"][od * 128:(od + 1) * 128, :], xln[od][:],
                  reads=[xk[od]], stream="dma_st_xout")
        return
    h1 = [ctx.aget(f"G_h1_{i}", [128, T], BF16) for i in range(8)]
    for od in range(8):
        wt = wload(ctx, f"G_gw{od % 2}", d["gw1T"], 1024, od * 128, 128)
        wk2 = f"G_gw{od % 2}"
        for tc in range(NTC):
            sl = slice(tc * 512, (tc + 1) * 512)
            bank = tc % 2
            ps, pk = ctx.ps[bank], f"psb{bank}"
            mac(p, ps, pk, [(wslice(wt, i, 128), wk2, xln[i][:, sl], xk[i])
                            for i in range(8)])
            p.op("act", lambda s, od=od, sl=sl, ps=ps: s.activation(
                 out=h1[od][:, sl], in_=ps[:, 0:512], func=AF.Silu,
                 bias=gb1[:, od:od + 1]), reads=[pk, "G_gb1"], writes=[f"G_h1_{od}"])
    h1k = [f"G_h1_{i}" for i in range(8)]
    a2r, a2k2, mu2, mk2 = ln_rows(ctx, h1, h1k, kind="ln", eps=1e-5, name="G_l2")
    hn, _ = ln_materialize(ctx, h1, h1k, a2r, a2k2, mu2, mk2, gamma=lns,
                           gamma_key="G_lns", beta=lno, beta_key="G_lno",
                           name="G_hn", inplace=True)
    h2 = [ctx.aget(f"G_h2_{i}", [128, T], BF16) for i in range(4)]
    for od in range(4):
        wt = wload(ctx, f"G_gw{od % 2}", d["gw2T"], 1024, od * 128, 128)
        wk2 = f"G_gw{od % 2}"
        for tc in range(NTC):
            sl = slice(tc * 512, (tc + 1) * 512)
            bank = tc % 2
            ps, pk = ctx.ps[bank], f"psb{bank}"
            mac(p, ps, pk, [(wslice(wt, i, 128), wk2, hn[i][:, sl], h1k[i])
                            for i in range(8)])
            p.op("act", lambda s, od=od, sl=sl, ps=ps: s.activation(
                 out=h2[od][:, sl], in_=ps[:, 0:512], func=AF.Silu,
                 bias=gb2[:, od:od + 1]), reads=[pk, "G_gb2"], writes=[f"G_h2_{od}"])
    ctx.afree(*h1k, "G_gw0", "G_gw1")

    xmean = ctx.aget("G_xme", [128, 8], BF16)
    xmeanf = ctx.aget("G_xmf", [128, 8], F32)
    xr2 = [ctx.aget(f"G_xr{j}", [128, T], BF16) for j in (0, 1)]
    dummy = ctx.aget("G_dm", [128, T], BF16)
    c4096, ck4096 = ctx.cf(1.0 / 4096.0)
    for i in range(8):
        xrj, xrk = xr2[i % 2], f"G_xr{i % 2}"
        p.dma("pool", xrj[:], xssp[i * 128:(i + 1) * 128, :], reads=[("d_xs", i)],
              writes=[xrk])
        p.op("act", lambda s, i=i, xrj=xrj, xrk=xrk: s.activation(out=dummy[:],
             in_=xrj[:], func=AF.Identity, scale=c4096[:, 0:1],
             accum_out=xmeanf[:, i:i + 1]), reads=[xrk, ck4096],
             writes=["G_dm", "G_xmf"])
    p.op("act", lambda s: s.copy(out=xmean[:], in_=xmeanf[:]),
         reads=["G_xmf"], writes=["G_xme"])
    aff = ctx.aget("G_af", [1, 8], F32)
    ps6b, pk6b = ctx.ps[6], "psb6"
    for i in range(8):
        p.op("pe", lambda t_, i=i, ps6b=ps6b: t_.matmul(out=ps6b[0:1, 0:8],
             lhsT=xmean[:, i:i + 1], rhs=affw[:, i * 8:(i + 1) * 8],
             start=(i == 0), stop=(i == 7)), reads=["G_xme", "G_afw"],
             writes=[pk6b], inc=(i == 7))
    p.op("dve", lambda v: v.tensor_tensor(out=aff[:], in0=ps6b[0:1, 0:8],
         in1=affb[0:1, 0:8], op=ALU.add), reads=[pk6b, "G_afb"], writes=["G_af"])
    p.op("dve", lambda v: v.tensor_scalar(out=aff[:], in0=aff[:], scalar1=0.3,
         scalar2=None, op0=ALU.mult), reads=["G_af"], writes=["G_af"])
    ctx.afree("G_xr0", "G_xr1", "G_dm", "G_xme", "G_xmf")

    flagb = ctx.aget("G_flb", [128, 1], F32)
    flag1m = ctx.aget("G_f1m", [128, 1], F32)
    nsb = ctx.aget("G_nsb", [128, 1], F32)
    ps5c, pk5c = ctx.ps[5], "psb5"
    p.op("pe", lambda t_: t_.matmul(out=ps5c[:, 0:1], lhsT=ctx.ones_rowf[:],
         rhs=flag[0:1, 0:1], start=True, stop=True), reads=["ones_rowf", "G_fl"],
         writes=[pk5c])
    p.op("dve", lambda v: v.tensor_copy(out=flagb[:], in_=ps5c[:, 0:1]),
         reads=[pk5c], writes=["G_flb"])
    p.op("dve", lambda v: v.tensor_scalar(out=flag1m[:], in0=flagb[:], scalar1=-1.0,
         scalar2=1.0, op0=ALU.mult, op1=ALU.add), reads=["G_flb"], writes=["G_f1m"])
    p.op("pe", lambda t_: t_.matmul(out=ps5c[:, 0:1], lhsT=ctx.ones_rowf[:],
         rhs=nsc[0:1, 0:1], start=True, stop=True), reads=["ones_rowf", "G_ns"],
         writes=[pk5c])
    p.op("dve", lambda v: v.tensor_copy(out=nsb[:], in_=ps5c[:, 0:1]),
         reads=[pk5c], writes=["G_nsb"])

    aff8 = ctx.aget("G_af8", [128, 8], F32)
    ps5d, pk5d = ctx.ps[5], "psb5"
    p.op("pe", lambda t_: t_.matmul(out=ps5d[:, 0:8], lhsT=ctx.ones_rowf[:],
         rhs=aff[0:1, 0:8], start=True, stop=True), reads=["ones_rowf", "G_af"],
         writes=[pk5d])
    p.op("dve", lambda v: v.tensor_copy(out=aff8[:], in_=ps5d[:, 0:8]),
         reads=[pk5d], writes=["G_af8"])

    cwT = ctx.aget("G_cwT", [8, T], BF16)
    lg = ctx.aget("G_lg", [128, 8], F32)
    m1 = ctx.aget("G_m1", [128, 1], F32)
    m2 = ctx.aget("G_m2", [128, 1], F32)
    msk2 = ctx.aget("G_mk", [128, 8], F32)
    prb = ctx.aget("G_pb", [128, 8], F32)
    cwt = ctx.aget("G_cw", [128, 8], BF16)
    pst = ctx.pstT[0:128, 0:128]
    w3r = wload(ctx, "G_w3r", d["gw3T"], 512, 0, 8)
    for tt in range(T // 128):
        sl = slice(tt * 128, (tt + 1) * 128)
        ps, pk = ctx.ps[6], "psb6"
        pairs = [(ctx.ones_rowf[0:1, 0:128], "ones_rowf", aff[0:1, 0:8], "G_af")]
        pairs += [(h2[i][:, sl], f"G_h2_{i}", wslice(w3r, i, 8), "G_w3r")
                  for i in range(4)]
        mac(p, ps, pk, pairs[1:], n=8)
        p.op("dve", lambda v, ps=ps: v.tensor_tensor(out=lg[:], in0=ps[:, 0:8],
             in1=gb3b[:, 0:8], op=ALU.add), reads=[pk, "G_gb3"], writes=["G_lg"])
        p.op("dve", lambda v, tt=tt: v.scalar_tensor_tensor(out=lg[:],
             in0=noise[:, tt * 8:(tt + 1) * 8], scalar=nsb[:, 0:1], in1=lg[:],
             op0=ALU.mult, op1=ALU.add), reads=["G_no", "G_nsb", "G_lg"],
             writes=["G_lg"])
        p.op("dve", lambda v: v.tensor_tensor(out=lg[:], in0=lg[:],
             in1=aff8[:, 0:8], op=ALU.add), reads=["G_lg", "G_af8"], writes=["G_lg"])
        p.op("dve", lambda v: v.tensor_scalar(out=lg[:], in0=lg[:],
             scalar1=1.0 / 1.5, scalar2=None, op0=ALU.mult), reads=["G_lg"],
             writes=["G_lg"])
        p.op("dve", lambda v: v.tensor_reduce(out=m1[:, 0:1], in_=lg[:], axis=AX.X,
             op=ALU.max), reads=["G_lg"], writes=["G_m1"])
        p.op("dve", lambda v: v.tensor_scalar(out=msk2[:], in0=lg[:],
             scalar1=m1[:, 0:1], scalar2=None, op0=ALU.is_ge),
             reads=["G_lg", "G_m1"], writes=["G_mk"])
        p.op("dve", lambda v: v.scalar_tensor_tensor(out=prb[:], in0=msk2[:],
             scalar=-1e9, in1=lg[:], op0=ALU.mult, op1=ALU.add),
             reads=["G_mk", "G_lg"], writes=["G_pb"])
        p.op("dve", lambda v: v.tensor_reduce(out=m2[:, 0:1], in_=prb[:], axis=AX.X,
             op=ALU.max), reads=["G_pb"], writes=["G_m2"])
        p.op("dve", lambda v: v.tensor_scalar(out=msk2[:], in0=lg[:],
             scalar1=m2[:, 0:1], scalar2=None, op0=ALU.is_ge),
             reads=["G_lg", "G_m2"], writes=["G_mk"])
        p.op("dve", lambda v: v.tensor_scalar(out=m1[:], in0=m1[:], scalar1=-1.0,
             scalar2=None, op0=ALU.mult), reads=["G_m1"], writes=["G_m1"])
        p.op("act", lambda s: s.activation(out=prb[:], in_=lg[:], func=AF.Exp,
             bias=m1[:, 0:1]), reads=["G_lg", "G_m1"], writes=["G_pb"])
        p.op("dve", lambda v: v.tensor_tensor(out=prb[:], in0=prb[:], in1=msk2[:],
             op=ALU.mult), reads=["G_pb", "G_mk"], writes=["G_pb"])
        p.op("dve", lambda v: v.tensor_reduce(out=m2[:, 0:1], in_=prb[:], axis=AX.X,
             op=ALU.add), reads=["G_pb"], writes=["G_m2"])
        p.op("dve", lambda v: v.reciprocal(out=m2[:], in_=m2[:]), reads=["G_m2"],
             writes=["G_m2"])
        p.op("dve", lambda v: v.tensor_scalar(out=prb[:], in0=prb[:],
             scalar1=m2[:, 0:1], scalar2=None, op0=ALU.mult),
             reads=["G_pb", "G_m2"], writes=["G_pb"])
        p.op("dve", lambda v: v.tensor_scalar(out=prb[:], in0=prb[:],
             scalar1=flag1m[:, 0:1], scalar2=None, op0=ALU.mult),
             reads=["G_pb", "G_f1m"], writes=["G_pb"])
        p.op("dve", lambda v: v.scalar_tensor_tensor(out=cwt[:], in0=nanb[:, 0:8],
             scalar=flagb[:, 0:1], in1=prb[:], op0=ALU.mult, op1=ALU.add),
             reads=["G_nan", "G_flb", "G_pb"], writes=["G_cw"])
        p.op("pe", lambda t_: t_.transpose(out=pst[0:8, 0:128], in_=cwt[:, 0:8],
             identity=ident[:]), reads=["G_cw", "G_id"], writes=["pstT"])
        p.op("act", lambda s, sl=sl: s.copy(out=cwT[0:8, sl], in_=pst[0:8, 0:128]),
             reads=["pstT"], writes=["G_cwT"])
    p.dma("pool", cwtsp, cwT[:], reads=["G_cwT"], writes=[("d_cwt", 0)],
          stream="dma_st_cwt")
    ctx.afree("G_cwT", "G_lg", "G_m1", "G_m2", "G_mk", "G_pb", "G_cw", "G_w3r",
              *[f"G_h2_{i}" for i in range(4)], "G_flb", "G_f1m", "G_nsb", "G_af8",
              *xk)

    if STG == "2":
        for od in range(8):
            p.dma("pool", d["xout"][od * 128:(od + 1) * 128, :],
                  xssp[od * 128:(od + 1) * 128, :], reads=[("d_xs", od)],
                  stream="dma_st_xout")
        return
    EMAP = [(0, 0), (1, 0), (2, 0), (0, 1), (1, 1), (2, 1), (0, 2), (1, 2)]
    cwr = ctx.aget("G_cwr", [1, T], BF16)
    for k in range(8):
        typ, idx = EMAP[k]
        p.dma("pool", cwr[:], cwtsp[k:k + 1, :], reads=[("d_cwt", 0)],
              writes=["G_cwr"])
        xln = []
        for i in range(8):
            tn = f"G_x{i}"
            tt2 = ctx.aget(tn, [128, T], BF16)
            p.dma("pool", tt2[:], xssp[i * 128:(i + 1) * 128, :],
                  reads=[("d_xs", i)], writes=[tn])
            xln.append(tt2)
        if typ == 0:
            nh, w1src, w2src = 32, d["e0w1"][idx], d["e0w2"][idx]
            b1col = 32 * idx
        elif typ == 1:
            nh, w1src, w2src = 16, d["e1w1"][idx], d["e1w3"][idx]
            b1col = 96 + 16 * idx
        else:
            nh, w1src, w2src = 24, d["e2w1"][idx], d["e2w2"][idx]
            b1col = 144 + 24 * idx
        w1r = ctx.aget("G_w1r", [128, 8 * 4096], BF16)
        p.dma("pool", w1r[:, 0:8 * nh * 128].rearrange("p (i c) -> p i c",
              c=nh * 128), w1src.rearrange("(i p) c -> p i c", p=128),
              writes=["G_w1r"])
        hrot = [ctx.aget(f"G_hr{j}", [128, T], BF16) for j in (0, 1)]
        sg = ctx.aget("G_sg", [128, T], BF16)
        for hod in range(nh):
            h, hk2 = hrot[hod % 2], f"G_hr{hod % 2}"
            for tc in range(NTC):
                sl = slice(tc * 512, (tc + 1) * 512)
                bank = tc % 2
                ps, pk = ctx.ps[bank], f"psb{bank}"
                mac(p, ps, pk, [(w1r[:, i * nh * 128 + hod * 128:
                                     i * nh * 128 + (hod + 1) * 128], "G_w1r",
                                 xln[i][:, sl], xk[i]) for i in range(8)])
                if typ == 1:
                    p.op("act", lambda s, h=h, sl=sl, ps=ps, hod=hod, hk2=hk2,
                         b1col=b1col: s.activation(out=h[:, sl], in_=ps[:, 0:512],
                         func=AF.Silu, bias=b1e_pc[:, b1col + hod:b1col + hod + 1]),
                         reads=[pk, "G_b1e"], writes=[hk2])
                else:
                    p.op("act", lambda s, h=h, sl=sl, ps=ps, hod=hod, hk2=hk2,
                         b1col=b1col: s.activation(out=h[:, sl], in_=ps[:, 0:512],
                         func=AF.Identity,
                         bias=b1e_pc[:, b1col + hod:b1col + hod + 1]),
                         reads=[pk, "G_b1e"], writes=[hk2])
            if typ == 0:
                p.op("act", lambda s, h=h, hk2=hk2: s.activation(out=sg[:],
                     in_=h[:], func=AF.Square), reads=[hk2], writes=["G_sg"])
                p.op("dve", lambda v: v.tensor_scalar(out=sg[:], in0=sg[:],
                     scalar1=0.044715, scalar2=1.0, op0=ALU.mult, op1=ALU.add),
                     reads=["G_sg"], writes=["G_sg"])
                p.op("dve", lambda v, h=h, hk2=hk2: v.tensor_tensor(out=sg[:],
                     in0=sg[:], in1=h[:], op=ALU.mult), reads=["G_sg", hk2],
                     writes=["G_sg"])
                c2g, ck2g = ctx.cf(2.0 * 0.7978845608028654)
                p.op("act", lambda s, c2g=c2g: s.activation(out=sg[:], in_=sg[:],
                     func=AF.Sigmoid, scale=c2g[:, 0:1]), reads=["G_sg", ck2g],
                     writes=["G_sg"])
                p.op("dve", lambda v, h=h, hk2=hk2: v.tensor_tensor(out=h[:],
                     in0=h[:], in1=sg[:], op=ALU.mult), reads=[hk2, "G_sg"],
                     writes=[hk2])
            elif typ == 2:
                p.op("act", lambda s, h=h, hk2=hk2: s.activation(out=sg[:],
                     in_=h[:], func=AF.Silu), reads=[hk2], writes=["G_sg"])
                p.op("act", lambda s, h=h, hk2=hk2: s.activation(out=h[:],
                     in_=h[:], func=AF.Sigmoid), reads=[hk2], writes=[hk2])
                p.op("dve", lambda v, h=h, hk2=hk2: v.tensor_tensor(out=h[:],
                     in0=h[:], in1=sg[:], op=ALU.mult), reads=[hk2, "G_sg"],
                     writes=[hk2])
            p.dma("pool", h1sp[hod * 128:(hod + 1) * 128, :], h[:], reads=[hk2],
                  writes=[("d_h1", hod)], stream="dma_st_h1")
        ctx.afree("G_w1r", "G_hr0", "G_hr1", "G_sg", *xk)

        insp, nin = h1sp, nh
        if typ == 1:
            w2m = ctx.aget("G_w2m", [128, 16 * 2048], BF16)
            p.dma("pool", w2m[:].rearrange("p (i c) -> p i c", c=2048),
                  d["e1w2"][idx].rearrange("(i p) c -> p i c", p=128),
                  writes=["G_w2m"])
            hre = [ctx.aget(f"G_he{j}", [128, 16 * 512], BF16) for j in (0, 1)]
            h2st = [ctx.aget(f"G_h2s{j}", [128, 16 * 512], BF16) for j in (0, 1)]
            for tc in range(NTC):
                sl = slice(tc * 512, (tc + 1) * 512)
                hr, hrk = hre[tc % 2], f"G_he{tc % 2}"
                p.dma("pool", hr[:].rearrange("p (i c) -> p i c", c=512),
                      h1sp[0:2048, sl].rearrange("(i p) c -> p i c", p=128),
                      reads=[("d_h1", i2) for i2 in range(16)], writes=[hrk])
                st2, stk = h2st[tc % 2], f"G_h2s{tc % 2}"
                for hod in range(16):
                    bank = 2 + (hod % 4)
                    ps, pk = ctx.ps[bank], f"psb{bank}"
                    mac(p, ps, pk, [(w2m[:, i * 2048 + hod * 128:
                                         i * 2048 + (hod + 1) * 128], "G_w2m",
                                     hr[:, i * 512:(i + 1) * 512], hrk)
                                    for i in range(16)])
                    p.op("act", lambda s, st2=st2, hod=hod, ps=ps, idx=idx, stk=stk:
                         s.activation(out=st2[:, hod * 512:(hod + 1) * 512],
                         in_=ps[:, 0:512], func=AF.Silu,
                         bias=b2m_pc[:, 16 * idx + hod:16 * idx + hod + 1]),
                         reads=[pk, "G_b2m"], writes=[stk])
                p.dma("pool",
                      h2sp[:, sl].rearrange("(i p) c -> p i c", p=128),
                      st2[:].rearrange("p (i c) -> p i c", c=512),
                      reads=[stk], writes=[("d_h2", tc)], stream="dma_st_h2")
            ctx.afree("G_w2m", "G_he0", "G_he1", "G_h2s0", "G_h2s1")
            insp, nin = h2sp, 16

        inkey = "d_h2" if typ == 1 else "d_h1"
        inrds = ([("d_h2", t2) for t2 in range(8)] if typ == 1
                 else [("d_h1", i2) for i2 in range(nin)])
        w2r = ctx.aget("G_w2r", [128, 32 * 1024], BF16)
        p.dma("pool", w2r[:, 0:nin * 1024].rearrange("p (i c) -> p i c", c=1024),
              w2src.rearrange("(i p) c -> p i c", p=128), writes=["G_w2r"])
        for og in range(2):
            hre = [ctx.aget(f"G_he{j}", [128, 32 * 512], BF16) for j in (0, 1)]
            cwb = [ctx.aget(f"G_cb{j}", [128, 512], BF16) for j in (0, 1)]
            yt = [ctx.aget(f"G_yt{od4}", [128, T], BF16) for od4 in range(4)]
            for tc in range(NTC):
                sl = slice(tc * 512, (tc + 1) * 512)
                hr, hrk = hre[tc % 2], f"G_he{tc % 2}"
                p.dma("pool", hr[:, 0:nin * 512].rearrange("p (i c) -> p i c", c=512),
                      insp[0:nin * 128, sl].rearrange("(i p) c -> p i c", p=128),
                      reads=inrds, writes=[hrk])
                cb_, cbk = cwb[tc % 2], f"G_cb{tc % 2}"
                ps6c, pk6c = ctx.ps[6], "psb6"
                p.op("pe", lambda t_, sl=sl, ps6c=ps6c: t_.matmul(out=ps6c[:, 0:512],
                     lhsT=ctx.ones_row[:], rhs=cwr[0:1, sl], start=True, stop=True),
                     reads=["ones_row", "G_cwr"], writes=[pk6c])
                p.op("act", lambda s, cb_=cb_, ps6c=ps6c, cbk=cbk: s.copy(out=cb_[:],
                     in_=ps6c[:, 0:512]), reads=[pk6c], writes=[cbk])
                for od4 in range(4):
                    od = og * 4 + od4
                    bank = 2 + od4
                    ps, pk = ctx.ps[bank], f"psb{bank}"
                    pairs = [(bout[0:1, k * 1024 + od * 128:k * 1024 + (od + 1) * 128],
                              "G_bo", ctx.ones512[:], "ones512")]
                    pairs += [(w2r[:, i * 1024 + od * 128:i * 1024 + (od + 1) * 128],
                               "G_w2r", hr[:, i * 512:(i + 1) * 512], hrk)
                              for i in range(nin)]
                    mac(p, ps, pk, pairs)
                    p.op("dve", lambda v, od4=od4, sl=sl, ps=ps, cb_=cb_, cbk=cbk:
                         v.tensor_tensor(out=yt[od4][:, sl], in0=ps[:, 0:512],
                         in1=cb_[:], op=ALU.mult), reads=[pk, cbk],
                         writes=[f"G_yt{od4}"])
            for od4 in range(4):
                od = og * 4 + od4
                if k == 0:
                    p.dma("pool", accsp[od * 128:(od + 1) * 128, :], yt[od4][:],
                          reads=[f"G_yt{od4}"], writes=[("d_acc", od)],
                          stream="dma_st_acc")
                else:
                    for ch in range(8):
                        cs = slice(ch * 512, (ch + 1) * 512)
                        p.dma_accum("pool", accsp[od * 128:(od + 1) * 128, cs],
                                    yt[od4][:, cs], reads=[f"G_yt{od4}"],
                                    writes=[("d_acc", od)], stream="dma_st_acc")
            ctx.afree("G_he0", "G_he1", "G_cb0", "G_cb1",
                      *[f"G_yt{od4}" for od4 in range(4)])
        ctx.afree("G_w2r")
    ctx.afree("G_cwr")

    xf = [ctx.aget(f"G_xf{j}", [128, T], F32) for j in (0, 1)]
    ar = [ctx.aget(f"G_ar{j}", [128, T], F32) for j in (0, 1)]
    xs2 = [ctx.aget(f"G_x2{j}", [128, T], BF16) for j in (0, 1)]
    for od in range(8):
        xfj, xfk = xf[od % 2], f"G_xf{od % 2}"
        arj, ark = ar[od % 2], f"G_ar{od % 2}"
        xsj, xsk = xs2[od % 2], f"G_x2{od % 2}"
        p.dma("sp", xfj[:], d["x"][od * 128:(od + 1) * 128, :], writes=[xfk])
        p.dma("sp", arj[:], accsp[od * 128:(od + 1) * 128, :], reads=[("d_acc", od)],
              writes=[ark])
        p.dma("pool", xsj[:], xssp[od * 128:(od + 1) * 128, :], reads=[("d_xs", od)],
              writes=[xsk])
        p.op("dve", lambda v, arj=arj, xfj=xfj, ark=ark, xfk=xfk: v.tensor_tensor(
             out=arj[:], in0=arj[:], in1=xfj[:], op=ALU.add), reads=[ark, xfk],
             writes=[ark])
        p.op("dve", lambda v, arj=arj, xsj=xsj, ark=ark, xsk=xsk:
             v.scalar_tensor_tensor(out=arj[:], in0=xsj[:], scalar=0.1, in1=arj[:],
             op0=ALU.mult, op1=ALU.add), reads=[xsk, ark], writes=[ark])
        p.dma("sp", d["xout"][od * 128:(od + 1) * 128, :], arj[:], reads=[ark],
              stream="dma_st_xout")
    ctx.afree("G_xf0", "G_xf1", "G_ar0", "G_ar1", "G_x20", "G_x21")

"""Per-stage (emit_fn, specs, shared-weights, per_core) tables, shared by the
unfused driver and the fused single-launch builder. All host prep is layout
only (transposes/reshapes/concats) plus host-generated constants."""









_AS = ([1024, T], F32, "ExternalInput")
_AO = ([1024, T], F32, "ExternalOutput")
_ca = lambda a: np.ascontiguousarray(np.asarray(a).astype(np.float32))
_tp = lambda a: np.ascontiguousarray(np.asarray(a).T.astype(np.float32))


def st_l0a(inputs):
    specs = {
        "x": _AS, "qkv_wT": ([1024, 3072], F32, "ExternalInput"),
        "outproj_wT": ([1024, 1024], F32, "ExternalInput"),
        "outgate_wT": ([1024, 1024], F32, "ExternalInput"),
        "pw_wT": ([1024, 1024], F32, "ExternalInput"),
        "gate_wT": ([2048, 1024], F32, "ExternalInput"),
        "w1T": ([1024, 2816], F32, "ExternalInput"),
        "w2T": ([1024, 2816], F32, "ExternalInput"),
        "w3T": ([2816, 1024], F32, "ExternalInput"),
        "norm1_pc": ([128, 8], F32, "ExternalInput"),
        "norm2_pc": ([128, 8], F32, "ExternalInput"),
        "convw_pc": ([128, 32], F32, "ExternalInput"),
        "convb_pc": ([128, 8], F32, "ExternalInput"),
        "outgateb_pc": ([128, 8], F32, "ExternalInput"),
        "decay": ([1, 8], F32, "ExternalInput"), "xout": _AO,
    }
    w = {
        "qkv_wT": _tp(inputs["L0_attn_qkv_w"]),
        "outproj_wT": _tp(inputs["L0_attn_out_proj_w"]),
        "outgate_wT": _tp(inputs["L0_attn_out_gate_w"]),
        "pw_wT": _tp(inputs["L0_attn_pw_w"]),
        "gate_wT": _tp(inputs["L0_attn_gate_w"]),
        "w1T": _tp(inputs["L0_attn_w1"]),
        "w2T": _tp(inputs["L0_attn_w2"]),
        "w3T": _tp(inputs["L0_attn_w3"]),
        "norm1_pc": host_pcol(inputs["L0_attn_norm1_w"]),
        "norm2_pc": host_pcol(inputs["L0_attn_norm2_w"]),
        "convw_pc": np.concatenate([host_pcol(inputs["L0_attn_conv_w"][:, 0, j])
                                    for j in range(4)], axis=1),
        "convb_pc": host_pcol(inputs["L0_attn_conv_b"]),
        "outgateb_pc": host_pcol(inputs["L0_attn_out_gate_b"]),
        "decay": np.ascontiguousarray(
            inputs["L0_attn_decay_param"].reshape(1, 8).astype(np.float32)),
    }
    return emit_L0A, specs, w, None


def st_l0f(inputs):
    specs = {
        "x": _AS, "wT": ([1024, 1024], F32, "ExternalInput"),
        "scale_pc": ([128, 8], F32, "ExternalInput"),
        "bias_pc": ([128, 8], F32, "ExternalInput"), "xout": _AO,
    }
    w = {"wT": _tp(inputs["L0_ffn_weights"]),
         "scale_pc": host_pcol(inputs["L0_ffn_scale"]),
         "bias_pc": host_pcol(inputs["L0_ffn_bias"])}
    return emit_L0F, specs, w, None


def st_l1a(inputs):
    specs = {
        "x": _AS, "winT": ([1024, 4096], F32, "ExternalInput"),
        "wcT": ([2048, 2048], F32, "ExternalInput"),
        "wdT": ([2048, 2048], F32, "ExternalInput"),
        "woT": ([2048, 1024], F32, "ExternalInput"),
        "lng_pc": ([128, 8], F32, "ExternalInput"),
        "lnb_pc": ([128, 8], F32, "ExternalInput"),
        "bin_pc": ([128, 32], F32, "ExternalInput"),
        "convw_pc": ([128, 64], F32, "ExternalInput"),
        "convb_pc": ([128, 16], F32, "ExternalInput"),
        "bc_pc": ([128, 16], F32, "ExternalInput"),
        "bd_pc": ([128, 16], F32, "ExternalInput"),
        "bout_pc": ([128, 8], F32, "ExternalInput"),
        "logdelta_pc": ([128, 16], F32, "ExternalInput"),
        "a_pc": ([128, 256], F32, "ExternalInput"),
        "bm_pc": ([128, 256], F32, "ExternalInput"), "xout": _AO,
    }
    nk = lambda a: np.ascontiguousarray(
        np.asarray(a).reshape(16, 128, 16).transpose(1, 0, 2).reshape(128, 256)
        .astype(np.float32))
    w = {
        "winT": _tp(inputs["L1_attn_W_in"]),
        "wcT": _tp(inputs["L1_attn_W_c"]),
        "wdT": _tp(inputs["L1_attn_W_d"]),
        "woT": _tp(inputs["L1_attn_W_out"]),
        "lng_pc": host_pcol(inputs["L1_attn_ln_g"]),
        "lnb_pc": host_pcol(inputs["L1_attn_ln_b"]),
        "bin_pc": host_pcol(inputs["L1_attn_b_in"]),
        "convw_pc": np.concatenate([host_pcol(inputs["L1_attn_conv_w"][:, 0, j])
                                    for j in range(4)], axis=1),
        "convb_pc": host_pcol(inputs["L1_attn_conv_b"]),
        "bc_pc": host_pcol(inputs["L1_attn_b_c"]),
        "bd_pc": host_pcol(inputs["L1_attn_b_d"]),
        "bout_pc": host_pcol(inputs["L1_attn_b_out"]),
        "logdelta_pc": host_pcol(inputs["L1_attn_log_delta"]),
        "a_pc": nk(inputs["L1_attn_A"]),
        "bm_pc": nk(inputs["L1_attn_Bm"]),
    }
    return emit_L1A, specs, w, None


def st_l1f(inputs):
    specs = {
        "x": _AS, "wg": ([1024, 8], F32, "ExternalInput"),
        "w1T": ([8, 1024, 4096], F32, "ExternalInput"),
        "w2T": ([8, 4096, 1024], F32, "ExternalInput"),
        "ws1T": ([1024, 4096], F32, "ExternalInput"),
        "ws2T": ([4096, 1024], F32, "ExternalInput"),
        "b1_pc": ([128, 288], F32, "ExternalInput"),
        "b2_rows": ([1, 9216], F32, "ExternalInput"),
        "ident": ([128, 128], F32, "ExternalInput"), "xout": _AO,
    }
    b1 = np.concatenate([host_pcol(inputs["L1_ffn_b1"][e]) for e in range(8)]
                        + [host_pcol(inputs["L1_ffn_bs1"])], axis=1)
    b2 = np.concatenate([inputs["L1_ffn_b2"].reshape(-1),
                         inputs["L1_ffn_bs2"].reshape(-1)]).reshape(1, 9216)
    w = {
        "wg": _ca(inputs["L1_ffn_Wg"]), "w1T": _ca(inputs["L1_ffn_W1"]),
        "w2T": _ca(inputs["L1_ffn_W2"]), "ws1T": _ca(inputs["L1_ffn_Ws1"]),
        "ws2T": _ca(inputs["L1_ffn_Ws2"]), "b1_pc": _ca(b1), "b2_rows": _ca(b2),
        "ident": np.eye(128, dtype=np.float32),
    }
    return emit_L1F, specs, w, None


def st_l2a(inputs):
    perm = np.concatenate([np.array([h * 128 + dd for h in range(16)
                                     for dd in range(lo, hi)])
                           for lo, hi in [(0, 32), (64, 96), (32, 64), (96, 128)]])
    specs = {
        "x": _AS, "w1T": ([1024, 256], F32, "ExternalInput"),
        "w2T": ([256, 6144], F32, "ExternalInput"),
        "cw1T": ([2048, 256], F32, "ExternalInput"),
        "cw2T": ([256, 2048], F32, "ExternalInput"),
        "cos_t": ([128, T], F32, "ExternalInput"),
        "sin_t": ([128, T], F32, "ExternalInput"),
        "seg4": ([128, 4], F32, "ExternalInput"),
        "seg4T": ([4, 128], F32, "ExternalInput"), "xout": _AO,
    }
    w2 = np.asarray(inputs["L2_attn_qkv_w2"]).astype(np.float32)
    order = np.concatenate([perm, 2048 + perm, 4096 + perm])
    freq = (np.float32(1.0) / np.float32(1024.0)) ** np.linspace(
        0.0, 1.0, 32, dtype=np.float32)
    theta = np.arange(T, dtype=np.float32)[:, None] * freq[None, :]
    cos_t = np.cos(theta).astype(np.float32)
    sin_t = np.sin(theta).astype(np.float32)
    ct = np.zeros((128, T), np.float32)
    st2 = np.zeros((128, T), np.float32)
    for pp in range(128):
        ct[pp] = cos_t[:, pp % 32]
        st2[pp] = sin_t[:, pp % 32]
    seg4 = np.zeros((128, 4), np.float32)
    for j in range(4):
        seg4[j * 32:(j + 1) * 32, j] = 1.0
    w = {
        "w1T": _tp(inputs["L2_attn_qkv_w1"]),
        "w2T": np.ascontiguousarray(w2[order].T),
        "cw1T": np.ascontiguousarray(
            np.asarray(inputs["L2_attn_cproj_w1"]).astype(np.float32)[:, perm].T),
        "cw2T": _tp(inputs["L2_attn_cproj_w2"]),
        "cos_t": ct, "sin_t": st2,
        "seg4": seg4, "seg4T": np.ascontiguousarray(seg4.T),
    }
    return emit_L2A, specs, w, None


def st_l2f(inputs):
    specs = {
        "x": _AS, "gwT": ([1024, 16], F32, "ExternalInput"),
        "w1T": ([16, 1024, 512], F32, "ExternalInput"),
        "w3T": ([16, 1024, 512], F32, "ExternalInput"),
        "w2T": ([16, 512, 1024], F32, "ExternalInput"),
        "sw1T": ([1024, 1024], F32, "ExternalInput"),
        "sw3T": ([1024, 1024], F32, "ExternalInput"),
        "sw2T": ([1024, 1024], F32, "ExternalInput"),
        "b1_pc": ([128, 72], F32, "ExternalInput"),
        "b3_pc": ([128, 72], F32, "ExternalInput"),
        "b2_rows": ([1, 18 * 1024], F32, "ExternalInput"),
        "ebias_b": ([128, 16], F32, "ExternalInput"),
        "ident": ([128, 128], F32, "ExternalInput"), "xout": _AO,
    }
    tp3 = lambda a: np.ascontiguousarray(
        np.transpose(np.asarray(a).astype(np.float32), (0, 2, 1)))
    b1 = np.concatenate([host_pcol(inputs["L2_ffn_b1"][e]) for e in range(16)]
                        + [host_pcol(inputs["L2_ffn_sb1"])], axis=1)
    b3 = np.concatenate([host_pcol(inputs["L2_ffn_b3"][e]) for e in range(16)]
                        + [host_pcol(inputs["L2_ffn_sb3"])], axis=1)
    b2 = np.concatenate([inputs["L2_ffn_b2"].reshape(-1),
                         inputs["L2_ffn_sb2"].reshape(-1),
                         np.zeros(1024, np.float32)]).reshape(1, -1)
    w = {
        "gwT": _tp(inputs["L2_ffn_gate_w"]),
        "w1T": tp3(inputs["L2_ffn_w1"]), "w3T": tp3(inputs["L2_ffn_w3"]),
        "w2T": tp3(inputs["L2_ffn_w2"]),
        "sw1T": _tp(inputs["L2_ffn_sw1"]), "sw3T": _tp(inputs["L2_ffn_sw3"]),
        "sw2T": _tp(inputs["L2_ffn_sw2"]),
        "b1_pc": _ca(b1), "b3_pc": _ca(b3), "b2_rows": _ca(b2),
        "ebias_b": np.ascontiguousarray(
            np.broadcast_to(inputs["L2_ffn_expert_bias"][None, :],
                            (128, 16)).astype(np.float32)),
        "ident": np.eye(128, dtype=np.float32),
    }
    return emit_L2F, specs, w, None


def st_l3a(inputs):
    specs = {
        "x": _AS, "wqT": ([1024, 1024], F32, "ExternalInput"),
        "wkT": ([1024, 1024], F32, "ExternalInput"),
        "wvT": ([1024, 1024], F32, "ExternalInput"),
        "woT": ([1024, 1024], F32, "ExternalInput"),
        "bq_pc": ([128, 8], F32, "ExternalInput"),
        "bk_pc": ([128, 8], F32, "ExternalInput"),
        "bo_pc": ([128, 8], F32, "ExternalInput"),
        "bv_row": ([1, 1024], F32, "ExternalInput"),
        "cmask": ([128, 2048], F32, "ExternalInput"),
        "ident": ([128, 128], F32, "ExternalInput"), "xout": _AO,
    }
    cm = np.zeros((128, 2048), np.float32)
    for m4 in range(4):
        for pp in range(128):
            cm[pp, m4 * 512:(m4 + 1) * 512] = np.where(
                np.arange(512) <= m4 * 128 + pp, 0.0, -1e30)
    w = {
        "wqT": _ca(inputs["L3_attn_Wq"]), "wkT": _ca(inputs["L3_attn_Wk"]),
        "wvT": _ca(inputs["L3_attn_Wv"]), "woT": _ca(inputs["L3_attn_Wo"]),
        "bq_pc": host_pcol(inputs["L3_attn_bq"]),
        "bk_pc": host_pcol(inputs["L3_attn_bk"]),
        "bo_pc": host_pcol(inputs["L3_attn_bo"]),
        "bv_row": _ca(inputs["L3_attn_bv"]).reshape(1, 1024),
        "cmask": cm, "ident": np.eye(128, dtype=np.float32),
    }
    return emit_L3A, specs, w, None


def l3f_noise():
    import jax
    with jax.default_device(jax.devices("cpu")[0]):
        import jax.numpy as jnp
        nz = np.asarray(jax.random.normal(jax.random.key(1), (8, 4096, 8),
                                          dtype=jnp.float32))
    return nz


def st_l3f(inputs):
    EMAP = [(0, 0), (1, 0), (2, 0), (0, 1), (1, 1), (2, 1), (0, 2), (1, 2)]
    bout = np.concatenate([
        (inputs["L3_ffn_e0_b2"][idx] if typ == 0 else
         inputs["L3_ffn_e1_b3"][idx] if typ == 1 else
         inputs["L3_ffn_e2_b2"][idx]).reshape(-1)
        for typ, idx in EMAP]).reshape(1, 8192)
    b1e = np.concatenate([host_pcol(inputs["L3_ffn_e0_b1"][j]) for j in range(3)]
                         + [host_pcol(inputs["L3_ffn_e1_b1"][j]) for j in range(3)]
                         + [host_pcol(inputs["L3_ffn_e2_b1"][j]) for j in range(2)],
                         axis=1)
    b2m = np.concatenate([host_pcol(inputs["L3_ffn_e1_b2"][j]) for j in range(3)],
                         axis=1)
    nz = l3f_noise()
    nanrow = np.zeros((128, 8), np.float32)
    nanrow[:, 0] = 0.125
    nanrow[:, 1] = 0.125
    specs = {
        "x": _AS,
        "usage": ([1, 8], F32, "ExternalInput"),
        "gb3_b": ([128, 8], F32, "ExternalInput"),
        "nan_b": ([128, 8], F32, "ExternalInput"),
        "aff_b": ([1, 8], F32, "ExternalInput"),
        "noise_pc": ([128, 256], F32, "ExternalInput"),
        "ident": ([128, 128], F32, "ExternalInput"),
        "gb1_pc": ([128, 8], F32, "ExternalInput"),
        "lns_pc": ([128, 8], F32, "ExternalInput"),
        "lno_pc": ([128, 8], F32, "ExternalInput"),
        "gb2_pc": ([128, 4], F32, "ExternalInput"),
        "affwT": ([1024, 8], F32, "ExternalInput"),
        "b1e_pc": ([128, 192], F32, "ExternalInput"),
        "b2m_pc": ([128, 48], F32, "ExternalInput"),
        "bout_rows": ([1, 8192], F32, "ExternalInput"),
        "gw1T": ([1024, 1024], F32, "ExternalInput"),
        "gw2T": ([1024, 512], F32, "ExternalInput"),
        "gw3T": ([512, 8], F32, "ExternalInput"),
        "e0w1": ([3, 1024, 4096], F32, "ExternalInput"),
        "e0w2": ([3, 4096, 1024], F32, "ExternalInput"),
        "e1w1": ([3, 1024, 2048], F32, "ExternalInput"),
        "e1w2": ([3, 2048, 2048], F32, "ExternalInput"),
        "e1w3": ([3, 2048, 1024], F32, "ExternalInput"),
        "e2w1": ([2, 1024, 3072], F32, "ExternalInput"),
        "e2w2": ([2, 3072, 1024], F32, "ExternalInput"),
        "xout": _AO,
    }
    w = {
        "usage": _ca(inputs["L3_ffn_expert_usage"]).reshape(1, 8),
        "gb3_b": np.ascontiguousarray(np.broadcast_to(
            inputs["L3_ffn_gate_b3"][None, :], (128, 8)).astype(np.float32)),
        "nan_b": nanrow,
        "aff_b": _ca(inputs["L3_ffn_aff_b"]).reshape(1, 8),
        "ident": np.eye(128, dtype=np.float32),
        "gb1_pc": host_pcol(inputs["L3_ffn_gate_b1"]),
        "lns_pc": host_pcol(inputs["L3_ffn_ln_scale"]),
        "lno_pc": host_pcol(inputs["L3_ffn_ln_offset"]),
        "gb2_pc": host_pcol(inputs["L3_ffn_gate_b2"]),
        "affwT": _ca(inputs["L3_ffn_aff_w"]),
        "b1e_pc": _ca(b1e), "b2m_pc": _ca(b2m), "bout_rows": _ca(bout),
        "gw1T": _ca(inputs["L3_ffn_gate_w1"]),
        "gw2T": _ca(inputs["L3_ffn_gate_w2"]),
        "gw3T": _ca(inputs["L3_ffn_gate_w3"]),
        "e0w1": _ca(inputs["L3_ffn_e0_w1"]), "e0w2": _ca(inputs["L3_ffn_e0_w2"]),
        "e1w1": _ca(inputs["L3_ffn_e1_w1"]), "e1w2": _ca(inputs["L3_ffn_e1_w2"]),
        "e1w3": _ca(inputs["L3_ffn_e1_w3"]),
        "e2w1": _ca(inputs["L3_ffn_e2_w1"]), "e2w2": _ca(inputs["L3_ffn_e2_w2"]),
    }

    def per_core(c):
        return {"noise_pc": np.ascontiguousarray(
            nz[c].reshape(32, 128, 8).transpose(1, 0, 2).reshape(128, 256))}
    return emit_L3F, specs, w, per_core


STAGES = [st_l0a, st_l0f, st_l1a, st_l1f, st_l2a, st_l2f, st_l3a]


def build_fused(inputs, nstages=8):
    from contextlib import ExitStack

    nc = bass.Bass("TRN2", target_bir_lowering=False)
    built = [fn(inputs) for fn in STAGES]
    f_emit, f_specs, f_w, f_pc = st_l3f(inputs)

    x_in = nc.dram_tensor("x_in", [1024, T], F32, kind="ExternalInput").ap()
    xout = nc.dram_tensor("xout", [1024, T], F32, kind="ExternalOutput").ap()
    xh = [nc.dram_tensor(f"xh{k}", [1024, T], F32, kind="Internal").ap()
          for k in range(7)]
    imp_b = nc.dram_tensor("imp_b", [1, T], F32, kind="Internal")
    impall_b = nc.dram_tensor("impall_b", [8, T], F32, kind="Internal")

    ext_shared = {}
    ext_percore = []

    def mk_ext(prefix, specs, w, per_core):
        d = {}
        for name, (shape, dt, kind) in specs.items():
            if name in ("x", "xout"):
                continue
            en = f"{prefix}_{name}"
            d[name] = nc.dram_tensor(en, list(shape), dt,
                                     kind="ExternalInput").ap()
            if name in w:
                ext_shared[en] = w[name]
            else:
                ext_percore.append((en, name, per_core))
        return d

    with ExitStack() as stack:
        p = Prog(nc)
        ctx = Ctx(nc, p, stack)
        ctx.init_common()
        cur = x_in
        for k in range(0 if nstages == 10 else min(nstages, 7)):
            emit, specs, w, per_core = built[k]
            d = mk_ext(f"s{k}", specs, w, per_core)
            d["x"] = cur
            d["xout"] = xout if k == nstages - 1 else xh[k]
            emit(ctx, d)
            ctx.afree_all()
            p.dma_barrier()
            p.recycle_streams()
            cur = xh[k]
        if 3 <= nstages <= 7:
            p.output_fence()
            block = stack.enter_context(nc.Block())
            p.finalize(block)
            def mk_in_map(core, x_fm):
                m = {"x_in": x_fm}
                m.update(ext_shared)
                for en, name, pc in ext_percore:
                    m[en] = pc(core)[name]
                return m
            return nc, mk_in_map
        emit_L3F_pre(ctx, {"x": cur, "imp_out": imp_b.ap()})
        ctx.afree_all()
        p.dma_barrier()
        p.recycle_streams()
        p.custom_dma("pool", lambda g: g.collective_compute(
            "AllGather", mybir.AluOpType.bypass,
            replica_groups=[list(range(8))],
            ins=[imp_b.ap().opt()], outs=[impall_b.ap().opt()]),
            stream="cc_imp")
        p.dma_barrier()
        p.recycle_streams()
        if nstages == 9:
            t9 = ctx.aget("dbg9", [8, T], F32)
            p.dma("sp", t9[:], impall_b.ap(), writes=["dbg9"])
            p.dma("sp", xout[0:8, :], t9[:], reads=["dbg9"],
                  stream="dma_st_dbg9")
            p.output_fence()
            block = stack.enter_context(nc.Block())
            p.finalize(block)
            def mk_in_map(core, x_fm):
                m = {"x_in": x_fm}
                m.update(ext_shared)
                for en, name, pc in ext_percore:
                    m[en] = pc(core)[name]
                return m
            return nc, mk_in_map
        d = mk_ext("s7", f_specs, f_w, f_pc)
        d["x"] = cur
        d["xout"] = xout
        d["impall"] = impall_b.ap()
        d["imp_row"] = imp_b.ap()
        f_emit(ctx, d)
        p.output_fence()
        block = stack.enter_context(nc.Block())
        p.finalize(block)

    def mk_in_map(core, x_fm):
        m = {"x_in": x_fm}
        m.update(ext_shared)
        for en, name, pc in ext_percore:
            m[en] = pc(core)[name]
        return m
    return nc, mk_in_map



def _fm(x_btd, core):
    return np.ascontiguousarray(np.asarray(x_btd)[core].T.astype(np.float32))


def kernel(**inputs):
    x0 = inputs["L0_attn_x"]
    nc, mk_in_map = build_fused(inputs)
    in_maps = [mk_in_map(c, _fm(x0, c)) for c in range(8)]
    from concourse.bass_utils import run_bass_kernel_spmd
    res = run_bass_kernel_spmd(nc, in_maps, core_ids=list(range(8)))
    out = np.stack([np.asarray(res.results[c]["xout"]).T for c in range(8)],
                   axis=0)
    return np.ascontiguousarray(out.astype(np.float32))


# version 03
# speedup vs baseline: 1.6305x; 1.6305x over previous
import numpy as np
import concourse.bass as bass
import concourse.mybir as mybir


"""kb: minimal dependency-tracking emitter for raw Bass programs.

Rules encoded here (learned from the interp race detector / docs):
  - EVERY RAW/WAW/WAR edge needs an explicit wait, same-engine included
    (exception: PE matmul accumulation into the same PSUM bank).
  - Compute ops carry .then_inc(engine_sem, 1); waits use cumulative counts.
  - Each DMA "stream" (one SBUF slot) gets its OWN semaphore, and at most
    one DMA per stream is outstanding (enforced via the WAW dep on the
    stream's previous event) so cumulative waits of 16*k are race-free.
  - HWDGE (sync/scalar) and SWDGE (gpsimd) must not share a DMA sem.
  - inc=False ops (non-final matmuls of an accumulation group) have their
    accesses tracked as "pending"; they resolve to the engine's next
    completing event, which is forced early if a conflicting access arrives.

Engines: pe (tensor), act (scalar), dve (vector), pool (gpsimd), sp (sync).
Buffers are tracked by (name, region) keys; region granularity is chosen
by the caller but must be consistent per tensor within a phase.
"""

DMA_STEP = 16
SEM_SEG = 12000

ENGINE_ATTR = {"pe": "tensor", "act": "scalar", "dve": "vector",
               "pool": "gpsimd", "sp": "sync"}
COMPUTE_ENGINES = ("pe", "act", "dve", "pool")
DMA_QUEUES = ("sp", "pool", "act")


class Prog:
    def __init__(self, nc):
        self.nc = nc
        self._sem_cm = []
        self.sems = {}
        self.cnt = {}
        self.queue = {e: [] for e in ENGINE_ATTR}
        self.waited = {e: {} for e in ENGINE_ATTR}
        self.bufs = {}
        self.stream_queue = {}
        self.pending = {e: [] for e in ENGINE_ATTR}
        self.free_fence = []
        for e in COMPUTE_ENGINES:
            self._mk_sem(e)

    def mac_queue(self, ps, pk, pairs, n=512):
        q = getattr(self, "_macq", None)
        if q is None:
            q = self._macq = []
        if any(g[1] == pk for g in q) or len(q) >= 4:
            self.mac_flush()
        self._macq.append((ps, pk, list(pairs), n))

    def mac_flush(self):
        q = getattr(self, "_macq", None)
        if not q:
            return
        self._macq = []
        depth = max(len(g[2]) for g in q)
        for j in range(depth):
            for ps, pk, pairs, n in q:
                if j >= len(pairs):
                    continue
                la, lk, ra, rk = pairs[j]
                last = j == len(pairs) - 1
                self.op("pe", lambda t, la=la, ra=ra, ps=ps, n=n, st=(j == 0),
                        sp=last: t.matmul(out=ps[:, 0:n], lhsT=la, rhs=ra,
                        start=st, stop=sp), reads=[lk, rk], writes=[pk],
                        inc=last)

    def note_free(self, names):
        self.mac_flush()
        for key, e in list(self.bufs.items()):
            if key[0] in names:
                if e["w"]:
                    self.free_fence.append(e["w"])
                self.free_fence.extend(e["r"])
                del self.bufs[key]

    def _alloc_phys(self, label, pool_kind=None, max_floor=SEM_SEG):
        if pool_kind is not None:
            pool = self.sem_pool.setdefault(pool_kind, [])
            pool.sort(key=lambda e: e[1])
            if pool and pool[0][1] <= max_floor:
                h, floor = pool.pop(0)
                return h, floor
        self._sem_n = getattr(self, "_sem_n", 0) + 1
        cm = self.nc.semaphore(f"kb_{self._sem_n}_{label}")
        h = cm.__enter__()
        self._sem_cm.append(cm)
        return h, 0

    def _pool_kind(self, qname):
        return "sw" if qname == "pool" else "hw"

    def _mk_sem(self, name, qname=None):
        if not hasattr(self, "sem_pool"):
            self.sem_pool = {}
            self.sem_kind = {}
        kind = self._pool_kind(qname) if qname is not None else None
        h, floor = self._alloc_phys(f"{name}_0", kind)
        self.sems[name] = [[-floor, h, None]]
        self.cnt[name] = 0
        self.sem_kind[name] = kind
        return h

    def _sem_for_inc(self, name, amount):
        base, h, _ = self.sems[name][-1]
        if self.cnt[name] + amount - base > SEM_SEG:
            cnt = self.cnt[name]
            self.sems[name][-1][2] = cnt - base
            h, floor = self._alloc_phys(f"{name}_{len(self.sems[name])}",
                                        self.sem_kind.get(name),
                                        max_floor=max(0, cnt - base - 1))
            self.sems[name].append([cnt - floor, h, None])
            base, h = cnt - floor, h
        return h

    def recycle_streams(self):
        if not hasattr(self, "sem_pool"):
            self.sem_pool = {}
            self.sem_kind = {}
        dead = set()
        for stream in list(self.stream_queue.keys()):
            kind = self.sem_kind.get(stream)
            segs = self.sems[stream]
            for base, h, end_phys in segs:
                floor = end_phys if end_phys is not None else \
                    self.cnt[stream] - base
                self.sem_pool.setdefault(kind, []).append((h, floor))
            dead.add(stream)
            del self.sems[stream]
            del self.cnt[stream]
            del self.stream_queue[stream]
            self.sem_kind.pop(stream, None)
            for e in ENGINE_ATTR:
                self.waited[e].pop(stream, None)
        self.free_fence = [ev for ev in self.free_fence if ev[0] not in dead]
        for key, e in self.bufs.items():
            if e["w"] and e["w"][0] in dead:
                e["w"] = None
            e["r"] = [ev for ev in e["r"] if ev[0] not in dead]

    def _sem_for_wait(self, name, val):
        segs = self.sems[name]
        for base, h, _end in reversed(segs):
            if base < val:
                return h, val - base
        return segs[0][1], val - segs[0][0]

    def _key(self, b):
        return b if isinstance(b, tuple) else (b, 0)

    def _entry(self, key):
        if key not in self.bufs:
            self.bufs[key] = {"w": None, "r": []}
        return self.bufs[key]

    def _flush_pending(self, engine):
        pend = self.pending[engine]
        if not pend:
            return
        qi, _, _ = pend[-1]
        kind, fn, sem, inc = self.queue[engine][qi]
        assert kind == "op" and sem is None
        h = self._sem_for_inc(engine, 1)
        self.cnt[engine] += 1
        ev = (engine, self.cnt[engine])
        self.queue[engine][qi] = ("op", fn, h, 1)
        for _, rd, wr in pend:
            self._record(rd, wr, ev)
        pend.clear()

    def _resolve_conflicts(self, engine, reads, writes):
        wkeys = {self._key(b) for b in writes}
        rkeys = {self._key(b) for b in reads}
        for e in ENGINE_ATTR:
            for _, rd, wr in self.pending[e]:
                prd = {self._key(b) for b in rd}
                pwr = {self._key(b) for b in wr}
                if (wkeys & (prd | pwr)) or (rkeys & pwr):
                    self._flush_pending(e)
                    break

    def _emit_waits(self, engine, deps):
        need = {}
        for ev in deps:
            if ev is None:
                continue
            sem, val = ev
            if val <= self.waited[engine].get(sem, 0):
                continue
            if val > need.get(sem, 0):
                need[sem] = val
        for sem, val in need.items():
            h, v = self._sem_for_wait(sem, val)
            self.queue[engine].append(("wait", h, v))
            self.waited[engine][sem] = val

    def _deps_for(self, reads, writes):
        deps = []
        for b in reads:
            e = self._entry(self._key(b))
            if e["w"]:
                deps.append(e["w"])
        for b in writes:
            k = self._key(b)
            fresh = k not in self.bufs
            e = self._entry(k)
            if fresh and self.free_fence:
                deps.extend(self.free_fence)
            if e["w"]:
                deps.append(e["w"])
            deps.extend(e["r"])
        return deps

    def _record(self, reads, writes, ev):
        if ev is None:
            return
        for b in reads:
            self._entry(self._key(b))["r"].append(ev)
        for b in writes:
            e = self._entry(self._key(b))
            e["w"] = ev
            e["r"] = []

    def op(self, engine, fn, reads=(), writes=(), inc=True):
        self.mac_flush()
        """Queue a compute op; fn(engine_obj) returns the instruction.

        inc=False: non-final op of a PE accumulation group (no completion
        event yet; the closing op must carry the same `writes`)."""
        assert engine in COMPUTE_ENGINES
        self._resolve_conflicts(engine, reads, writes)
        self._emit_waits(engine, self._deps_for(reads, writes))
        if inc:
            h = self._sem_for_inc(engine, 1)
            self.cnt[engine] += 1
            ev = (engine, self.cnt[engine])
            self.queue[engine].append(("op", fn, h, 1))
            for _, rd, wr in self.pending[engine]:
                self._record(rd, wr, ev)
            self.pending[engine].clear()
            self._record(reads, writes, ev)
        else:
            ev = None
            self.queue[engine].append(("op", fn, None, 0))
            self.pending[engine].append((len(self.queue[engine]) - 1,
                                         list(reads), list(writes)))
        return ev

    def dma_accum(self, qname, out, in_, reads=(), writes=(), stream=None):
        return self.dma(qname, out, in_, reads=reads, writes=writes, stream=stream,
                        accum=mybir.AluOpType.add)

    def dma(self, qname, out, in_, reads=(), writes=(), stream=None, accum=None):
        self.mac_flush()
        """Queue a DMA on queue qname. Stream defaults to the SBUF-side
        buffer name: writes[0] for loads, 'st_'+reads[0] for stores."""
        assert qname in DMA_QUEUES
        if stream is None:
            if writes:
                stream = "dma_" + str(self._key(writes[0])[0])
            else:
                stream = "dma_st_" + str(self._key(reads[0])[0])
        if stream not in self.sems:
            self._mk_sem(stream, qname)
            self.stream_queue[stream] = qname
        assert self.stream_queue[stream] == qname, (
            f"stream {stream} used from {qname} but bound to {self.stream_queue[stream]}")
        self._resolve_conflicts(qname, reads, writes)
        deps = self._deps_for(reads, writes)
        prev = self.cnt[stream]
        if prev:
            deps.append((stream, prev))
        self._emit_waits(qname, deps)
        h = self._sem_for_inc(stream, DMA_STEP)
        self.cnt[stream] += DMA_STEP
        ev = (stream, self.cnt[stream])
        self.queue[qname].append(("dma", out, in_, h, DMA_STEP, accum))
        self._record(reads, writes, ev)
        return ev

    def custom_dma(self, qname, emit, reads=(), writes=(), stream=None):
        self.mac_flush()
        """Like dma() but `emit(eng)` builds the instruction (e.g. a
        collective). then_inc value must stay at the instruction default
        (explicit values on a collective produce an unloadable NEFF);
        the default increment is 1, so book 1."""
        assert qname in DMA_QUEUES and stream is not None
        if stream not in self.sems:
            self._mk_sem(stream, qname)
            self.stream_queue[stream] = qname
        assert self.stream_queue[stream] == qname
        self._resolve_conflicts(qname, reads, writes)
        deps = self._deps_for(reads, writes)
        prev = self.cnt[stream]
        if prev:
            deps.append((stream, prev))
        self._emit_waits(qname, deps)
        h = self._sem_for_inc(stream, 1)
        self.cnt[stream] += 1
        ev = (stream, self.cnt[stream])
        self.queue[qname].append(("op", emit, h, None))
        self._record(reads, writes, ev)
        return ev

    def dma_barrier(self):
        self.mac_flush()
        """Stage boundary: every DMA queue waits every stream's current
        count, so all in-flight DMAs (incl. collectives) complete before
        any later DMA issues. Compute engines are ordered through their
        own data deps on the tiles those DMAs fill."""
        for stream, cnt in list(self.cnt.items()):
            if stream not in self.stream_queue:
                continue
            if cnt == 0:
                continue
            for q in DMA_QUEUES:
                if self.waited[q].get(stream, 0) >= cnt:
                    continue
                h, v = self._sem_for_wait(stream, cnt)
                self.queue[q].append(("wait", h, v))
                self.waited[q][stream] = cnt

    def barrier_read(self, engine, bufs):
        self.mac_flush()
        deps = []
        for b in bufs:
            e = self._entry(self._key(b))
            if e["w"]:
                deps.append(e["w"])
        self._emit_waits(engine, deps)

    def wait_event(self, engine, ev):
        self._emit_waits(engine, [ev])

    def output_fence(self):
        self.mac_flush()
        """Timing contract: each DMA queue waits its streams' final counts."""
        for stream, q in self.stream_queue.items():
            if self.cnt[stream] > self.waited[q].get(stream, 0):
                h, v = self._sem_for_wait(stream, self.cnt[stream])
                self.queue[q].append(("wait", h, v))
                self.waited[q][stream] = self.cnt[stream]

    def finalize(self, block):
        self.mac_flush()
        for engine, items in self.queue.items():
            if not items:
                continue

            def make_section(items=items):
                def section(eng):
                    for it in items:
                        if it[0] == "wait":
                            eng.wait_ge(it[1], it[2])
                        elif it[0] == "op":
                            inst = it[1](eng)
                            if it[2] is not None:
                                inst.then_inc(it[2], it[3])
                        elif it[5] is None:
                            eng.dma_start(out=it[1], in_=it[2]).then_inc(it[3], it[4])
                        else:
                            eng.dma_start(out=it[1], in_=it[2],
                                          accum_op=it[5]).then_inc(it[3], it[4])
                return section
            getattr(block, ENGINE_ATTR[engine])(make_section())

"""klib: block emitters for the mk_seq_preln_d4 megakernel (per-core programs).

Conventions:
  - feature-major ("fm") activations: [128 features, T=4096 tokens] tiles,
    full sequence in the free dim; d_model=1024 -> 8 tiles.
  - residual x lives in HBM as [1024, 4096] fp32 (host pre-transposed).
  - weights in HBM as [in, out] fp32 (host pre-transposed); streamed to SBUF
    via gpsimd cast-DMA to bf16.
  - per-feature vectors host-packed as [128, n_tiles] ("pcol"): column j =
    feature-tile j's 128 values.
  - matmul: out = lhsT.T @ rhs, operands at partition base 0 always.
  - SBUF frees must be LIFO; Ctx.aget/afree implements arena reuse instead
    (kb free_fence handles aliasing edges).
"""

F32 = mybir.dt.float32
BF16 = mybir.dt.bfloat16
AF = mybir.ActivationFunctionType
ALU = mybir.AluOpType
AX = mybir.AxisListType

T = 4096
NTC = T // 512


class Ctx:
    def __init__(self, nc, p, stack):
        self.nc = nc
        self.p = p
        self.stack = stack
        self.n = 0
        self.ps = []
        self.ones_col = None
        self.ones_row = None
        self._afree, self._abusy = {}, {}
        self._consts = {}

    def uniq(self, pref):
        self.n += 1
        return f"{pref}{self.n}"

    def sb(self, name, shape, dtype):
        import os as _os
        if _os.environ.get("ARENA_LOG"):
            dsz = 4 if dtype == F32 else 2
            print(f"[sb] {name} {shape} {shape[1] * dsz}B rem={self.nc.sbuf_bytes_remaining}")
        return self.stack.enter_context(self.nc.sbuf_tensor(name, shape, dtype))

    ARENA_B = 65536
    GRAN = 256

    def _arena_new(self, min_bytes):
        self.n += 1
        rem = self.nc.sbuf_bytes_remaining
        size = min(self.ARENA_B, (rem - 512) // self.GRAN * self.GRAN)
        size = max(size, min_bytes)
        h = self.sb(f"arena{self.n}", [128, size // 4], F32)
        a = {"h": h, "free": [(0, size)], "size": size}
        self._arenas.append(a)
        busy = sum(v[2] for v in self._abusy.values())
        import sys
        print(f"[arena] new {size}B (total arenas {sum(x['size'] for x in self._arenas)}B, busy {busy}B)", file=sys.stderr)
        return a

    def aget(self, name, shape, dtype):
        if not hasattr(self, "_arenas"):
            self._arenas = []
        assert name not in self._abusy, name
        dsz = 4 if dtype == F32 else 2
        need = -(-shape[1] * dsz // self.GRAN) * self.GRAN
        assert need <= self.ARENA_B, (name, shape)
        spot = None
        for a in self._arenas:
            for fi, (off, size) in enumerate(a["free"]):
                if size >= need:
                    spot = (a, fi, off)
                    break
            if spot:
                break
        if spot is None:
            a = self._arena_new(need)
            spot = (a, 0, 0)
        a, fi, off = spot
        o2, s2 = a["free"][fi]
        if s2 == need:
            a["free"].pop(fi)
        else:
            a["free"][fi] = (o2 + need, s2 - need)
        ap = a["h"][:].bitcast(dtype)[0:shape[0], off // dsz:off // dsz + shape[1]]
        self._abusy[name] = (a, off, need, ap)
        import os as _os
        if _os.environ.get("ARENA_LOG"):
            ai = self._arenas.index(a)
            print(f"[aget] {name} arena{ai} bytes {off}..{off + need} shape {shape} "
                  f"dsz {dsz}")
        return ap

    def afree(self, *names):
        self.p.note_free(list(names))
        for n in names:
            if n in self._abusy:
                a, off, need, _ = self._abusy.pop(n)
                a["free"].append((off, need))
                a["free"].sort()
                merged = []
                for o, s in a["free"]:
                    if merged and merged[-1][0] + merged[-1][1] == o:
                        merged[-1] = (merged[-1][0], merged[-1][1] + s)
                    else:
                        merged.append((o, s))
                a["free"] = [tuple(x) for x in merged]

    def psum(self, name, shape, dtype=F32):
        return self.stack.enter_context(self.nc.psum_tensor(name, shape, dtype))

    def afree_all(self):
        if self._abusy:
            self.afree(*list(self._abusy.keys()))

    def init_common(self):
        p = self.p
        for k in range(7):
            self.ps.append(self.psum(f"psb{k}", [128, 512], F32))
        self.ones_col = self.sb("ones_col", [128, 1], BF16)
        self.ones_row = self.sb("ones_row", [1, 128], BF16)
        self.ones_rowf = self.sb("ones_rowf", [1, 128], F32)
        self.ones512 = self.sb("ones512", [1, 512], BF16)
        self.pstT = self.psum("pstT", [128, 512], BF16)
        p.op("dve", lambda v: v.memset(self.ones_col[:], 1.0), writes=["ones_col"])
        p.op("dve", lambda v: v.memset(self.ones_row[:], 1.0), writes=["ones_row"])
        p.op("dve", lambda v: v.memset(self.ones_rowf[:], 1.0), writes=["ones_rowf"])
        p.op("dve", lambda v: v.memset(self.ones512[:], 1.0), writes=["ones512"])

    def cf(self, value):
        if value not in self._consts:
            name = f"cst{len(self._consts)}"
            t = self.sb(name, [128, 1], F32)
            self.p.op("dve", lambda v: v.memset(t[:], float(value)), writes=[name])
            self._consts[value] = (t, name)
        return self._consts[value]


def load_pcol(ctx, name, hbm_ap, ntiles):
    t = ctx.aget(name, [128, ntiles], F32)
    ctx.p.dma("sp", t[:], hbm_ap, writes=[name])
    return t


def load_xbf(ctx, name, x_hbm, ntiles=8, reads=(), dtype=BF16, queue="pool"):
    tiles = []
    for i in range(ntiles):
        tn = f"{name}{i}"
        t = ctx.aget(tn, [128, T], dtype)
        rd = [(reads, i)] if reads else []
        rd = [(reads[0], i)] if reads else []
        ctx.p.dma(queue, t[:], x_hbm[i * 128:(i + 1) * 128, :], reads=rd, writes=[tn])
        tiles.append(t)
    return tiles


def bcast_row_bf(ctx, name, row_ap, row_key, out=None, psbank=6, n=T):
    p = ctx.p
    t = out if out is not None else ctx.aget(name, [128, n], BF16)
    ps = ctx.ps[psbank]
    pskey = f"psb{psbank}"
    for tc in range(n // 512):
        sl = slice(tc * 512, (tc + 1) * 512)
        p.op("pe", lambda t_, sl=sl: t_.matmul(out=ps[:, 0:512], lhsT=ctx.ones_row[:],
             rhs=row_ap[0:1, sl], start=True, stop=True),
             reads=["ones_row", row_key], writes=[pskey])
        p.op("act", lambda s, sl=sl: s.copy(out=t[:, sl], in_=ps[:, 0:512]),
             reads=[pskey], writes=[name])
    return t


def ln_rows(ctx, xbf_tiles, xkeys, kind="ln", eps=1e-5, d=1024,
            banks=(5, 6), name=None, dscale=1.0, plain_eps=None):
    p = ctx.p
    nm = name or ctx.uniq("lnr")
    ntiles = len(xbf_tiles)
    sq = [ctx.aget(f"{nm}_sq{j}", [128, 512], BF16) for j in range(2)]
    a_row = ctx.aget(f"{nm}_a", [1, T], BF16)
    mu_row = ctx.aget(f"{nm}_mu", [1, T], BF16) if kind == "ln" else None
    var_f = ctx.aget(f"{nm}_vf", [1, 512], F32)
    a2_row = ctx.aget(f"{nm}_a2", [1, T], BF16) if plain_eps is not None else None
    v2_f = ctx.aget(f"{nm}_v2", [1, 512], F32) if plain_eps is not None else None
    mu_f = ctx.aget(f"{nm}_mf", [1, 512], F32) if kind == "ln" else None
    m2_f = ctx.aget(f"{nm}_m2", [1, 512], F32) if kind == "ln" else None
    psS = ctx.ps[banks[0]]
    psM = ctx.ps[banks[1]] if kind == "ln" else None
    kS, kM = f"psb{banks[0]}", f"psb{banks[1]}"
    for tc in range(NTC):
        sl = slice(tc * 512, (tc + 1) * 512)
        for i in range(ntiles):
            sqb = sq[i % 2]
            sqk = f"{nm}_sq{i % 2}"
            p.op("act", lambda s, i=i, sl=sl, sqb=sqb: s.activation(out=sqb[:],
                 in_=xbf_tiles[i][:, sl], func=AF.Square), reads=[xkeys[i]], writes=[sqk])
            p.op("pe", lambda t_, i=i, sqb=sqb: t_.matmul(out=psS[0:1, 0:512],
                 lhsT=ctx.ones_col[:], rhs=sqb[:], start=(i == 0), stop=(i == ntiles - 1)),
                 reads=[sqk, "ones_col"], writes=[kS], inc=(i == ntiles - 1))
            if kind == "ln":
                p.op("pe", lambda t_, i=i, sl=sl: t_.matmul(out=psM[0:1, 0:512],
                     lhsT=ctx.ones_col[:], rhs=xbf_tiles[i][:, sl],
                     start=(i == 0), stop=(i == ntiles - 1)),
                     reads=[xkeys[i], "ones_col"], writes=[kM], inc=(i == ntiles - 1))
        if kind == "ln":
            p.op("dve", lambda v: v.tensor_scalar(out=mu_f[0:1, :], in0=psM[0:1, 0:512],
                 scalar1=1.0 / d, scalar2=None, op0=ALU.mult), reads=[kM], writes=[f"{nm}_mf"])
            p.op("dve", lambda v: v.tensor_scalar(out=var_f[0:1, :], in0=psS[0:1, 0:512],
                 scalar1=dscale / d, scalar2=None, op0=ALU.mult), reads=[kS], writes=[f"{nm}_vf"])
            p.op("dve", lambda v: v.scalar_tensor_tensor(out=m2_f[0:1, :],
                 in0=mu_f[0:1, :], scalar=-dscale, in1=mu_f[0:1, :], op0=ALU.mult,
                 op1=ALU.mult), reads=[f"{nm}_mf"], writes=[f"{nm}_m2"])
            p.op("dve", lambda v: v.tensor_tensor(out=var_f[0:1, :], in0=var_f[0:1, :],
                 in1=m2_f[0:1, :], op=ALU.add), reads=[f"{nm}_vf", f"{nm}_m2"],
                 writes=[f"{nm}_vf"])
            cneg, ckneg = ctx.cf(-1.0)
            p.op("act", lambda s, sl=sl: s.activation(out=mu_row[0:1, sl], in_=mu_f[0:1, :],
                 func=AF.Identity, scale=cneg[0:1, 0:1]), reads=[f"{nm}_mf", ckneg],
                 writes=[nm + "_mu"])
        else:
            p.op("dve", lambda v: v.tensor_scalar(out=var_f[0:1, :], in0=psS[0:1, 0:512],
                 scalar1=1.0 / d, scalar2=None, op0=ALU.mult), reads=[kS], writes=[f"{nm}_vf"])
        ce, cke = ctx.cf(float(eps))
        ch, ckh = ctx.cf(-0.5)
        if plain_eps is not None:
            c2, ck2 = ctx.cf(float(plain_eps))
            p.op("act", lambda s: s.activation(out=v2_f[0:1, :], in_=var_f[0:1, :],
                 func=AF.Ln, bias=c2[0:1, 0:1]), reads=[f"{nm}_vf", ck2],
                 writes=[f"{nm}_v2"])
            p.op("act", lambda s, sl=sl: s.activation(out=a2_row[0:1, sl], in_=v2_f[0:1, :],
                 func=AF.Exp, scale=ch[0:1, 0:1]), reads=[f"{nm}_v2", ckh],
                 writes=[nm + "_a2"])
        p.op("act", lambda s: s.activation(out=var_f[0:1, :], in_=var_f[0:1, :],
             func=AF.Ln, bias=ce[0:1, 0:1]), reads=[f"{nm}_vf", cke], writes=[f"{nm}_vf"])
        p.op("act", lambda s, sl=sl: s.activation(out=a_row[0:1, sl], in_=var_f[0:1, :],
             func=AF.Exp, scale=ch[0:1, 0:1]), reads=[f"{nm}_vf", ckh], writes=[nm + "_a"])
    ctx.afree(f"{nm}_sq0", f"{nm}_sq1", f"{nm}_vf",
              *([f"{nm}_mf", f"{nm}_m2"] if kind == "ln" else []),
              *([f"{nm}_v2"] if plain_eps is not None else []))
    if plain_eps is not None:
        return a_row, nm + "_a", mu_row, nm + "_mu", a2_row, nm + "_a2"
    return a_row, nm + "_a", mu_row, (nm + "_mu" if kind == "ln" else None)


def ln_materialize(ctx, xbf_tiles, xkeys, a_row, a_key, negmu_row, negmu_key,
                   gamma=None, gamma_key=None, beta=None, beta_key=None,
                   name=None, inplace=False, free_rows=True):
    p = ctx.p
    nm = name or ctx.uniq("xln")
    a_b = bcast_row_bf(ctx, nm + "_ab", a_row, a_key, psbank=6)
    mu_b = None
    if negmu_row is not None:
        mu_b = bcast_row_bf(ctx, nm + "_mb", negmu_row, negmu_key, psbank=5)
    tiles, keys = [], []
    for i in range(len(xbf_tiles)):
        if inplace:
            t, tn = xbf_tiles[i], xkeys[i]
        else:
            tn = f"{nm}{i}"
            t = ctx.aget(tn, [128, T], BF16)
        if mu_b is not None:
            p.op("dve", lambda v, i=i, t=t, tn=tn: v.tensor_tensor(out=t[:],
                 in0=xbf_tiles[i][:], in1=mu_b[:], op=ALU.add),
                 reads=[xkeys[i], nm + "_mb"], writes=[tn])
            p.op("dve", lambda v, t=t, tn=tn: v.tensor_tensor(out=t[:], in0=t[:],
                 in1=a_b[:], op=ALU.mult), reads=[tn, nm + "_ab"], writes=[tn])
        else:
            p.op("dve", lambda v, i=i, t=t, tn=tn: v.tensor_tensor(out=t[:],
                 in0=xbf_tiles[i][:], in1=a_b[:], op=ALU.mult),
                 reads=[xkeys[i], nm + "_ab"], writes=[tn])
        if gamma is not None:
            bk = [beta_key] if beta is not None else []
            p.op("act", lambda s, t=t, i=i, tn=tn: s.activation(out=t[:], in_=t[:],
                 func=AF.Identity, scale=gamma[:, i:i + 1],
                 bias=(beta[:, i:i + 1] if beta is not None else 0.0)),
                 reads=[tn, gamma_key] + bk, writes=[tn])
        tiles.append(t)
        keys.append(tn)
    ctx.afree(nm + "_ab")
    if mu_b is not None:
        ctx.afree(nm + "_mb")
    if free_rows:
        ctx.afree(a_key, *([negmu_key] if negmu_key else []))
    return tiles, keys


def emit_L0F(ctx, d):
    p = ctx.p
    xbf = load_xbf(ctx, "l0f_x", d["x"])
    xkeys = [f"l0f_x{i}" for i in range(8)]
    scale = load_pcol(ctx, "l0f_sc", d["scale_pc"], 8)
    bias = load_pcol(ctx, "l0f_bi", d["bias_pc"], 8)
    a_row, ak, mu_row, mk = ln_rows(ctx, xbf, xkeys, kind="ln", eps=1e-5, name="l0f_ln")
    xln, lnkeys = ln_materialize(ctx, xbf, xkeys, a_row, ak, mu_row, mk,
                                 name="l0f_xln", inplace=True)

    wsb = []
    for i in range(8):
        tn = f"l0f_ws{i}"
        t = ctx.aget(tn, [128, 1024], BF16)
        p.dma("pool", t[:], d["wT"][i * 128:(i + 1) * 128, :], writes=[tn])
        p.op("dve", lambda v, t=t, tn=tn: v.tensor_scalar(out=t[:], in0=t[:], scalar1=0.0,
             scalar2=2.0, op0=ALU.is_ge, op1=ALU.mult), reads=[tn], writes=[tn])
        p.op("dve", lambda v, t=t, tn=tn: v.tensor_scalar(out=t[:], in0=t[:], scalar1=-1.0,
             scalar2=None, op0=ALU.add), reads=[tn], writes=[tn])
        wsb.append(t)

    xf32 = [ctx.aget(f"l0f_xf{j}", [128, T], F32) for j in range(2)]
    outt = [ctx.aget(f"l0f_o{i}", [128, T], F32) for i in range(2)]

    for od in range(8):
        ot, ok = outt[od % 2], f"l0f_o{od % 2}"
        xk = f"l0f_xf{od % 2}"
        p.dma("sp", xf32[od % 2][:], d["x"][od * 128:(od + 1) * 128, :], writes=[xk])
        for tc in range(NTC):
            sl = slice(tc * 512, (tc + 1) * 512)
            bank = (od * NTC + tc) % 4
            ps, pk = ctx.ps[bank], f"psb{bank}"
            for i in range(8):
                p.op("pe", lambda t_, i=i, sl=sl, od=od, ps=ps:
                     t_.matmul(out=ps[:, 0:512], lhsT=wsb[i][:, od * 128:(od + 1) * 128],
                               rhs=xln[i][:, sl], start=(i == 0), stop=(i == 7)),
                     reads=[f"l0f_ws{i}", lnkeys[i]], writes=[pk], inc=(i == 7))
            p.op("act", lambda s, sl=sl, od=od, ot=ot, ps=ps: s.activation(out=ot[:, sl],
                 in_=ps[:, 0:512], func=AF.Identity, bias=bias[:, od:od + 1],
                 scale=scale[:, od:od + 1]), reads=[pk, "l0f_sc", "l0f_bi"], writes=[ok])
            p.op("dve", lambda v, sl=sl, ot=ot: v.tensor_scalar(out=ot[:, sl],
                 in0=ot[:, sl], scalar1=100.0, scalar2=-100.0, op0=ALU.min, op1=ALU.max),
                 reads=[ok], writes=[ok])
            p.op("dve", lambda v, sl=sl, ot=ot, j=od % 2: v.tensor_tensor(out=ot[:, sl],
                 in0=ot[:, sl], in1=xf32[j][:, sl], op=ALU.add), reads=[ok, xk], writes=[ok])
        p.dma("sp", d["xout"][od * 128:(od + 1) * 128, :], ot[:], reads=[ok],
              stream="dma_st_l0f")
    ctx.afree(*xkeys, *[f"l0f_ws{i}" for i in range(8)], "l0f_xf0", "l0f_xf1",
              "l0f_o0", "l0f_o1")


def host_pcol(v):
    return np.ascontiguousarray(np.asarray(v).reshape(-1, 128).T.astype(np.float32))


def build_block(emit_fn, dram_specs, host_prep, core_ids=range(8)):
    from contextlib import ExitStack
    nc = bass.Bass("TRN2", target_bir_lowering=False)
    d = {}
    for name, (shape, dt, kind) in dram_specs.items():
        d[name] = nc.dram_tensor(name, list(shape), dt, kind=kind).ap()
    with ExitStack() as stack:
        p = Prog(nc)
        ctx = Ctx(nc, p, stack)
        ctx.init_common()
        emit_fn(ctx, d)
        p.output_fence()
        block = stack.enter_context(nc.Block())
        p.finalize(block)
    in_maps = [host_prep(c) for c in core_ids]
    return nc, in_maps

"""L0_attn block emitter: GLA block with conv path, per-head decay scans,
output gating and an internal gated FFN (inter=2816).

x_out = x_mid + ffn(rms2(x_mid));  x_mid = x + gmix*local + (1-gmix)*attn
where the block input is h0 = LN(x) and h = rmsnorm(h0, norm1_w, 1e-6)
(fused into one scaled LN here).
"""

NH = 8
INTER = 2816
NI = INTER // 128


def mac(p, ps, pk, pairs, n=512):
    p.mac_queue(ps, pk, pairs, n)


def wload(ctx, name, w_hbm, IN, col0, ncols):
    n_in = IN // 128
    if name in ctx._abusy:
        t = ctx._abusy[name][3]
    else:
        t = ctx.aget(name, [128, n_in * ncols], BF16)
    src = w_hbm[:, col0:col0 + ncols].rearrange("(i p) c -> p i c", p=128)
    ctx.p.dma("pool", t[:].rearrange("p (i c) -> p i c", c=ncols), src, writes=[name])
    return t


def wslice(wt, i, ncols):
    return wt[:, i * ncols:(i + 1) * ncols]


DBG_SLOTS = {}


def dbg(ctx, d, name, ap, key, queue="pool"):
    import os
    if "dbg" not in d or not os.environ.get("L0A_DBG"):
        return
    slot = DBG_SLOTS.setdefault(name, len(DBG_SLOTS))
    ctx.p.dma(queue, d["dbg"][slot, :, :], ap, reads=[key], stream="dma_dbg_" + queue)


def emit_L0A(ctx, d):
    p = ctx.p
    nc = ctx.nc
    vsp = nc.dram_tensor("A_vsp", [1024, T], BF16, kind="Internal").ap()
    lsp = nc.dram_tensor("A_lsp", [1024, T], BF16, kind="Internal").ap()
    xmid = nc.dram_tensor("A_xmid", [1024, T], F32, kind="Internal").ap()
    xsum = nc.dram_tensor("A_xsum", [1024, T], F32, kind="Internal").ap()
    dsp = nc.dram_tensor("A_dsp", [1024, T], BF16, kind="Internal").ap()
    msp = nc.dram_tensor("A_msp", [INTER, T], BF16, kind="Internal").ap()

    g1 = load_pcol(ctx, "A_g1", d["norm1_pc"], 8)
    g2 = load_pcol(ctx, "A_g2", d["norm2_pc"], 8)
    cw = load_pcol(ctx, "A_cw", d["convw_pc"], 32)
    cb = load_pcol(ctx, "A_cb", d["convb_pc"], 8)
    ogb = load_pcol(ctx, "A_ogb", d["outgateb_pc"], 8)
    dp = ctx.aget("A_dp", [1, 8], F32)
    p.dma("sp", dp[:], d["decay"], writes=["A_dp"])
    dec8 = ctx.aget("A_dec8", [1, 8], F32)
    p.op("act", lambda s: s.activation(out=dec8[:], in_=dp[:], func=AF.Sigmoid),
         reads=["A_dp"], writes=["A_dec8"])

    xbf = load_xbf(ctx, "A_x", d["x"])
    hk = [f"A_x{i}" for i in range(8)]
    a_row, ak, mu_row, mk, a2_row, a2k = ln_rows(ctx, xbf, hk, kind="ln", eps=1e-11,
                                                 dscale=1.000001, name="A_ln",
                                                 plain_eps=1e-5)
    h, _ = ln_materialize(ctx, xbf, hk, a_row, ak, mu_row, mk, gamma=g1,
                          gamma_key="A_g1", name="A_hm", inplace=True,
                          free_rows=False)
    ctx.afree(ak)

    for i in range(8):
        dbg(ctx, d, f"h{i}", h[i][:], hk[i])
    cacc = [ctx.aget(f"A_cv{j}", [128, T], BF16) for j in (0, 1)]
    sc = [ctx.aget(f"A_sc{i}", [128, T], BF16) for i in range(8)]
    for i in range(8):
        cv, ck = cacc[i % 2], f"A_cv{i % 2}"
        p.op("dve", lambda v, i=i, cv=cv: v.tensor_scalar(out=cv[:], in0=h[i][:],
             scalar1=cw[:, 24 + i:25 + i], scalar2=None, op0=ALU.mult),
             reads=[hk[i], "A_cw"], writes=[ck])
        for k in (1, 2, 3):
            tap = 3 - k
            p.op("dve", lambda v, i=i, k=k, tap=tap, cv=cv: v.scalar_tensor_tensor(
                 out=cv[:, k:T], in0=h[i][:, 0:T - k], scalar=cw[:, tap * 8 + i:tap * 8 + i + 1],
                 in1=cv[:, k:T], op0=ALU.mult, op1=ALU.add),
                 reads=[hk[i], ck, "A_cw"], writes=[ck])
        p.op("act", lambda s, i=i, cv=cv: s.activation(out=sc[i][:], in_=cv[:],
             func=AF.Silu, bias=cb[:, i:i + 1]), reads=[ck, "A_cb"], writes=[f"A_sc{i}"])
    ctx.afree("A_cv0", "A_cv1")
    lo = [ctx.aget(f"A_lo{j}", [128, T], BF16) for j in (0, 1)]
    for od in range(8):
        wt = wload(ctx, f"A_pw{od % 2}", d["pw_wT"], 1024, od * 128, 128)
        lt, lk = lo[od % 2], f"A_lo{od % 2}"
        for tc in range(NTC):
            bank = tc % 2
            ps, pk = ctx.ps[bank], f"psb{bank}"
            mac(p, ps, pk, [(wslice(wt, i, 128), f"A_pw{od % 2}",
                             sc[i][:, tc * 512:(tc + 1) * 512], f"A_sc{i}") for i in range(8)])
            p.op("act", lambda s, lt=lt, tc=tc, ps=ps: s.copy(
                 out=lt[:, tc * 512:(tc + 1) * 512], in_=ps[:, 0:512]),
                 reads=[pk], writes=[lk])
        dbg(ctx, d, f"local{od}", lt[:], lk)
        p.dma("pool", lsp[od * 128:(od + 1) * 128, :], lt[:], reads=[lk], writes=[("d_lsp", od)], stream="dma_st_lsp")
    ctx.afree("A_lo0", "A_lo1", "A_pw0", "A_pw1", *[f"A_sc{i}" for i in range(8)])

    ot = [ctx.aget(f"A_o{hd}", [128, T], BF16) for hd in range(NH)]
    tA = ctx.aget("A_tA", [128, T], BF16)
    qb = ctx.aget("A_qb", [128, T], BF16)
    kb2 = ctx.aget("A_kb", [128, T], BF16)
    vb = ctx.aget("A_vb", [128, T], BF16)
    skb = ctx.aget("A_skb", [128, T], BF16)
    dfs = ctx.aget("A_dfs", [1, 512], F32)
    dech = ctx.aget("A_dech", [128, 1], F32)

    for hd in range(NH):
        p.op("pe", lambda t_, hd=hd: t_.matmul(out=ctx.ps[4][:, 0:1], lhsT=ctx.ones_rowf[:],
             rhs=dec8[0:1, hd:hd + 1], start=True, stop=True),
             reads=["ones_rowf", "A_dec8"], writes=["psb4"])
        p.op("dve", lambda v: v.tensor_copy(out=dech[:], in_=ctx.ps[4][:, 0:1]),
             reads=["psb4"], writes=["A_dech"])

        for part, colbase, dst, dk in (("q", hd * 128, qb, "A_qb"),
                                       ("k", 1024 + hd * 128, kb2, "A_kb"),
                                       ("v", 2048 + hd * 128, vb, "A_vb")):
            wt = wload(ctx, f"A_wqkv{['q', 'k', 'v'].index(part) % 2}", d["qkv_wT"],
                       1024, colbase, 128)
            wk = f"A_wqkv{['q', 'k', 'v'].index(part) % 2}"
            for tc in range(NTC):
                bank = tc % 2
                ps, pk = ctx.ps[bank], f"psb{bank}"
                mac(p, ps, pk, [(wslice(wt, i, 128), wk,
                                 h[i][:, tc * 512:(tc + 1) * 512], hk[i]) for i in range(8)])
                p.op("act", lambda s, tc=tc, ps=ps, dst=dst: s.copy(
                     out=dst[:, tc * 512:(tc + 1) * 512], in_=ps[:, 0:512]),
                     reads=[pk], writes=[dk])
            if part == "v":
                p.dma("pool", vsp[hd * 128:(hd + 1) * 128, :], vb[:], reads=["A_vb"],
                      writes=[("d_vsp", hd)], stream="dma_st_vsp")
            else:
                p.op("dve", lambda v, dst=dst: v.tensor_scalar(out=tA[:], in0=dst[:],
                     scalar1=0.0, scalar2=None, op0=ALU.min), reads=[dk], writes=["A_tA"])
                p.op("act", lambda s: s.activation(out=skb[:], in_=tA[:], func=AF.Exp),
                     reads=["A_tA"], writes=["A_skb"])
                p.op("dve", lambda v, dst=dst: v.tensor_scalar(out=dst[:], in0=dst[:],
                     scalar1=0.0, scalar2=None, op0=ALU.max), reads=[dk], writes=[dk])
                p.op("dve", lambda v, dst=dst: v.tensor_tensor(out=dst[:], in0=dst[:],
                     in1=skb[:], op=ALU.add), reads=[dk, "A_skb"], writes=[dk])
        if hd == 0:
            dbg(ctx, d, "q0", qb[:], "A_qb")
            dbg(ctx, d, "k0", kb2[:], "A_kb")
            dbg(ctx, d, "v0", vb[:], "A_vb")
        p.op("dve", lambda v: v.tensor_tensor_scan(out=skb[:],
             data0=dech[:, 0:1].broadcast_to((128, T)), data1=kb2[:], initial=0.0,
             op0=ALU.mult, op1=ALU.add), reads=["A_dech", "A_kb"], writes=["A_skb"])
        p.op("dve", lambda v: v.tensor_tensor(out=tA[:], in0=qb[:], in1=skb[:],
             op=ALU.mult), reads=["A_qb", "A_skb"], writes=["A_tA"])
        p.op("dve", lambda v: v.tensor_tensor(out=vb[:], in0=kb2[:], in1=vb[:],
             op=ALU.mult), reads=["A_kb", "A_vb"], writes=["A_vb"])
        p.op("dve", lambda v: v.tensor_tensor_scan(out=skb[:],
             data0=dech[:, 0:1].broadcast_to((128, T)), data1=vb[:], initial=0.0,
             op0=ALU.mult, op1=ALU.add), reads=["A_dech", "A_vb"], writes=["A_skb"])
        p.op("dve", lambda v: v.tensor_tensor(out=qb[:], in0=qb[:], in1=skb[:],
             op=ALU.mult), reads=["A_qb", "A_skb"], writes=["A_qb"])
        for tc in range(NTC):
            sl = slice(tc * 512, (tc + 1) * 512)
            ps, pk = ctx.ps[4], "psb4"
            p.op("pe", lambda t_, sl=sl, ps=ps: t_.matmul(out=ps[0:1, 0:512],
                 lhsT=ctx.ones_col[:], rhs=tA[:, sl], start=True, stop=True),
                 reads=["A_tA", "ones_col"], writes=[pk])
            p.op("dve", lambda v, ps=ps: v.tensor_scalar(out=dfs[0:1, :],
                 in0=ps[0:1, 0:512], scalar1=1e-6, scalar2=None, op0=ALU.max),
                 reads=[pk], writes=["A_dfs"])
            p.op("dve", lambda v: v.reciprocal(out=dfs[0:1, :], in_=dfs[0:1, :]),
                 reads=["A_dfs"], writes=["A_dfs"])
            p.op("pe", lambda t_, ps=ps: t_.matmul(out=ps[:, 0:512],
                 lhsT=ctx.ones_rowf[:], rhs=dfs[0:1, :], start=True, stop=True),
                 reads=["ones_rowf", "A_dfs"], writes=[pk])
            p.op("dve", lambda v, hd=hd, sl=sl, ps=ps: v.tensor_tensor(
                 out=ot[hd][:, sl], in0=qb[:, sl], in1=ps[:, 0:512], op=ALU.mult),
                 reads=["A_qb", pk], writes=[f"A_o{hd}"])

    for i in range(8):
        dbg(ctx, d, f"o{i}", ot[i][:], f"A_o{i}")
    ctx.afree("A_tA", "A_qb", "A_kb", "A_vb", "A_skb",
              "A_dfs", "A_dech", "A_wqkv0", "A_wqkv1",
              *[f"A_x{i}" for i in range(8)])

    gt = [ctx.aget(f"A_gt{od}", [128, T], BF16) for od in range(8)]
    vre = [ctx.aget(f"A_vr{j}", [128, T], BF16) for j in (0, 1)]
    for od in range(8):
        wt = wload(ctx, f"A_wog{od % 2}", d["outgate_wT"], 1024, od * 128, 128)
        wk = f"A_wog{od % 2}"
        g, gk = gt[od], f"A_gt{od}"
        for tc in range(NTC):
            bank = tc % 2
            ps, pk = ctx.ps[bank], f"psb{bank}"
            mac(p, ps, pk, [(wslice(wt, i, 128), wk,
                             ot[i][:, tc * 512:(tc + 1) * 512], f"A_o{i}") for i in range(8)])
            p.op("act", lambda s, g=g, tc=tc, ps=ps, od=od: s.activation(
                 out=g[:, tc * 512:(tc + 1) * 512], in_=ps[:, 0:512], func=AF.Sigmoid,
                 bias=ogb[:, od:od + 1]), reads=[pk, "A_ogb"], writes=[gk])
    for od in range(8):
        g, gk = gt[od], f"A_gt{od}"
        vr, vk = vre[od % 2], f"A_vr{od % 2}"
        p.dma("pool", vr[:], vsp[od * 128:(od + 1) * 128, :], reads=[("d_vsp", od)], writes=[vk])
        p.op("dve", lambda v, od=od, vr=vr: v.tensor_tensor(out=ot[od][:], in0=ot[od][:],
             in1=vr[:], op=ALU.subtract), reads=[f"A_o{od}", vk], writes=[f"A_o{od}"])
        p.op("dve", lambda v, od=od, g=g: v.tensor_tensor(out=ot[od][:], in0=ot[od][:],
             in1=g[:], op=ALU.mult), reads=[f"A_o{od}", gk], writes=[f"A_o{od}"])
        p.op("dve", lambda v, od=od, vr=vr: v.tensor_tensor(out=ot[od][:], in0=ot[od][:],
             in1=vr[:], op=ALU.add), reads=[f"A_o{od}", vk], writes=[f"A_o{od}"])
    ctx.afree(*[f"A_gt{od}" for od in range(8)], "A_vr0", "A_vr1", "A_wog0", "A_wog1")

    at = [ctx.aget(f"A_at{od}", [128, T], BF16) for od in range(8)]
    for od in range(8):
        wt = wload(ctx, f"A_wop{od % 2}", d["outproj_wT"], 1024, od * 128, 128)
        wk = f"A_wop{od % 2}"
        for tc in range(NTC):
            bank = tc % 2
            ps, pk = ctx.ps[bank], f"psb{bank}"
            mac(p, ps, pk, [(wslice(wt, i, 128), wk,
                             ot[i][:, tc * 512:(tc + 1) * 512], f"A_o{i}") for i in range(8)])
            p.op("act", lambda s, od=od, tc=tc, ps=ps: s.copy(
                 out=at[od][:, tc * 512:(tc + 1) * 512], in_=ps[:, 0:512]),
                 reads=[pk], writes=[f"A_at{od}"])
    ctx.afree("A_wop0", "A_wop1", *[f"A_o{i}" for i in range(8)])

    lo_t = []
    for i in range(8):
        tn = f"A_lr{i}"
        tt = ctx.aget(tn, [128, T], BF16)
        p.dma("pool", tt[:], lsp[i * 128:(i + 1) * 128, :], reads=[("d_lsp", i)], writes=[tn])
        lo_t.append(tt)
    lok = [f"A_lr{i}" for i in range(8)]
    gm = [ctx.aget(f"A_gm{j}", [128, T], BF16) for j in (0, 1)]
    tD = [ctx.aget(f"A_tD{j}", [128, T], BF16) for j in (0, 1)]
    for od in range(8):
        wt = wload(ctx, f"A_wgx{od % 2}", d["gate_wT"], 2048, od * 128, 128)
        wk = f"A_wgx{od % 2}"
        g, gk = gm[od % 2], f"A_gm{od % 2}"
        ins = [(lo_t[i], lok[i]) for i in range(8)] + [(at[i], f"A_at{i}") for i in range(8)]
        for tc in range(NTC):
            bank = tc % 2
            ps, pk = ctx.ps[bank], f"psb{bank}"
            mac(p, ps, pk, [(wslice(wt, i, 128), wk,
                             ins[i][0][:, tc * 512:(tc + 1) * 512], ins[i][1])
                            for i in range(16)])
            p.op("act", lambda s, g=g, tc=tc, ps=ps: s.activation(
                 out=g[:, tc * 512:(tc + 1) * 512], in_=ps[:, 0:512], func=AF.Sigmoid),
                 reads=[pk], writes=[gk])
        td, tdk = tD[od % 2], f"A_tD{od % 2}"
        p.op("dve", lambda v, od=od, td=td, tdk=tdk: v.tensor_tensor(out=td[:],
             in0=lo_t[od][:], in1=at[od][:], op=ALU.subtract),
             reads=[lok[od], f"A_at{od}"], writes=[tdk])
        p.op("dve", lambda v, td=td, tdk=tdk, g=g, gk=gk: v.tensor_tensor(out=td[:],
             in0=td[:], in1=g[:], op=ALU.mult), reads=[tdk, gk], writes=[tdk])
        p.op("dve", lambda v, od=od, td=td, tdk=tdk: v.tensor_tensor(out=td[:],
             in0=td[:], in1=at[od][:], op=ALU.add), reads=[tdk, f"A_at{od}"], writes=[tdk])
        p.dma("pool", dsp[od * 128:(od + 1) * 128, :], td[:], reads=[tdk],
              writes=[("d_dsp", od)], stream="dma_st_dsp")
    ctx.afree(*lok, *[f"A_at{i}" for i in range(8)], "A_gm0", "A_gm1", "A_wgx0",
              "A_wgx1", "A_tD0", "A_tD1")

    mu_b = bcast_row_bf(ctx, "A_mub", mu_row, "A_ln_mu", psbank=5)
    a2_b = bcast_row_bf(ctx, "A_a2b", a2_row, "A_ln_a2", psbank=6)
    xf = [ctx.aget(f"A_xf{j}", [128, T], F32) for j in (0, 1)]
    xh = [ctx.aget(f"A_xh{j}", [128, T], F32) for j in (0, 1)]
    dd = [ctx.aget(f"A_dd{j}", [128, T], BF16) for j in (0, 1)]
    for od in range(8):
        xfj, xfk = xf[od % 2], f"A_xf{od % 2}"
        xhj, xhk = xh[od % 2], f"A_xh{od % 2}"
        ddj, ddk = dd[od % 2], f"A_dd{od % 2}"
        p.dma("sp", xfj[:], d["x"][od * 128:(od + 1) * 128, :], writes=[xfk])
        p.dma("pool", ddj[:], dsp[od * 128:(od + 1) * 128, :], reads=[("d_dsp", od)],
              writes=[ddk])
        p.op("dve", lambda v, xhj=xhj, xhk=xhk, xfj=xfj, xfk=xfk: v.tensor_tensor(
             out=xhj[:], in0=xfj[:], in1=mu_b[:], op=ALU.add),
             reads=[xfk, "A_mub"], writes=[xhk])
        p.op("dve", lambda v, xhj=xhj, xhk=xhk: v.tensor_tensor(out=xhj[:], in0=xhj[:],
             in1=a2_b[:], op=ALU.mult), reads=[xhk, "A_a2b"], writes=[xhk])
        p.op("dve", lambda v, xhj=xhj, xhk=xhk, ddj=ddj, ddk=ddk: v.tensor_tensor(
             out=xhj[:], in0=xhj[:], in1=ddj[:], op=ALU.add), reads=[xhk, ddk],
             writes=[xhk])
        p.dma("sp", xmid[od * 128:(od + 1) * 128, :], xhj[:], reads=[xhk],
              writes=[("d_xmid", od)], stream="dma_st_xmid")
        p.op("dve", lambda v, xfj=xfj, xfk=xfk, xhj=xhj, xhk=xhk: v.tensor_tensor(
             out=xfj[:], in0=xfj[:], in1=xhj[:], op=ALU.add), reads=[xfk, xhk],
             writes=[xfk])
        p.dma("sp", xsum[od * 128:(od + 1) * 128, :], xfj[:], reads=[xfk],
              writes=[("d_xsum", od)], stream="dma_st_xsum")
    ctx.afree("A_mub", "A_a2b", "A_ln_mu", "A_ln_a2", "A_xf0", "A_xf1",
              "A_xh0", "A_xh1", "A_dd0", "A_dd1")

    xmb = []
    for i in range(8):
        tn = f"A_xmb{i}"
        tt = ctx.aget(tn, [128, T], BF16)
        p.dma("pool", tt[:], xmid[i * 128:(i + 1) * 128, :], reads=[("d_xmid", i)],
              writes=[tn])
        xmb.append(tt)

    xmbk = [f"A_xmb{i}" for i in range(8)]
    a2, a2k, _, _ = ln_rows(ctx, xmb, xmbk, kind="rms", eps=1e-6, name="A_r2")
    h2, _ = ln_materialize(ctx, xmb, xmbk, a2, a2k, None, None, gamma=g2,
                           gamma_key="A_g2", name="A_h2", inplace=True)
    dbg(ctx, d, "h2_0", h2[0][:], xmbk[0])
    dbg(ctx, d, "h2_7", h2[7][:], xmbk[7])
    ut = [ctx.aget(f"A_u{j}", [128, T], BF16) for j in (0, 1)]
    mt = [ctx.aget(f"A_m{j}", [128, T], BF16) for j in (0, 1)]
    for od in range(NI):
        w1s = wload(ctx, f"A_w1s{od % 2}", d["w1T"], 1024, od * 128, 128)
        w2s = wload(ctx, f"A_w2s{od % 2}", d["w2T"], 1024, od * 128, 128)
        k1, k2 = f"A_w1s{od % 2}", f"A_w2s{od % 2}"
        u, uk = ut[od % 2], f"A_u{od % 2}"
        m, mk2 = mt[od % 2], f"A_m{od % 2}"
        for tc in range(NTC):
            sl = slice(tc * 512, (tc + 1) * 512)
            b1, b2 = (tc % 2) * 2, (tc % 2) * 2 + 1
            ps1, pk1 = ctx.ps[b1], f"psb{b1}"
            ps2, pk2 = ctx.ps[b2], f"psb{b2}"
            mac(p, ps1, pk1, [(wslice(w1s, i, 128), k1, h2[i][:, sl], xmbk[i])
                              for i in range(8)])
            mac(p, ps2, pk2, [(wslice(w2s, i, 128), k2, h2[i][:, sl], xmbk[i])
                              for i in range(8)])
            p.op("act", lambda s, u=u, sl=sl, ps1=ps1: s.activation(out=u[:, sl],
                 in_=ps1[:, 0:512], func=AF.Silu), reads=[pk1], writes=[uk])
            p.op("act", lambda s, m=m, sl=sl, ps2=ps2: s.copy(out=m[:, sl],
                 in_=ps2[:, 0:512]), reads=[pk2], writes=[mk2])
            p.op("dve", lambda v, u=u, m=m, sl=sl: v.tensor_tensor(out=m[:, sl],
                 in0=u[:, sl], in1=m[:, sl], op=ALU.mult), reads=[uk, mk2], writes=[mk2])
        if od in (0, 21):
            dbg(ctx, d, f"m{od}", m[:], mk2)
        p.dma("pool", msp[od * 128:(od + 1) * 128, :], m[:], reads=[mk2],
              writes=[("d_msp", od)], stream="dma_st_msp")
    ctx.afree("A_u0", "A_u1", "A_m0", "A_m1", "A_w1s0", "A_w1s1", "A_w2s0",
              "A_w2s1", *[f"A_xmb{i}" for i in range(8)])

    w3s = []
    for od in range(8):
        w3s.append(wload(ctx, f"A_w3s{od}", d["w3T"], INTER, od * 128, 128))
    mre = [ctx.aget(f"A_mr{j}", [128, NI * 512], BF16) for j in (0, 1)]
    xr = [ctx.aget(f"A_xr{j}", [128, 512], F32) for j in (0, 1)]
    out = [ctx.aget(f"A_ot{j}", [128, 512], F32) for j in (0, 1)]
    for t8 in range(NTC):
        mr, mrk = mre[t8 % 2], f"A_mr{t8 % 2}"
        p.dma("pool", mr[:].rearrange("p (i c) -> p i c", c=512),
              msp[:, t8 * 512:(t8 + 1) * 512].rearrange("(i p) c -> p i c", p=128),
              reads=[("d_msp", i2) for i2 in range(NI)], writes=[mrk])
        for od in range(8):
            xrj, xrk = xr[od % 2], f"A_xr{od % 2}"
            oj, ok = out[od % 2], f"A_ot{od % 2}"
            p.dma("sp", xrj[:], xsum[od * 128:(od + 1) * 128, t8 * 512:(t8 + 1) * 512],
                  reads=[("d_xsum", od)], writes=[xrk])
            ps, pk = ctx.ps[2 + od % 4], f"psb{2 + od % 4}"
            mac(p, ps, pk, [(w3s[od][:, i * 128:(i + 1) * 128], f"A_w3s{od}",
                             mr[:, i * 512:(i + 1) * 512], mrk) for i in range(NI)])
            p.op("dve", lambda v, oj=oj, ps=ps, xrj=xrj: v.tensor_tensor(
                 out=oj[:], in0=ps[:, 0:512], in1=xrj[:], op=ALU.add),
                 reads=[pk, xrk], writes=[ok])
            p.dma("sp", d["xout"][od * 128:(od + 1) * 128, t8 * 512:(t8 + 1) * 512],
                  oj[:], reads=[ok], stream="dma_st_xout")
    ctx.afree("A_mr0", "A_mr1", "A_xr0", "A_xr1", "A_ot0", "A_ot1",
              *[f"A_w3s{od}" for od in range(8)])

"""L1_attn block emitter: SSM block.

Math (x_in = LN_outer(x); the SSM collapses to per-channel scalar scans):
  x_ln = layernorm(x_in, g, b)  [fused with outer LN via dscale trick]
  xp|gate = x_ln @ W_in.T + b_in           (E=2048 each)
  xc = silu(causal_conv4(xp) + conv_b)
  u = (xc @ W_c.T + b_c) * sigmoid(xc @ W_d.T + b_d)
  a_e = exp(-softplus(log_delta)); dot_e = sum_n A[e,n]*Bm[e,n]
  y = dot * scan(u; a)
  x_gated = y * sigmoid(gate)
  out = x_gated @ W_out.T + b_out
  x_next = x + x_in + out
"""



E = 2048
NE = E // 128


def emit_L1A(ctx, d):
    p = ctx.p
    nc = ctx.nc
    gsp = nc.dram_tensor("B_gsp", [E, T], BF16, kind="Internal").ap()
    xpsp = nc.dram_tensor("B_xpsp", [E, T], BF16, kind="Internal").ap()
    ysp = nc.dram_tensor("B_ysp", [E, T], BF16, kind="Internal").ap()
    xsum = nc.dram_tensor("B_xsum", [1024, T], F32, kind="Internal").ap()

    lng = load_pcol(ctx, "B_lng", d["lng_pc"], 8)
    lnb = load_pcol(ctx, "B_lnb", d["lnb_pc"], 8)
    bin_ = load_pcol(ctx, "B_bin", d["bin_pc"], 32)
    cw = load_pcol(ctx, "B_cw", d["convw_pc"], 64)
    cb = load_pcol(ctx, "B_cb", d["convb_pc"], 16)
    bc = load_pcol(ctx, "B_bc", d["bc_pc"], 16)
    bd = load_pcol(ctx, "B_bd", d["bd_pc"], 16)
    bout = load_pcol(ctx, "B_bout", d["bout_pc"], 8)
    ld_ = load_pcol(ctx, "B_ld", d["logdelta_pc"], 16)
    apc = load_pcol(ctx, "B_apc", d["a_pc"], 256)
    bpc = load_pcol(ctx, "B_bpc", d["bm_pc"], 256)

    adiag = ctx.aget("B_adiag", [128, 16], F32)
    t1 = ctx.aget("B_t1", [128, 16], F32)
    p.op("dve", lambda v: v.tensor_scalar(out=t1[:], in0=ld_[:], scalar1=-1.0,
         scalar2=None, op0=ALU.mult), reads=["B_ld"], writes=["B_t1"])
    p.op("dve", lambda v: v.tensor_tensor(out=t1[:], in0=t1[:], in1=ld_[:],
         op=ALU.max), reads=["B_t1", "B_ld"], writes=["B_t1"])
    cn1, ckn1 = ctx.cf(-1.0)
    c1, ck1 = ctx.cf(1.0)
    p.op("act", lambda s: s.activation(out=t1[:], in_=t1[:], func=AF.Exp,
         scale=cn1[:, 0:1]), reads=["B_t1", ckn1], writes=["B_t1"])
    p.op("act", lambda s: s.activation(out=t1[:], in_=t1[:], func=AF.Ln,
         bias=c1[:, 0:1]), reads=["B_t1", ck1], writes=["B_t1"])
    p.op("dve", lambda v: v.tensor_scalar(out=adiag[:], in0=ld_[:], scalar1=0.0,
         scalar2=None, op0=ALU.max), reads=["B_ld"], writes=["B_adiag"])
    p.op("dve", lambda v: v.tensor_tensor(out=adiag[:], in0=adiag[:], in1=t1[:],
         op=ALU.add), reads=["B_adiag", "B_t1"], writes=["B_adiag"])
    p.op("act", lambda s: s.activation(out=adiag[:], in_=adiag[:], func=AF.Exp,
         scale=cn1[:, 0:1]), reads=["B_adiag", ckn1], writes=["B_adiag"])
    dot = ctx.aget("B_dot", [128, 16], F32)
    ab = ctx.aget("B_ab", [128, 256], F32)
    p.op("dve", lambda v: v.tensor_tensor(out=ab[:], in0=apc[:], in1=bpc[:],
         op=ALU.mult), reads=["B_apc", "B_bpc"], writes=["B_ab"])
    p.op("dve", lambda v: v.tensor_reduce(out=dot[:, 0:16],
         in_=ab[:].rearrange("p (e n) -> p e n", n=16), axis=AX.X, op=ALU.add),
         reads=["B_ab"], writes=["B_dot"])

    xbf = load_xbf(ctx, "B_x", d["x"])
    xk = [f"B_x{i}" for i in range(8)]
    a_row, ak, mu_row, mk, a2_row, a2k = ln_rows(ctx, xbf, xk, kind="ln",
        eps=1e-10, dscale=1.00001, name="B_ln", plain_eps=1e-5)
    mu_b = bcast_row_bf(ctx, "B_mub", mu_row, mk, psbank=5)
    a2_b = bcast_row_bf(ctx, "B_a2b", a2_row, a2k, psbank=6)
    xf = [ctx.aget(f"B_xf{j}", [128, T], F32) for j in (0, 1)]
    xh = [ctx.aget(f"B_xh{j}", [128, T], BF16) for j in (0, 1)]
    for od in range(8):
        xfj, xfk = xf[od % 2], f"B_xf{od % 2}"
        xhj, xhk = xh[od % 2], f"B_xh{od % 2}"
        p.dma("sp", xfj[:], d["x"][od * 128:(od + 1) * 128, :], writes=[xfk])
        p.op("dve", lambda v, od=od, xhj=xhj: v.tensor_tensor(out=xhj[:],
             in0=xbf[od][:], in1=mu_b[:], op=ALU.add), reads=[xk[od], "B_mub"],
             writes=[xhk])
        p.op("dve", lambda v, xhj=xhj: v.tensor_tensor(out=xhj[:], in0=xhj[:],
             in1=a2_b[:], op=ALU.mult), reads=[xhk, "B_a2b"], writes=[xhk])
        p.op("dve", lambda v, xfj=xfj, xhj=xhj: v.tensor_tensor(out=xfj[:],
             in0=xfj[:], in1=xhj[:], op=ALU.add), reads=[xfk, xhk], writes=[xfk])
        p.dma("sp", xsum[od * 128:(od + 1) * 128, :], xfj[:], reads=[xfk],
              writes=[("d_xsum", od)], stream="dma_st_xsum")
    ctx.afree("B_mub", "B_a2b", a2k, "B_xf0", "B_xf1", "B_xh0", "B_xh1")
    xln, _ = ln_materialize(ctx, xbf, xk, a_row, ak, mu_row, mk, gamma=lng,
                            gamma_key="B_lng", beta=lnb, beta_key="B_lnb",
                            name="B_xm", inplace=True, free_rows=True)

    xp = [ctx.aget(f"B_xp{j}", [128, T], BF16) for j in (0, 1)]
    for od in range(32):
        wt = wload(ctx, f"B_win{od % 2}", d["winT"], 1024, od * 128, 128)
        wk = f"B_win{od % 2}"
        dst, dk = xp[od % 2], f"B_xp{od % 2}"
        for tc in range(NTC):
            bank = tc % 2
            ps, pk = ctx.ps[bank], f"psb{bank}"
            mac(p, ps, pk, [(wslice(wt, i, 128), wk,
                             xln[i][:, tc * 512:(tc + 1) * 512], xk[i]) for i in range(8)])
            p.op("act", lambda s, tc=tc, ps=ps, dst=dst, od=od: s.activation(
                 out=dst[:, tc * 512:(tc + 1) * 512], in_=ps[:, 0:512],
                 func=AF.Identity, bias=bin_[:, od:od + 1]),
                 reads=[pk, "B_bin"], writes=[dk])
        if od < 16:
            p.dma("pool", xpsp[od * 128:(od + 1) * 128, :], dst[:], reads=[dk],
                  writes=[("d_xpsp", od)], stream="dma_st_xpsp")
        else:
            p.dma("pool", gsp[(od - 16) * 128:(od - 15) * 128, :], dst[:], reads=[dk],
                  writes=[("d_gsp", od - 16)], stream="dma_st_gsp")
    ctx.afree("B_xp0", "B_xp1", "B_win0", "B_win1", *[xk[i] for i in range(8)])

    xc = [ctx.aget(f"B_xc{e}", [128, T], BF16) for e in range(NE)]
    xpr = [ctx.aget(f"B_xr2{j}", [128, T], BF16) for j in (0, 1)]
    cv = [ctx.aget(f"B_cv{j}", [128, T], BF16) for j in (0, 1)]
    for e in range(NE):
        dst, dk = xpr[e % 2], f"B_xr2{e % 2}"
        p.dma("pool", dst[:], xpsp[e * 128:(e + 1) * 128, :], reads=[("d_xpsp", e)],
              writes=[dk])
        cvj, cvk = cv[e % 2], f"B_cv{e % 2}"
        p.op("dve", lambda v, dst=dst, cvj=cvj, e=e, dk=dk, cvk=cvk: v.tensor_scalar(
             out=cvj[:], in0=dst[:], scalar1=cw[:, 48 + e:49 + e], scalar2=None,
             op0=ALU.mult), reads=[dk, "B_cw"], writes=[cvk])
        for k in (1, 2, 3):
            tap = 3 - k
            p.op("dve", lambda v, dst=dst, cvj=cvj, e=e, k=k, tap=tap, dk=dk, cvk=cvk:
                 v.scalar_tensor_tensor(out=cvj[:, k:T], in0=dst[:, 0:T - k],
                 scalar=cw[:, tap * 16 + e:tap * 16 + e + 1], in1=cvj[:, k:T],
                 op0=ALU.mult, op1=ALU.add), reads=[dk, cvk, "B_cw"], writes=[cvk])
        p.op("act", lambda s, cvj=cvj, e=e, cvk=cvk: s.activation(out=xc[e][:],
             in_=cvj[:], func=AF.Silu, bias=cb[:, e:e + 1]), reads=[cvk, "B_cb"],
             writes=[f"B_xc{e}"])
    ctx.afree("B_xr20", "B_xr21", "B_cv0", "B_cv1")

    uc = [ctx.aget(f"B_uc{j}", [128, T], BF16) for j in (0, 1)]
    ud = [ctx.aget(f"B_ud{j}", [128, T], BF16) for j in (0, 1)]
    gr = [ctx.aget(f"B_gr{j}", [128, T], BF16) for j in (0, 1)]
    xck = [f"B_xc{e}" for e in range(NE)]
    for od in range(NE):
        wc = wload(ctx, f"B_wc{od % 2}", d["wcT"], E, od * 128, 128)
        wd = wload(ctx, f"B_wd{od % 2}", d["wdT"], E, od * 128, 128)
        kc, kd = f"B_wc{od % 2}", f"B_wd{od % 2}"
        ucj, uck = uc[od % 2], f"B_uc{od % 2}"
        udj, udk = ud[od % 2], f"B_ud{od % 2}"
        for tc in range(NTC):
            sl = slice(tc * 512, (tc + 1) * 512)
            b1, b2 = (tc % 2) * 2, (tc % 2) * 2 + 1
            ps1, pk1 = ctx.ps[b1], f"psb{b1}"
            ps2, pk2 = ctx.ps[b2], f"psb{b2}"
            mac(p, ps1, pk1, [(wslice(wc, i, 128), kc, xc[i][:, sl], xck[i])
                              for i in range(NE)])
            mac(p, ps2, pk2, [(wslice(wd, i, 128), kd, xc[i][:, sl], xck[i])
                              for i in range(NE)])
            p.op("act", lambda s, sl=sl, ps1=ps1, ucj=ucj, od=od: s.activation(
                 out=ucj[:, sl], in_=ps1[:, 0:512], func=AF.Identity,
                 bias=bc[:, od:od + 1]), reads=[pk1, "B_bc"], writes=[uck])
            p.op("act", lambda s, sl=sl, ps2=ps2, udj=udj, od=od: s.activation(
                 out=udj[:, sl], in_=ps2[:, 0:512], func=AF.Sigmoid,
                 bias=bd[:, od:od + 1]), reads=[pk2, "B_bd"], writes=[udk])
        p.op("dve", lambda v, ucj=ucj, udj=udj: v.tensor_tensor(out=ucj[:],
             in0=ucj[:], in1=udj[:], op=ALU.mult), reads=[uck, udk], writes=[uck])
        p.op("dve", lambda v, ucj=ucj, od=od: v.tensor_tensor_scan(out=ucj[:],
             data0=adiag[:, od:od + 1].broadcast_to((128, T)), data1=ucj[:],
             initial=0.0, op0=ALU.mult, op1=ALU.add),
             reads=["B_adiag", uck], writes=[uck])
        p.op("dve", lambda v, ucj=ucj, od=od: v.tensor_scalar(out=ucj[:], in0=ucj[:],
             scalar1=dot[:, od:od + 1], scalar2=None, op0=ALU.mult),
             reads=[uck, "B_dot"], writes=[uck])
        grj, grk = gr[od % 2], f"B_gr{od % 2}"
        p.dma("pool", grj[:], gsp[od * 128:(od + 1) * 128, :], reads=[("d_gsp", od)],
              writes=[grk])
        p.op("act", lambda s, grj=grj: s.activation(out=grj[:], in_=grj[:],
             func=AF.Sigmoid), reads=[grk], writes=[grk])
        p.op("dve", lambda v, ucj=ucj, grj=grj: v.tensor_tensor(out=ucj[:],
             in0=ucj[:], in1=grj[:], op=ALU.mult), reads=[uck, grk], writes=[uck])
        p.dma("pool", ysp[od * 128:(od + 1) * 128, :], ucj[:], reads=[uck],
              writes=[("d_ysp", od)], stream="dma_st_ysp")
    ctx.afree("B_uc0", "B_uc1", "B_ud0", "B_ud1", "B_gr0", "B_gr1",
              "B_wc0", "B_wc1", "B_wd0", "B_wd1", *xck)

    wo = []
    for od in range(8):
        wo.append(wload(ctx, f"B_wo{od}", d["woT"], E, od * 128, 128))
    yre = [ctx.aget(f"B_yr{j}", [128, NE * 512], BF16) for j in (0, 1)]
    xr = [ctx.aget(f"B_xr{j}", [128, 512], F32) for j in (0, 1)]
    out = [ctx.aget(f"B_ot{j}", [128, 512], F32) for j in (0, 1)]
    for t8 in range(NTC):
        yr, yrk = yre[t8 % 2], f"B_yr{t8 % 2}"
        p.dma("pool", yr[:].rearrange("p (i c) -> p i c", c=512),
              ysp[:, t8 * 512:(t8 + 1) * 512].rearrange("(i p) c -> p i c", p=128),
              reads=[("d_ysp", i2) for i2 in range(NE)], writes=[yrk])
        for od in range(8):
            xrj, xrk = xr[od % 2], f"B_xr{od % 2}"
            oj, ok = out[od % 2], f"B_ot{od % 2}"
            p.dma("sp", xrj[:], xsum[od * 128:(od + 1) * 128, t8 * 512:(t8 + 1) * 512],
                  reads=[("d_xsum", od)], writes=[xrk])
            ps, pk = ctx.ps[2 + od % 4], f"psb{2 + od % 4}"
            mac(p, ps, pk, [(wo[od][:, i * 128:(i + 1) * 128], f"B_wo{od}",
                             yr[:, i * 512:(i + 1) * 512], yrk) for i in range(NE)])
            p.op("act", lambda s, oj=oj, ps=ps, od=od: s.activation(out=oj[:],
                 in_=ps[:, 0:512], func=AF.Identity, bias=bout[:, od:od + 1]),
                 reads=[pk, "B_bout"], writes=[ok])
            p.op("dve", lambda v, oj=oj, xrj=xrj: v.tensor_tensor(out=oj[:],
                 in0=oj[:], in1=xrj[:], op=ALU.add), reads=[ok, xrk], writes=[ok])
            p.dma("sp", d["xout"][od * 128:(od + 1) * 128, t8 * 512:(t8 + 1) * 512],
                  oj[:], reads=[ok], stream="dma_st_xout")
    ctx.afree("B_yr0", "B_yr1", "B_xr0", "B_xr1", "B_ot0", "B_ot1",
              *[f"B_wo{od}" for od in range(8)])

"""L1_ffn block emitter: dense 8-expert top-2 MoE + shared FFN (F=4096).

x_in = LN_outer(x); logits = x_in @ Wg; probs = softmax; top2 -> cw (T,8)
out = sum_e cw_e*(silu(x_in@W1[e]+b1[e])@W2[e]+b2[e]) + 0.1*(shared)
x_next = x + out       (no inner residual)

Dense v1: all 8 experts computed for all tokens; shared = expert 8 with
cw==0.1. Expert h spilled to HBM; expert outputs DMA-accumulated into an
HBM f32 accumulator (probed exact on HW).
"""



FF = 4096
NF = FF // 128


def emit_L1F(ctx, d):
    p = ctx.p
    nc = ctx.nc
    hsp = nc.dram_tensor("C_hsp", [FF, T], BF16, kind="Internal").ap()
    xlsp = nc.dram_tensor("C_xlsp", [1024, T], BF16, kind="Internal").ap()
    accsp = nc.dram_tensor("C_acc", [1024, T], F32, kind="Internal").ap()
    cwtsp = nc.dram_tensor("C_cwt", [8, T], BF16, kind="Internal").ap()

    b1pc = load_pcol(ctx, "C_b1", d["b1_pc"], 9 * 32)
    b2sb = ctx.aget("C_b2", [1, 9 * 1024], BF16)
    p.dma("pool", b2sb[:], d["b2_rows"], writes=["C_b2"])
    wg = ctx.aget("C_wg", [128, 64], BF16)
    p.dma("pool", wg[:].rearrange("p (i c) -> p i c", c=8),
          d["wg"].rearrange("(i p) c -> p i c", p=128), writes=["C_wg"])
    ident = ctx.aget("C_id", [128, 128], BF16)
    p.dma("pool", ident[:], d["ident"], writes=["C_id"])

    xbf = load_xbf(ctx, "C_x", d["x"])
    xk = [f"C_x{i}" for i in range(8)]
    a_row, ak, mu_row, mk = ln_rows(ctx, xbf, xk, kind="ln", eps=1e-5, name="C_ln")
    xln, _ = ln_materialize(ctx, xbf, xk, a_row, ak, mu_row, mk, name="C_xm",
                            inplace=True)

    import os
    STAGE = os.environ.get("L1F_STAGE", "")
    cwT = ctx.aget("C_cwT", [8, T], BF16)
    lg = ctx.aget("C_lg", [128, 8], F32)
    r1 = ctx.aget("C_r1", [128, 1], F32)
    r2 = ctx.aget("C_r2", [128, 1], F32)
    msk = ctx.aget("C_msk", [128, 8], F32)
    cwt = ctx.aget("C_cw", [128, 8], BF16)
    pst = ctx.pstT[0:128, 0:128]
    if STAGE == "2":
        p.op("dve", lambda v: v.memset(cwT[:], 0.2), writes=["C_cwT"])
    for tt in range(T // 128) if STAGE != "2" else []:
        sl = slice(tt * 128, (tt + 1) * 128)
        ps, pk = ctx.ps[6], "psb6"
        for i in range(8):
            p.op("pe", lambda t_, i=i, sl=sl, ps=ps: t_.matmul(out=ps[:, 0:8],
                 lhsT=xln[i][:, sl], rhs=wg[:, i * 8:(i + 1) * 8],
                 start=(i == 0), stop=(i == 7)), reads=[xk[i], "C_wg"],
                 writes=[pk], inc=(i == 7))
        p.op("dve", lambda v, ps=ps: v.tensor_reduce(out=r1[:, 0:1], in_=ps[:, 0:8],
             axis=AX.X, op=ALU.max), reads=[pk], writes=["C_r1"])
        p.op("dve", lambda v: v.tensor_scalar(out=r1[:], in0=r1[:], scalar1=-1.0,
             scalar2=None, op0=ALU.mult), reads=["C_r1"], writes=["C_r1"])
        p.op("act", lambda s, ps=ps: s.activation(out=lg[:], in_=ps[:, 0:8],
             func=AF.Exp, bias=r1[:, 0:1]), reads=[pk, "C_r1"], writes=["C_lg"])
        p.op("dve", lambda v: v.tensor_reduce(out=r2[:, 0:1], in_=lg[:], axis=AX.X,
             op=ALU.add), reads=["C_lg"], writes=["C_r2"])
        p.op("dve", lambda v: v.reciprocal(out=r2[:], in_=r2[:]), reads=["C_r2"],
             writes=["C_r2"])
        p.op("dve", lambda v: v.tensor_scalar(out=lg[:], in0=lg[:],
             scalar1=r2[:, 0:1], scalar2=None, op0=ALU.mult), reads=["C_lg", "C_r2"],
             writes=["C_lg"])
        p.op("dve", lambda v: v.tensor_reduce(out=r1[:, 0:1], in_=lg[:], axis=AX.X,
             op=ALU.max), reads=["C_lg"], writes=["C_r1"])
        p.op("dve", lambda v: v.tensor_scalar(out=msk[:], in0=lg[:],
             scalar1=r1[:, 0:1], scalar2=None, op0=ALU.is_ge), reads=["C_lg", "C_r1"],
             writes=["C_msk"])
        p.op("dve", lambda v: v.scalar_tensor_tensor(out=msk[:], in0=msk[:],
             scalar=-1e9, in1=lg[:], op0=ALU.mult, op1=ALU.add),
             reads=["C_msk", "C_lg"], writes=["C_msk"])
        p.op("dve", lambda v: v.tensor_reduce(out=r2[:, 0:1], in_=msk[:], axis=AX.X,
             op=ALU.max), reads=["C_msk"], writes=["C_r2"])
        p.op("dve", lambda v: v.tensor_scalar(out=msk[:], in0=lg[:],
             scalar1=r2[:, 0:1], scalar2=None, op0=ALU.is_ge), reads=["C_lg", "C_r2"],
             writes=["C_msk"])
        p.op("dve", lambda v: v.tensor_tensor(out=r2[:], in0=r2[:], in1=r1[:],
             op=ALU.add), reads=["C_r2", "C_r1"], writes=["C_r2"])
        p.op("dve", lambda v: v.tensor_scalar(out=r2[:], in0=r2[:], scalar1=1e-9,
             scalar2=None, op0=ALU.add), reads=["C_r2"], writes=["C_r2"])
        p.op("dve", lambda v: v.reciprocal(out=r2[:], in_=r2[:]), reads=["C_r2"],
             writes=["C_r2"])
        p.op("dve", lambda v: v.tensor_tensor(out=cwt[:], in0=lg[:], in1=msk[:],
             op=ALU.mult), reads=["C_lg", "C_msk"], writes=["C_cw"])
        p.op("dve", lambda v: v.tensor_scalar(out=cwt[:], in0=cwt[:],
             scalar1=r2[:, 0:1], scalar2=None, op0=ALU.mult), reads=["C_cw", "C_r2"],
             writes=["C_cw"])
        p.op("pe", lambda t_: t_.transpose(out=pst[0:8, 0:128], in_=cwt[:, 0:8],
             identity=ident[:]), reads=["C_cw", "C_id"], writes=["pstT"])
        p.op("act", lambda s, sl=sl: s.copy(out=cwT[0:8, sl], in_=pst[0:8, 0:128]),
             reads=["pstT"], writes=["C_cwT"])
    p.dma("pool", cwtsp, cwT[:], reads=["C_cwT"], writes=[("d_cwt", 0)],
          stream="dma_st_cwt")
    ctx.afree("C_cwT", "C_lg", "C_r1", "C_r2", "C_msk", "C_cw")

    for i in range(8):
        p.dma("pool", xlsp[i * 128:(i + 1) * 128, :], xln[i][:], reads=[xk[i]],
              writes=[("d_xl", i)], stream="dma_st_xl")
    ctx.afree(*xk)

    hrot = [ctx.aget(f"C_h{j}", [128, T], BF16) for j in (0, 1)]
    NE_LIM = int(os.environ.get("L1F_NE", "9"))
    for e in list(range(9))[:NE_LIM] if STAGE != "1" else []:
        w1r = ctx.aget("C_w1r", [128, 8 * FF], BF16)
        w1src = d["ws1T"] if e == 8 else d["w1T"][e]
        p.dma("pool", w1r[:].rearrange("p (i c) -> p i c", c=FF),
              w1src.rearrange("(i p) c -> p i c", p=128), writes=["C_w1r"])
        xl2 = []
        for i in range(8):
            tn = f"C_xl{i}"
            tt2 = ctx.aget(tn, [128, T], BF16)
            p.dma("pool", tt2[:], xlsp[i * 128:(i + 1) * 128, :], reads=[("d_xl", i)],
                  writes=[tn])
            xl2.append(tt2)
        xl2k = [f"C_xl{i}" for i in range(8)]
        for hod in range(NF):
            h, hk2 = hrot[hod % 2], f"C_h{hod % 2}"
            for tc4 in range(0, NTC, 4):
                for tc in range(tc4, tc4 + 4):
                    sl = slice(tc * 512, (tc + 1) * 512)
                    ps, pk = ctx.ps[tc % 4], f"psb{tc % 4}"
                    mac(p, ps, pk, [(w1r[:, i * FF + hod * 128:
                                         i * FF + (hod + 1) * 128],
                                     "C_w1r", xl2[i][:, sl], xl2k[i])
                                    for i in range(8)])
                for tc in range(tc4, tc4 + 4):
                    sl = slice(tc * 512, (tc + 1) * 512)
                    ps, pk = ctx.ps[tc % 4], f"psb{tc % 4}"
                    p.op("act", lambda s, h=h, sl=sl, ps=ps, e=e, hod=hod, hk2=hk2:
                         s.activation(out=h[:, sl], in_=ps[:, 0:512], func=AF.Silu,
                         bias=b1pc[:, e * 32 + hod:e * 32 + hod + 1]),
                         reads=[pk, "C_b1"], writes=[hk2])
            p.dma("pool", hsp[hod * 128:(hod + 1) * 128, :], h[:], reads=[hk2],
                  writes=[("d_hsp", hod)], stream="dma_st_hsp")
        ctx.afree("C_w1r", *xl2k)

        ctx.afree("C_h0", "C_h1")
        w2src = d["ws2T"] if e == 8 else d["w2T"][e]
        cwr = ctx.aget("C_cwr", [1, T], BF16)
        if e == 8:
            p.op("dve", lambda v, cwr=cwr: v.memset(cwr[:], 0.1), writes=["C_cwr"])
        else:
            p.dma("pool", cwr[:], cwtsp[e:e + 1, :], reads=[("d_cwt", 0)],
                  writes=["C_cwr"])
        for og in range(2):
            w2r = ctx.aget("C_w2r", [128, NF * 512], BF16)
            p.dma("pool", w2r[:].rearrange("p (i c) -> p i c", c=512),
                  w2src[:, og * 512:(og + 1) * 512].rearrange("(i p) c -> p i c", p=128),
                  writes=["C_w2r"])
            hre = [ctx.aget(f"C_hr{j}", [128, NF * 512], BF16) for j in (0, 1)]
            cwb = [ctx.aget(f"C_cb{j}", [128, 512], BF16) for j in (0, 1)]
            yt = [ctx.aget(f"C_yt{od4}", [128, T], BF16) for od4 in range(4)]
            for tc in range(NTC):
                sl = slice(tc * 512, (tc + 1) * 512)
                hr, hrk = hre[tc % 2], f"C_hr{tc % 2}"
                p.dma("pool", hr[:].rearrange("p (i c) -> p i c", c=512),
                      hsp[:, sl].rearrange("(i p) c -> p i c", p=128),
                      reads=[("d_hsp", i2) for i2 in range(NF)], writes=[hrk])
                cb_, cbk = cwb[tc % 2], f"C_cb{tc % 2}"
                ps7, pk7 = ctx.ps[6], "psb6"
                p.op("pe", lambda t_, sl=sl, cwr=cwr, ps7=ps7: t_.matmul(
                     out=ps7[:, 0:512], lhsT=ctx.ones_row[:], rhs=cwr[0:1, sl],
                     start=True, stop=True), reads=["ones_row", "C_cwr"], writes=[pk7])
                p.op("act", lambda s, cb_=cb_, ps7=ps7, cbk=cbk: s.copy(out=cb_[:],
                     in_=ps7[:, 0:512]), reads=[pk7], writes=[cbk])
                for od4 in range(4):
                    od = og * 4 + od4
                    ps, pk = ctx.ps[2 + od4], f"psb{2 + od4}"
                    pairs = [(b2sb[0:1, e * 1024 + od * 128:e * 1024 + (od + 1) * 128],
                              "C_b2", ctx.ones512[:], "ones512")]
                    pairs += [(w2r[:, i * 512 + od4 * 128:i * 512 + (od4 + 1) * 128],
                               "C_w2r", hr[:, i * 512:(i + 1) * 512], hrk)
                              for i in range(NF)]
                    mac(p, ps, pk, pairs)
                for od4 in range(4):
                    ps, pk = ctx.ps[2 + od4], f"psb{2 + od4}"
                    p.op("dve", lambda v, od4=od4, sl=sl, ps=ps, cb_=cb_, cbk=cbk:
                         v.tensor_tensor(out=yt[od4][:, sl], in0=ps[:, 0:512],
                         in1=cb_[:], op=ALU.mult), reads=[pk, cbk],
                         writes=[f"C_yt{od4}"])
            for od4 in range(4):
                od = og * 4 + od4
                if e == 0:
                    p.dma("pool", accsp[od * 128:(od + 1) * 128, :], yt[od4][:],
                          reads=[f"C_yt{od4}"], writes=[("d_acc", od)],
                          stream="dma_st_acc")
                else:
                    for ch in range(8):
                        cs = slice(ch * 512, (ch + 1) * 512)
                        p.dma_accum("pool", accsp[od * 128:(od + 1) * 128, cs],
                                    yt[od4][:, cs], reads=[f"C_yt{od4}"],
                                    writes=[("d_acc", od)], stream="dma_st_acc")
            ctx.afree("C_w2r", "C_hr0", "C_hr1", "C_cb0", "C_cb1",
                      *[f"C_yt{od4}" for od4 in range(4)])
        ctx.afree("C_cwr")
        hrot = [ctx.aget(f"C_h{j}", [128, T], BF16) for j in (0, 1)]
    ctx.afree("C_h0", "C_h1")
    ctx.afree("C_h0", "C_h1")

    xf = [ctx.aget(f"C_xf{j}", [128, T], F32) for j in (0, 1)]
    ar = [ctx.aget(f"C_ar{j}", [128, T], F32) for j in (0, 1)]
    if STAGE == "1":
        for od in range(8):
            xfj, xfk = xf[od % 2], f"C_xf{od % 2}"
            p.dma("sp", xfj[:], d["x"][od * 128:(od + 1) * 128, :], writes=[xfk])
            p.dma("sp", d["xout"][od * 128:(od + 1) * 128, :], xfj[:], reads=[xfk],
                  stream="dma_st_xout")
        return
    for od in range(8):
        xfj, xfk = xf[od % 2], f"C_xf{od % 2}"
        arj, ark = ar[od % 2], f"C_ar{od % 2}"
        p.dma("sp", xfj[:], d["x"][od * 128:(od + 1) * 128, :], writes=[xfk])
        p.dma("sp", arj[:], accsp[od * 128:(od + 1) * 128, :], reads=[("d_acc", od)],
              writes=[ark])
        p.op("dve", lambda v, arj=arj, xfj=xfj, ark=ark, xfk=xfk: v.tensor_tensor(
             out=arj[:], in0=arj[:], in1=xfj[:], op=ALU.add), reads=[ark, xfk],
             writes=[ark])
        p.dma("sp", d["xout"][od * 128:(od + 1) * 128, :], arj[:], reads=[ark],
              stream="dma_st_xout")
    ctx.afree("C_xf0", "C_xf1", "C_ar0", "C_ar1")

"""L2_attn block emitter.

x_in = LN_outer(x)
h = relu(rmsnorm(x_in) @ w1.T)^2            [256]
qkv = h @ w2.T                              [6144] -> q,k,v each [16h x 128d]
q,k: per-head rmsnorm (eps=f32 eps), then RoPE where only dims
[0:32)u[64:96) of each head rotate. HOST PERMUTES the q/k/v feature order
to [rotA(16hx32) rotB idA idB] so rope groups are partition-contiguous.
y = relu(q) * cumsum(sigmoid(k) * v, t)     [2048]
uv = y @ cw1.T @ cw2.T -> u|g               [1024|1024]
x_next = x + x_in + u * silu(g)
"""



EPS_F32 = 1.1920929e-07


def emit_L2A(ctx, d):
    p = ctx.p
    nc = ctx.nc
    qkvsp = nc.dram_tensor("D_qkv", [6144, T], BF16, kind="Internal").ap()
    xsum = nc.dram_tensor("D_xsum", [1024, T], F32, kind="Internal").ap()
    ssp = nc.dram_tensor("D_ssp", [2048, T], BF16, kind="Internal").ap()
    ysp = nc.dram_tensor("D_ysp", [2048, T], BF16, kind="Internal").ap()
    qr_sp = nc.dram_tensor("D_qr", [16, T], BF16, kind="Internal").ap()
    kr_sp = nc.dram_tensor("D_kr", [16, T], BF16, kind="Internal").ap()

    seg4 = ctx.aget("D_seg4", [4, 128], BF16)
    p.dma("pool", seg4[:], d["seg4T"], writes=["D_seg4"])
    seg4c = ctx.aget("D_seg4c", [128, 4], BF16)
    p.dma("pool", seg4c[:], d["seg4"], writes=["D_seg4c"])
    cosb = ctx.aget("D_cos", [128, T], BF16)
    sinb = ctx.aget("D_sin", [128, T], BF16)
    p.dma("pool", cosb[:], d["cos_t"], writes=["D_cos"])
    p.dma("pool", sinb[:], d["sin_t"], writes=["D_sin"])

    xbf = load_xbf(ctx, "D_x", d["x"])
    xk = [f"D_x{i}" for i in range(8)]
    a_row, ak, mu_row, mk, a2_row, a2k = ln_rows(ctx, xbf, xk, kind="ln",
        eps=EPS_F32 * 1e-5, dscale=1.0 + EPS_F32, name="D_ln", plain_eps=1e-5)
    mu_b = bcast_row_bf(ctx, "D_mub", mu_row, mk, psbank=5)
    a2_b = bcast_row_bf(ctx, "D_a2b", a2_row, a2k, psbank=6)
    xf = [ctx.aget(f"D_xf{j}", [128, T], F32) for j in (0, 1)]
    xh = [ctx.aget(f"D_xh{j}", [128, T], BF16) for j in (0, 1)]
    for od in range(8):
        xfj, xfk = xf[od % 2], f"D_xf{od % 2}"
        xhj, xhk = xh[od % 2], f"D_xh{od % 2}"
        p.dma("sp", xfj[:], d["x"][od * 128:(od + 1) * 128, :], writes=[xfk])
        p.op("dve", lambda v, od=od, xhj=xhj, xhk=xhk: v.tensor_tensor(out=xhj[:],
             in0=xbf[od][:], in1=mu_b[:], op=ALU.add), reads=[xk[od], "D_mub"],
             writes=[xhk])
        p.op("dve", lambda v, xhj=xhj, xhk=xhk: v.tensor_tensor(out=xhj[:], in0=xhj[:],
             in1=a2_b[:], op=ALU.mult), reads=[xhk, "D_a2b"], writes=[xhk])
        p.op("dve", lambda v, xfj=xfj, xfk=xfk, xhj=xhj, xhk=xhk: v.tensor_tensor(
             out=xfj[:], in0=xfj[:], in1=xhj[:], op=ALU.add), reads=[xfk, xhk],
             writes=[xfk])
        p.dma("sp", xsum[od * 128:(od + 1) * 128, :], xfj[:], reads=[xfk],
              writes=[("d_xsum", od)], stream="dma_st_xsum")
    ctx.afree("D_mub", "D_a2b", a2k, "D_xf0", "D_xf1", "D_xh0", "D_xh1")
    xn, _ = ln_materialize(ctx, xbf, xk, a_row, ak, mu_row, mk, name="D_xn",
                           inplace=True, free_rows=True)

    ht = [ctx.aget(f"D_h{j}", [128, T], BF16) for j in (0, 1)]
    for od in range(2):
        wt = wload(ctx, f"D_w1{od}", d["w1T"], 1024, od * 128, 128)
        h, hk = ht[od], f"D_h{od}"
        for tc in range(NTC):
            sl = slice(tc * 512, (tc + 1) * 512)
            bank = tc % 2
            ps, pk = ctx.ps[bank], f"psb{bank}"
            mac(p, ps, pk, [(wslice(wt, i, 128), f"D_w1{od}", xn[i][:, sl], xk[i])
                            for i in range(8)])
            p.op("dve", lambda v, h=h, sl=sl, ps=ps, hk=hk: v.tensor_scalar(
                 out=h[:, sl], in0=ps[:, 0:512], scalar1=0.0, scalar2=None,
                 op0=ALU.max), reads=[pk], writes=[hk])
        p.op("act", lambda s, h=h, hk=hk: s.activation(out=h[:], in_=h[:],
             func=AF.Square), reads=[hk], writes=[hk])
    ctx.afree(*xk, "D_w10", "D_w11")

    qv = [ctx.aget(f"D_q{j}", [128, T], BF16) for j in (0, 1)]
    for od in range(48):
        wt = wload(ctx, f"D_w2{od % 2}", d["w2T"], 256, od * 128, 128)
        dst, dk = qv[od % 2], f"D_q{od % 2}"
        for tc in range(NTC):
            sl = slice(tc * 512, (tc + 1) * 512)
            bank = tc % 2
            ps, pk = ctx.ps[bank], f"psb{bank}"
            mac(p, ps, pk, [(wslice(wt, i, 128), f"D_w2{od % 2}", ht[i][:, sl],
                             f"D_h{i}") for i in range(2)])
            p.op("act", lambda s, dst=dst, sl=sl, ps=ps, dk=dk: s.copy(
                 out=dst[:, sl], in_=ps[:, 0:512]), reads=[pk], writes=[dk])
        p.dma("pool", qkvsp[od * 128:(od + 1) * 128, :], dst[:], reads=[dk],
              writes=[("d_qkv", od)], stream="dma_st_qkv")
    ctx.afree("D_h0", "D_h1", "D_q0", "D_q1", "D_w20", "D_w21")

    sq = [ctx.aget(f"D_sq{j}", [128, 512], BF16) for j in (0, 1)]
    rr = ctx.aget("D_rr", [4, 512], F32)
    ce, cke = ctx.cf(float(EPS_F32))
    ch, ckh = ctx.cf(-0.5)
    c128, ck128 = ctx.cf(1.0 / 128.0)
    for qk, rsp in (("q", qr_sp), ("k", kr_sp)):
        base = 0 if qk == "q" else 16
        rows = ctx.aget("D_rows", [4, T], BF16)
        tl = [ctx.aget(f"D_tl{j}", [128, T], BF16) for j in range(4)]
        for pt in range(4):
            for g in range(4):
                od = base + g * 4 + pt
                p.dma("pool", tl[g][:], qkvsp[od * 128:(od + 1) * 128, :],
                      reads=[("d_qkv", od)], writes=[f"D_tl{g}"])
            for tc in range(NTC):
                sl = slice(tc * 512, (tc + 1) * 512)
                ps, pk = ctx.ps[4], "psb4"
                for g in range(4):
                    sqb, sqk = sq[g % 2], f"D_sq{g % 2}"
                    p.op("act", lambda s, g=g, sl=sl, sqb=sqb:
                         s.activation(out=sqb[:], in_=tl[g][:, sl], func=AF.Square),
                         reads=[f"D_tl{g}"], writes=[sqk])
                    p.op("pe", lambda t_, sqb=sqb, g=g, ps=ps, sqk=sqk: t_.matmul(
                         out=ps[0:4, 0:512], lhsT=seg4c[:], rhs=sqb[:],
                         start=(g == 0), stop=(g == 3)),
                         reads=[sqk, "D_seg4c"], writes=[pk], inc=(g == 3))
                p.op("dve", lambda v, ps=ps: v.tensor_scalar(out=rr[:], in0=ps[0:4, 0:512],
                     scalar1=1.0 / 128, scalar2=None, op0=ALU.mult), reads=[pk],
                     writes=["D_rr"])
                p.op("act", lambda s: s.activation(out=rr[:], in_=rr[:], func=AF.Ln,
                     bias=ce[0:4, 0:1]), reads=["D_rr", cke], writes=["D_rr"])
                p.op("act", lambda s, sl=sl: s.activation(out=rows[0:4, sl], in_=rr[:],
                     func=AF.Exp, scale=ch[0:4, 0:1]), reads=["D_rr", ckh],
                     writes=["D_rows"])
            p.dma("pool", rsp[pt * 4:(pt + 1) * 4, :], rows[0:4, :], reads=["D_rows"],
                  writes=[("d_r" + qk, pt)], stream="dma_st_r" + qk)
        ctx.afree("D_rows", "D_tl0", "D_tl1", "D_tl2", "D_tl3")
    ctx.afree("D_sq0", "D_sq1", "D_rr")

    one_sc = ctx.aget("D_one", [128, 1], F32)
    p.op("dve", lambda v: v.memset(one_sc[:], 1.0), writes=["D_one"])

    def process(qk, base, rsp, rkey, consume):
        rows = ctx.aget("D_rw2", [4, T], BF16)
        ta = ctx.aget("D_ta", [128, T], BF16)
        tb = ctx.aget("D_tb", [128, T], BF16)
        tn1 = ctx.aget("D_tn1", [128, T], BF16)
        tn2 = ctx.aget("D_tn2", [128, T], BF16)
        rb = ctx.aget("D_rb", [128, T], BF16)
        for pt in range(4):
            p.dma("pool", rows[0:4, :], rsp[pt * 4:(pt + 1) * 4, :],
                  reads=[(rkey, pt)], writes=["D_rw2"])
            for tc in range(NTC):
                sl = slice(tc * 512, (tc + 1) * 512)
                ps, pk = ctx.ps[4], "psb4"
                p.op("pe", lambda t_, sl=sl, ps=ps: t_.matmul(out=ps[:, 0:512],
                     lhsT=seg4[:], rhs=rows[0:4, sl], start=True, stop=True),
                     reads=["D_seg4", "D_rw2"], writes=[pk])
                p.op("act", lambda s, sl=sl, ps=ps: s.copy(out=rb[:, sl],
                     in_=ps[:, 0:512]), reads=[pk], writes=["D_rb"])
            odA, odB = base + pt, base + 4 + pt
            p.dma("pool", ta[:], qkvsp[odA * 128:(odA + 1) * 128, :],
                  reads=[("d_qkv", odA)], writes=["D_ta"])
            p.dma("pool", tb[:], qkvsp[odB * 128:(odB + 1) * 128, :],
                  reads=[("d_qkv", odB)], writes=["D_tb"])
            p.op("dve", lambda v: v.tensor_tensor(out=ta[:], in0=ta[:], in1=rb[:],
                 op=ALU.mult), reads=["D_ta", "D_rb"], writes=["D_ta"])
            p.op("dve", lambda v: v.tensor_tensor(out=tb[:], in0=tb[:], in1=rb[:],
                 op=ALU.mult), reads=["D_tb", "D_rb"], writes=["D_tb"])
            p.op("dve", lambda v: v.tensor_tensor(out=tn1[:], in0=ta[:], in1=cosb[:],
                 op=ALU.mult), reads=["D_ta", "D_cos"], writes=["D_tn1"])
            p.op("dve", lambda v: v.tensor_tensor(out=tn2[:], in0=tb[:], in1=sinb[:],
                 op=ALU.mult), reads=["D_tb", "D_sin"], writes=["D_tn2"])
            p.op("dve", lambda v: v.tensor_tensor(out=tn1[:], in0=tn1[:], in1=tn2[:],
                 op=ALU.add), reads=["D_tn1", "D_tn2"], writes=["D_tn1"])
            p.op("dve", lambda v: v.tensor_tensor(out=tn2[:], in0=tb[:], in1=cosb[:],
                 op=ALU.mult), reads=["D_tb", "D_cos"], writes=["D_tn2"])
            p.op("dve", lambda v: v.tensor_tensor(out=ta[:], in0=ta[:], in1=sinb[:],
                 op=ALU.mult), reads=["D_ta", "D_sin"], writes=["D_ta"])
            p.op("dve", lambda v: v.tensor_tensor(out=tn2[:], in0=tn2[:], in1=ta[:],
                 op=ALU.subtract), reads=["D_tn2", "D_ta"], writes=["D_tn2"])
            consume(pt, tn1, "D_tn1")
            consume(4 + pt, tn2, "D_tn2")
            for gi, odI in ((8, base + 8 + pt), (12, base + 12 + pt)):
                p.dma("pool", ta[:], qkvsp[odI * 128:(odI + 1) * 128, :],
                      reads=[("d_qkv", odI)], writes=["D_ta"])
                p.op("dve", lambda v: v.tensor_tensor(out=ta[:], in0=ta[:], in1=rb[:],
                     op=ALU.mult), reads=["D_ta", "D_rb"], writes=["D_ta"])
                consume(gi + pt, ta, "D_ta")
        ctx.afree("D_rw2", "D_ta", "D_tb", "D_tn1", "D_tn2", "D_rb")

    vl = ctx.aget("D_vl", [128, T], BF16)
    sc = ctx.aget("D_sc", [128, T], BF16)

    def consume_k(odp, tile, tkey):
        vod = 32 + odp
        p.dma("pool", vl[:], qkvsp[vod * 128:(vod + 1) * 128, :],
              reads=[("d_qkv", vod)], writes=["D_vl"])
        p.op("act", lambda s, tile=tile: s.activation(out=sc[:], in_=tile[:],
             func=AF.Sigmoid), reads=[tkey], writes=["D_sc"])
        p.op("dve", lambda v: v.tensor_tensor(out=sc[:], in0=sc[:], in1=vl[:],
             op=ALU.mult), reads=["D_sc", "D_vl"], writes=["D_sc"])
        p.op("dve", lambda v: v.tensor_tensor_scan(out=sc[:],
             data0=one_sc[:, 0:1].broadcast_to((128, T)), data1=sc[:], initial=0.0,
             op0=ALU.mult, op1=ALU.add), reads=["D_one", "D_sc"], writes=["D_sc"])
        p.dma("pool", ssp[odp * 128:(odp + 1) * 128, :], sc[:], reads=["D_sc"],
              writes=[("d_ssp", odp)], stream="dma_st_ssp")

    process("k", 16, kr_sp, "d_rk", consume_k)
    ctx.afree("D_vl", "D_sc")

    yt = [ctx.aget(f"D_y{j}", [128, T], BF16) for j in (0, 1)]
    sl_ = ctx.aget("D_sl", [128, T], BF16)

    def consume_q(odp, tile, tkey):
        yj, yk = yt[odp % 2], f"D_y{odp % 2}"
        p.dma("pool", sl_[:], ssp[odp * 128:(odp + 1) * 128, :],
              reads=[("d_ssp", odp)], writes=["D_sl"])
        p.op("dve", lambda v, yj=yj, tile=tile, yk=yk: v.tensor_scalar(out=yj[:],
             in0=tile[:], scalar1=0.0, scalar2=None, op0=ALU.max),
             reads=[tkey], writes=[yk])
        p.op("dve", lambda v, yj=yj, yk=yk: v.tensor_tensor(out=yj[:], in0=yj[:],
             in1=sl_[:], op=ALU.mult), reads=[yk, "D_sl"], writes=[yk])
        p.dma("pool", ysp[odp * 128:(odp + 1) * 128, :], yj[:], reads=[yk],
              writes=[("d_ysp", odp)], stream="dma_st_ysp")

    process("q", 0, qr_sp, "d_rq", consume_q)
    ctx.afree("D_sl", "D_y0", "D_y1")

    mid = [ctx.aget(f"D_m{j}", [128, T], BF16) for j in (0, 1)]
    yre = [ctx.aget(f"D_yr{j}", [128, 16 * 512], BF16) for j in (0, 1)]
    w10 = wload(ctx, "D_cw1a", d["cw1T"], 2048, 0, 128)
    w11 = wload(ctx, "D_cw1b", d["cw1T"], 2048, 128, 128)
    for tc in range(NTC):
        sl = slice(tc * 512, (tc + 1) * 512)
        yr, yrk = yre[tc % 2], f"D_yr{tc % 2}"
        p.dma("pool", yr[:].rearrange("p (i c) -> p i c", c=512),
              ysp[:, sl].rearrange("(i p) c -> p i c", p=128),
              reads=[("d_ysp", i2) for i2 in range(16)], writes=[yrk])
        for od, wt, wtk in ((0, w10, "D_cw1a"), (1, w11, "D_cw1b")):
            bank = 2 + od
            ps, pk = ctx.ps[bank], f"psb{bank}"
            mac(p, ps, pk, [(wslice(wt, i, 128), wtk, yr[:, i * 512:(i + 1) * 512],
                             yrk) for i in range(16)])
            p.op("act", lambda s, od=od, sl=sl, ps=ps: s.copy(out=mid[od][:, sl],
                 in_=ps[:, 0:512]), reads=[pk], writes=[f"D_m{od}"])
    ctx.afree("D_yr0", "D_yr1", "D_cw1a", "D_cw1b")

    gsil = [ctx.aget(f"D_g{od}", [128, T], BF16) for od in range(8)]
    for od in range(8):
        wt = wload(ctx, f"D_cw2{od % 2}", d["cw2T"], 256, 1024 + od * 128, 128)
        g, gk = gsil[od], f"D_g{od}"
        for tc in range(NTC):
            sl = slice(tc * 512, (tc + 1) * 512)
            bank = tc % 2
            ps, pk = ctx.ps[bank], f"psb{bank}"
            mac(p, ps, pk, [(wslice(wt, i, 128), f"D_cw2{od % 2}", mid[i][:, sl],
                             f"D_m{i}") for i in range(2)])
            p.op("act", lambda s, g=g, sl=sl, ps=ps, gk=gk: s.activation(out=g[:, sl],
                 in_=ps[:, 0:512], func=AF.Silu), reads=[pk], writes=[gk])
    xr = [ctx.aget(f"D_xr{j}", [128, T], F32) for j in (0, 1)]
    ot = [ctx.aget(f"D_ot{j}", [128, T], F32) for j in (0, 1)]
    for od in range(8):
        wt = wload(ctx, f"D_cw2{od % 2}", d["cw2T"], 256, od * 128, 128)
        xrj, xrk = xr[od % 2], f"D_xr{od % 2}"
        oj, ok = ot[od % 2], f"D_ot{od % 2}"
        p.dma("sp", xrj[:], xsum[od * 128:(od + 1) * 128, :],
              reads=[("d_xsum", od)], writes=[xrk])
        for tc in range(NTC):
            sl = slice(tc * 512, (tc + 1) * 512)
            bank = tc % 2
            ps, pk = ctx.ps[bank], f"psb{bank}"
            mac(p, ps, pk, [(wslice(wt, i, 128), f"D_cw2{od % 2}", mid[i][:, sl],
                             f"D_m{i}") for i in range(2)])
            p.op("dve", lambda v, oj=oj, sl=sl, ps=ps, od=od, ok=ok: v.tensor_tensor(
                 out=oj[:, sl], in0=ps[:, 0:512], in1=gsil[od][:, sl], op=ALU.mult),
                 reads=[pk, f"D_g{od}"], writes=[ok])
            p.op("dve", lambda v, oj=oj, sl=sl, xrj=xrj, ok=ok, xrk=xrk:
                 v.tensor_tensor(out=oj[:, sl], in0=oj[:, sl], in1=xrj[:, sl],
                 op=ALU.add), reads=[ok, xrk], writes=[ok])
        p.dma("sp", d["xout"][od * 128:(od + 1) * 128, :], oj[:], reads=[ok],
              stream="dma_st_xout")
    ctx.afree("D_m0", "D_m1", "D_cw20", "D_cw21", "D_xr0", "D_xr1", "D_ot0",
              "D_ot1", *[f"D_g{od}" for od in range(8)])

"""L2_ffn block emitter: dense 16-expert top-2 MoE (DFF=512) + fused shared
(FUSED_DFF=1024, weight 1).

x_in = LN_outer(x); logits = x_in @ gate_w.T; scores = sigmoid(logits);
top2 by (logits + expert_bias); cw = scores@top2 / (sum + 1e-10)
per e: h = silu(x@w1[e].T+b1[e]) * (x@w3[e].T+b3[e]); y += cw_e*(h@w2[e].T+b2[e])
shared: same with sw1/sw3/sw2, cw=1
x_next = x + y
"""




def emit_L2F(ctx, d):
    p = ctx.p
    nc = ctx.nc
    accsp = nc.dram_tensor("E_acc", [1024, T], F32, kind="Internal").ap()
    cwtsp = nc.dram_tensor("E_cwt", [16, T], BF16, kind="Internal").ap()

    b1pc = load_pcol(ctx, "E_b1", d["b1_pc"], 16 * 4 + 8)
    b3pc = load_pcol(ctx, "E_b3", d["b3_pc"], 16 * 4 + 8)
    b2sb = ctx.aget("E_b2", [1, 18 * 1024], BF16)
    p.dma("pool", b2sb[:], d["b2_rows"], writes=["E_b2"])
    wg = ctx.aget("E_wg", [128, 8 * 16], BF16)
    p.dma("pool", wg[:].rearrange("p (i c) -> p i c", c=16),
          d["gwT"].rearrange("(i p) c -> p i c", p=128), writes=["E_wg"])
    ident = ctx.aget("E_id", [128, 128], BF16)
    p.dma("pool", ident[:], d["ident"], writes=["E_id"])
    ebias = ctx.aget("E_eb", [128, 16], F32)
    p.dma("sp", ebias[:], d["ebias_b"], writes=["E_eb"])

    xbf = load_xbf(ctx, "E_x", d["x"])
    xk = [f"E_x{i}" for i in range(8)]
    a_row, ak, mu_row, mk = ln_rows(ctx, xbf, xk, kind="ln", eps=1e-5, name="E_ln")
    xln, _ = ln_materialize(ctx, xbf, xk, a_row, ak, mu_row, mk, name="E_xm",
                            inplace=True)

    cwT = ctx.aget("E_cwT", [16, T], BF16)
    lg = ctx.aget("E_lg", [128, 16], F32)
    sc = ctx.aget("E_sc", [128, 16], F32)
    m1 = ctx.aget("E_m1", [128, 1], F32)
    m2 = ctx.aget("E_m2", [128, 1], F32)
    s12 = ctx.aget("E_s12", [128, 1], F32)
    msk = ctx.aget("E_msk", [128, 16], F32)
    sel = ctx.aget("E_sel", [128, 16], F32)
    cwt = ctx.aget("E_cw", [128, 16], BF16)
    pst = ctx.pstT[0:128, 0:128]
    for tt in range(T // 128):
        sl = slice(tt * 128, (tt + 1) * 128)
        ps, pk = ctx.ps[6], "psb6"
        for i in range(8):
            p.op("pe", lambda t_, i=i, sl=sl, ps=ps: t_.matmul(out=ps[:, 0:16],
                 lhsT=xln[i][:, sl], rhs=wg[:, i * 16:(i + 1) * 16],
                 start=(i == 0), stop=(i == 7)), reads=[xk[i], "E_wg"],
                 writes=[pk], inc=(i == 7))
        p.op("act", lambda s, ps=ps: s.activation(out=sc[:], in_=ps[:, 0:16],
             func=AF.Sigmoid), reads=[pk], writes=["E_sc"])
        p.op("dve", lambda v, ps=ps: v.tensor_tensor(out=lg[:], in0=ps[:, 0:16],
             in1=ebias[:, 0:16], op=ALU.add), reads=[pk, "E_eb"], writes=["E_lg"])
        p.op("dve", lambda v: v.tensor_reduce(out=m1[:, 0:1], in_=lg[:], axis=AX.X,
             op=ALU.max), reads=["E_lg"], writes=["E_m1"])
        p.op("dve", lambda v: v.tensor_scalar(out=msk[:], in0=lg[:],
             scalar1=m1[:, 0:1], scalar2=None, op0=ALU.is_ge), reads=["E_lg", "E_m1"],
             writes=["E_msk"])
        p.op("dve", lambda v: v.scalar_tensor_tensor(out=sel[:], in0=msk[:],
             scalar=-1e9, in1=lg[:], op0=ALU.mult, op1=ALU.add),
             reads=["E_msk", "E_lg"], writes=["E_sel"])
        p.op("dve", lambda v: v.tensor_reduce(out=m2[:, 0:1], in_=sel[:], axis=AX.X,
             op=ALU.max), reads=["E_sel"], writes=["E_m2"])
        p.op("dve", lambda v: v.tensor_scalar(out=sel[:], in0=sel[:],
             scalar1=m2[:, 0:1], scalar2=None, op0=ALU.is_ge), reads=["E_sel", "E_m2"],
             writes=["E_sel"])
        p.op("dve", lambda v: v.tensor_tensor(out=msk[:], in0=msk[:], in1=sel[:],
             op=ALU.add), reads=["E_msk", "E_sel"], writes=["E_msk"])
        p.op("dve", lambda v: v.tensor_tensor(out=msk[:], in0=msk[:], in1=sc[:],
             op=ALU.mult), reads=["E_msk", "E_sc"], writes=["E_msk"])
        p.op("dve", lambda v: v.tensor_reduce(out=s12[:, 0:1], in_=msk[:], axis=AX.X,
             op=ALU.add), reads=["E_msk"], writes=["E_s12"])
        p.op("dve", lambda v: v.tensor_scalar(out=s12[:], in0=s12[:], scalar1=1e-10,
             scalar2=None, op0=ALU.add), reads=["E_s12"], writes=["E_s12"])
        p.op("dve", lambda v: v.reciprocal(out=s12[:], in_=s12[:]), reads=["E_s12"],
             writes=["E_s12"])
        p.op("dve", lambda v: v.tensor_scalar(out=cwt[:], in0=msk[:],
             scalar1=s12[:, 0:1], scalar2=None, op0=ALU.mult),
             reads=["E_msk", "E_s12"], writes=["E_cw"])
        p.op("pe", lambda t_: t_.transpose(out=pst[0:16, 0:128], in_=cwt[:, 0:16],
             identity=ident[:]), reads=["E_cw", "E_id"], writes=["pstT"])
        p.op("act", lambda s, sl=sl: s.copy(out=cwT[0:16, sl], in_=pst[0:16, 0:128]),
             reads=["pstT"], writes=["E_cwT"])
    p.dma("pool", cwtsp, cwT[:], reads=["E_cwT"], writes=[("d_cwt", 0)],
          stream="dma_st_cwt")
    ctx.afree("E_cwT", "E_lg", "E_sc", "E_m1", "E_m2", "E_s12", "E_msk", "E_sel",
              "E_cw")

    for e in range(18):
        nh = 4
        if e == 16:
            w1src, w3src = d["sw1T"][:, 0:512], d["sw3T"][:, 0:512]
            w2src = d["sw2T"][0:512, :]
        elif e == 17:
            w1src, w3src = d["sw1T"][:, 512:1024], d["sw3T"][:, 512:1024]
            w2src = d["sw2T"][512:1024, :]
        else:
            w1src, w3src, w2src = d["w1T"][e], d["w3T"][e], d["w2T"][e]
        w1r = ctx.aget("E_w1r", [128, 8 * 512], BF16)
        w3r = ctx.aget("E_w3r", [128, 8 * 512], BF16)
        w2r = ctx.aget("E_w2r", [128, 4 * 1024], BF16)
        p.dma("pool", w1r[:].rearrange("p (i c) -> p i c", c=nh * 128),
              w1src.rearrange("(i p) c -> p i c", p=128), writes=["E_w1r"])
        p.dma("pool", w3r[:].rearrange("p (i c) -> p i c", c=nh * 128),
              w3src.rearrange("(i p) c -> p i c", p=128), writes=["E_w3r"])
        p.dma("pool", w2r[:].rearrange("p (i c) -> p i c", c=1024),
              w2src.rearrange("(i p) c -> p i c", p=128), writes=["E_w2r"])
        cwr = ctx.aget("E_cwr", [1, T], BF16)
        if e >= 16:
            p.op("dve", lambda v, cwr=cwr: v.memset(cwr[:], 1.0), writes=["E_cwr"])
        else:
            p.dma("pool", cwr[:], cwtsp[e:e + 1, :], reads=[("d_cwt", 0)],
                  writes=["E_cwr"])
        ht = [ctx.aget(f"E_h{j}", [128, T], BF16) for j in range(nh)]
        t2r = [ctx.aget(f"E_t2{j}", [128, 512], BF16) for j in (0, 1)]
        boff = e * 4
        for hod in range(nh):
            h, hk2 = ht[hod], f"E_h{hod}"
            for tc in range(NTC):
                sl = slice(tc * 512, (tc + 1) * 512)
                b1_, b2_ = (tc % 2) * 2, (tc % 2) * 2 + 1
                ps1, pk1 = ctx.ps[b1_], f"psb{b1_}"
                ps2, pk2 = ctx.ps[b2_], f"psb{b2_}"
                mac(p, ps1, pk1, [(w1r[:, i * nh * 128 + hod * 128:
                                       i * nh * 128 + (hod + 1) * 128], "E_w1r",
                                   xln[i][:, sl], xk[i]) for i in range(8)])
                mac(p, ps2, pk2, [(w3r[:, i * nh * 128 + hod * 128:
                                       i * nh * 128 + (hod + 1) * 128], "E_w3r",
                                   xln[i][:, sl], xk[i]) for i in range(8)])
                t2, t2k = t2r[tc % 2], f"E_t2{tc % 2}"
                p.op("act", lambda s, h=h, sl=sl, ps1=ps1, hod=hod, hk2=hk2:
                     s.activation(out=h[:, sl], in_=ps1[:, 0:512], func=AF.Silu,
                     bias=b1pc[:, boff + hod:boff + hod + 1]),
                     reads=[pk1, "E_b1"], writes=[hk2])
                p.op("act", lambda s, t2=t2, ps2=ps2, hod=hod, t2k=t2k:
                     s.activation(out=t2[:], in_=ps2[:, 0:512], func=AF.Identity,
                     bias=b3pc[:, boff + hod:boff + hod + 1]),
                     reads=[pk2, "E_b3"], writes=[t2k])
                p.op("dve", lambda v, h=h, sl=sl, t2=t2, hk2=hk2, t2k=t2k:
                     v.tensor_tensor(out=h[:, sl], in0=h[:, sl], in1=t2[:],
                     op=ALU.mult), reads=[hk2, t2k], writes=[hk2])
        ctx.afree("E_w1r", "E_w3r")
        for og in range(2):
            cwb = [ctx.aget(f"E_cb{j}", [128, 512], BF16) for j in (0, 1)]
            yt = [ctx.aget(f"E_yt{od4}", [128, T], BF16) for od4 in range(4)]
            for tc in range(NTC):
                sl = slice(tc * 512, (tc + 1) * 512)
                cb_, cbk = cwb[tc % 2], f"E_cb{tc % 2}"
                ps7, pk7 = ctx.ps[6], "psb6"
                p.op("pe", lambda t_, sl=sl, cwr=cwr, ps7=ps7: t_.matmul(
                     out=ps7[:, 0:512], lhsT=ctx.ones_row[:], rhs=cwr[0:1, sl],
                     start=True, stop=True), reads=["ones_row", "E_cwr"], writes=[pk7])
                p.op("act", lambda s, cb_=cb_, ps7=ps7, cbk=cbk: s.copy(out=cb_[:],
                     in_=ps7[:, 0:512]), reads=[pk7], writes=[cbk])
                for od4 in range(4):
                    od = og * 4 + od4
                    bank = 2 + od4
                    ps, pk = ctx.ps[bank], f"psb{bank}"
                    pairs = [(b2sb[0:1, e * 1024 + od * 128:e * 1024 + (od + 1) * 128],
                              "E_b2", ctx.ones512[:], "ones512")]
                    pairs += [(w2r[:, i * 1024 + od * 128:i * 1024 + (od + 1) * 128],
                               "E_w2r", ht[i][:, sl], f"E_h{i}")
                              for i in range(nh)]
                    mac(p, ps, pk, pairs)
                    p.op("dve", lambda v, od4=od4, sl=sl, ps=ps, cb_=cb_, cbk=cbk:
                         v.tensor_tensor(out=yt[od4][:, sl], in0=ps[:, 0:512],
                         in1=cb_[:], op=ALU.mult), reads=[pk, cbk],
                         writes=[f"E_yt{od4}"])
            for od4 in range(4):
                od = og * 4 + od4
                if e == 0:
                    p.dma("pool", accsp[od * 128:(od + 1) * 128, :], yt[od4][:],
                          reads=[f"E_yt{od4}"], writes=[("d_acc", od)],
                          stream="dma_st_acc")
                else:
                    for ch in range(8):
                        cs = slice(ch * 512, (ch + 1) * 512)
                        p.dma_accum("pool", accsp[od * 128:(od + 1) * 128, cs],
                                    yt[od4][:, cs], reads=[f"E_yt{od4}"],
                                    writes=[("d_acc", od)], stream="dma_st_acc")
            ctx.afree("E_cb0", "E_cb1", *[f"E_yt{od4}" for od4 in range(4)])
        ctx.afree("E_w1r", "E_w3r", "E_w2r", "E_cwr",
                  *[f"E_h{j}" for j in range(nh)], "E_t20", "E_t21")

    ctx.afree(*xk)
    xf = [ctx.aget(f"E_xf{j}", [128, T], F32) for j in (0, 1)]
    ar = [ctx.aget(f"E_ar{j}", [128, T], F32) for j in (0, 1)]
    for od in range(8):
        xfj, xfk = xf[od % 2], f"E_xf{od % 2}"
        arj, ark = ar[od % 2], f"E_ar{od % 2}"
        p.dma("sp", xfj[:], d["x"][od * 128:(od + 1) * 128, :], writes=[xfk])
        p.dma("sp", arj[:], accsp[od * 128:(od + 1) * 128, :], reads=[("d_acc", od)],
              writes=[ark])
        p.op("dve", lambda v, arj=arj, xfj=xfj, ark=ark, xfk=xfk: v.tensor_tensor(
             out=arj[:], in0=arj[:], in1=xfj[:], op=ALU.add), reads=[ark, xfk],
             writes=[ark])
        p.dma("sp", d["xout"][od * 128:(od + 1) * 128, :], arj[:], reads=[ark],
              stream="dma_st_xout")
    ctx.afree("E_xf0", "E_xf1", "E_ar0", "E_ar1")

"""L3_attn block emitter: causal softmax attention, d=1024, T=4096.

x_in = LN_outer(x); q=(x_in@Wq+bq)/32; k=x_in@Wk+bk; v=x_in@Wv+bv
P = softmax(causal(q @ k.T)); ctx = P @ v; out = ctx@Wo+bo; x_next = x + out
Weights are [in,out] native. Per q-tile (128): scores vs all k<=q in 512
windows (fp32 in SBUF), single max/exp pass (accum rowsum), P^T via PE
transpose, PV accumulation, ctx->fm via transpose, Wo in fm + residual.
"""




def emit_L3A(ctx, d):
    p = ctx.p
    nc = ctx.nc
    qsp = nc.dram_tensor("F_qsp", [1024, T], BF16, kind="Internal").ap()
    ksp = nc.dram_tensor("F_ksp", [1024, T], BF16, kind="Internal").ap()

    bqp = load_pcol(ctx, "F_bq", d["bq_pc"], 8)
    bkp = load_pcol(ctx, "F_bk", d["bk_pc"], 8)
    bop = load_pcol(ctx, "F_bo", d["bo_pc"], 8)
    bvr = ctx.aget("F_bv", [1, 1024], BF16)
    p.dma("pool", bvr[:], d["bv_row"], writes=["F_bv"])
    ident = ctx.aget("F_id", [128, 128], BF16)
    p.dma("pool", ident[:], d["ident"], writes=["F_id"])
    cmask = ctx.aget("F_cm", [128, 4 * 512], F32)
    p.dma("sp", cmask[:], d["cmask"], writes=["F_cm"])
    c32, ck32 = ctx.cf(1.0 / 32.0)

    p.op("dve", lambda v: v.tensor_scalar(out=bqp[:], in0=bqp[:], scalar1=1.0 / 32,
         scalar2=None, op0=ALU.mult), reads=["F_bq"], writes=["F_bq"])

    xbf = load_xbf(ctx, "F_x", d["x"])
    xk = [f"F_x{i}" for i in range(8)]
    a_row, ak, mu_row, mk = ln_rows(ctx, xbf, xk, kind="ln", eps=1e-5, name="F_ln")
    xln, _ = ln_materialize(ctx, xbf, xk, a_row, ak, mu_row, mk, name="F_xm",
                            inplace=True)

    qv = [ctx.aget(f"F_q{j}", [128, T], BF16) for j in (0, 1)]
    for part, sp, key in (("q", qsp, "d_qsp"), ("k", ksp, "d_ksp")):
        wt0 = d["wqT"] if part == "q" else d["wkT"]
        for od in range(8):
            wt = wload(ctx, f"F_w{od % 2}", wt0, 1024, od * 128, 128)
            wk2 = f"F_w{od % 2}"
            dst, dk = qv[od % 2], f"F_q{od % 2}"
            for tc in range(NTC):
                sl = slice(tc * 512, (tc + 1) * 512)
                bank = tc % 2
                ps, pk = ctx.ps[bank], f"psb{bank}"
                mac(p, ps, pk, [(wslice(wt, i, 128), wk2, xln[i][:, sl], xk[i])
                                for i in range(8)])
                if part == "q":
                    p.op("act", lambda s, dst=dst, sl=sl, ps=ps, od=od, dk=dk:
                         s.activation(out=dst[:, sl], in_=ps[:, 0:512],
                         func=AF.Identity, bias=bqp[:, od:od + 1],
                         scale=c32[:, 0:1]), reads=[pk, "F_bq", ck32], writes=[dk])
                else:
                    p.op("act", lambda s, dst=dst, sl=sl, ps=ps, od=od, dk=dk:
                         s.activation(out=dst[:, sl], in_=ps[:, 0:512],
                         func=AF.Identity, bias=bkp[:, od:od + 1]),
                         reads=[pk, "F_bk"], writes=[dk])
            p.dma("pool", sp[od * 128:(od + 1) * 128, :], dst[:], reads=[dk],
                  writes=[(key, od)], stream="dma_st_" + part)
    ctx.afree("F_q0", "F_q1", "F_w0", "F_w1")

    vres = ctx.aget("F_vres", [128, 32 * 1024], BF16)
    wv = []
    for half in range(2):
        wv.append(wload(ctx, f"F_wv{half}", d["wvT"], 1024, half * 512, 512))
    for vt in range(32):
        tsl = slice(vt * 128, (vt + 1) * 128)
        for half in range(2):
            bank = 4 + half
            ps, pk = ctx.ps[bank], f"psb{bank}"
            pairs = [(ctx.ones_row[:], "ones_row",
                      bvr[0:1, half * 512:(half + 1) * 512], "F_bv")]
            pairs += [(xln[i][:, tsl], xk[i],
                       wslice(wv[half], i, 512), f"F_wv{half}") for i in range(8)]
            mac(p, ps, pk, pairs)
            p.op("act", lambda s, vt=vt, half=half, ps=ps: s.copy(
                 out=vres[:, vt * 1024 + half * 512:vt * 1024 + (half + 1) * 512],
                 in_=ps[:, 0:512]), reads=[pk], writes=["F_vres"])
    ctx.afree("F_wv0", "F_wv1")

    ctx.afree(*xk)
    kt_ = []
    for i in range(8):
        tn = f"F_k{i}"
        tt2 = ctx.aget(tn, [128, T], BF16)
        p.dma("pool", tt2[:], ksp[i * 128:(i + 1) * 128, :], reads=[("d_ksp", i)],
              writes=[tn])
        kt_.append(tt2)

    wo = wload(ctx, "F_wo", d["woT"], 1024, 0, 1024)
    qsl = ctx.aget("F_qsl", [128, 8 * 128], BF16)
    srow = ctx.aget("F_sr", [128, T], F32)
    prow = ctx.aget("F_pr", [128, T], BF16)
    mrow = ctx.aget("F_mr", [128, 1], F32)
    lrow = ctx.aget("F_lr", [128, 1], F32)
    ptall = ctx.aget("F_pta", [128, 32 * 128], BF16)
    cfm = ctx.aget("F_cfm", [128, 8 * 128], BF16)
    xr = ctx.aget("F_xr", [128, 128], F32)
    pst = ctx.pstT[0:128, 0:512]
    for qt in range(32):
        p.dma("pool", qsl[:].rearrange("p (i c) -> p i c", c=128),
              qsp[:, qt * 128:(qt + 1) * 128].rearrange("(i p) c -> p i c", p=128),
              reads=[("d_qsp", i2) for i2 in range(8)], writes=["F_qsl"])
        nw = qt // 4 + 1
        dw = qt // 4
        for w in range(nw):
            bank = w % 2
            ps, pk = ctx.ps[bank], f"psb{bank}"
            mac(p, ps, pk, [(qsl[:, i * 128:(i + 1) * 128], "F_qsl",
                             kt_[i][:, w * 512:(w + 1) * 512], f"F_k{i}")
                            for i in range(8)])
            if w == dw:
                m4 = qt % 4
                p.op("dve", lambda v, w=w, ps=ps, m4=m4: v.tensor_tensor(
                     out=srow[:, w * 512:(w + 1) * 512], in0=ps[:, 0:512],
                     in1=cmask[:, m4 * 512:(m4 + 1) * 512], op=ALU.add),
                     reads=[pk, "F_cm"], writes=["F_sr"])
            else:
                p.op("act", lambda s, w=w, ps=ps: s.copy(
                     out=srow[:, w * 512:(w + 1) * 512], in_=ps[:, 0:512]),
                     reads=[pk], writes=["F_sr"])
        ncol = nw * 512
        p.op("dve", lambda v, ncol=ncol: v.tensor_reduce(out=mrow[:, 0:1],
             in_=srow[:, 0:ncol], axis=AX.X, op=ALU.max), reads=["F_sr"],
             writes=["F_mr"])
        p.op("dve", lambda v: v.tensor_scalar(out=mrow[:], in0=mrow[:], scalar1=-1.0,
             scalar2=None, op0=ALU.mult), reads=["F_mr"], writes=["F_mr"])
        p.op("act", lambda s, ncol=ncol: s.activation(out=prow[:, 0:ncol],
             in_=srow[:, 0:ncol], func=AF.Exp, bias=mrow[:, 0:1],
             accum_out=lrow[:, 0:1]), reads=["F_sr", "F_mr"],
             writes=["F_pr", "F_lr"])
        p.op("dve", lambda v: v.reciprocal(out=lrow[:], in_=lrow[:]),
             reads=["F_lr"], writes=["F_lr"])
        for kt in range(qt + 1):
            p.op("pe", lambda t_, kt=kt: t_.transpose(out=pst[0:128, 0:128],
                 in_=prow[:, kt * 128:(kt + 1) * 128], identity=ident[:]),
                 reads=["F_pr", "F_id"], writes=["pstT"])
            p.op("act", lambda s, kt=kt: s.copy(out=ptall[:, kt * 128:(kt + 1) * 128],
                 in_=pst[0:128, 0:128]), reads=["pstT"], writes=["F_pta"])
        for half in range(2):
            bank = 4 + half
            ps, pk = ctx.ps[bank], f"psb{bank}"
            for kt in range(qt + 1):
                p.op("pe", lambda t_, kt=kt, half=half, ps=ps, qt=qt: t_.matmul(
                     out=ps[:, 0:512], lhsT=ptall[:, kt * 128:(kt + 1) * 128],
                     rhs=vres[:, kt * 1024 + half * 512:kt * 1024 + (half + 1) * 512],
                     start=(kt == 0), stop=(kt == qt)),
                     reads=["F_pta", "F_vres"], writes=[pk], inc=(kt == qt))
        for half in range(2):
            bank = 4 + half
            ps, pk = ctx.ps[bank], f"psb{bank}"
            p.op("act", lambda s, half=half, ps=ps: s.activation(
                 out=prow[:, half * 512:(half + 1) * 512], in_=ps[:, 0:512],
                 func=AF.Identity, scale=lrow[:, 0:1]),
                 reads=[pk, "F_lr"], writes=["F_pr"])
        for i in range(8):
            p.op("pe", lambda t_, i=i: t_.transpose(out=pst[0:128, 0:128],
                 in_=prow[:, i * 128:(i + 1) * 128], identity=ident[:]),
                 reads=["F_pr", "F_id"], writes=["pstT"])
            p.op("act", lambda s, i=i: s.copy(out=cfm[:, i * 128:(i + 1) * 128],
                 in_=pst[0:128, 0:128]), reads=["pstT"], writes=["F_cfm"])
        for od in range(8):
            bank = 2 + od % 2
            ps, pk = ctx.ps[bank], f"psb{bank}"
            mac(p, ps, pk, [(wo[:, i * 1024 + od * 128:i * 1024 + (od + 1) * 128],
                             "F_wo", cfm[:, i * 128:(i + 1) * 128], "F_cfm")
                            for i in range(8)], n=128)
            p.dma("sp", xr[:], d["x"][od * 128:(od + 1) * 128,
                                      qt * 128:(qt + 1) * 128], writes=["F_xr"])
            p.op("act", lambda s, od=od, ps=ps: s.activation(out=srow[:, 0:128],
                 in_=ps[:, 0:128], func=AF.Identity, bias=bop[:, od:od + 1]),
                 reads=[pk, "F_bo"], writes=["F_sr"])
            p.op("dve", lambda v: v.tensor_tensor(out=srow[:, 0:128],
                 in0=srow[:, 0:128], in1=xr[:], op=ALU.add),
                 reads=["F_sr", "F_xr"], writes=["F_sr"])
            p.dma("sp", d["xout"][od * 128:(od + 1) * 128, qt * 128:(qt + 1) * 128],
                  srow[:, 0:128], reads=["F_sr"], stream="dma_st_xout")
    ctx.afree("F_qsl", "F_sr", "F_pr", "F_mr", "F_lr", "F_pta",
              "F_cfm", "F_xr", "F_wo", "F_vres", *[f"F_k{i}" for i in range(8)])

"""L3_ffn block emitter: spiky MoE with global percentile + NaN-usage branch.

Unfused operation: a separate tiny "pre" program computes each core's imp
row; the host concatenates to impall [8, 4096] which this program receives.

x_in = LN(x); imp = 32*sqrt(v/(v+1e-5)) per token (exact from LN stats)
thr = percentile(impall, 90) (two bisections + linear interp)
temp = 1 + 0.1*std(impall); xs = where(imp>thr, x_in, 0)/((L1+1e-8)*temp)
gate: h1=silu(xs@gw1+gb1); hn=LN(h1)*lns+lno; h2=silu(hn@gw2+gb2);
logits=(h2@gw3+gb3+0.3*aff+noise*ns)/1.5 ; aff = mean_t(xs)@aff_w+aff_b
flag = any(usage+1e-8<=0): cw = flag ? [1/8,1/8,0..] : top2-masked softmax
experts: id k -> (e0,e1,e2 types); out = sum_k cw_k*expert_k(xs) + 0.1*xs
x_next = x + out
"""



KRANK0 = 29491
KRANK1 = 29492


def emit_L3F_pre(ctx, d):
    p = ctx.p
    xbf = load_xbf(ctx, "P_x", d["x"])
    xk = [f"P_x{i}" for i in range(8)]
    sq = [ctx.aget(f"P_sq{j}", [128, 512], BF16) for j in range(2)]
    vrow = ctx.aget("P_v", [1, T], F32)
    r1 = ctx.aget("P_r1", [1, 512], F32)
    r2 = ctx.aget("P_r2", [1, 512], F32)
    for tc in range(NTC):
        sl = slice(tc * 512, (tc + 1) * 512)
        psS, kS = ctx.ps[5], "psb5"
        psM, kM = ctx.ps[6], "psb6"
        for i in range(8):
            sqb, sqk = sq[i % 2], f"P_sq{i % 2}"
            p.op("act", lambda s, i=i, sl=sl, sqb=sqb: s.activation(out=sqb[:],
                 in_=xbf[i][:, sl], func=AF.Square), reads=[xk[i]], writes=[sqk])
            p.op("pe", lambda t_, i=i, sqb=sqb, psS=psS: t_.matmul(out=psS[0:1, 0:512],
                 lhsT=ctx.ones_col[:], rhs=sqb[:], start=(i == 0), stop=(i == 7)),
                 reads=[sqk, "ones_col"], writes=[kS], inc=(i == 7))
            p.op("pe", lambda t_, i=i, sl=sl, psM=psM: t_.matmul(out=psM[0:1, 0:512],
                 lhsT=ctx.ones_col[:], rhs=xbf[i][:, sl], start=(i == 0), stop=(i == 7)),
                 reads=[xk[i], "ones_col"], writes=[kM], inc=(i == 7))
        p.op("dve", lambda v, psM=psM: v.tensor_scalar(out=r1[0:1, :],
             in0=psM[0:1, 0:512], scalar1=1.0 / 1024, scalar2=None, op0=ALU.mult),
             reads=[kM], writes=["P_r1"])
        p.op("dve", lambda v: v.scalar_tensor_tensor(out=r1[0:1, :], in0=r1[0:1, :],
             scalar=-1.0, in1=r1[0:1, :], op0=ALU.mult, op1=ALU.mult),
             reads=["P_r1"], writes=["P_r1"])
        p.op("dve", lambda v, psS=psS: v.tensor_scalar(out=r2[0:1, :],
             in0=psS[0:1, 0:512], scalar1=1.0 / 1024, scalar2=None, op0=ALU.mult),
             reads=[kS], writes=["P_r2"])
        p.op("dve", lambda v, sl=sl: v.tensor_tensor(out=vrow[0:1, sl], in0=r2[0:1, :],
             in1=r1[0:1, :], op=ALU.add), reads=["P_r2", "P_r1"], writes=["P_v"])
    t1 = ctx.aget("P_t1", [1, T], F32)
    ce, cke = ctx.cf(1e-5)
    ch, ckh = ctx.cf(0.5)
    p.op("act", lambda s: s.activation(out=t1[0:1, :], in_=vrow[0:1, :], func=AF.Ln,
         bias=ce[0:1, 0:1]), reads=["P_v", cke], writes=["P_t1"])
    p.op("act", lambda s: s.activation(out=vrow[0:1, :], in_=vrow[0:1, :], func=AF.Ln),
         reads=["P_v"], writes=["P_v"])
    p.op("dve", lambda v: v.tensor_tensor(out=vrow[0:1, :], in0=vrow[0:1, :],
         in1=t1[0:1, :], op=ALU.subtract), reads=["P_v", "P_t1"], writes=["P_v"])
    p.op("act", lambda s: s.activation(out=vrow[0:1, :], in_=vrow[0:1, :], func=AF.Exp,
         scale=ch[0:1, 0:1]), reads=["P_v", ckh], writes=["P_v"])
    p.op("dve", lambda v: v.tensor_scalar(out=vrow[0:1, :], in0=vrow[0:1, :],
         scalar1=32.0, scalar2=None, op0=ALU.mult), reads=["P_v"], writes=["P_v"])
    p.dma("sp", d["imp_out"], vrow[0:1, :], reads=["P_v"], stream="dma_st_imp")
    ctx.afree(*xk, "P_sq0", "P_sq1", "P_v", "P_r1", "P_r2", "P_t1")


def bisect_rank(ctx, impall, krank, lo0, hi0, name):
    p = ctx.p
    lo = ctx.aget(name + "lo", [1, 1], F32)
    hi = ctx.aget(name + "hi", [1, 1], F32)
    mid = ctx.aget(name + "md", [1, 1], F32)
    mid8 = ctx.aget(name + "m8", [8, 1], F32)
    cmp8 = ctx.aget(name + "c8", [8, T], F32)
    cnt8 = ctx.aget(name + "ct", [8, 1], F32)
    cnt = ctx.aget(name + "cn", [1, 1], F32)
    pred = ctx.aget(name + "pd", [1, 1], F32)
    ta = ctx.aget(name + "ta", [1, 1], F32)
    tb = ctx.aget(name + "tb", [1, 1], F32)
    klo, khi, kmd = name + "lo", name + "hi", name + "md"
    kta, ktb = name + "ta", name + "tb"
    p.op("dve", lambda v: v.memset(lo[:], lo0), writes=[klo])
    p.op("dve", lambda v: v.memset(hi[:], hi0), writes=[khi])
    for it in range(36):
        p.op("dve", lambda v: v.tensor_tensor(out=mid[:], in0=lo[:], in1=hi[:],
             op=ALU.add), reads=[klo, khi], writes=[kmd])
        p.op("dve", lambda v: v.tensor_scalar(out=mid[:], in0=mid[:], scalar1=0.5,
             scalar2=None, op0=ALU.mult), reads=[kmd], writes=[kmd])
        ps, pk = ctx.ps[6], "psb6"
        p.op("pe", lambda t_: t_.matmul(out=ps[0:8, 0:1],
             lhsT=ctx.ones_rowf[0:1, 0:8], rhs=mid[0:1, 0:1], start=True, stop=True),
             reads=["ones_rowf", kmd], writes=[pk])
        p.op("dve", lambda v, ps=ps: v.tensor_copy(out=mid8[:], in_=ps[0:8, 0:1]),
             reads=[pk], writes=[name + "m8"])
        p.op("dve", lambda v: v.tensor_scalar(out=cmp8[:], in0=impall[:],
             scalar1=mid8[0:8, 0:1], scalar2=None, op0=ALU.is_le),
             reads=["G_imp", name + "m8"], writes=[name + "c8"])
        p.op("dve", lambda v: v.tensor_reduce(out=cnt8[:, 0:1], in_=cmp8[:],
             axis=AX.X, op=ALU.add), reads=[name + "c8"], writes=[name + "ct"])
        p.op("pe", lambda t_, ps=ps: t_.matmul(out=ps[0:1, 0:1],
             lhsT=ctx.ones_colf[0:8, 0:1], rhs=cnt8[0:8, 0:1], start=True, stop=True),
             reads=["ones_colf", name + "ct"], writes=[pk])
        p.op("dve", lambda v, ps=ps: v.tensor_copy(out=cnt[:], in_=ps[0:1, 0:1]),
             reads=[pk], writes=[name + "cn"])
        p.op("dve", lambda v, krank=krank: v.tensor_scalar(out=pred[:], in0=cnt[:],
             scalar1=float(krank), scalar2=None, op0=ALU.is_ge),
             reads=[name + "cn"], writes=[name + "pd"])
        p.op("dve", lambda v: v.tensor_tensor(out=ta[:], in0=mid[:],
             in1=hi[:], op=ALU.subtract), reads=[kmd, khi], writes=[kta])
        p.op("dve", lambda v: v.tensor_tensor(out=ta[:], in0=ta[:],
             in1=pred[:], op=ALU.mult), reads=[kta, name + "pd"], writes=[kta])
        p.op("dve", lambda v: v.tensor_tensor(out=hi[:], in0=hi[:], in1=ta[:],
             op=ALU.add), reads=[khi, kta], writes=[khi])
        p.op("dve", lambda v: v.tensor_tensor(out=ta[:], in0=mid[:],
             in1=lo[:], op=ALU.subtract), reads=[kmd, klo], writes=[kta])
        p.op("dve", lambda v: v.tensor_scalar(out=tb[:], in0=pred[:],
             scalar1=-1.0, scalar2=1.0, op0=ALU.mult, op1=ALU.add),
             reads=[name + "pd"], writes=[ktb])
        p.op("dve", lambda v: v.tensor_tensor(out=ta[:], in0=ta[:],
             in1=tb[:], op=ALU.mult), reads=[kta, ktb], writes=[kta])
        p.op("dve", lambda v: v.tensor_tensor(out=lo[:], in0=lo[:],
             in1=ta[:], op=ALU.add), reads=[klo, kta], writes=[klo])
    ctx.afree(name + "md", name + "m8", name + "c8", name + "ct", name + "cn",
              name + "pd", name + "ta", name + "tb", name + "lo")
    return hi, khi


def emit_L3F(ctx, d):
    p = ctx.p
    nc = ctx.nc
    accsp = nc.dram_tensor("G_acc", [1024, T], F32, kind="Internal").ap()
    cwtsp = nc.dram_tensor("G_cwt", [8, T], BF16, kind="Internal").ap()
    xssp = nc.dram_tensor("G_xssp", [1024, T], BF16, kind="Internal").ap()
    h1sp = nc.dram_tensor("G_h1sp", [4096, T], BF16, kind="Internal").ap()
    h2sp = nc.dram_tensor("G_h2sp", [2048, T], BF16, kind="Internal").ap()

    ctx.ones_colf = ctx.aget("ones_colf", [128, 1], F32)
    p.op("dve", lambda v: v.memset(ctx.ones_colf[:], 1.0), writes=["ones_colf"])

    impall = ctx.aget("G_imp", [8, T], F32)
    p.dma("sp", impall[:], d["impall"], writes=["G_imp"])
    usage = ctx.aget("G_us", [1, 8], F32)
    p.dma("sp", usage[:], d["usage"], writes=["G_us"])
    gb3b = ctx.aget("G_gb3", [128, 8], F32)
    p.dma("sp", gb3b[:], d["gb3_b"], writes=["G_gb3"])
    nanb = ctx.aget("G_nan", [128, 8], F32)
    p.dma("sp", nanb[:], d["nan_b"], writes=["G_nan"])
    affb = ctx.aget("G_afb", [1, 8], F32)
    p.dma("sp", affb[:], d["aff_b"], writes=["G_afb"])
    noise = ctx.aget("G_no", [128, 256], F32)
    p.dma("sp", noise[:], d["noise_pc"], writes=["G_no"])
    ident = ctx.aget("G_id", [128, 128], BF16)
    p.dma("pool", ident[:], d["ident"], writes=["G_id"])
    gb1 = load_pcol(ctx, "G_gb1", d["gb1_pc"], 8)
    lns = load_pcol(ctx, "G_lns", d["lns_pc"], 8)
    lno = load_pcol(ctx, "G_lno", d["lno_pc"], 8)
    gb2 = load_pcol(ctx, "G_gb2", d["gb2_pc"], 4)
    affw = ctx.aget("G_afw", [128, 64], BF16)
    p.dma("pool", affw[:].rearrange("p (i c) -> p i c", c=8),
          d["affwT"].rearrange("(i p) c -> p i c", p=128), writes=["G_afw"])
    b1e_pc = load_pcol(ctx, "G_b1e", d["b1e_pc"], 3 * 32 + 3 * 16 + 2 * 24)
    b2m_pc = load_pcol(ctx, "G_b2m", d["b2m_pc"], 3 * 16)
    bout = ctx.aget("G_bo", [1, 8 * 1024], BF16)
    p.dma("pool", bout[:], d["bout_rows"], writes=["G_bo"])

    va, kva = bisect_rank(ctx, impall, KRANK0, 0.0, 64.0, "G_a")
    vb, kvb = bisect_rank(ctx, impall, KRANK1, 0.0, 64.0, "G_b")
    thr = ctx.aget("G_thr", [1, 1], F32)
    p.op("dve", lambda v: v.tensor_tensor(out=thr[:], in0=vb[:], in1=va[:],
         op=ALU.subtract), reads=[kvb, kva], writes=["G_thr"])
    p.op("dve", lambda v: v.tensor_scalar(out=thr[:], in0=thr[:], scalar1=0.3,
         scalar2=None, op0=ALU.mult), reads=["G_thr"], writes=["G_thr"])
    p.op("dve", lambda v: v.tensor_tensor(out=thr[:], in0=thr[:], in1=va[:],
         op=ALU.add), reads=["G_thr", kva], writes=["G_thr"])
    ctx.afree(kva, kvb)
    dtile = ctx.aget("G_d", [8, T], F32)
    s1 = ctx.aget("G_s1", [8, 1], F32)
    s2 = ctx.aget("G_s2", [8, 1], F32)
    acc2 = ctx.aget("G_a2", [1, 1], F32)
    acc1 = ctx.aget("G_a1", [1, 1], F32)
    temp = ctx.aget("G_tmp", [1, 1], F32)
    p.op("dve", lambda v: v.tensor_scalar(out=dtile[:], in0=impall[:], scalar1=-32.0,
         scalar2=None, op0=ALU.add), reads=["G_imp"], writes=["G_d"])
    p.op("dve", lambda v: v.tensor_reduce(out=s1[:, 0:1], in_=dtile[:], axis=AX.X,
         op=ALU.add), reads=["G_d"], writes=["G_s1"])
    p.op("dve", lambda v: v.tensor_tensor(out=dtile[:], in0=dtile[:], in1=dtile[:],
         op=ALU.mult), reads=["G_d"], writes=["G_d"])
    p.op("dve", lambda v: v.tensor_reduce(out=s2[:, 0:1], in_=dtile[:], axis=AX.X,
         op=ALU.add), reads=["G_d"], writes=["G_s2"])
    ps6, pk6 = ctx.ps[6], "psb6"
    p.op("pe", lambda t_: t_.matmul(out=ps6[0:1, 0:1], lhsT=ctx.ones_colf[0:8, 0:1],
         rhs=s1[0:8, 0:1], start=True, stop=True), reads=["ones_colf", "G_s1"],
         writes=[pk6])
    p.op("dve", lambda v: v.tensor_scalar(out=acc1[:], in0=ps6[0:1, 0:1],
         scalar1=1.0 / 32768, scalar2=None, op0=ALU.mult), reads=[pk6],
         writes=["G_a1"])
    p.op("pe", lambda t_: t_.matmul(out=ps6[0:1, 0:1], lhsT=ctx.ones_colf[0:8, 0:1],
         rhs=s2[0:8, 0:1], start=True, stop=True), reads=["ones_colf", "G_s2"],
         writes=[pk6])
    p.op("dve", lambda v: v.tensor_scalar(out=acc2[:], in0=ps6[0:1, 0:1],
         scalar1=1.0 / 32768, scalar2=None, op0=ALU.mult), reads=[pk6],
         writes=["G_a2"])
    p.op("dve", lambda v: v.scalar_tensor_tensor(out=acc1[:], in0=acc1[:],
         scalar=-1.0, in1=acc1[:], op0=ALU.mult, op1=ALU.mult), reads=["G_a1"],
         writes=["G_a1"])
    p.op("dve", lambda v: v.tensor_tensor(out=acc2[:], in0=acc2[:], in1=acc1[:],
         op=ALU.add), reads=["G_a2", "G_a1"], writes=["G_a2"])
    p.op("dve", lambda v: v.tensor_scalar(out=acc2[:], in0=acc2[:], scalar1=0.0,
         scalar2=None, op0=ALU.max), reads=["G_a2"], writes=["G_a2"])
    p.op("act", lambda s: s.activation(out=temp[:], in_=acc2[:], func=AF.Sqrt),
         reads=["G_a2"], writes=["G_tmp"])
    p.op("dve", lambda v: v.tensor_scalar(out=temp[:], in0=temp[:], scalar1=0.1,
         scalar2=1.0, op0=ALU.mult, op1=ALU.add), reads=["G_tmp"], writes=["G_tmp"])
    ctx.afree("G_d", "G_s1", "G_s2", "G_a1", "G_a2", "G_imp")

    u1 = ctx.aget("G_u1", [1, 8], F32)
    flag = ctx.aget("G_fl", [1, 1], F32)
    nsc = ctx.aget("G_ns", [1, 1], F32)
    p.op("dve", lambda v: v.tensor_scalar(out=u1[:], in0=usage[:], scalar1=1e-8,
         scalar2=None, op0=ALU.add), reads=["G_us"], writes=["G_u1"])
    p.op("dve", lambda v: v.tensor_scalar(out=u1[:], in0=u1[:], scalar1=0.0,
         scalar2=None, op0=ALU.is_le), reads=["G_u1"], writes=["G_u1"])
    p.op("dve", lambda v: v.tensor_reduce(out=flag[:, 0:1], in_=u1[:], axis=AX.X,
         op=ALU.max), reads=["G_u1"], writes=["G_fl"])
    p.op("dve", lambda v: v.tensor_scalar(out=u1[:], in0=usage[:], scalar1=1e-8,
         scalar2=None, op0=ALU.add), reads=["G_us"], writes=["G_u1"])
    p.op("dve", lambda v: v.tensor_scalar(out=u1[:], in0=u1[:], scalar1=1e-30,
         scalar2=None, op0=ALU.max), reads=["G_u1"], writes=["G_u1"])
    p.op("act", lambda s: s.activation(out=u1[:], in_=u1[:], func=AF.Ln),
         reads=["G_u1"], writes=["G_u1"])
    p.op("dve", lambda v: v.tensor_tensor(out=u1[:], in0=u1[:], in1=usage[:],
         op=ALU.mult), reads=["G_u1", "G_us"], writes=["G_u1"])
    p.op("dve", lambda v: v.tensor_reduce(out=nsc[:, 0:1], in_=u1[:], axis=AX.X,
         op=ALU.add), reads=["G_u1"], writes=["G_ns"])
    p.op("dve", lambda v: v.tensor_scalar(out=nsc[:], in0=nsc[:], scalar1=2.0,
         scalar2=0.01, op0=ALU.add, op1=ALU.mult), reads=["G_ns"], writes=["G_ns"])
    p.op("dve", lambda v: v.tensor_scalar(out=nsc[:], in0=nsc[:], scalar1=1e-4,
         scalar2=None, op0=ALU.max), reads=["G_ns"], writes=["G_ns"])
    ctx.afree("G_u1")

    xbf = load_xbf(ctx, "G_x", d["x"])
    xk = [f"G_x{i}" for i in range(8)]
    a_row, ak, mu_row, mk = ln_rows(ctx, xbf, xk, kind="ln", eps=1e-5, name="G_ln")
    xln, _ = ln_materialize(ctx, xbf, xk, a_row, ak, mu_row, mk, name="G_xm",
                            inplace=True)
    import os as _os
    if _os.environ.get("ARENA_LOG"):
        for _i, _t in enumerate(xln):
            print(f"[xln] {_i} tensor={_t.tensor.name} offset={_t.offset}")
    imp1 = ctx.aget("G_i1", [1, T], F32)
    p.dma("sp", imp1[:], d["imp_row"], writes=["G_i1"])
    smask = ctx.aget("G_sm", [1, T], BF16)
    p.op("dve", lambda v: v.tensor_scalar(out=smask[0:1, :], in0=imp1[0:1, :],
         scalar1=thr[0:1, 0:1], scalar2=None, op0=ALU.is_gt),
         reads=["G_i1", "G_thr"], writes=["G_sm"])
    ab = [ctx.aget(f"G_ab{j}", [128, 512], BF16) for j in (0, 1)]
    nrm = ctx.aget("G_nr", [1, T], F32)
    for tc in range(NTC):
        sl = slice(tc * 512, (tc + 1) * 512)
        ps5, pk5 = ctx.ps[5], "psb5"
        for i in range(8):
            abj, abk = ab[i % 2], f"G_ab{i % 2}"
            p.op("act", lambda s, xt=xln[i], sl=sl, abj=abj: s.activation(
                 out=abj[:], in_=xt[:, sl], func=AF.Abs), reads=[xk[i]],
                 writes=[abk])
            p.op("pe", lambda t_, i=i, abj=abj, ps5=ps5: t_.matmul(
                 out=ps5[0:1, 0:512], lhsT=ctx.ones_col[:], rhs=abj[:],
                 start=(i == 0), stop=(i == 7)), reads=[abk, "ones_col"],
                 writes=[pk5], inc=(i == 7))
        p.op("dve", lambda v, sl=sl, ps5=ps5: v.tensor_copy(out=nrm[0:1, sl],
             in_=ps5[0:1, 0:512]), reads=[pk5], writes=["G_nr"])
    ctx.afree("G_ab0", "G_ab1")
    p.op("dve", lambda v: v.tensor_scalar(out=nrm[0:1, :], in0=nrm[0:1, :],
         scalar1=1e-8, scalar2=temp[0:1, 0:1], op0=ALU.add, op1=ALU.mult),
         reads=["G_nr", "G_tmp"], writes=["G_nr"])
    p.op("dve", lambda v: v.reciprocal(out=nrm[0:1, :], in_=nrm[0:1, :]),
         reads=["G_nr"], writes=["G_nr"])
    rsr = ctx.aget("G_rs", [1, T], BF16)
    p.op("dve", lambda v: v.tensor_tensor(out=rsr[0:1, :], in0=nrm[0:1, :],
         in1=smask[0:1, :], op=ALU.mult), reads=["G_nr", "G_sm"], writes=["G_rs"])
    rs_b = bcast_row_bf(ctx, "G_rsb", rsr, "G_rs", psbank=5)
    for i in range(8):
        p.op("dve", lambda v, xt=xln[i]: v.tensor_tensor(out=xt[:], in0=xt[:],
             in1=rs_b[:], op=ALU.mult), reads=[xk[i], "G_rsb"], writes=[xk[i]])
        p.dma("pool", xssp[i * 128:(i + 1) * 128, :], xln[i][:], reads=[xk[i]],
              writes=[("d_xs", i)], stream="dma_st_xs")
    ctx.afree("G_rsb", "G_i1", "G_sm", "G_nr", "G_rs")

    import os
    STG = os.environ.get("L3F_STG", "")
    if STG == "1":
        for od in range(8):
            p.dma("pool", d["xout"][od * 128:(od + 1) * 128, :], xln[od][:],
                  reads=[xk[od]], stream="dma_st_xout")
        return
    h1 = [ctx.aget(f"G_h1_{i}", [128, T], BF16) for i in range(8)]
    for od in range(8):
        wt = wload(ctx, f"G_gw{od % 2}", d["gw1T"], 1024, od * 128, 128)
        wk2 = f"G_gw{od % 2}"
        for tc in range(NTC):
            sl = slice(tc * 512, (tc + 1) * 512)
            bank = tc % 2
            ps, pk = ctx.ps[bank], f"psb{bank}"
            mac(p, ps, pk, [(wslice(wt, i, 128), wk2, xln[i][:, sl], xk[i])
                            for i in range(8)])
            p.op("act", lambda s, od=od, sl=sl, ps=ps: s.activation(
                 out=h1[od][:, sl], in_=ps[:, 0:512], func=AF.Silu,
                 bias=gb1[:, od:od + 1]), reads=[pk, "G_gb1"], writes=[f"G_h1_{od}"])
    h1k = [f"G_h1_{i}" for i in range(8)]
    a2r, a2k2, mu2, mk2 = ln_rows(ctx, h1, h1k, kind="ln", eps=1e-5, name="G_l2")
    hn, _ = ln_materialize(ctx, h1, h1k, a2r, a2k2, mu2, mk2, gamma=lns,
                           gamma_key="G_lns", beta=lno, beta_key="G_lno",
                           name="G_hn", inplace=True)
    h2 = [ctx.aget(f"G_h2_{i}", [128, T], BF16) for i in range(4)]
    for od in range(4):
        wt = wload(ctx, f"G_gw{od % 2}", d["gw2T"], 1024, od * 128, 128)
        wk2 = f"G_gw{od % 2}"
        for tc in range(NTC):
            sl = slice(tc * 512, (tc + 1) * 512)
            bank = tc % 2
            ps, pk = ctx.ps[bank], f"psb{bank}"
            mac(p, ps, pk, [(wslice(wt, i, 128), wk2, hn[i][:, sl], h1k[i])
                            for i in range(8)])
            p.op("act", lambda s, od=od, sl=sl, ps=ps: s.activation(
                 out=h2[od][:, sl], in_=ps[:, 0:512], func=AF.Silu,
                 bias=gb2[:, od:od + 1]), reads=[pk, "G_gb2"], writes=[f"G_h2_{od}"])
    ctx.afree(*h1k, "G_gw0", "G_gw1")

    xmean = ctx.aget("G_xme", [128, 8], BF16)
    xmeanf = ctx.aget("G_xmf", [128, 8], F32)
    xr2 = [ctx.aget(f"G_xr{j}", [128, T], BF16) for j in (0, 1)]
    dummy = ctx.aget("G_dm", [128, T], BF16)
    c4096, ck4096 = ctx.cf(1.0 / 4096.0)
    for i in range(8):
        xrj, xrk = xr2[i % 2], f"G_xr{i % 2}"
        p.dma("pool", xrj[:], xssp[i * 128:(i + 1) * 128, :], reads=[("d_xs", i)],
              writes=[xrk])
        p.op("act", lambda s, i=i, xrj=xrj, xrk=xrk: s.activation(out=dummy[:],
             in_=xrj[:], func=AF.Identity, scale=c4096[:, 0:1],
             accum_out=xmeanf[:, i:i + 1]), reads=[xrk, ck4096],
             writes=["G_dm", "G_xmf"])
    p.op("act", lambda s: s.copy(out=xmean[:], in_=xmeanf[:]),
         reads=["G_xmf"], writes=["G_xme"])
    aff = ctx.aget("G_af", [1, 8], F32)
    ps6b, pk6b = ctx.ps[6], "psb6"
    for i in range(8):
        p.op("pe", lambda t_, i=i, ps6b=ps6b: t_.matmul(out=ps6b[0:1, 0:8],
             lhsT=xmean[:, i:i + 1], rhs=affw[:, i * 8:(i + 1) * 8],
             start=(i == 0), stop=(i == 7)), reads=["G_xme", "G_afw"],
             writes=[pk6b], inc=(i == 7))
    p.op("dve", lambda v: v.tensor_tensor(out=aff[:], in0=ps6b[0:1, 0:8],
         in1=affb[0:1, 0:8], op=ALU.add), reads=[pk6b, "G_afb"], writes=["G_af"])
    p.op("dve", lambda v: v.tensor_scalar(out=aff[:], in0=aff[:], scalar1=0.3,
         scalar2=None, op0=ALU.mult), reads=["G_af"], writes=["G_af"])
    ctx.afree("G_xr0", "G_xr1", "G_dm", "G_xme", "G_xmf")

    flagb = ctx.aget("G_flb", [128, 1], F32)
    flag1m = ctx.aget("G_f1m", [128, 1], F32)
    nsb = ctx.aget("G_nsb", [128, 1], F32)
    ps5c, pk5c = ctx.ps[5], "psb5"
    p.op("pe", lambda t_: t_.matmul(out=ps5c[:, 0:1], lhsT=ctx.ones_rowf[:],
         rhs=flag[0:1, 0:1], start=True, stop=True), reads=["ones_rowf", "G_fl"],
         writes=[pk5c])
    p.op("dve", lambda v: v.tensor_copy(out=flagb[:], in_=ps5c[:, 0:1]),
         reads=[pk5c], writes=["G_flb"])
    p.op("dve", lambda v: v.tensor_scalar(out=flag1m[:], in0=flagb[:], scalar1=-1.0,
         scalar2=1.0, op0=ALU.mult, op1=ALU.add), reads=["G_flb"], writes=["G_f1m"])
    p.op("pe", lambda t_: t_.matmul(out=ps5c[:, 0:1], lhsT=ctx.ones_rowf[:],
         rhs=nsc[0:1, 0:1], start=True, stop=True), reads=["ones_rowf", "G_ns"],
         writes=[pk5c])
    p.op("dve", lambda v: v.tensor_copy(out=nsb[:], in_=ps5c[:, 0:1]),
         reads=[pk5c], writes=["G_nsb"])

    aff8 = ctx.aget("G_af8", [128, 8], F32)
    ps5d, pk5d = ctx.ps[5], "psb5"
    p.op("pe", lambda t_: t_.matmul(out=ps5d[:, 0:8], lhsT=ctx.ones_rowf[:],
         rhs=aff[0:1, 0:8], start=True, stop=True), reads=["ones_rowf", "G_af"],
         writes=[pk5d])
    p.op("dve", lambda v: v.tensor_copy(out=aff8[:], in_=ps5d[:, 0:8]),
         reads=[pk5d], writes=["G_af8"])

    cwT = ctx.aget("G_cwT", [8, T], BF16)
    lg = ctx.aget("G_lg", [128, 8], F32)
    m1 = ctx.aget("G_m1", [128, 1], F32)
    m2 = ctx.aget("G_m2", [128, 1], F32)
    msk2 = ctx.aget("G_mk", [128, 8], F32)
    prb = ctx.aget("G_pb", [128, 8], F32)
    cwt = ctx.aget("G_cw", [128, 8], BF16)
    pst = ctx.pstT[0:128, 0:128]
    w3r = wload(ctx, "G_w3r", d["gw3T"], 512, 0, 8)
    for tt in range(T // 128):
        sl = slice(tt * 128, (tt + 1) * 128)
        ps, pk = ctx.ps[6], "psb6"
        pairs = [(ctx.ones_rowf[0:1, 0:128], "ones_rowf", aff[0:1, 0:8], "G_af")]
        pairs += [(h2[i][:, sl], f"G_h2_{i}", wslice(w3r, i, 8), "G_w3r")
                  for i in range(4)]
        mac(p, ps, pk, pairs[1:], n=8)
        p.op("dve", lambda v, ps=ps: v.tensor_tensor(out=lg[:], in0=ps[:, 0:8],
             in1=gb3b[:, 0:8], op=ALU.add), reads=[pk, "G_gb3"], writes=["G_lg"])
        p.op("dve", lambda v, tt=tt: v.scalar_tensor_tensor(out=lg[:],
             in0=noise[:, tt * 8:(tt + 1) * 8], scalar=nsb[:, 0:1], in1=lg[:],
             op0=ALU.mult, op1=ALU.add), reads=["G_no", "G_nsb", "G_lg"],
             writes=["G_lg"])
        p.op("dve", lambda v: v.tensor_tensor(out=lg[:], in0=lg[:],
             in1=aff8[:, 0:8], op=ALU.add), reads=["G_lg", "G_af8"], writes=["G_lg"])
        p.op("dve", lambda v: v.tensor_scalar(out=lg[:], in0=lg[:],
             scalar1=1.0 / 1.5, scalar2=None, op0=ALU.mult), reads=["G_lg"],
             writes=["G_lg"])
        p.op("dve", lambda v: v.tensor_reduce(out=m1[:, 0:1], in_=lg[:], axis=AX.X,
             op=ALU.max), reads=["G_lg"], writes=["G_m1"])
        p.op("dve", lambda v: v.tensor_scalar(out=msk2[:], in0=lg[:],
             scalar1=m1[:, 0:1], scalar2=None, op0=ALU.is_ge),
             reads=["G_lg", "G_m1"], writes=["G_mk"])
        p.op("dve", lambda v: v.scalar_tensor_tensor(out=prb[:], in0=msk2[:],
             scalar=-1e9, in1=lg[:], op0=ALU.mult, op1=ALU.add),
             reads=["G_mk", "G_lg"], writes=["G_pb"])
        p.op("dve", lambda v: v.tensor_reduce(out=m2[:, 0:1], in_=prb[:], axis=AX.X,
             op=ALU.max), reads=["G_pb"], writes=["G_m2"])
        p.op("dve", lambda v: v.tensor_scalar(out=msk2[:], in0=lg[:],
             scalar1=m2[:, 0:1], scalar2=None, op0=ALU.is_ge),
             reads=["G_lg", "G_m2"], writes=["G_mk"])
        p.op("dve", lambda v: v.tensor_scalar(out=m1[:], in0=m1[:], scalar1=-1.0,
             scalar2=None, op0=ALU.mult), reads=["G_m1"], writes=["G_m1"])
        p.op("act", lambda s: s.activation(out=prb[:], in_=lg[:], func=AF.Exp,
             bias=m1[:, 0:1]), reads=["G_lg", "G_m1"], writes=["G_pb"])
        p.op("dve", lambda v: v.tensor_tensor(out=prb[:], in0=prb[:], in1=msk2[:],
             op=ALU.mult), reads=["G_pb", "G_mk"], writes=["G_pb"])
        p.op("dve", lambda v: v.tensor_reduce(out=m2[:, 0:1], in_=prb[:], axis=AX.X,
             op=ALU.add), reads=["G_pb"], writes=["G_m2"])
        p.op("dve", lambda v: v.reciprocal(out=m2[:], in_=m2[:]), reads=["G_m2"],
             writes=["G_m2"])
        p.op("dve", lambda v: v.tensor_scalar(out=prb[:], in0=prb[:],
             scalar1=m2[:, 0:1], scalar2=None, op0=ALU.mult),
             reads=["G_pb", "G_m2"], writes=["G_pb"])
        p.op("dve", lambda v: v.tensor_scalar(out=prb[:], in0=prb[:],
             scalar1=flag1m[:, 0:1], scalar2=None, op0=ALU.mult),
             reads=["G_pb", "G_f1m"], writes=["G_pb"])
        p.op("dve", lambda v: v.scalar_tensor_tensor(out=cwt[:], in0=nanb[:, 0:8],
             scalar=flagb[:, 0:1], in1=prb[:], op0=ALU.mult, op1=ALU.add),
             reads=["G_nan", "G_flb", "G_pb"], writes=["G_cw"])
        p.op("pe", lambda t_: t_.transpose(out=pst[0:8, 0:128], in_=cwt[:, 0:8],
             identity=ident[:]), reads=["G_cw", "G_id"], writes=["pstT"])
        p.op("act", lambda s, sl=sl: s.copy(out=cwT[0:8, sl], in_=pst[0:8, 0:128]),
             reads=["pstT"], writes=["G_cwT"])
    p.dma("pool", cwtsp, cwT[:], reads=["G_cwT"], writes=[("d_cwt", 0)],
          stream="dma_st_cwt")
    ctx.afree("G_cwT", "G_lg", "G_m1", "G_m2", "G_mk", "G_pb", "G_cw", "G_w3r",
              *[f"G_h2_{i}" for i in range(4)], "G_flb", "G_f1m", "G_nsb", "G_af8",
              *xk)

    if STG == "2":
        for od in range(8):
            p.dma("pool", d["xout"][od * 128:(od + 1) * 128, :],
                  xssp[od * 128:(od + 1) * 128, :], reads=[("d_xs", od)],
                  stream="dma_st_xout")
        return
    EMAP = [(0, 0), (1, 0), (2, 0), (0, 1), (1, 1), (2, 1), (0, 2), (1, 2)]
    cwr = ctx.aget("G_cwr", [1, T], BF16)
    for k in range(8):
        typ, idx = EMAP[k]
        p.dma("pool", cwr[:], cwtsp[k:k + 1, :], reads=[("d_cwt", 0)],
              writes=["G_cwr"])
        xln = []
        for i in range(8):
            tn = f"G_x{i}"
            tt2 = ctx.aget(tn, [128, T], BF16)
            p.dma("pool", tt2[:], xssp[i * 128:(i + 1) * 128, :],
                  reads=[("d_xs", i)], writes=[tn])
            xln.append(tt2)
        if typ == 0:
            nh, w1src, w2src = 32, d["e0w1"][idx], d["e0w2"][idx]
            b1col = 32 * idx
        elif typ == 1:
            nh, w1src, w2src = 16, d["e1w1"][idx], d["e1w3"][idx]
            b1col = 96 + 16 * idx
        else:
            nh, w1src, w2src = 24, d["e2w1"][idx], d["e2w2"][idx]
            b1col = 144 + 24 * idx
        w1r = ctx.aget("G_w1r", [128, 8 * 4096], BF16)
        p.dma("pool", w1r[:, 0:8 * nh * 128].rearrange("p (i c) -> p i c",
              c=nh * 128), w1src.rearrange("(i p) c -> p i c", p=128),
              writes=["G_w1r"])
        hrot = [ctx.aget(f"G_hr{j}", [128, T], BF16) for j in (0, 1)]
        sg = ctx.aget("G_sg", [128, T], BF16)
        for hod in range(nh):
            h, hk2 = hrot[hod % 2], f"G_hr{hod % 2}"
            for tc4 in range(0, NTC, 4):
                for tc in range(tc4, tc4 + 4):
                    sl = slice(tc * 512, (tc + 1) * 512)
                    ps, pk = ctx.ps[tc % 4], f"psb{tc % 4}"
                    mac(p, ps, pk, [(w1r[:, i * nh * 128 + hod * 128:
                                         i * nh * 128 + (hod + 1) * 128], "G_w1r",
                                     xln[i][:, sl], xk[i]) for i in range(8)])
                for tc in range(tc4, tc4 + 4):
                    sl = slice(tc * 512, (tc + 1) * 512)
                    ps, pk = ctx.ps[tc % 4], f"psb{tc % 4}"
                    fn = AF.Silu if typ == 1 else AF.Identity
                    p.op("act", lambda s, h=h, sl=sl, ps=ps, hod=hod, hk2=hk2,
                         fn=fn, b1col=b1col: s.activation(out=h[:, sl],
                         in_=ps[:, 0:512], func=fn,
                         bias=b1e_pc[:, b1col + hod:b1col + hod + 1]),
                         reads=[pk, "G_b1e"], writes=[hk2])
            if typ == 0:
                p.op("act", lambda s, h=h, hk2=hk2: s.activation(out=sg[:],
                     in_=h[:], func=AF.Square), reads=[hk2], writes=["G_sg"])
                p.op("dve", lambda v: v.tensor_scalar(out=sg[:], in0=sg[:],
                     scalar1=0.044715, scalar2=1.0, op0=ALU.mult, op1=ALU.add),
                     reads=["G_sg"], writes=["G_sg"])
                p.op("dve", lambda v, h=h, hk2=hk2: v.tensor_tensor(out=sg[:],
                     in0=sg[:], in1=h[:], op=ALU.mult), reads=["G_sg", hk2],
                     writes=["G_sg"])
                c2g, ck2g = ctx.cf(2.0 * 0.7978845608028654)
                p.op("act", lambda s, c2g=c2g: s.activation(out=sg[:], in_=sg[:],
                     func=AF.Sigmoid, scale=c2g[:, 0:1]), reads=["G_sg", ck2g],
                     writes=["G_sg"])
                p.op("dve", lambda v, h=h, hk2=hk2: v.tensor_tensor(out=h[:],
                     in0=h[:], in1=sg[:], op=ALU.mult), reads=[hk2, "G_sg"],
                     writes=[hk2])
            elif typ == 2:
                p.op("act", lambda s, h=h, hk2=hk2: s.activation(out=sg[:],
                     in_=h[:], func=AF.Silu), reads=[hk2], writes=["G_sg"])
                p.op("act", lambda s, h=h, hk2=hk2: s.activation(out=h[:],
                     in_=h[:], func=AF.Sigmoid), reads=[hk2], writes=[hk2])
                p.op("dve", lambda v, h=h, hk2=hk2: v.tensor_tensor(out=h[:],
                     in0=h[:], in1=sg[:], op=ALU.mult), reads=[hk2, "G_sg"],
                     writes=[hk2])
            p.dma("pool", h1sp[hod * 128:(hod + 1) * 128, :], h[:], reads=[hk2],
                  writes=[("d_h1", hod)], stream="dma_st_h1")
        ctx.afree("G_w1r", "G_hr0", "G_hr1", "G_sg", *xk)

        insp, nin = h1sp, nh
        if typ == 1:
            w2m = ctx.aget("G_w2m", [128, 16 * 2048], BF16)
            p.dma("pool", w2m[:].rearrange("p (i c) -> p i c", c=2048),
                  d["e1w2"][idx].rearrange("(i p) c -> p i c", p=128),
                  writes=["G_w2m"])
            hre = [ctx.aget(f"G_he{j}", [128, 16 * 512], BF16) for j in (0, 1)]
            h2st = [ctx.aget(f"G_h2s{j}", [128, 16 * 512], BF16) for j in (0, 1)]
            for tc in range(NTC):
                sl = slice(tc * 512, (tc + 1) * 512)
                hr, hrk = hre[tc % 2], f"G_he{tc % 2}"
                p.dma("pool", hr[:].rearrange("p (i c) -> p i c", c=512),
                      h1sp[0:2048, sl].rearrange("(i p) c -> p i c", p=128),
                      reads=[("d_h1", i2) for i2 in range(16)], writes=[hrk])
                st2, stk = h2st[tc % 2], f"G_h2s{tc % 2}"
                for hod4 in range(0, 16, 4):
                    for hod in range(hod4, hod4 + 4):
                        bank = 2 + (hod % 4)
                        ps, pk = ctx.ps[bank], f"psb{bank}"
                        mac(p, ps, pk, [(w2m[:, i * 2048 + hod * 128:
                                             i * 2048 + (hod + 1) * 128], "G_w2m",
                                         hr[:, i * 512:(i + 1) * 512], hrk)
                                        for i in range(16)])
                    for hod in range(hod4, hod4 + 4):
                        bank = 2 + (hod % 4)
                        ps, pk = ctx.ps[bank], f"psb{bank}"
                        p.op("act", lambda s, st2=st2, hod=hod, ps=ps, idx=idx,
                             stk=stk: s.activation(
                             out=st2[:, hod * 512:(hod + 1) * 512],
                             in_=ps[:, 0:512], func=AF.Silu,
                             bias=b2m_pc[:, 16 * idx + hod:16 * idx + hod + 1]),
                             reads=[pk, "G_b2m"], writes=[stk])
                p.dma("pool",
                      h2sp[:, sl].rearrange("(i p) c -> p i c", p=128),
                      st2[:].rearrange("p (i c) -> p i c", c=512),
                      reads=[stk], writes=[("d_h2", tc)], stream="dma_st_h2")
            ctx.afree("G_w2m", "G_he0", "G_he1", "G_h2s0", "G_h2s1")
            insp, nin = h2sp, 16

        inkey = "d_h2" if typ == 1 else "d_h1"
        inrds = ([("d_h2", t2) for t2 in range(8)] if typ == 1
                 else [("d_h1", i2) for i2 in range(nin)])
        w2r = ctx.aget("G_w2r", [128, 32 * 1024], BF16)
        p.dma("pool", w2r[:, 0:nin * 1024].rearrange("p (i c) -> p i c", c=1024),
              w2src.rearrange("(i p) c -> p i c", p=128), writes=["G_w2r"])
        for og in range(2):
            hre = [ctx.aget(f"G_he{j}", [128, 32 * 512], BF16) for j in (0, 1)]
            cwb = [ctx.aget(f"G_cb{j}", [128, 512], BF16) for j in (0, 1)]
            yt = [ctx.aget(f"G_yt{od4}", [128, T], BF16) for od4 in range(4)]
            for tc in range(NTC):
                sl = slice(tc * 512, (tc + 1) * 512)
                hr, hrk = hre[tc % 2], f"G_he{tc % 2}"
                p.dma("pool", hr[:, 0:nin * 512].rearrange("p (i c) -> p i c", c=512),
                      insp[0:nin * 128, sl].rearrange("(i p) c -> p i c", p=128),
                      reads=inrds, writes=[hrk])
                cb_, cbk = cwb[tc % 2], f"G_cb{tc % 2}"
                ps6c, pk6c = ctx.ps[6], "psb6"
                p.op("pe", lambda t_, sl=sl, ps6c=ps6c: t_.matmul(out=ps6c[:, 0:512],
                     lhsT=ctx.ones_row[:], rhs=cwr[0:1, sl], start=True, stop=True),
                     reads=["ones_row", "G_cwr"], writes=[pk6c])
                p.op("act", lambda s, cb_=cb_, ps6c=ps6c, cbk=cbk: s.copy(out=cb_[:],
                     in_=ps6c[:, 0:512]), reads=[pk6c], writes=[cbk])
                for od4 in range(4):
                    od = og * 4 + od4
                    ps, pk = ctx.ps[2 + od4], f"psb{2 + od4}"
                    pairs = [(bout[0:1, k * 1024 + od * 128:k * 1024 + (od + 1) * 128],
                              "G_bo", ctx.ones512[:], "ones512")]
                    pairs += [(w2r[:, i * 1024 + od * 128:i * 1024 + (od + 1) * 128],
                               "G_w2r", hr[:, i * 512:(i + 1) * 512], hrk)
                              for i in range(nin)]
                    mac(p, ps, pk, pairs)
                for od4 in range(4):
                    ps, pk = ctx.ps[2 + od4], f"psb{2 + od4}"
                    p.op("dve", lambda v, od4=od4, sl=sl, ps=ps, cb_=cb_, cbk=cbk:
                         v.tensor_tensor(out=yt[od4][:, sl], in0=ps[:, 0:512],
                         in1=cb_[:], op=ALU.mult), reads=[pk, cbk],
                         writes=[f"G_yt{od4}"])
            for od4 in range(4):
                od = og * 4 + od4
                if k == 0:
                    p.dma("pool", accsp[od * 128:(od + 1) * 128, :], yt[od4][:],
                          reads=[f"G_yt{od4}"], writes=[("d_acc", od)],
                          stream="dma_st_acc")
                else:
                    for ch in range(8):
                        cs = slice(ch * 512, (ch + 1) * 512)
                        p.dma_accum("pool", accsp[od * 128:(od + 1) * 128, cs],
                                    yt[od4][:, cs], reads=[f"G_yt{od4}"],
                                    writes=[("d_acc", od)], stream="dma_st_acc")
            ctx.afree("G_he0", "G_he1", "G_cb0", "G_cb1",
                      *[f"G_yt{od4}" for od4 in range(4)])
        ctx.afree("G_w2r")
    ctx.afree("G_cwr")

    xf = [ctx.aget(f"G_xf{j}", [128, T], F32) for j in (0, 1)]
    ar = [ctx.aget(f"G_ar{j}", [128, T], F32) for j in (0, 1)]
    xs2 = [ctx.aget(f"G_x2{j}", [128, T], BF16) for j in (0, 1)]
    for od in range(8):
        xfj, xfk = xf[od % 2], f"G_xf{od % 2}"
        arj, ark = ar[od % 2], f"G_ar{od % 2}"
        xsj, xsk = xs2[od % 2], f"G_x2{od % 2}"
        p.dma("sp", xfj[:], d["x"][od * 128:(od + 1) * 128, :], writes=[xfk])
        p.dma("sp", arj[:], accsp[od * 128:(od + 1) * 128, :], reads=[("d_acc", od)],
              writes=[ark])
        p.dma("pool", xsj[:], xssp[od * 128:(od + 1) * 128, :], reads=[("d_xs", od)],
              writes=[xsk])
        p.op("dve", lambda v, arj=arj, xfj=xfj, ark=ark, xfk=xfk: v.tensor_tensor(
             out=arj[:], in0=arj[:], in1=xfj[:], op=ALU.add), reads=[ark, xfk],
             writes=[ark])
        p.op("dve", lambda v, arj=arj, xsj=xsj, ark=ark, xsk=xsk:
             v.scalar_tensor_tensor(out=arj[:], in0=xsj[:], scalar=0.1, in1=arj[:],
             op0=ALU.mult, op1=ALU.add), reads=[xsk, ark], writes=[ark])
        p.dma("sp", d["xout"][od * 128:(od + 1) * 128, :], arj[:], reads=[ark],
              stream="dma_st_xout")
    ctx.afree("G_xf0", "G_xf1", "G_ar0", "G_ar1", "G_x20", "G_x21")

"""Per-stage (emit_fn, specs, shared-weights, per_core) tables, shared by the
unfused driver and the fused single-launch builder. All host prep is layout
only (transposes/reshapes/concats) plus host-generated constants."""









_AS = ([1024, T], F32, "ExternalInput")
_AO = ([1024, T], F32, "ExternalOutput")
_ca = lambda a: np.ascontiguousarray(np.asarray(a).astype(np.float32))
_tp = lambda a: np.ascontiguousarray(np.asarray(a).T.astype(np.float32))


def st_l0a(inputs):
    specs = {
        "x": _AS, "qkv_wT": ([1024, 3072], F32, "ExternalInput"),
        "outproj_wT": ([1024, 1024], F32, "ExternalInput"),
        "outgate_wT": ([1024, 1024], F32, "ExternalInput"),
        "pw_wT": ([1024, 1024], F32, "ExternalInput"),
        "gate_wT": ([2048, 1024], F32, "ExternalInput"),
        "w1T": ([1024, 2816], F32, "ExternalInput"),
        "w2T": ([1024, 2816], F32, "ExternalInput"),
        "w3T": ([2816, 1024], F32, "ExternalInput"),
        "norm1_pc": ([128, 8], F32, "ExternalInput"),
        "norm2_pc": ([128, 8], F32, "ExternalInput"),
        "convw_pc": ([128, 32], F32, "ExternalInput"),
        "convb_pc": ([128, 8], F32, "ExternalInput"),
        "outgateb_pc": ([128, 8], F32, "ExternalInput"),
        "decay": ([1, 8], F32, "ExternalInput"), "xout": _AO,
    }
    w = {
        "qkv_wT": _tp(inputs["L0_attn_qkv_w"]),
        "outproj_wT": _tp(inputs["L0_attn_out_proj_w"]),
        "outgate_wT": _tp(inputs["L0_attn_out_gate_w"]),
        "pw_wT": _tp(inputs["L0_attn_pw_w"]),
        "gate_wT": _tp(inputs["L0_attn_gate_w"]),
        "w1T": _tp(inputs["L0_attn_w1"]),
        "w2T": _tp(inputs["L0_attn_w2"]),
        "w3T": _tp(inputs["L0_attn_w3"]),
        "norm1_pc": host_pcol(inputs["L0_attn_norm1_w"]),
        "norm2_pc": host_pcol(inputs["L0_attn_norm2_w"]),
        "convw_pc": np.concatenate([host_pcol(inputs["L0_attn_conv_w"][:, 0, j])
                                    for j in range(4)], axis=1),
        "convb_pc": host_pcol(inputs["L0_attn_conv_b"]),
        "outgateb_pc": host_pcol(inputs["L0_attn_out_gate_b"]),
        "decay": np.ascontiguousarray(
            inputs["L0_attn_decay_param"].reshape(1, 8).astype(np.float32)),
    }
    return emit_L0A, specs, w, None


def st_l0f(inputs):
    specs = {
        "x": _AS, "wT": ([1024, 1024], F32, "ExternalInput"),
        "scale_pc": ([128, 8], F32, "ExternalInput"),
        "bias_pc": ([128, 8], F32, "ExternalInput"), "xout": _AO,
    }
    w = {"wT": _tp(inputs["L0_ffn_weights"]),
         "scale_pc": host_pcol(inputs["L0_ffn_scale"]),
         "bias_pc": host_pcol(inputs["L0_ffn_bias"])}
    return emit_L0F, specs, w, None


def st_l1a(inputs):
    specs = {
        "x": _AS, "winT": ([1024, 4096], F32, "ExternalInput"),
        "wcT": ([2048, 2048], F32, "ExternalInput"),
        "wdT": ([2048, 2048], F32, "ExternalInput"),
        "woT": ([2048, 1024], F32, "ExternalInput"),
        "lng_pc": ([128, 8], F32, "ExternalInput"),
        "lnb_pc": ([128, 8], F32, "ExternalInput"),
        "bin_pc": ([128, 32], F32, "ExternalInput"),
        "convw_pc": ([128, 64], F32, "ExternalInput"),
        "convb_pc": ([128, 16], F32, "ExternalInput"),
        "bc_pc": ([128, 16], F32, "ExternalInput"),
        "bd_pc": ([128, 16], F32, "ExternalInput"),
        "bout_pc": ([128, 8], F32, "ExternalInput"),
        "logdelta_pc": ([128, 16], F32, "ExternalInput"),
        "a_pc": ([128, 256], F32, "ExternalInput"),
        "bm_pc": ([128, 256], F32, "ExternalInput"), "xout": _AO,
    }
    nk = lambda a: np.ascontiguousarray(
        np.asarray(a).reshape(16, 128, 16).transpose(1, 0, 2).reshape(128, 256)
        .astype(np.float32))
    w = {
        "winT": _tp(inputs["L1_attn_W_in"]),
        "wcT": _tp(inputs["L1_attn_W_c"]),
        "wdT": _tp(inputs["L1_attn_W_d"]),
        "woT": _tp(inputs["L1_attn_W_out"]),
        "lng_pc": host_pcol(inputs["L1_attn_ln_g"]),
        "lnb_pc": host_pcol(inputs["L1_attn_ln_b"]),
        "bin_pc": host_pcol(inputs["L1_attn_b_in"]),
        "convw_pc": np.concatenate([host_pcol(inputs["L1_attn_conv_w"][:, 0, j])
                                    for j in range(4)], axis=1),
        "convb_pc": host_pcol(inputs["L1_attn_conv_b"]),
        "bc_pc": host_pcol(inputs["L1_attn_b_c"]),
        "bd_pc": host_pcol(inputs["L1_attn_b_d"]),
        "bout_pc": host_pcol(inputs["L1_attn_b_out"]),
        "logdelta_pc": host_pcol(inputs["L1_attn_log_delta"]),
        "a_pc": nk(inputs["L1_attn_A"]),
        "bm_pc": nk(inputs["L1_attn_Bm"]),
    }
    return emit_L1A, specs, w, None


def st_l1f(inputs):
    specs = {
        "x": _AS, "wg": ([1024, 8], F32, "ExternalInput"),
        "w1T": ([8, 1024, 4096], F32, "ExternalInput"),
        "w2T": ([8, 4096, 1024], F32, "ExternalInput"),
        "ws1T": ([1024, 4096], F32, "ExternalInput"),
        "ws2T": ([4096, 1024], F32, "ExternalInput"),
        "b1_pc": ([128, 288], F32, "ExternalInput"),
        "b2_rows": ([1, 9216], F32, "ExternalInput"),
        "ident": ([128, 128], F32, "ExternalInput"), "xout": _AO,
    }
    b1 = np.concatenate([host_pcol(inputs["L1_ffn_b1"][e]) for e in range(8)]
                        + [host_pcol(inputs["L1_ffn_bs1"])], axis=1)
    b2 = np.concatenate([inputs["L1_ffn_b2"].reshape(-1),
                         inputs["L1_ffn_bs2"].reshape(-1)]).reshape(1, 9216)
    w = {
        "wg": _ca(inputs["L1_ffn_Wg"]), "w1T": _ca(inputs["L1_ffn_W1"]),
        "w2T": _ca(inputs["L1_ffn_W2"]), "ws1T": _ca(inputs["L1_ffn_Ws1"]),
        "ws2T": _ca(inputs["L1_ffn_Ws2"]), "b1_pc": _ca(b1), "b2_rows": _ca(b2),
        "ident": np.eye(128, dtype=np.float32),
    }
    return emit_L1F, specs, w, None


def st_l2a(inputs):
    perm = np.concatenate([np.array([h * 128 + dd for h in range(16)
                                     for dd in range(lo, hi)])
                           for lo, hi in [(0, 32), (64, 96), (32, 64), (96, 128)]])
    specs = {
        "x": _AS, "w1T": ([1024, 256], F32, "ExternalInput"),
        "w2T": ([256, 6144], F32, "ExternalInput"),
        "cw1T": ([2048, 256], F32, "ExternalInput"),
        "cw2T": ([256, 2048], F32, "ExternalInput"),
        "cos_t": ([128, T], F32, "ExternalInput"),
        "sin_t": ([128, T], F32, "ExternalInput"),
        "seg4": ([128, 4], F32, "ExternalInput"),
        "seg4T": ([4, 128], F32, "ExternalInput"), "xout": _AO,
    }
    w2 = np.asarray(inputs["L2_attn_qkv_w2"]).astype(np.float32)
    order = np.concatenate([perm, 2048 + perm, 4096 + perm])
    freq = (np.float32(1.0) / np.float32(1024.0)) ** np.linspace(
        0.0, 1.0, 32, dtype=np.float32)
    theta = np.arange(T, dtype=np.float32)[:, None] * freq[None, :]
    cos_t = np.cos(theta).astype(np.float32)
    sin_t = np.sin(theta).astype(np.float32)
    ct = np.zeros((128, T), np.float32)
    st2 = np.zeros((128, T), np.float32)
    for pp in range(128):
        ct[pp] = cos_t[:, pp % 32]
        st2[pp] = sin_t[:, pp % 32]
    seg4 = np.zeros((128, 4), np.float32)
    for j in range(4):
        seg4[j * 32:(j + 1) * 32, j] = 1.0
    w = {
        "w1T": _tp(inputs["L2_attn_qkv_w1"]),
        "w2T": np.ascontiguousarray(w2[order].T),
        "cw1T": np.ascontiguousarray(
            np.asarray(inputs["L2_attn_cproj_w1"]).astype(np.float32)[:, perm].T),
        "cw2T": _tp(inputs["L2_attn_cproj_w2"]),
        "cos_t": ct, "sin_t": st2,
        "seg4": seg4, "seg4T": np.ascontiguousarray(seg4.T),
    }
    return emit_L2A, specs, w, None


def st_l2f(inputs):
    specs = {
        "x": _AS, "gwT": ([1024, 16], F32, "ExternalInput"),
        "w1T": ([16, 1024, 512], F32, "ExternalInput"),
        "w3T": ([16, 1024, 512], F32, "ExternalInput"),
        "w2T": ([16, 512, 1024], F32, "ExternalInput"),
        "sw1T": ([1024, 1024], F32, "ExternalInput"),
        "sw3T": ([1024, 1024], F32, "ExternalInput"),
        "sw2T": ([1024, 1024], F32, "ExternalInput"),
        "b1_pc": ([128, 72], F32, "ExternalInput"),
        "b3_pc": ([128, 72], F32, "ExternalInput"),
        "b2_rows": ([1, 18 * 1024], F32, "ExternalInput"),
        "ebias_b": ([128, 16], F32, "ExternalInput"),
        "ident": ([128, 128], F32, "ExternalInput"), "xout": _AO,
    }
    tp3 = lambda a: np.ascontiguousarray(
        np.transpose(np.asarray(a).astype(np.float32), (0, 2, 1)))
    b1 = np.concatenate([host_pcol(inputs["L2_ffn_b1"][e]) for e in range(16)]
                        + [host_pcol(inputs["L2_ffn_sb1"])], axis=1)
    b3 = np.concatenate([host_pcol(inputs["L2_ffn_b3"][e]) for e in range(16)]
                        + [host_pcol(inputs["L2_ffn_sb3"])], axis=1)
    b2 = np.concatenate([inputs["L2_ffn_b2"].reshape(-1),
                         inputs["L2_ffn_sb2"].reshape(-1),
                         np.zeros(1024, np.float32)]).reshape(1, -1)
    w = {
        "gwT": _tp(inputs["L2_ffn_gate_w"]),
        "w1T": tp3(inputs["L2_ffn_w1"]), "w3T": tp3(inputs["L2_ffn_w3"]),
        "w2T": tp3(inputs["L2_ffn_w2"]),
        "sw1T": _tp(inputs["L2_ffn_sw1"]), "sw3T": _tp(inputs["L2_ffn_sw3"]),
        "sw2T": _tp(inputs["L2_ffn_sw2"]),
        "b1_pc": _ca(b1), "b3_pc": _ca(b3), "b2_rows": _ca(b2),
        "ebias_b": np.ascontiguousarray(
            np.broadcast_to(inputs["L2_ffn_expert_bias"][None, :],
                            (128, 16)).astype(np.float32)),
        "ident": np.eye(128, dtype=np.float32),
    }
    return emit_L2F, specs, w, None


def st_l3a(inputs):
    specs = {
        "x": _AS, "wqT": ([1024, 1024], F32, "ExternalInput"),
        "wkT": ([1024, 1024], F32, "ExternalInput"),
        "wvT": ([1024, 1024], F32, "ExternalInput"),
        "woT": ([1024, 1024], F32, "ExternalInput"),
        "bq_pc": ([128, 8], F32, "ExternalInput"),
        "bk_pc": ([128, 8], F32, "ExternalInput"),
        "bo_pc": ([128, 8], F32, "ExternalInput"),
        "bv_row": ([1, 1024], F32, "ExternalInput"),
        "cmask": ([128, 2048], F32, "ExternalInput"),
        "ident": ([128, 128], F32, "ExternalInput"), "xout": _AO,
    }
    cm = np.zeros((128, 2048), np.float32)
    for m4 in range(4):
        for pp in range(128):
            cm[pp, m4 * 512:(m4 + 1) * 512] = np.where(
                np.arange(512) <= m4 * 128 + pp, 0.0, -1e30)
    w = {
        "wqT": _ca(inputs["L3_attn_Wq"]), "wkT": _ca(inputs["L3_attn_Wk"]),
        "wvT": _ca(inputs["L3_attn_Wv"]), "woT": _ca(inputs["L3_attn_Wo"]),
        "bq_pc": host_pcol(inputs["L3_attn_bq"]),
        "bk_pc": host_pcol(inputs["L3_attn_bk"]),
        "bo_pc": host_pcol(inputs["L3_attn_bo"]),
        "bv_row": _ca(inputs["L3_attn_bv"]).reshape(1, 1024),
        "cmask": cm, "ident": np.eye(128, dtype=np.float32),
    }
    return emit_L3A, specs, w, None


def l3f_noise():
    import jax
    with jax.default_device(jax.devices("cpu")[0]):
        import jax.numpy as jnp
        nz = np.asarray(jax.random.normal(jax.random.key(1), (8, 4096, 8),
                                          dtype=jnp.float32))
    return nz


def st_l3f(inputs):
    EMAP = [(0, 0), (1, 0), (2, 0), (0, 1), (1, 1), (2, 1), (0, 2), (1, 2)]
    bout = np.concatenate([
        (inputs["L3_ffn_e0_b2"][idx] if typ == 0 else
         inputs["L3_ffn_e1_b3"][idx] if typ == 1 else
         inputs["L3_ffn_e2_b2"][idx]).reshape(-1)
        for typ, idx in EMAP]).reshape(1, 8192)
    b1e = np.concatenate([host_pcol(inputs["L3_ffn_e0_b1"][j]) for j in range(3)]
                         + [host_pcol(inputs["L3_ffn_e1_b1"][j]) for j in range(3)]
                         + [host_pcol(inputs["L3_ffn_e2_b1"][j]) for j in range(2)],
                         axis=1)
    b2m = np.concatenate([host_pcol(inputs["L3_ffn_e1_b2"][j]) for j in range(3)],
                         axis=1)
    nz = l3f_noise()
    nanrow = np.zeros((128, 8), np.float32)
    nanrow[:, 0] = 0.125
    nanrow[:, 1] = 0.125
    specs = {
        "x": _AS,
        "usage": ([1, 8], F32, "ExternalInput"),
        "gb3_b": ([128, 8], F32, "ExternalInput"),
        "nan_b": ([128, 8], F32, "ExternalInput"),
        "aff_b": ([1, 8], F32, "ExternalInput"),
        "noise_pc": ([128, 256], F32, "ExternalInput"),
        "ident": ([128, 128], F32, "ExternalInput"),
        "gb1_pc": ([128, 8], F32, "ExternalInput"),
        "lns_pc": ([128, 8], F32, "ExternalInput"),
        "lno_pc": ([128, 8], F32, "ExternalInput"),
        "gb2_pc": ([128, 4], F32, "ExternalInput"),
        "affwT": ([1024, 8], F32, "ExternalInput"),
        "b1e_pc": ([128, 192], F32, "ExternalInput"),
        "b2m_pc": ([128, 48], F32, "ExternalInput"),
        "bout_rows": ([1, 8192], F32, "ExternalInput"),
        "gw1T": ([1024, 1024], F32, "ExternalInput"),
        "gw2T": ([1024, 512], F32, "ExternalInput"),
        "gw3T": ([512, 8], F32, "ExternalInput"),
        "e0w1": ([3, 1024, 4096], F32, "ExternalInput"),
        "e0w2": ([3, 4096, 1024], F32, "ExternalInput"),
        "e1w1": ([3, 1024, 2048], F32, "ExternalInput"),
        "e1w2": ([3, 2048, 2048], F32, "ExternalInput"),
        "e1w3": ([3, 2048, 1024], F32, "ExternalInput"),
        "e2w1": ([2, 1024, 3072], F32, "ExternalInput"),
        "e2w2": ([2, 3072, 1024], F32, "ExternalInput"),
        "xout": _AO,
    }
    w = {
        "usage": _ca(inputs["L3_ffn_expert_usage"]).reshape(1, 8),
        "gb3_b": np.ascontiguousarray(np.broadcast_to(
            inputs["L3_ffn_gate_b3"][None, :], (128, 8)).astype(np.float32)),
        "nan_b": nanrow,
        "aff_b": _ca(inputs["L3_ffn_aff_b"]).reshape(1, 8),
        "ident": np.eye(128, dtype=np.float32),
        "gb1_pc": host_pcol(inputs["L3_ffn_gate_b1"]),
        "lns_pc": host_pcol(inputs["L3_ffn_ln_scale"]),
        "lno_pc": host_pcol(inputs["L3_ffn_ln_offset"]),
        "gb2_pc": host_pcol(inputs["L3_ffn_gate_b2"]),
        "affwT": _ca(inputs["L3_ffn_aff_w"]),
        "b1e_pc": _ca(b1e), "b2m_pc": _ca(b2m), "bout_rows": _ca(bout),
        "gw1T": _ca(inputs["L3_ffn_gate_w1"]),
        "gw2T": _ca(inputs["L3_ffn_gate_w2"]),
        "gw3T": _ca(inputs["L3_ffn_gate_w3"]),
        "e0w1": _ca(inputs["L3_ffn_e0_w1"]), "e0w2": _ca(inputs["L3_ffn_e0_w2"]),
        "e1w1": _ca(inputs["L3_ffn_e1_w1"]), "e1w2": _ca(inputs["L3_ffn_e1_w2"]),
        "e1w3": _ca(inputs["L3_ffn_e1_w3"]),
        "e2w1": _ca(inputs["L3_ffn_e2_w1"]), "e2w2": _ca(inputs["L3_ffn_e2_w2"]),
    }

    def per_core(c):
        return {"noise_pc": np.ascontiguousarray(
            nz[c].reshape(32, 128, 8).transpose(1, 0, 2).reshape(128, 256))}
    return emit_L3F, specs, w, per_core


STAGES = [st_l0a, st_l0f, st_l1a, st_l1f, st_l2a, st_l2f, st_l3a]


def build_fused(inputs, nstages=8):
    from contextlib import ExitStack

    nc = bass.Bass("TRN2", target_bir_lowering=False)
    built = [fn(inputs) for fn in STAGES]
    f_emit, f_specs, f_w, f_pc = st_l3f(inputs)

    x_in = nc.dram_tensor("x_in", [1024, T], F32, kind="ExternalInput").ap()
    xout = nc.dram_tensor("xout", [1024, T], F32, kind="ExternalOutput").ap()
    xh = [nc.dram_tensor(f"xh{k}", [1024, T], F32, kind="Internal").ap()
          for k in range(7)]
    imp_b = nc.dram_tensor("imp_b", [1, T], F32, kind="Internal")
    impall_b = nc.dram_tensor("impall_b", [8, T], F32, kind="Internal")

    ext_shared = {}
    ext_percore = []

    def mk_ext(prefix, specs, w, per_core):
        d = {}
        for name, (shape, dt, kind) in specs.items():
            if name in ("x", "xout"):
                continue
            en = f"{prefix}_{name}"
            d[name] = nc.dram_tensor(en, list(shape), dt,
                                     kind="ExternalInput").ap()
            if name in w:
                ext_shared[en] = w[name]
            else:
                ext_percore.append((en, name, per_core))
        return d

    with ExitStack() as stack:
        p = Prog(nc)
        ctx = Ctx(nc, p, stack)
        ctx.init_common()
        cur = x_in
        for k in range(0 if nstages == 10 else min(nstages, 7)):
            emit, specs, w, per_core = built[k]
            d = mk_ext(f"s{k}", specs, w, per_core)
            d["x"] = cur
            d["xout"] = xout if k == nstages - 1 else xh[k]
            emit(ctx, d)
            ctx.afree_all()
            p.dma_barrier()
            p.recycle_streams()
            cur = xh[k]
        if 3 <= nstages <= 7:
            p.output_fence()
            block = stack.enter_context(nc.Block())
            p.finalize(block)
            def mk_in_map(core, x_fm):
                m = {"x_in": x_fm}
                m.update(ext_shared)
                for en, name, pc in ext_percore:
                    m[en] = pc(core)[name]
                return m
            return nc, mk_in_map
        emit_L3F_pre(ctx, {"x": cur, "imp_out": imp_b.ap()})
        ctx.afree_all()
        p.dma_barrier()
        p.recycle_streams()
        p.custom_dma("pool", lambda g: g.collective_compute(
            "AllGather", mybir.AluOpType.bypass,
            replica_groups=[list(range(8))],
            ins=[imp_b.ap().opt()], outs=[impall_b.ap().opt()]),
            stream="cc_imp")
        p.dma_barrier()
        p.recycle_streams()
        if nstages == 9:
            t9 = ctx.aget("dbg9", [8, T], F32)
            p.dma("sp", t9[:], impall_b.ap(), writes=["dbg9"])
            p.dma("sp", xout[0:8, :], t9[:], reads=["dbg9"],
                  stream="dma_st_dbg9")
            p.output_fence()
            block = stack.enter_context(nc.Block())
            p.finalize(block)
            def mk_in_map(core, x_fm):
                m = {"x_in": x_fm}
                m.update(ext_shared)
                for en, name, pc in ext_percore:
                    m[en] = pc(core)[name]
                return m
            return nc, mk_in_map
        d = mk_ext("s7", f_specs, f_w, f_pc)
        d["x"] = cur
        d["xout"] = xout
        d["impall"] = impall_b.ap()
        d["imp_row"] = imp_b.ap()
        f_emit(ctx, d)
        p.output_fence()
        block = stack.enter_context(nc.Block())
        p.finalize(block)

    def mk_in_map(core, x_fm):
        m = {"x_in": x_fm}
        m.update(ext_shared)
        for en, name, pc in ext_percore:
            m[en] = pc(core)[name]
        return m
    return nc, mk_in_map



def _fm(x_btd, core):
    return np.ascontiguousarray(np.asarray(x_btd)[core].T.astype(np.float32))


def kernel(**inputs):
    x0 = inputs["L0_attn_x"]
    nc, mk_in_map = build_fused(inputs)
    in_maps = [mk_in_map(c, _fm(x0, c)) for c in range(8)]
    from concourse.bass_utils import run_bass_kernel_spmd
    res = run_bass_kernel_spmd(nc, in_maps, core_ids=list(range(8)))
    out = np.stack([np.asarray(res.results[c]["xout"]).T for c in range(8)],
                   axis=0)
    return np.ascontiguousarray(out.astype(np.float32))
